# version 05
# baseline (speedup 1.0000x reference)
import numpy as np
import ml_dtypes
import concourse.bass as bass
import concourse.mybir as mybir
from concourse.bass_utils import run_bass_kernel_spmd
from contextlib import ExitStack

F32 = mybir.dt.float32
BF16 = mybir.dt.bfloat16
I32 = mybir.dt.int32
AF = mybir.ActivationFunctionType
ALU = mybir.AluOpType
AX = mybir.AxisListType

DM, DI, DS, DC, HF = 1024, 2048, 16, 4, 2816
CD, CI, CH = DM // 128, DI // 128, HF // 128
TT = 512
LN_EPS = 1e-5
QMAXF = 127.0
NCORE = 8
ENGS = ("sync", "vector", "scalar", "tensor", "gpsimd")


class KB:
    def __init__(self, S, n_cores):
        self.S = S
        self.NT = S // TT
        self.n_cores = n_cores
        self.nc = bass.Bass("TRN2", target_bir_lowering=False)
        self.es = ExitStack()
        self.q = {e: [] for e in ENGS}
        self._mk_dram()

    def _mk_dram(self):
        nc, S = self.nc, self.S
        self.di = {}

        def din(name, shape, dt=F32):
            self.di[name] = nc.dram_tensor(name, shape, dt,
                                           kind="ExternalInput").ap()

        din("x", [DM, S])
        din("W_in", [DM, 2 * DI])
        din("W_dt", [DI, DI])
        din("W_out", [DI, DM])
        din("WxB", [128, CI * DS])
        din("bxB", [1, DS])
        din("w_gate", [HF, DM])
        din("w_up", [HF, DM])
        din("w_down", [DM, HF])
        din("Alog", [128, CI * DS])
        din("convw", [128, CI * DC])
        din("convb", [128, CI])
        din("Dvec", [128, CI])
        din("bxh", [128, CI])
        din("bz", [128, CI])
        din("bdt", [128, CI])
        din("bout", [128, CD])
        din("clipx", [1, 2])
        din("clipd", [1, 2])
        din("cvals", [128, 8])
        din("eyeb", [128, 128], BF16)
        din("eyef", [128, 128], F32)
        din("onesb", [128, 2], BF16)
        din("ones1", [1, 128])
        din("onesrow_b", [1, TT], BF16)
        din("oneh", [DS, DS * 128], BF16)
        self.d_out = nc.dram_tensor("out", [DM, S], F32,
                                    kind="ExternalOutput").ap()

        def scr(name, shape, dt):
            return nc.dram_tensor(name, shape, dt, kind="Internal").ap()

        self.s_wdtb = scr("s_wdtb", [DI, DI], BF16)
        self.s_woutb = scr("s_woutb", [DI, DM], BF16)
        self.s_wgT = scr("s_wgT", [CH, CD * 128 * 128], BF16)
        self.s_wuT = scr("s_wuT", [CH, CD * 128 * 128], BF16)
        self.s_wdT = scr("s_wdT", [CH, CD * 128 * 128], BF16)
        self.s_h = scr("s_h", [CH, 128 * S], BF16)
        self.s_sz = scr("s_sz", [CI, 128 * S], BF16)
        self.s_yg = scr("s_yg", [CI, 128 * S], BF16)
        self.s_x1 = scr("s_x1", [CD, 128 * S], BF16)
        self.s_brep = scr("s_brep", [6, 128 * S], BF16)
        self.s_cci = scr("s_cci", [1, 2], F32)
        self.s_cco = scr("s_cco", [1, 2], F32)

    def mk_mem(self):
        nc, S, es = self.nc, self.S, self.es

        def sb(name, shape, dt=F32):
            return es.enter_context(nc.sbuf_tensor(name, shape, dt))

        self.bf = sb("bf_arena", [128, 32 * S], BF16)
        self.f32 = sb("f32_arena", [128, 6 * S], F32)
        self.int_t = sb("int_t", [128, S], I32)
        self.c = {}
        for name, shape, dt in [
            ("negA", [128, CI * DS], F32), ("convw", [128, CI * DC], F32),
            ("convb", [128, CI], F32), ("D", [128, CI], F32),
            ("bxh", [128, CI], F32), ("bz", [128, CI], F32),
            ("bdt", [128, CI], F32), ("bout", [128, CD], F32),
            ("WxB", [128, CI * DS], BF16), ("bxB", [1, DS], BF16),
            ("eyeb", [128, 128], BF16), ("eyef", [128, 128], F32),
            ("onesb", [128, 2], BF16),
            ("ones1", [1, 128], F32), ("onesrow_b", [1, TT], BF16),
            ("oneh", [DS, DS * 128], BF16),
            ("clipx", [1, 2], F32), ("clipd", [1, 2], F32),
            ("cvals", [128, 8], F32),
            ("swg", [128, CH], F32), ("swu", [128, CH], F32),
            ("swd", [128, CD], F32),
            ("sxw_g", [128, CH], F32), ("sxw_u", [128, CH], F32),
            ("sxw_d", [128, CD], F32),
            ("amax", [128, 32], F32), ("r1", [128, 2], F32),
            ("sc", [1, 8], F32), ("pp", [128, 4], F32),
            ("pad8", [128, 8], BF16),
        ]:
            self.c[name] = sb("c_" + name, shape, dt)
        self.sems = {}
        for n in ["pe", "vec", "scl", "gps", "cc"]:
            self.sems[n] = es.enter_context(nc.semaphore(n))
        self.cnt = {n: 0 for n in self.sems}
        self.sempool = [es.enter_context(nc.semaphore(f"dmas{i}"))
                        for i in range(44)]
        self.sempool_next = 0
        self.streams = {}
        self.ps_main = es.enter_context(
            nc.psum_tensor("ps_main", [128, 4 * TT], F32))
        self._ps_tb_scope = ExitStack()
        self.ps_tb = self._ps_tb_scope.enter_context(
            nc.psum_tensor("ps_tb", [128, 1024], BF16))
        self.ps_aux = None
        for col, val in enumerate([LN_EPS, 1.0, -1.0, -0.5]):
            self.nc.const_aps.register(F32, val, self.c["cvals"][:, col:col + 1])

    def close_p0_psum(self):
        self._ps_tb_scope.close()
        self.ps_tb = None
        self.ps_aux = self.es.enter_context(
            self.nc.psum_tensor("ps_aux", [128, 4 * TT], F32))

    def op(self, eng, fn, inc=None, n=1):
        if inc is None:
            self.q[eng].append(fn)
            return None
        self.cnt[inc] += n
        sem = self.sems[inc]
        self.q[eng].append(
            lambda e, fn=fn, sem=sem, n=n: fn(e).then_inc(sem, n))
        return self.cnt[inc]

    def wait(self, eng, semname, val):
        if val and val > 0:
            sem = self.sems[semname]
            self.q[eng].append(
                lambda e, sem=sem, val=val: e.wait_ge(sem, val))

    def _stream(self, name):
        st = self.streams.get(name)
        if st is None:
            a = self.sempool[self.sempool_next]
            b = self.sempool[self.sempool_next + 1]
            self.sempool_next += 2
            st = {"sems": [a, b], "cnt": [0, 0], "par": 0}
            self.streams[name] = st
        return st

    def dma(self, eng, fn, stream):
        st = self._stream(stream)
        p = st["par"]
        sem = st["sems"][p]
        if st["cnt"][p] > 0:
            v = st["cnt"][p]
            self.q[eng].append(
                lambda e, sem=sem, v=v: e.wait_ge(sem, v))
        st["cnt"][p] += 16
        v = st["cnt"][p]
        self.q[eng].append(
            lambda e, fn=fn, sem=sem: fn(e).then_inc(sem, 16))
        st["par"] = 1 - p
        return (sem, v)

    def wtok(self, eng, tok):
        if tok is None:
            return
        sem, v = tok
        self.q[eng].append(lambda e, sem=sem, v=v: e.wait_ge(sem, v))

    def swait(self, eng, stream):
        st = self.streams.get(stream)
        if not st:
            return
        for p in (0, 1):
            if st["cnt"][p] > 0:
                sem, v = st["sems"][p], st["cnt"][p]
                self.q[eng].append(
                    lambda e, sem=sem, v=v: e.wait_ge(sem, v))

    def swait_all_dma(self, eng):
        for name in list(self.streams):
            self.swait(eng, name)

    def drain(self, eng):
        self.q[eng].append(lambda e: e.drain())

    def barrier(self, writers, waiters):
        snap = []
        sync_snap = []
        for w in writers:
            if w == "sync":
                for name, st in self.streams.items():
                    for p in (0, 1):
                        if st["cnt"][p] > 0:
                            sync_snap.append((st["sems"][p], st["cnt"][p]))
            else:
                s = {"vector": "vec", "scalar": "scl", "tensor": "pe",
                     "gpsimd": "gps"}[w]
                snap.append((w, s, self.cnt[s]))
        for eng in waiters:
            for w, s, v in snap:
                if eng == w:
                    self.drain(eng)
                else:
                    self.wait(eng, s, v)
            for sem, v in sync_snap:
                self.q[eng].append(
                    lambda e, sem=sem, v=v: e.wait_ge(sem, v))

    def bfv(self, off, cols):
        return self.bf[:, off:off + cols]

    def f32v(self, off, cols):
        return self.f32[:, off:off + cols]


def build(S, n_cores):
    kb = KB(S, n_cores)
    with kb.es:
        kb.mk_mem()
        _emit(kb)
        blk = kb.es.enter_context(kb.nc.Block())
        for eng in ENGS:
            ops = kb.q[eng]
            if ops:
                getattr(blk, eng)(lambda e, ops=ops: [f(e) for f in ops])
    return kb


def _emit(kb: KB):
    S, NT = kb.S, kb.NT
    c, di = kb.c, kb.di
    int_t = kb.int_t
    ps_main = kb.ps_main

    def U(k):
        return k * S

    O_X, O_LN, O_XQ = U(0), U(8), U(16)
    O_SG, O_SU, O_HFF = U(24), U(25), U(26)
    O_WS = [U(27), U(27) + 1024, U(28), U(28) + 1024]
    O_T0, O_T1 = U(29), U(29) + TT
    O_T2, O_T3 = U(29) + 2 * TT, U(29) + 3 * TT
    O_STG = U(30)
    O_AB = U(31)
    O_BLK = U(24)

    def xh_off(ci):
        return U(8 + ci) if ci < 8 else U(ci - 8)

    def xhp_off(ci):
        return U(ci) if ci < 8 else U(16 + ci - 8)

    def brep_off(s):
        return U(16 + s)

    NBREP_RES = 10
    O_BS = [U(26), U(27)]
    O_WDT = U(28)
    O_DU = U(29)
    O_SZ = [U(30), U(30) + TT]
    O_DBU = U(31)
    O_BMAT = U(29)

    FA, FB, FH, FY, FL, FA2 = 0, S, 2 * S, 3 * S, 4 * S, 5 * S
    ROW_M, ROW_T, ROW_R = FA, FA + TT, FA + 2 * TT

    W, OP, DR = kb.wait, kb.op, kb.drain

    def rowv(off):
        return kb.f32[0:1, off:off + TT]

    def bmatv(t0, cols):
        return kb.bf[0:DS, O_BMAT + t0:O_BMAT + t0 + cols]

    for name in ["cvals", "convw", "convb", "bxh", "bz", "bdt", "bout",
                 "eyeb", "eyef", "onesb", "ones1", "onesrow_b", "oneh",
                 "clipx", "clipd"]:
        kb.dma("sync", lambda e, n=name: e.dma_start(out=c[n][:], in_=di[n]),
               "cst")
    kb.dma("sync", lambda e: e.dma_start(out=c["D"][:], in_=di["Dvec"]),
           "cst")
    kb.dma("sync", lambda e: e.dma_start(out=c["negA"][:], in_=di["Alog"]),
           "cst")
    kb.dma("sync", lambda e: e.dma_start(out=kb.f32[:, 0:CI * DS],
                                         in_=di["WxB"]), "cst")
    kb.dma("sync", lambda e: e.dma_start(
        out=kb.f32[0:1, CI * DS:CI * DS + DS], in_=di["bxB"]), "cst")
    kb.swait("scalar", "cst")
    OP("scalar", lambda e: e.copy(c["WxB"][:], kb.f32[:, 0:CI * DS]))
    OP("scalar", lambda e: e.copy(c["bxB"][:],
                                  kb.f32[0:1, CI * DS:CI * DS + DS]), "scl")
    OP("scalar", lambda e: e.activation(c["negA"][:], c["negA"][:], AF.Exp))
    DR("scalar")
    OP("scalar", lambda e: e.mul(c["negA"][:], c["negA"][:], -1.0), "scl")
    kb.swait("vector", "cst")
    OP("vector", lambda e: e.memset(c["pad8"][:], 0.0), "vec")

    xcopy_marks = []
    W("sync", "scl", kb.cnt["scl"])
    for ch in range(CD):
        slot = FL if ch % 2 == 0 else FA
        if ch >= 2:
            W("sync", "scl", xcopy_marks[ch - 2])
        tok = kb.dma("sync", lambda e, ch=ch, slot=slot: e.dma_start(
            out=kb.f32v(slot, S), in_=di["x"][ch * 128:(ch + 1) * 128, :]),
            "xw")
        kb.wtok("scalar", tok)
        xcopy_marks.append(OP("scalar", lambda e, ch=ch, slot=slot: e.copy(
            kb.bfv(O_X + ch * S, S), kb.f32v(slot, S)), "scl"))

    qmarks = []

    def quant_transpose(wname, nch, kd, stile, dstT, scatter_rows=False):
        nkb = kd // 128
        for i in range(nch):
            slot = 0 if i % 2 == 0 else 2816
            if len(qmarks) >= 2:
                W("sync", "vec", qmarks[-2])
            tok = kb.dma("sync", lambda e, w=wname, i=i, slot=slot, kd=kd:
                         e.dma_start(out=kb.f32[:, slot:slot + kd],
                                     in_=di[w][i * 128:(i + 1) * 128, :]),
                         "xw")
            kb.wtok("vector", tok)
            OP("vector", lambda e, slot=slot, kd=kd: e.tensor_reduce(
                c["r1"][:, 0:1], kb.f32[:, slot:slot + kd], AX.X, ALU.max,
                apply_absolute_value=True))
            DR("vector")
            OP("vector", lambda e, st=stile, i=i: e.tensor_scalar(
                c[st][:, i:i + 1], c["r1"][:, 0:1], 1.0 / QMAXF, None,
                ALU.mult))
            OP("vector", lambda e: e.reciprocal(c["r1"][:, 1:2],
                                                c["r1"][:, 0:1]))
            DR("vector")
            W("vector", "pe", kb.cnt["pe"])
            mlast = None
            for p0 in range(0, kd, S):
                pn = min(S, kd - p0)
                OP("vector", lambda e, slot=slot, p0=p0, pn=pn:
                   e.tensor_scalar(int_t[:, 0:pn],
                                   kb.f32[:, slot + p0:slot + p0 + pn],
                                   c["r1"][:, 1:2], QMAXF,
                                   ALU.mult, ALU.mult))
                DR("vector")
                mlast = OP("vector", lambda e, p0=p0, pn=pn: e.tensor_copy(
                    kb.bf[:, O_BLK + p0:O_BLK + p0 + pn], int_t[:, 0:pn]), "vec")
                DR("vector")
            qmarks.append(mlast)
            kb.swait("scalar", "stw")
            for g0 in range(0, nkb, 8):
                gn = min(8, nkb - g0)
                W("tensor", "vec", kb.cnt["vec"])
                W("tensor", "scl", kb.cnt["scl"])
                for j in range(gn):
                    OP("tensor", lambda e, g0=g0, j=j, pstb=kb.ps_tb:
                       e.transpose(
                           pstb[:, j * 128:(j + 1) * 128],
                           kb.bf[:, O_BLK + (g0 + j) * 128:O_BLK + (g0 + j + 1) * 128],
                           c["eyeb"][:]), "pe" if j == gn - 1 else None)
                W("scalar", "pe", kb.cnt["pe"])
                for j in range(gn):
                    OP("scalar", lambda e, g0=g0, j=j, pstb=kb.ps_tb: e.copy(
                        kb.bfv(O_STG + (g0 + j) * 128, 128),
                        pstb[:, j * 128:(j + 1) * 128]),
                       "scl" if j == gn - 1 else None)
            W("sync", "scl", kb.cnt["scl"])
            if not scatter_rows:
                kb.dma("sync", lambda e, dstT=dstT, i=i, nkb=nkb: e.dma_start(
                    out=dstT[i, 0:nkb * 128 * 128].rearrange(
                        "(k p j) -> p k j", p=128, j=128),
                    in_=kb.bfv(O_STG, nkb * 128).rearrange(
                        "p (k j) -> p k j", j=128)), "stw")
            else:
                kb.dma("sync", lambda e, dstT=dstT, i=i, nkb=nkb: e.dma_start(
                    out=dstT[:, i * 128 * 128:(i + 1) * 128 * 128].rearrange(
                        "k (p j) -> p k j", j=128),
                    in_=kb.bfv(O_STG, nkb * 128).rearrange(
                        "p (k j) -> p k j", j=128)), "stw")

    kb.barrier(["scalar"], ["sync"])
    quant_transpose("w_gate", CH, DM, "swg", kb.s_wgT)
    quant_transpose("w_up", CH, DM, "swu", kb.s_wuT)
    quant_transpose("w_down", CD, HF, "swd", kb.s_wdT,
                    scatter_rows=True)

    cast_marks = []
    for wname, dst, nch, kd in [("W_dt", kb.s_wdtb, CI, DI),
                                ("W_out", kb.s_woutb, CI, DM)]:
        for i in range(nch):
            slot = 0 if i % 2 == 0 else 2816
            if len(cast_marks) >= 2:
                W("sync", "scl", cast_marks[-2])
            tok = kb.dma("sync", lambda e, w=wname, i=i, slot=slot, kd=kd:
                         e.dma_start(out=kb.f32[:, slot:slot + kd],
                                     in_=di[w][i * 128:(i + 1) * 128, :]),
                         "xw")
            kb.wtok("scalar", tok)
            kb.swait("scalar", "stw")
            cast_marks.append(OP("scalar", lambda e, slot=slot, kd=kd: e.copy(
                kb.bfv(O_STG, kd), kb.f32[:, slot:slot + kd]), "scl"))
            W("sync", "scl", kb.cnt["scl"])
            kb.dma("sync", lambda e, dst=dst, i=i, kd=kd: e.dma_start(
                out=dst[i * 128:(i + 1) * 128, :], in_=kb.bfv(O_STG, kd)),
                "stw")

    kb.barrier(["sync", "vector", "scalar", "tensor"],
               ["sync", "vector", "scalar", "tensor"])
    kb.close_p0_psum()
    ps_aux = kb.ps_aux

    def layernorm(src_o, dst_o):
        prev_stats_read = 0
        for t in range(NT):
            tn = t * TT
            sq_marks = []
            pe_sq = []
            W("tensor", "vec", prev_stats_read)
            for ch in range(CD):
                sq = O_T0 if ch % 2 == 0 else O_T1
                if ch >= 2:
                    W("scalar", "pe", pe_sq[ch - 2])
                sq_marks.append(OP(
                    "scalar", lambda e, ch=ch, sq=sq, src_o=src_o, tn=tn:
                    e.activation(kb.bfv(sq, TT),
                                 kb.bfv(src_o + ch * S + tn, TT), AF.Square),
                    "scl"))
                OP("tensor", lambda e, ch=ch, src_o=src_o, tn=tn: e.matmul(
                    ps_aux[0:1, 0:TT], c["onesb"][:, 0:1],
                    kb.bfv(src_o + ch * S + tn, TT),
                    start=(ch == 0), stop=(ch == CD - 1),
                    skip_group_check=True))
                W("tensor", "scl", sq_marks[ch])
                pe_sq.append(OP("tensor", lambda e, ch=ch, sq=sq: e.matmul(
                    ps_aux[0:1, TT:2 * TT], c["onesb"][:, 0:1],
                    kb.bfv(sq, TT),
                    start=(ch == 0), stop=(ch == CD - 1),
                    skip_group_check=True), "pe"))
            W("vector", "pe", kb.cnt["pe"])
            OP("vector", lambda e: e.tensor_scalar(
                rowv(ROW_M), ps_aux[0:1, 0:TT], 1.0 / DM, None, ALU.mult))
            OP("vector", lambda e: e.tensor_scalar(
                rowv(ROW_T), ps_aux[0:1, TT:2 * TT], 1.0 / DM, None,
                ALU.mult))
            DR("vector")
            OP("vector", lambda e: e.tensor_mul(rowv(ROW_R), rowv(ROW_M),
                                                rowv(ROW_M)))
            DR("vector")
            OP("vector", lambda e: e.tensor_sub(rowv(ROW_T), rowv(ROW_T),
                                                rowv(ROW_R)), "vec")
            stats_read = kb.cnt["vec"]
            W("scalar", "vec", stats_read)
            OP("scalar", lambda e: e.activation(rowv(ROW_R), rowv(ROW_T),
                                                AF.Ln, bias=LN_EPS))
            DR("scalar")
            OP("scalar", lambda e: e.activation(rowv(ROW_R), rowv(ROW_R),
                                                AF.Exp, scale=-0.5), "scl")
            W("vector", "scl", kb.cnt["scl"])
            OP("vector", lambda e: e.tensor_mul(rowv(ROW_M), rowv(ROW_M),
                                                rowv(ROW_R)), "vec")
            W("tensor", "vec", kb.cnt["vec"])
            OP("tensor", lambda e: e.matmul(
                ps_aux[:, 2 * TT:3 * TT], c["ones1"][:],
                kb.f32[0:1, ROW_R:ROW_R + TT],
                start=True, stop=True))
            OP("tensor", lambda e: e.matmul(
                ps_aux[:, 3 * TT:4 * TT], c["ones1"][:],
                kb.f32[0:1, ROW_M:ROW_M + TT],
                start=True, stop=True), "pe")
            W("scalar", "pe", kb.cnt["pe"])
            OP("scalar", lambda e: e.copy(kb.bfv(O_AB, TT),
                                          ps_aux[:, 2 * TT:3 * TT]))
            OP("scalar", lambda e: e.copy(kb.bfv(O_AB + TT, TT),
                                          ps_aux[:, 3 * TT:4 * TT]), "scl")
            W("vector", "scl", kb.cnt["scl"])
            for ch in range(CD):
                d = dst_o + ch * S + tn
                OP("vector", lambda e, d=d, ch=ch, src_o=src_o, tn=tn:
                   e.tensor_mul(kb.bfv(d, TT),
                                kb.bfv(src_o + ch * S + tn, TT),
                                kb.bfv(O_AB, TT)))
                DR("vector")
                OP("vector", lambda e, d=d: e.tensor_sub(
                    kb.bfv(d, TT), kb.bfv(d, TT), kb.bfv(O_AB + TT, TT)))
                DR("vector")
                OP("vector", lambda e, d=d, ch=ch, t=t:
                   e.tensor_reduce(c["amax"][:, t * CD + ch:t * CD + ch + 1],
                                   kb.bfv(d, TT), AX.X, ALU.max,
                                   apply_absolute_value=True),
                   "vec" if ch == CD - 1 else None)
            prev_stats_read = stats_read

    def exchange(clip_name, amax_cols):
        DR("vector")
        OP("vector", lambda e, n=amax_cols: e.tensor_reduce(
            c["r1"][:, 0:1], c["amax"][:, 0:n], AX.X, ALU.max), "vec")
        W("tensor", "vec", kb.cnt["vec"])
        OP("tensor", lambda e: e.transpose(
            ps_aux[0:1, 0:128], c["r1"][:, 0:1], c["eyef"][:]), "pe")
        W("vector", "pe", kb.cnt["pe"])
        OP("vector", lambda e: e.reduce_max(
            c["sc"][0:1, 0:1], ps_aux[0:1, 0:128], AX.X), "vec")
        W("sync", "vec", kb.cnt["vec"])
        tok = kb.dma("sync", lambda e: e.dma_start(
            out=kb.s_cci[0:1, 0:1], in_=c["sc"][0:1, 0:1]), "stc")
        kb.wtok("gpsimd", tok)
        OP("gpsimd", lambda e: e.collective_compute(
            "AllReduce", ALU.max,
            replica_groups=[list(range(kb.n_cores))],
            ins=[kb.s_cci[0:1, 0:1]], outs=[kb.s_cco[0:1, 0:1]]), "cc")
        W("sync", "cc", kb.cnt["cc"])
        tok = kb.dma("sync", lambda e: e.dma_start(
            out=c["sc"][0:1, 1:2], in_=kb.s_cco[0:1, 0:1]), "ccl")
        kb.wtok("scalar", tok)
        OP("scalar", lambda e, cn=clip_name: e.activation(
            c["sc"][0:1, 2:3], c[cn][0:1, 0:1], AF.Exp, scale=-1.0), "scl")
        W("vector", "scl", kb.cnt["scl"])
        OP("vector", lambda e: e.tensor_scalar(
            c["sc"][0:1, 3:4], c["sc"][0:1, 2:3], 1.0, None, ALU.add))
        DR("vector")
        OP("vector", lambda e: e.reciprocal(c["sc"][0:1, 2:3],
                                            c["sc"][0:1, 3:4]))
        DR("vector")
        OP("vector", lambda e: e.tensor_mul(c["sc"][0:1, 3:4],
                                            c["sc"][0:1, 2:3],
                                            c["sc"][0:1, 1:2]))
        DR("vector")
        OP("vector", lambda e: e.tensor_scalar(
            c["sc"][0:1, 4:5], c["sc"][0:1, 3:4], 1.0 / QMAXF, None,
            ALU.mult))
        DR("vector")
        OP("vector", lambda e: e.reciprocal(c["sc"][0:1, 5:6],
                                            c["sc"][0:1, 4:5]), "vec")
        W("tensor", "vec", kb.cnt["vec"])
        OP("tensor", lambda e: e.matmul(
            ps_aux[:, 0:2], c["ones1"][:], c["sc"][0:1, 4:6],
            start=True, stop=True), "pe")
        W("vector", "pe", kb.cnt["pe"])
        OP("vector", lambda e: e.tensor_copy(c["pp"][:, 0:2],
                                             ps_aux[:, 0:2]), "vec")
        DR("vector")

    def quantize(src_o, dst_o, nch):
        for ch in range(nch):
            OP("vector", lambda e, ch=ch, src_o=src_o: e.tensor_scalar(
                int_t[:, 0:S], kb.bfv(src_o + ch * S, S), c["pp"][:, 1:2],
                None, ALU.mult))
            DR("vector")
            OP("vector", lambda e, ch=ch, dst_o=dst_o: e.tensor_scalar(
                kb.bfv(dst_o + ch * S, S), int_t[:, 0:S], -128.0, 127.0,
                ALU.max, ALU.min), "vec")
            DR("vector")

    def scalevec(dst, srctile, ncols, extra):
        OP("vector", lambda e, dst=dst, s=srctile, n=ncols, x=extra:
           e.tensor_scalar(c[dst][:, 0:n], c[s][:, 0:n], c["pp"][:, 0:1],
                           x, ALU.mult, ALU.mult), "vec")
        DR("vector")

    def stile(scratch, k, t):
        return scratch[k, :].rearrange("(p s) -> p s", p=128)[
            :, t * TT:(t + 1) * TT]

    def ffn(final):
        layernorm(O_X, O_LN)
        exchange("clipx", NT * CD)
        scalevec("sxw_g", "swg", CH, 1.0)
        scalevec("sxw_u", "swu", CH, 1.0)
        quantize(O_LN, O_XQ, CD)
        xq_ready = kb.cnt["vec"]

        pe_w_marks = []
        gate_evac = {}
        up_evac = {t: 0 for t in range(NT)}
        h_marks = {}
        for i in range(CH):
            wg_s, wu_s = O_WS[i % 2], O_WS[2 + i % 2]
            if i >= 2:
                W("sync", "pe", pe_w_marks[i - 2])
            tokg = kb.dma("sync", lambda e, i=i, wg_s=wg_s: e.dma_start(
                out=kb.bfv(wg_s, 1024).rearrange("p (k j) -> p k j", j=128),
                in_=kb.s_wgT[i, :].rearrange("(k p j) -> p k j",
                                             p=128, j=128)), "wg")
            toku = kb.dma("sync", lambda e, i=i, wu_s=wu_s: e.dma_start(
                out=kb.bfv(wu_s, 1024).rearrange("p (k j) -> p k j", j=128),
                in_=kb.s_wuT[i, :].rearrange("(k p j) -> p k j",
                                             p=128, j=128)), "wu")
            kb.wtok("tensor", tokg)
            kb.wtok("tensor", toku)
            if i == 0:
                W("tensor", "vec", xq_ready)
            if i > 0:
                for t in range(NT):
                    W("tensor", "scl", gate_evac[t])
            for k in range(CD):
                for t in range(NT):
                    OP("tensor", lambda e, t=t, k=k, wg_s=wg_s: e.matmul(
                        ps_main[:, t * TT:(t + 1) * TT],
                        kb.bfv(wg_s + k * 128, 128),
                        kb.bfv(O_XQ + k * S + t * TT, TT),
                        start=(k == 0), stop=(k == CD - 1),
                        skip_group_check=True),
                       "pe" if (k == CD - 1) else None)
            gmm = kb.cnt["pe"]
            W("scalar", "pe", gmm)
            if i > 0:
                W("scalar", "vec", h_marks[i - 1])
            for t in range(NT):
                gate_evac[t] = OP(
                    "scalar", lambda e, t=t, i=i: e.activation(
                        kb.bfv(O_SG + t * TT, TT),
                        ps_main[:, t * TT:(t + 1) * TT], AF.Silu,
                        scale=c["sxw_g"][:, i:i + 1]), "scl")
            if i > 0:
                for t in range(NT):
                    W("tensor", "scl", up_evac[t])
            for k in range(CD):
                for t in range(NT):
                    OP("tensor", lambda e, t=t, k=k, wu_s=wu_s: e.matmul(
                        ps_aux[:, t * TT:(t + 1) * TT],
                        kb.bfv(wu_s + k * 128, 128),
                        kb.bfv(O_XQ + k * S + t * TT, TT),
                        start=(k == 0), stop=(k == CD - 1),
                        skip_group_check=True),
                       "pe" if (k == CD - 1) else None)
            pe_w_marks.append(kb.cnt["pe"])
            W("scalar", "pe", kb.cnt["pe"])
            for t in range(NT):
                up_evac[t] = OP(
                    "scalar", lambda e, t=t, i=i: e.activation(
                        kb.bfv(O_SU + t * TT, TT),
                        ps_aux[:, t * TT:(t + 1) * TT], AF.Identity,
                        scale=c["sxw_u"][:, i:i + 1]), "scl")
            W("vector", "scl", kb.cnt["scl"])
            if i > 0:
                kb.swait("vector", "sth")
            OP("vector", lambda e: e.tensor_mul(
                kb.bfv(O_HFF, S), kb.bfv(O_SG, S), kb.bfv(O_SU, S)))
            DR("vector")
            h_marks[i] = OP("vector", lambda e, i=i: e.tensor_reduce(
                c["amax"][:, i:i + 1], kb.bfv(O_HFF, S), AX.X, ALU.max,
                apply_absolute_value=True), "vec")
            W("sync", "vec", h_marks[i])
            kb.dma("sync", lambda e, i=i: e.dma_start(
                out=kb.s_h[i, :].rearrange("(p s) -> p s", p=128),
                in_=kb.bfv(O_HFF, S)), "sth")

        exchange("clipd", CH)
        scalevec("sxw_d", "swd", CD, 0.5)
        kb.swait("sync", "sth")

        dn_pe = {}
        hq_vec = {}
        x_evac = {}
        W("tensor", "scl", kb.cnt["scl"])
        W("tensor", "vec", kb.cnt["vec"])
        for t in range(NT):
            for k in range(CH):
                wslot = O_WS[k % 2]
                hslot = O_T0 if k % 2 == 0 else O_T1
                qslot = O_T2 if k % 2 == 0 else O_T3
                prev = (t, k - 2) if k >= 2 else (t - 1, CH - 2 + k)
                if prev in dn_pe:
                    W("sync", "pe", dn_pe[prev])
                if prev in hq_vec:
                    W("sync", "vec", hq_vec[prev])
                tokw = kb.dma("sync", lambda e, k=k, wslot=wslot:
                              e.dma_start(
                                  out=kb.bfv(wslot, 1024).rearrange(
                                      "p (d j) -> p d j", j=128),
                                  in_=kb.s_wdT[k, :].rearrange(
                                      "(d p j) -> p d j", p=128, j=128)),
                              "wg")
                tokh = kb.dma("sync", lambda e, k=k, t=t, hslot=hslot:
                              e.dma_start(out=kb.bfv(hslot, TT),
                                          in_=stile(kb.s_h, k, t)), "hl")
                kb.wtok("vector", tokh)
                OP("vector", lambda e, hslot=hslot: e.tensor_scalar(
                    int_t[:, 0:TT], kb.bfv(hslot, TT), c["pp"][:, 1:2],
                    None, ALU.mult))
                DR("vector")
                hq_vec[(t, k)] = OP(
                    "vector", lambda e, qslot=qslot: e.tensor_scalar(
                        kb.bfv(qslot, TT), int_t[:, 0:TT], -128.0, 127.0,
                        ALU.max, ALU.min), "vec")
                DR("vector")
                kb.wtok("tensor", tokw)
                W("tensor", "vec", hq_vec[(t, k)])
                if k == 0 and (t - 1) in x_evac:
                    W("tensor", "vec", x_evac[t - 1])
                for d in range(CD):
                    bank = ps_main if d < 4 else ps_aux
                    OP("tensor", lambda e, d=d, k=k, wslot=wslot,
                       qslot=qslot, bank=bank: e.matmul(
                           bank[:, (d % 4) * TT:(d % 4 + 1) * TT],
                           kb.bfv(wslot + d * 128, 128), kb.bfv(qslot, TT),
                           start=(k == 0), stop=(k == CH - 1),
                           skip_group_check=True),
                       "pe" if d == CD - 1 else None)
                dn_pe[(t, k)] = kb.cnt["pe"]
            W("vector", "pe", kb.cnt["pe"])
            for d in range(CD):
                bank = ps_main if d < 4 else ps_aux
                if not final:
                    OP("vector", lambda e, d=d, t=t, bank=bank:
                       e.scalar_tensor_tensor(
                           kb.bfv(O_X + d * S + t * TT, TT),
                           bank[:, (d % 4) * TT:(d % 4 + 1) * TT],
                           c["sxw_d"][:, d:d + 1],
                           kb.bfv(O_X + d * S + t * TT, TT),
                           ALU.mult, ALU.add), "vec")
                    DR("vector")
                else:
                    pk = (d - 2) % 8
                    if (t > 0 or d >= 2) and pk in out_st:
                        kb.wtok("vector", out_st[pk])
                    OP("vector", lambda e, d=d, t=t, bank=bank:
                       e.scalar_tensor_tensor(
                           kb.f32v(FL + (d % 2) * TT, TT),
                           bank[:, (d % 4) * TT:(d % 4 + 1) * TT],
                           c["sxw_d"][:, d:d + 1],
                           kb.bfv(O_X + d * S + t * TT, TT),
                           ALU.mult, ALU.add), "vec")
                    DR("vector")
                    W("sync", "vec", kb.cnt["vec"])
                    out_st[d] = kb.dma(
                        "sync", lambda e, d=d, t=t: e.dma_start(
                            out=kb.d_out[d * 128:(d + 1) * 128,
                                         t * TT:(t + 1) * TT],
                            in_=kb.f32v(FL + (d % 2) * TT, TT)), "out")
            x_evac[t] = kb.cnt["vec"]

    out_st = {}

    def mamba():
        layernorm(O_X, O_LN)
        W("sync", "vec", kb.cnt["vec"])
        for d in range(CD):
            kb.dma("sync", lambda e, d=d: e.dma_start(
                out=kb.s_x1[d, :].rearrange("(p s) -> p s", p=128),
                in_=kb.bfv(O_X + d * S, S)), "stx")

        win_cast = []
        win_pe = {}
        win_evac = {}
        last_sz_st = [0]
        kb.swait("scalar", "stx")
        for o in range(2 * CI):
            fslot = 0 if o % 2 == 0 else 2816
            wslot = O_WS[o % 2]
            if o >= 2:
                W("sync", "scl", win_cast[o - 2])
            tok = kb.dma("sync", lambda e, o=o, fslot=fslot: e.dma_start(
                out=kb.f32[:, fslot:fslot + 1024].rearrange(
                    "p (k j) -> p k j", j=128),
                in_=di["W_in"][:, o * 128:(o + 1) * 128].rearrange(
                    "(k p) j -> p k j", p=128)), "xw")
            kb.wtok("scalar", tok)
            if o >= 2:
                W("scalar", "pe", win_pe[o - 2])
            win_cast.append(OP(
                "scalar", lambda e, fslot=fslot, wslot=wslot: e.copy(
                    kb.bfv(wslot, 1024), kb.f32[:, fslot:fslot + 1024]),
                "scl"))
            W("tensor", "scl", kb.cnt["scl"])
            for t in range(NT):
                if (o - 2, t) in win_evac:
                    W("tensor", "scl", win_evac[(o - 2, t)])
            for k in range(CD):
                for t in range(NT):
                    OP("tensor", lambda e, t=t, k=k, wslot=wslot: e.matmul(
                        ps_main[:, t * TT:(t + 1) * TT],
                        kb.bfv(wslot + k * 128, 128),
                        kb.bfv(O_LN + k * S + t * TT, TT),
                        start=(k == 0), stop=(k == CD - 1),
                        skip_group_check=True),
                       "pe" if (k == CD - 1) else None)
            win_pe[o] = kb.cnt["pe"]
            for t in range(NT):
                W("scalar", "pe", kb.cnt["pe"])
                if o < CI:
                    win_evac[(o, t)] = OP(
                        "scalar", lambda e, o=o, t=t: e.activation(
                            kb.bfv(xhp_off(o) + t * TT, TT),
                            ps_main[:, t * TT:(t + 1) * TT], AF.Identity,
                            bias=c["bxh"][:, o:o + 1]), "scl")
                else:
                    if t == 0:
                        kb.swait("scalar", "stz")
                    win_evac[(o, t)] = OP(
                        "scalar", lambda e, o=o, t=t: e.activation(
                            kb.bfv(O_STG + t * TT, TT),
                            ps_main[:, t * TT:(t + 1) * TT], AF.Silu,
                            bias=c["bz"][:, o - CI:o - CI + 1]), "scl")
            if o >= CI:
                W("sync", "scl", kb.cnt["scl"])
                kb.dma("sync", lambda e, o=o: e.dma_start(
                    out=kb.s_sz[o - CI, :].rearrange("(p s) -> p s", p=128),
                    in_=kb.bfv(O_STG, S)), "stz")

        kb.barrier(["scalar", "sync"], ["vector"])
        conv_scl = 0
        for ci in range(CI):
            xp = xhp_off(ci)
            xh = xh_off(ci)
            if conv_scl:
                W("vector", "scl", conv_scl)
            OP("vector", lambda e, xp=xp: e.tensor_copy(
                c["pad8"][:, 4:8], kb.bfv(xp, 4)))
            DR("vector")
            OP("vector", lambda e, ci=ci: e.tensor_scalar(
                kb.f32v(FA, 4), c["pad8"][:, 4:8],
                c["convw"][:, ci * 4 + 3:ci * 4 + 4], None, ALU.mult))
            OP("vector", lambda e, xp=xp, ci=ci: e.tensor_scalar(
                kb.f32v(FA + 4, S - 4), kb.bfv(xp + 4, S - 4),
                c["convw"][:, ci * 4 + 3:ci * 4 + 4], None, ALU.mult))
            for j in range(3):
                DR("vector")
                OP("vector", lambda e, j=j, ci=ci: e.scalar_tensor_tensor(
                    kb.f32v(FA, 4), c["pad8"][:, 1 + j:5 + j],
                    c["convw"][:, ci * 4 + j:ci * 4 + j + 1],
                    kb.f32v(FA, 4), ALU.mult, ALU.add))
                isl = OP("vector", lambda e, j=j, ci=ci, xp=xp:
                         e.scalar_tensor_tensor(
                             kb.f32v(FA + 4, S - 4),
                             kb.bfv(xp + 1 + j, S - 4),
                             c["convw"][:, ci * 4 + j:ci * 4 + j + 1],
                             kb.f32v(FA + 4, S - 4), ALU.mult, ALU.add),
                         "vec" if j == 2 else None)
            W("scalar", "vec", kb.cnt["vec"])
            conv_scl = OP("scalar", lambda e, xh=xh, ci=ci: e.activation(
                kb.bfv(xh, S), kb.f32v(FA, S), AF.Silu,
                bias=c["convb"][:, ci:ci + 1]), "scl")

        kb.barrier(["scalar"], ["tensor", "vector"])
        bm_evac = {}
        for t in range(NT):
            if (t - 1) in bm_evac:
                W("tensor", "scl", bm_evac[t - 1])
            for k in range(CI):
                OP("tensor", lambda e, t=t, k=k: e.matmul(
                    ps_aux[0:DS, 0:TT], c["WxB"][:, k * DS:(k + 1) * DS],
                    kb.bfv(xh_off(k) + t * TT, TT),
                    start=(k == 0), stop=False, skip_group_check=True))
            OP("tensor", lambda e: e.matmul(
                ps_aux[0:DS, 0:TT], c["bxB"][0:1, :],
                c["onesrow_b"][0:1, :], start=False, stop=True,
                skip_group_check=True), "pe")
            W("scalar", "pe", kb.cnt["pe"])
            bm_evac[t] = OP("scalar", lambda e, t=t: e.copy(
                bmatv(t * TT, TT), ps_aux[0:DS, 0:TT]), "scl")
        br_evac = {}
        br_bank_user = {}
        for s in range(DS):
            for t in range(NT):
                bank = TT + (t % 2) * TT
                if bank in br_bank_user:
                    W("tensor", "scl", br_bank_user[bank])
                W("tensor", "scl", bm_evac[t])
                OP("tensor", lambda e, s=s, t=t, bank=bank: e.matmul(
                    ps_aux[:, bank:bank + TT],
                    c["oneh"][:, s * 128:(s + 1) * 128],
                    bmatv(t * TT, TT), start=True, stop=True,
                    skip_group_check=True), "pe")
                W("scalar", "pe", kb.cnt["pe"])
                if s < NBREP_RES:
                    br_evac[(s, t)] = OP(
                        "scalar", lambda e, s=s, t=t, bank=bank: e.copy(
                            kb.bfv(brep_off(s) + t * TT, TT),
                            ps_aux[:, bank:bank + TT]), "scl")
                else:
                    if s > NBREP_RES or t > 0:
                        kb.swait("scalar", "stx")
                    br_evac[(s, t)] = OP(
                        "scalar", lambda e, s=s, t=t, bank=bank: e.copy(
                            kb.bfv(O_STG + t * TT, TT),
                            ps_aux[:, bank:bank + TT]), "scl")
                br_bank_user[bank] = br_evac[(s, t)]
            if s >= NBREP_RES:
                W("sync", "scl", kb.cnt["scl"])
                kb.dma("sync", lambda e, s=s: e.dma_start(
                    out=kb.s_brep[s - NBREP_RES, :].rearrange(
                        "(p s) -> p s", p=128),
                    in_=kb.bfv(O_STG, S)), "stx")
        kb.barrier(["scalar", "sync", "vector"],
                   ["vector", "tensor", "sync"])

        wdt_pe = {}
        sp_evac = {}
        scan_marks = {}
        add_marks = {}
        for ci in range(CI):
            if (ci - 1) in wdt_pe:
                W("sync", "pe", wdt_pe[ci - 1])
            tok = kb.dma("sync", lambda e, ci=ci: e.dma_start(
                out=kb.bfv(O_WDT, 2048).rearrange("p (k j) -> p k j", j=128),
                in_=kb.s_wdtb[:, ci * 128:(ci + 1) * 128].rearrange(
                    "(k p) j -> p k j", p=128)), "xw")
            kb.wtok("tensor", tok)
            if (ci - 1) in sp_evac:
                W("tensor", "scl", sp_evac[ci - 1])
            for k in range(CI):
                for t in range(NT):
                    OP("tensor", lambda e, t=t, k=k: e.matmul(
                        ps_main[:, t * TT:(t + 1) * TT],
                        kb.bfv(O_WDT + k * 128, 128),
                        kb.bfv(xh_off(k) + t * TT, TT),
                        start=(k == 0), stop=(k == CI - 1),
                        skip_group_check=True),
                       "pe" if (k == CI - 1) else None)
            wdt_pe[ci] = kb.cnt["pe"]
            W("scalar", "pe", kb.cnt["pe"])
            if ci > 0:
                W("scalar", "vec", scan_marks[(ci - 1, DS - 1)])
                kb.wtok("scalar", add_marks[(ci - 1, DS - 1)])
            for t in range(NT):
                OP("scalar", lambda e, t=t, ci=ci: e.activation(
                    kb.f32v(FL + t * TT, TT),
                    ps_main[:, t * TT:(t + 1) * TT], AF.Abs,
                    bias=c["bdt"][:, ci:ci + 1]))
                DR("scalar")
                OP("scalar", lambda e, t=t: e.activation(
                    kb.f32v(FL + t * TT, TT), kb.f32v(FL + t * TT, TT),
                    AF.Exp, scale=-1.0))
                DR("scalar")
                OP("scalar", lambda e, t=t: e.activation(
                    kb.f32v(FL + t * TT, TT), kb.f32v(FL + t * TT, TT),
                    AF.Ln, bias=1.0))
                DR("scalar")
                OP("scalar", lambda e, t=t, ci=ci: e.activation(
                    kb.f32v(FB + t * TT, TT),
                    ps_main[:, t * TT:(t + 1) * TT], AF.Relu,
                    bias=c["bdt"][:, ci:ci + 1]),
                   "scl" if t == NT - 1 else None)
            sp_evac[ci] = kb.cnt["scl"]
            W("vector", "scl", kb.cnt["scl"])
            OP("vector", lambda e: e.tensor_add(
                kb.f32v(FL, S), kb.f32v(FL, S), kb.f32v(FB, S)))
            DR("vector")
            kb.swait("vector", "sty")
            OP("vector", lambda e, ci=ci: e.tensor_mul(
                kb.bfv(O_DU, S), kb.f32v(FL, S), kb.bfv(xh_off(ci), S)))
            OP("vector", lambda e, ci=ci: e.tensor_scalar(
                kb.f32v(FY, S), kb.bfv(xh_off(ci), S),
                c["D"][:, ci:ci + 1], None, ALU.mult), "vec")
            DR("vector")
            delta_mark = kb.cnt["vec"]
            bs_tok = {}
            for sp in (NBREP_RES, NBREP_RES + 1):
                W("sync", "vec", kb.cnt["vec"])
                bs_tok[sp] = kb.dma(
                    "sync", lambda e, sp=sp, bsrc=O_BS[sp % 2]: e.dma_start(
                        out=kb.bfv(bsrc, S),
                        in_=kb.s_brep[sp - NBREP_RES, :].rearrange(
                            "(p s) -> p s", p=128)), "bs")
            for s in range(DS):
                dslot = FA if s % 2 == 0 else FA2
                hslot = FH if s % 2 == 0 else FB
                W("scalar", "vec", delta_mark)
                if s >= 2:
                    W("scalar", "vec", scan_marks[(ci, s - 2)])
                elif ci > 0:
                    W("scalar", "vec", scan_marks[(ci - 1, DS - 2 + s)])
                da = OP("scalar", lambda e, ci=ci, s=s, dslot=dslot:
                        e.activation(
                            kb.f32v(dslot, S), kb.f32v(FL, S), AF.Exp,
                            scale=c["negA"][:, ci * DS + s:ci * DS + s + 1]),
                        "scl")
                if s < NBREP_RES:
                    bsrc = brep_off(s)
                else:
                    bsrc = O_BS[s % 2]
                    kb.wtok("vector", bs_tok[s])
                OP("vector", lambda e, bsrc=bsrc: e.tensor_mul(
                    kb.bfv(O_DBU, S), kb.bfv(O_DU, S), kb.bfv(bsrc, S)))
                DR("vector")
                W("vector", "scl", da)
                if s >= 2:
                    kb.wtok("vector", add_marks[(ci, s - 2)])
                elif ci > 0:
                    kb.wtok("vector", add_marks[(ci - 1, DS - 2 + s)])
                scan_marks[(ci, s)] = OP(
                    "vector", lambda e, dslot=dslot, hslot=hslot:
                    e.tensor_tensor_scan(
                        kb.f32v(hslot, S), kb.f32v(dslot, S),
                        kb.bfv(O_DBU, S), 0.0, ALU.mult, ALU.add), "vec")
                DR("vector")
                W("gpsimd", "vec", scan_marks[(ci, s)])
                if s > 0:
                    kb.wtok("gpsimd", add_marks[(ci, s - 1)])
                elif ci > 0:
                    kb.wtok("gpsimd", add_marks[(ci - 1, DS - 1)])
                add_marks[(ci, s)] = kb.dma(
                    "gpsimd", lambda e, hslot=hslot: e.dma_start(
                        out=kb.f32v(FY, S), in_=kb.f32v(hslot, S),
                        accum_op=ALU.add), "yacc")
                if NBREP_RES <= s <= DS - 3:
                    W("sync", "vec", scan_marks[(ci, s)])
                    bs_tok[s + 2] = kb.dma(
                        "sync", lambda e, s=s, bsrc=O_BS[s % 2]: e.dma_start(
                            out=kb.bfv(bsrc, S),
                            in_=kb.s_brep[s + 2 - NBREP_RES, :].rearrange(
                                "(p s) -> p s", p=128)), "bs")
            kb.wtok("vector", add_marks[(ci, DS - 1)])
            for t in range(NT):
                szslot = O_SZ[t % 2]
                W("sync", "vec", kb.cnt["vec"])
                tok = kb.dma("sync", lambda e, ci=ci, t=t, szslot=szslot:
                             e.dma_start(out=kb.bfv(szslot, TT),
                                         in_=stile(kb.s_sz, ci, t)), "szl")
                kb.wtok("vector", tok)
                OP("vector", lambda e, t=t, szslot=szslot: e.tensor_mul(
                    kb.bfv(O_DU + t * TT, TT), kb.f32v(FY + t * TT, TT),
                    kb.bfv(szslot, TT)), "vec")
                DR("vector")
            W("sync", "vec", kb.cnt["vec"])
            kb.dma("sync", lambda e, ci=ci: e.dma_start(
                out=kb.s_yg[ci, :].rearrange("(p s) -> p s", p=128),
                in_=kb.bfv(O_DU, S)), "sty")
        kb.barrier(["vector", "sync", "scalar"],
                   ["sync", "tensor", "vector"])

        wo_pe = {}
        wo_evac = {}
        for t in range(NT):
            for k in range(CI):
                wslot = O_WS[k % 2]
                ygslot = O_T0 if k % 2 == 0 else O_T1
                prev = (t, k - 2) if k >= 2 else (t - 1, CI - 2 + k)
                if prev in wo_pe:
                    W("sync", "pe", wo_pe[prev])
                tokw = kb.dma("sync", lambda e, k=k, wslot=wslot:
                              e.dma_start(
                                  out=kb.bfv(wslot, 1024),
                                  in_=kb.s_woutb[k * 128:(k + 1) * 128, :]),
                              "wg")
                toky = kb.dma("sync", lambda e, k=k, t=t, ygslot=ygslot:
                              e.dma_start(out=kb.bfv(ygslot, TT),
                                          in_=stile(kb.s_yg, k, t)), "ygl")
                kb.wtok("tensor", tokw)
                kb.wtok("tensor", toky)
                if k == 0 and (t - 1) in wo_evac:
                    W("tensor", "vec", wo_evac[t - 1])
                for d in range(CD):
                    bank = ps_main if d < 4 else ps_aux
                    OP("tensor", lambda e, d=d, k=k, wslot=wslot,
                       ygslot=ygslot, bank=bank: e.matmul(
                           bank[:, (d % 4) * TT:(d % 4 + 1) * TT],
                           kb.bfv(wslot + d * 128, 128),
                           kb.bfv(ygslot, TT),
                           start=(k == 0), stop=(k == CI - 1),
                           skip_group_check=True),
                       "pe" if d == CD - 1 else None)
                wo_pe[(t, k)] = kb.cnt["pe"]
            W("vector", "pe", kb.cnt["pe"])
            for d in range(CD):
                xslot = O_T2 if d % 2 == 0 else O_T3
                W("sync", "vec", kb.cnt["vec"])
                tok = kb.dma("sync", lambda e, d=d, t=t, xslot=xslot:
                             e.dma_start(out=kb.bfv(xslot, TT),
                                         in_=stile(kb.s_x1, d, t)), "x1l")
                kb.wtok("vector", tok)
                bank = ps_main if d < 4 else ps_aux
                OP("vector", lambda e, d=d, t=t, xslot=xslot, bank=bank:
                   e.scalar_tensor_tensor(
                       kb.bfv(O_X + d * S + t * TT, TT),
                       bank[:, (d % 4) * TT:(d % 4 + 1) * TT],
                       c["bout"][:, d:d + 1],
                       kb.bfv(xslot, TT), ALU.add, ALU.add), "vec")
                DR("vector")
            wo_evac[t] = kb.cnt["vec"]
        kb.barrier(["vector", "sync", "gpsimd"],
                   ["scalar", "tensor", "vector", "sync"])

    ffn(final=False)
    kb.barrier(["vector", "scalar", "sync", "tensor"],
               ["vector", "scalar", "sync", "tensor"])
    mamba()
    ffn(final=True)
    kb.swait_all_dma("sync")


_BUILD_CACHE = {}


def _get_build(S, n_cores):
    key = (S, n_cores)
    if key not in _BUILD_CACHE:
        _BUILD_CACHE[key] = build(S, n_cores)
    return _BUILD_CACHE[key]


def _pack_rows(v, nch):
    return np.ascontiguousarray(np.asarray(v, np.float32).reshape(nch, 128).T)


def make_inputs(xc, w, S):
    bf = ml_dtypes.bfloat16
    Alog = np.asarray(w["L0_attn_A_log"], np.float32)
    convw = np.asarray(w["L0_attn_conv_w"], np.float32)[:, 0, :]
    one = {}
    one["x"] = np.ascontiguousarray(np.asarray(xc, np.float32).T)
    one["W_in"] = np.ascontiguousarray(np.asarray(w["L0_attn_W_in"],
                                                  np.float32))
    one["W_dt"] = np.ascontiguousarray(np.asarray(w["L0_attn_W_dt"],
                                                   np.float32))
    one["W_out"] = np.ascontiguousarray(np.asarray(w["L0_attn_W_out"],
                                                    np.float32))
    Wx = np.asarray(w["L0_attn_W_x"], np.float32)
    one["WxB"] = np.ascontiguousarray(
        Wx[:, DS:].reshape(CI, 128, DS).transpose(1, 0, 2)
        .reshape(128, CI * DS))
    one["bxB"] = np.ascontiguousarray(
        np.asarray(w["L0_attn_b_x"], np.float32)[DS:].reshape(1, DS))
    one["w_gate"] = np.ascontiguousarray(np.asarray(w["L0_ffn_w_gate"],
                                                     np.float32))
    one["w_up"] = np.ascontiguousarray(np.asarray(w["L0_ffn_w_up"],
                                                   np.float32))
    one["w_down"] = np.ascontiguousarray(np.asarray(w["L0_ffn_w_down"],
                                                     np.float32))
    one["Alog"] = np.ascontiguousarray(
        Alog.reshape(CI, 128, DS).transpose(1, 0, 2).reshape(128, CI * DS))
    one["convw"] = np.ascontiguousarray(
        convw.reshape(CI, 128, DC).transpose(1, 0, 2).reshape(128, CI * DC))
    one["convb"] = _pack_rows(w["L0_attn_conv_b"], CI)
    one["Dvec"] = _pack_rows(w["L0_attn_D"], CI)
    bin_ = np.asarray(w["L0_attn_b_in"], np.float32)
    one["bxh"] = _pack_rows(bin_[:DI], CI)
    one["bz"] = _pack_rows(bin_[DI:], CI)
    one["bdt"] = _pack_rows(w["L0_attn_b_dt"], CI)
    one["bout"] = _pack_rows(w["L0_attn_b_out"], CD)
    one["clipx"] = np.asarray(
        [[float(np.asarray(w["L0_ffn_clip_x"]).reshape(-1)[0]), 0.0]],
        np.float32)
    one["clipd"] = np.asarray(
        [[float(np.asarray(w["L0_ffn_clip_down"]).reshape(-1)[0]), 0.0]],
        np.float32)
    one["eyeb"] = np.eye(128, dtype=bf)
    one["eyef"] = np.eye(128, dtype=np.float32)
    one["onesb"] = np.ones((128, 2), dtype=bf)
    one["ones1"] = np.ones((1, 128), np.float32)
    one["onesrow_b"] = np.ones((1, TT), dtype=bf)
    cv = np.zeros((128, 8), np.float32)
    cv[:, 0] = LN_EPS
    cv[:, 1] = 1.0
    cv[:, 2] = -1.0
    cv[:, 3] = -0.5
    one["cvals"] = cv
    oneh = np.zeros((DS, DS * 128), dtype=bf)
    for s in range(DS):
        oneh[s, s * 128:(s + 1) * 128] = 1
    one["oneh"] = oneh
    return one


def kernel(**inputs):
    x = np.asarray(inputs["L0_attn_x"], np.float32)
    S = x.shape[1]
    kb = _get_build(S, NCORE)
    in_maps = [make_inputs(x[cidx], inputs, S) for cidx in range(NCORE)]
    res = run_bass_kernel_spmd(kb.nc, in_maps, core_ids=list(range(NCORE)))
    outs = [np.asarray(r["out"]).T for r in res.results]
    return np.stack(outs, axis=0).astype(np.float32)


# version 06
# speedup vs baseline: 1.0314x; 1.0314x over previous
import numpy as np
import ml_dtypes
import concourse.bass as bass
import concourse.mybir as mybir
from concourse.bass_utils import run_bass_kernel_spmd
from contextlib import ExitStack

F32 = mybir.dt.float32
BF16 = mybir.dt.bfloat16
I32 = mybir.dt.int32
AF = mybir.ActivationFunctionType
ALU = mybir.AluOpType
AX = mybir.AxisListType

DM, DI, DS, DC, HF = 1024, 2048, 16, 4, 2816
CD, CI, CH = DM // 128, DI // 128, HF // 128
TT = 512
LN_EPS = 1e-5
QMAXF = 127.0
NCORE = 8
ENGS = ("sync", "vector", "scalar", "tensor", "gpsimd")


class KB:
    def __init__(self, S, n_cores):
        self.S = S
        self.NT = S // TT
        self.n_cores = n_cores
        self.nc = bass.Bass("TRN2", target_bir_lowering=False)
        self.es = ExitStack()
        self.q = {e: [] for e in ENGS}
        self._mk_dram()

    def _mk_dram(self):
        nc, S = self.nc, self.S
        self.di = {}

        def din(name, shape, dt=F32):
            self.di[name] = nc.dram_tensor(name, shape, dt,
                                           kind="ExternalInput").ap()

        din("x", [DM, S])
        din("W_in", [DM, 2 * DI])
        din("W_dt", [DI, DI])
        din("W_out", [DI, DM])
        din("WxB", [128, CI * DS])
        din("bxB", [1, DS])
        din("w_gate", [HF, DM])
        din("w_up", [HF, DM])
        din("w_down", [DM, HF])
        din("Alog", [128, CI * DS])
        din("convw", [128, CI * DC])
        din("convb", [128, CI])
        din("Dvec", [128, CI])
        din("bxh", [128, CI])
        din("bz", [128, CI])
        din("bdt", [128, CI])
        din("bout", [128, CD])
        din("clipx", [1, 2])
        din("clipd", [1, 2])
        din("cvals", [128, 8])
        din("eyeb", [128, 128], BF16)
        din("eyef", [128, 128], F32)
        din("onesb", [128, 2], BF16)
        din("ones1", [1, 128])
        din("onesrow_b", [1, TT], BF16)
        din("oneh", [DS, DS * 128], BF16)
        self.d_out = nc.dram_tensor("out", [DM, S], F32,
                                    kind="ExternalOutput").ap()

        def scr(name, shape, dt):
            return nc.dram_tensor(name, shape, dt, kind="Internal").ap()

        self.s_wdtb = scr("s_wdtb", [DI, DI], BF16)
        self.s_woutb = scr("s_woutb", [DI, DM], BF16)
        self.s_wgT = scr("s_wgT", [CH, CD * 128 * 128], BF16)
        self.s_wuT = scr("s_wuT", [CH, CD * 128 * 128], BF16)
        self.s_wdT = scr("s_wdT", [CH, CD * 128 * 128], BF16)
        self.s_h = scr("s_h", [CH, 128 * S], BF16)
        self.s_sz = scr("s_sz", [CI, 128 * S], BF16)
        self.s_yg = scr("s_yg", [CI, 128 * S], BF16)
        self.s_x1 = scr("s_x1", [CD, 128 * S], BF16)
        self.s_brep = scr("s_brep", [6, 128 * S], BF16)
        self.s_cci = scr("s_cci", [1, 2], F32)
        self.s_cco = scr("s_cco", [1, 2], F32)

    def mk_mem(self):
        nc, S, es = self.nc, self.S, self.es

        def sb(name, shape, dt=F32):
            return es.enter_context(nc.sbuf_tensor(name, shape, dt))

        self.bf = sb("bf_arena", [128, 32 * S], BF16)
        self.f32 = sb("f32_arena", [128, 6 * S], F32)
        self.int_t = sb("int_t", [128, S], I32)
        self.c = {}
        for name, shape, dt in [
            ("negA", [128, CI * DS], F32), ("convw", [128, CI * DC], F32),
            ("convb", [128, CI], F32), ("D", [128, CI], F32),
            ("bxh", [128, CI], F32), ("bz", [128, CI], F32),
            ("bdt", [128, CI], F32), ("bout", [128, CD], F32),
            ("WxB", [128, CI * DS], BF16), ("bxB", [1, DS], BF16),
            ("eyeb", [128, 128], BF16), ("eyef", [128, 128], F32),
            ("onesb", [128, 2], BF16),
            ("ones1", [1, 128], F32), ("onesrow_b", [1, TT], BF16),
            ("oneh", [DS, DS * 128], BF16),
            ("clipx", [1, 2], F32), ("clipd", [1, 2], F32),
            ("cvals", [128, 8], F32),
            ("swg", [128, CH], F32), ("swu", [128, CH], F32),
            ("swd", [128, CD], F32),
            ("sxw_g", [128, CH], F32), ("sxw_u", [128, CH], F32),
            ("sxw_d", [128, CD], F32),
            ("amax", [128, 32], F32), ("r1", [128, 2], F32),
            ("sc", [1, 8], F32), ("pp", [128, 4], F32),
            ("pad8", [128, 8], BF16),
        ]:
            self.c[name] = sb("c_" + name, shape, dt)
        self.sems = {}
        for n in ["pe", "vec", "scl", "gps", "cc"]:
            self.sems[n] = es.enter_context(nc.semaphore(n))
        self.cnt = {n: 0 for n in self.sems}
        self.sempool = [es.enter_context(nc.semaphore(f"dmas{i}"))
                        for i in range(44)]
        self.sempool_next = 0
        self.streams = {}
        self.ps_main = es.enter_context(
            nc.psum_tensor("ps_main", [128, 4 * TT], F32))
        self._ps_tb_scope = ExitStack()
        self.ps_tb = self._ps_tb_scope.enter_context(
            nc.psum_tensor("ps_tb", [128, 1024], BF16))
        self.ps_aux = None
        for col, val in enumerate([LN_EPS, 1.0, -1.0, -0.5]):
            self.nc.const_aps.register(F32, val, self.c["cvals"][:, col:col + 1])

    def close_p0_psum(self):
        self._ps_tb_scope.close()
        self.ps_tb = None
        self.ps_aux = self.es.enter_context(
            self.nc.psum_tensor("ps_aux", [128, 4 * TT], F32))

    def op(self, eng, fn, inc=None, n=1):
        if inc is None:
            self.q[eng].append(fn)
            return None
        self.cnt[inc] += n
        sem = self.sems[inc]
        self.q[eng].append(
            lambda e, fn=fn, sem=sem, n=n: fn(e).then_inc(sem, n))
        return self.cnt[inc]

    def wait(self, eng, semname, val):
        if val and val > 0:
            sem = self.sems[semname]
            self.q[eng].append(
                lambda e, sem=sem, val=val: e.wait_ge(sem, val))

    def _stream(self, name):
        st = self.streams.get(name)
        if st is None:
            a = self.sempool[self.sempool_next]
            b = self.sempool[self.sempool_next + 1]
            self.sempool_next += 2
            st = {"sems": [a, b], "cnt": [0, 0], "par": 0}
            self.streams[name] = st
        return st

    def dma(self, eng, fn, stream):
        st = self._stream(stream)
        p = st["par"]
        sem = st["sems"][p]
        if st["cnt"][p] > 0:
            v = st["cnt"][p]
            self.q[eng].append(
                lambda e, sem=sem, v=v: e.wait_ge(sem, v))
        st["cnt"][p] += 16
        v = st["cnt"][p]
        self.q[eng].append(
            lambda e, fn=fn, sem=sem: fn(e).then_inc(sem, 16))
        st["par"] = 1 - p
        return (sem, v)

    def wtok(self, eng, tok):
        if tok is None:
            return
        sem, v = tok
        self.q[eng].append(lambda e, sem=sem, v=v: e.wait_ge(sem, v))

    def swait(self, eng, stream):
        st = self.streams.get(stream)
        if not st:
            return
        for p in (0, 1):
            if st["cnt"][p] > 0:
                sem, v = st["sems"][p], st["cnt"][p]
                self.q[eng].append(
                    lambda e, sem=sem, v=v: e.wait_ge(sem, v))

    def swait_all_dma(self, eng):
        for name in list(self.streams):
            self.swait(eng, name)

    def drain(self, eng):
        self.q[eng].append(lambda e: e.drain())

    def barrier(self, writers, waiters):
        snap = []
        sync_snap = []
        for w in writers:
            if w == "sync":
                for name, st in self.streams.items():
                    for p in (0, 1):
                        if st["cnt"][p] > 0:
                            sync_snap.append((st["sems"][p], st["cnt"][p]))
            else:
                s = {"vector": "vec", "scalar": "scl", "tensor": "pe",
                     "gpsimd": "gps"}[w]
                snap.append((w, s, self.cnt[s]))
        for eng in waiters:
            for w, s, v in snap:
                if eng == w:
                    self.drain(eng)
                else:
                    self.wait(eng, s, v)
            for sem, v in sync_snap:
                self.q[eng].append(
                    lambda e, sem=sem, v=v: e.wait_ge(sem, v))

    def bfv(self, off, cols):
        return self.bf[:, off:off + cols]

    def f32v(self, off, cols):
        return self.f32[:, off:off + cols]


def build(S, n_cores):
    kb = KB(S, n_cores)
    with kb.es:
        kb.mk_mem()
        _emit(kb)
        blk = kb.es.enter_context(kb.nc.Block())
        for eng in ENGS:
            ops = kb.q[eng]
            if ops:
                getattr(blk, eng)(lambda e, ops=ops: [f(e) for f in ops])
    return kb


def _emit(kb: KB):
    S, NT = kb.S, kb.NT
    c, di = kb.c, kb.di
    int_t = kb.int_t
    ps_main = kb.ps_main

    def U(k):
        return k * S

    O_X, O_LN, O_XQ = U(0), U(8), U(16)
    O_SG, O_SU, O_HFF = U(24), U(25), U(26)
    O_WS = [U(27), U(27) + 1024, U(28), U(28) + 1024]
    O_T0, O_T1 = U(29), U(29) + TT
    O_T2, O_T3 = U(29) + 2 * TT, U(29) + 3 * TT
    O_STG = U(30)
    O_AB = U(26)
    O_BLK = U(24)

    def xh_off(ci):
        return U(8 + ci) if ci < 8 else U(ci - 8)

    def xhp_off(ci):
        return U(ci) if ci < 8 else U(16 + ci - 8)

    def brep_off(s):
        return U(16 + s)

    NBREP_RES = 10
    O_BS = [U(26), U(27)]
    O_WDT = U(28)
    O_DU = U(29)
    O_SZ = [U(30), U(30) + TT]
    O_DBU = U(31)
    O_BMAT = U(29)

    FA, FB, FH, FY, FL, FA2 = 0, S, 2 * S, 3 * S, 4 * S, 5 * S
    ROW_M, ROW_T, ROW_R = FA2, FA2 + TT, FA2 + 2 * TT

    W, OP, DR = kb.wait, kb.op, kb.drain

    def rowv(off):
        return kb.f32[0:1, off:off + TT]

    def bmatv(t0, cols):
        return kb.bf[0:DS, O_BMAT + t0:O_BMAT + t0 + cols]

    for name in ["cvals", "convw", "convb", "bxh", "bz", "bdt", "bout",
                 "eyeb", "eyef", "onesb", "ones1", "onesrow_b", "oneh",
                 "clipx", "clipd"]:
        kb.dma("sync", lambda e, n=name: e.dma_start(out=c[n][:], in_=di[n]),
               "cst")
    kb.dma("sync", lambda e: e.dma_start(out=c["D"][:], in_=di["Dvec"]),
           "cst")
    kb.dma("sync", lambda e: e.dma_start(out=c["negA"][:], in_=di["Alog"]),
           "cst")
    kb.dma("sync", lambda e: e.dma_start(out=kb.f32[:, 0:CI * DS],
                                         in_=di["WxB"]), "cst")
    kb.dma("sync", lambda e: e.dma_start(
        out=kb.f32[0:1, CI * DS:CI * DS + DS], in_=di["bxB"]), "cst")
    kb.swait("scalar", "cst")
    OP("scalar", lambda e: e.copy(c["WxB"][:], kb.f32[:, 0:CI * DS]))
    OP("scalar", lambda e: e.copy(c["bxB"][:],
                                  kb.f32[0:1, CI * DS:CI * DS + DS]), "scl")
    OP("scalar", lambda e: e.activation(c["negA"][:], c["negA"][:], AF.Exp))
    DR("scalar")
    OP("scalar", lambda e: e.mul(c["negA"][:], c["negA"][:], -1.0), "scl")
    kb.swait("vector", "cst")
    OP("vector", lambda e: e.memset(c["pad8"][:], 0.0), "vec")

    xcopy_marks = []
    W("sync", "scl", kb.cnt["scl"])
    for ch in range(CD):
        slot = FL if ch % 2 == 0 else FA
        if ch >= 2:
            W("sync", "scl", xcopy_marks[ch - 2])
        tok = kb.dma("sync", lambda e, ch=ch, slot=slot: e.dma_start(
            out=kb.f32v(slot, S), in_=di["x"][ch * 128:(ch + 1) * 128, :]),
            "xw")
        kb.wtok("scalar", tok)
        xcopy_marks.append(OP("scalar", lambda e, ch=ch, slot=slot: e.copy(
            kb.bfv(O_X + ch * S, S), kb.f32v(slot, S)), "scl"))

    qmarks = []

    def quant_transpose(wname, nch, kd, stile, dstT, toks,
                        scatter_rows=False):
        nkb = kd // 128
        for i in range(nch):
            slot = 0 if i % 2 == 0 else 2816
            if len(qmarks) >= 2:
                W("sync", "vec", qmarks[-2])
            tok = kb.dma("sync", lambda e, w=wname, i=i, slot=slot, kd=kd:
                         e.dma_start(out=kb.f32[:, slot:slot + kd],
                                     in_=di[w][i * 128:(i + 1) * 128, :]),
                         "xw")
            kb.wtok("vector", tok)
            OP("vector", lambda e, slot=slot, kd=kd: e.tensor_reduce(
                c["r1"][:, 0:1], kb.f32[:, slot:slot + kd], AX.X, ALU.max,
                apply_absolute_value=True))
            DR("vector")
            OP("vector", lambda e, st=stile, i=i: e.tensor_scalar(
                c[st][:, i:i + 1], c["r1"][:, 0:1], 1.0 / QMAXF, None,
                ALU.mult))
            OP("vector", lambda e: e.reciprocal(c["r1"][:, 1:2],
                                                c["r1"][:, 0:1]))
            DR("vector")
            W("vector", "pe", kb.cnt["pe"])
            mlast = None
            for p0 in range(0, kd, S):
                pn = min(S, kd - p0)
                OP("vector", lambda e, slot=slot, p0=p0, pn=pn:
                   e.tensor_scalar(int_t[:, 0:pn],
                                   kb.f32[:, slot + p0:slot + p0 + pn],
                                   c["r1"][:, 1:2], QMAXF,
                                   ALU.mult, ALU.mult))
                DR("vector")
                mlast = OP("vector", lambda e, p0=p0, pn=pn: e.tensor_copy(
                    kb.bf[:, O_BLK + p0:O_BLK + p0 + pn], int_t[:, 0:pn]), "vec")
                DR("vector")
            qmarks.append(mlast)
            kb.swait("scalar", "stw")
            for g0 in range(0, nkb, 8):
                gn = min(8, nkb - g0)
                W("tensor", "vec", kb.cnt["vec"])
                W("tensor", "scl", kb.cnt["scl"])
                for j in range(gn):
                    OP("tensor", lambda e, g0=g0, j=j, pstb=kb.ps_tb:
                       e.transpose(
                           pstb[:, j * 128:(j + 1) * 128],
                           kb.bf[:, O_BLK + (g0 + j) * 128:O_BLK + (g0 + j + 1) * 128],
                           c["eyeb"][:]), "pe" if j == gn - 1 else None)
                W("scalar", "pe", kb.cnt["pe"])
                for j in range(gn):
                    OP("scalar", lambda e, g0=g0, j=j, pstb=kb.ps_tb: e.copy(
                        kb.bfv(O_STG + (g0 + j) * 128, 128),
                        pstb[:, j * 128:(j + 1) * 128]),
                       "scl" if j == gn - 1 else None)
            W("sync", "scl", kb.cnt["scl"])
            if not scatter_rows:
                toks[i] = kb.dma(
                    "sync", lambda e, dstT=dstT, i=i, nkb=nkb: e.dma_start(
                        out=dstT[i, 0:nkb * 128 * 128].rearrange(
                            "(k p j) -> p k j", p=128, j=128),
                        in_=kb.bfv(O_STG, nkb * 128).rearrange(
                            "p (k j) -> p k j", j=128)), "stw")
            else:
                toks[i] = kb.dma(
                    "sync", lambda e, dstT=dstT, i=i, nkb=nkb: e.dma_start(
                        out=dstT[:, i * 128 * 128:(i + 1) * 128 * 128]
                        .rearrange("k (p j) -> p k j", j=128),
                        in_=kb.bfv(O_STG, nkb * 128).rearrange(
                            "p (k j) -> p k j", j=128)), "stw")

    kb.barrier(["scalar"], ["sync"])
    wgT_tok, wuT_tok, wdT_tok = {}, {}, {}
    quant_transpose("w_gate", CH, DM, "swg", kb.s_wgT, wgT_tok)
    quant_transpose("w_up", CH, DM, "swu", kb.s_wuT, wuT_tok)
    quant_transpose("w_down", CD, HF, "swd", kb.s_wdT, wdT_tok,
                    scatter_rows=True)

    cast_marks = []
    for wname, dst, nch, kd in [("W_dt", kb.s_wdtb, CI, DI),
                                ("W_out", kb.s_woutb, CI, DM)]:
        for i in range(nch):
            slot = 0 if i % 2 == 0 else 2816
            if len(cast_marks) >= 2:
                W("sync", "scl", cast_marks[-2])
            tok = kb.dma("sync", lambda e, w=wname, i=i, slot=slot, kd=kd:
                         e.dma_start(out=kb.f32[:, slot:slot + kd],
                                     in_=di[w][i * 128:(i + 1) * 128, :]),
                         "xw")
            kb.wtok("scalar", tok)
            kb.swait("scalar", "stw")
            cast_marks.append(OP("scalar", lambda e, slot=slot, kd=kd: e.copy(
                kb.bfv(O_STG, kd), kb.f32[:, slot:slot + kd]), "scl"))
            W("sync", "scl", kb.cnt["scl"])
            kb.dma("sync", lambda e, dst=dst, i=i, kd=kd: e.dma_start(
                out=dst[i * 128:(i + 1) * 128, :], in_=kb.bfv(O_STG, kd)),
                "stw")

    kb.close_p0_psum()
    ps_aux = kb.ps_aux

    def layernorm(src_o, dst_o):
        prev_stats_read = 0
        W("tensor", "scl", kb.cnt["scl"])
        for t in range(NT):
            tn = t * TT
            sq_marks = []
            pe_sq = []
            W("tensor", "vec", prev_stats_read)
            for ch in range(CD):
                sq = O_T0 if ch % 2 == 0 else O_T1
                if ch >= 2:
                    W("scalar", "pe", pe_sq[ch - 2])
                sq_marks.append(OP(
                    "scalar", lambda e, ch=ch, sq=sq, src_o=src_o, tn=tn:
                    e.activation(kb.bfv(sq, TT),
                                 kb.bfv(src_o + ch * S + tn, TT), AF.Square),
                    "scl"))
                OP("tensor", lambda e, ch=ch, src_o=src_o, tn=tn: e.matmul(
                    ps_main[0:1, 0:TT], c["onesb"][:, 0:1],
                    kb.bfv(src_o + ch * S + tn, TT),
                    start=(ch == 0), stop=(ch == CD - 1),
                    skip_group_check=True))
                W("tensor", "scl", sq_marks[ch])
                pe_sq.append(OP("tensor", lambda e, ch=ch, sq=sq: e.matmul(
                    ps_main[0:1, TT:2 * TT], c["onesb"][:, 0:1],
                    kb.bfv(sq, TT),
                    start=(ch == 0), stop=(ch == CD - 1),
                    skip_group_check=True), "pe"))
            W("vector", "pe", kb.cnt["pe"])
            OP("vector", lambda e: e.tensor_scalar(
                rowv(ROW_M), ps_main[0:1, 0:TT], 1.0 / DM, None, ALU.mult))
            OP("vector", lambda e: e.tensor_scalar(
                rowv(ROW_T), ps_main[0:1, TT:2 * TT], 1.0 / DM, None,
                ALU.mult))
            DR("vector")
            OP("vector", lambda e: e.tensor_mul(rowv(ROW_R), rowv(ROW_M),
                                                rowv(ROW_M)))
            DR("vector")
            OP("vector", lambda e: e.tensor_sub(rowv(ROW_T), rowv(ROW_T),
                                                rowv(ROW_R)), "vec")
            stats_read = kb.cnt["vec"]
            W("scalar", "vec", stats_read)
            OP("scalar", lambda e: e.activation(rowv(ROW_R), rowv(ROW_T),
                                                AF.Ln, bias=LN_EPS))
            DR("scalar")
            OP("scalar", lambda e: e.activation(rowv(ROW_R), rowv(ROW_R),
                                                AF.Exp, scale=-0.5), "scl")
            W("vector", "scl", kb.cnt["scl"])
            OP("vector", lambda e: e.tensor_mul(rowv(ROW_M), rowv(ROW_M),
                                                rowv(ROW_R)), "vec")
            W("tensor", "vec", kb.cnt["vec"])
            OP("tensor", lambda e: e.matmul(
                ps_main[:, 2 * TT:3 * TT], c["ones1"][:],
                kb.f32[0:1, ROW_R:ROW_R + TT],
                start=True, stop=True))
            OP("tensor", lambda e: e.matmul(
                ps_main[:, 3 * TT:4 * TT], c["ones1"][:],
                kb.f32[0:1, ROW_M:ROW_M + TT],
                start=True, stop=True), "pe")
            W("scalar", "pe", kb.cnt["pe"])
            OP("scalar", lambda e: e.copy(kb.bfv(O_AB, TT),
                                          ps_main[:, 2 * TT:3 * TT]))
            OP("scalar", lambda e: e.copy(kb.bfv(O_AB + TT, TT),
                                          ps_main[:, 3 * TT:4 * TT]), "scl")
            W("vector", "scl", kb.cnt["scl"])
            for ch in range(CD):
                d = dst_o + ch * S + tn
                OP("vector", lambda e, d=d, ch=ch, src_o=src_o, tn=tn:
                   e.tensor_mul(kb.bfv(d, TT),
                                kb.bfv(src_o + ch * S + tn, TT),
                                kb.bfv(O_AB, TT)))
                DR("vector")
                OP("vector", lambda e, d=d: e.tensor_sub(
                    kb.bfv(d, TT), kb.bfv(d, TT), kb.bfv(O_AB + TT, TT)))
                DR("vector")
                OP("vector", lambda e, d=d, ch=ch, t=t:
                   e.tensor_reduce(c["amax"][:, t * CD + ch:t * CD + ch + 1],
                                   kb.bfv(d, TT), AX.X, ALU.max,
                                   apply_absolute_value=True),
                   "vec" if ch == CD - 1 else None)
            prev_stats_read = stats_read

    def exchange(clip_name, amax_cols):
        DR("vector")
        OP("vector", lambda e, n=amax_cols: e.tensor_reduce(
            c["r1"][:, 0:1], c["amax"][:, 0:n], AX.X, ALU.max), "vec")
        W("tensor", "vec", kb.cnt["vec"])
        OP("tensor", lambda e: e.transpose(
            ps_main[0:1, 0:128], c["r1"][:, 0:1], c["eyef"][:]), "pe")
        W("vector", "pe", kb.cnt["pe"])
        OP("vector", lambda e: e.reduce_max(
            c["sc"][0:1, 0:1], ps_main[0:1, 0:128], AX.X), "vec")
        W("sync", "vec", kb.cnt["vec"])
        tok = kb.dma("sync", lambda e: e.dma_start(
            out=kb.s_cci[0:1, 0:1], in_=c["sc"][0:1, 0:1]), "stc")
        kb.wtok("gpsimd", tok)
        OP("gpsimd", lambda e: e.collective_compute(
            "AllReduce", ALU.max,
            replica_groups=[list(range(kb.n_cores))],
            ins=[kb.s_cci[0:1, 0:1]], outs=[kb.s_cco[0:1, 0:1]]), "cc")
        W("sync", "cc", kb.cnt["cc"])
        tok = kb.dma("sync", lambda e: e.dma_start(
            out=c["sc"][0:1, 1:2], in_=kb.s_cco[0:1, 0:1]), "ccl")
        kb.wtok("scalar", tok)
        OP("scalar", lambda e, cn=clip_name: e.activation(
            c["sc"][0:1, 2:3], c[cn][0:1, 0:1], AF.Exp, scale=-1.0), "scl")
        W("vector", "scl", kb.cnt["scl"])
        OP("vector", lambda e: e.tensor_scalar(
            c["sc"][0:1, 3:4], c["sc"][0:1, 2:3], 1.0, None, ALU.add))
        DR("vector")
        OP("vector", lambda e: e.reciprocal(c["sc"][0:1, 2:3],
                                            c["sc"][0:1, 3:4]))
        DR("vector")
        OP("vector", lambda e: e.tensor_mul(c["sc"][0:1, 3:4],
                                            c["sc"][0:1, 2:3],
                                            c["sc"][0:1, 1:2]))
        DR("vector")
        OP("vector", lambda e: e.tensor_scalar(
            c["sc"][0:1, 4:5], c["sc"][0:1, 3:4], 1.0 / QMAXF, None,
            ALU.mult))
        DR("vector")
        OP("vector", lambda e: e.reciprocal(c["sc"][0:1, 5:6],
                                            c["sc"][0:1, 4:5]), "vec")
        W("tensor", "vec", kb.cnt["vec"])
        OP("tensor", lambda e: e.matmul(
            ps_main[:, 0:2], c["ones1"][:], c["sc"][0:1, 4:6],
            start=True, stop=True), "pe")
        W("vector", "pe", kb.cnt["pe"])
        OP("vector", lambda e: e.tensor_copy(c["pp"][:, 0:2],
                                             ps_main[:, 0:2]), "vec")
        DR("vector")

    def quantize(src_o, dst_o, nch):
        for ch in range(nch):
            OP("vector", lambda e, ch=ch, src_o=src_o: e.tensor_scalar(
                int_t[:, 0:S], kb.bfv(src_o + ch * S, S), c["pp"][:, 1:2],
                None, ALU.mult))
            DR("vector")
            OP("vector", lambda e, ch=ch, dst_o=dst_o: e.tensor_scalar(
                kb.bfv(dst_o + ch * S, S), int_t[:, 0:S], -128.0, 127.0,
                ALU.max, ALU.min), "vec")
            DR("vector")

    def scalevec(dst, srctile, ncols, extra):
        OP("vector", lambda e, dst=dst, s=srctile, n=ncols, x=extra:
           e.tensor_scalar(c[dst][:, 0:n], c[s][:, 0:n], c["pp"][:, 0:1],
                           x, ALU.mult, ALU.mult), "vec")
        DR("vector")

    def stile(scratch, k, t):
        return scratch[k, :].rearrange("(p s) -> p s", p=128)[
            :, t * TT:(t + 1) * TT]

    def ffn(final):
        layernorm(O_X, O_LN)
        exchange("clipx", NT * CD)
        scalevec("sxw_g", "swg", CH, 1.0)
        scalevec("sxw_u", "swu", CH, 1.0)
        quantize(O_LN, O_XQ, CD)
        xq_ready = kb.cnt["vec"]

        pe_w_marks = []
        gate_evac = {}
        up_evac = {t: 0 for t in range(NT)}
        h_marks = {}
        for i in range(CH):
            wg_s, wu_s = O_WS[i % 2], O_WS[2 + i % 2]
            if i >= 2:
                W("sync", "pe", pe_w_marks[i - 2])
            kb.wtok("sync", wgT_tok.get(i))
            kb.wtok("sync", wuT_tok.get(i))
            tokg = kb.dma("sync", lambda e, i=i, wg_s=wg_s: e.dma_start(
                out=kb.bfv(wg_s, 1024).rearrange("p (k j) -> p k j", j=128),
                in_=kb.s_wgT[i, :].rearrange("(k p j) -> p k j",
                                             p=128, j=128)), "wg")
            toku = kb.dma("sync", lambda e, i=i, wu_s=wu_s: e.dma_start(
                out=kb.bfv(wu_s, 1024).rearrange("p (k j) -> p k j", j=128),
                in_=kb.s_wuT[i, :].rearrange("(k p j) -> p k j",
                                             p=128, j=128)), "wu")
            kb.wtok("tensor", tokg)
            kb.wtok("tensor", toku)
            if i == 0:
                W("tensor", "vec", xq_ready)
            if i > 0:
                for t in range(NT):
                    W("tensor", "scl", gate_evac[t])
            for k in range(CD):
                for t in range(NT):
                    OP("tensor", lambda e, t=t, k=k, wg_s=wg_s: e.matmul(
                        ps_main[:, t * TT:(t + 1) * TT],
                        kb.bfv(wg_s + k * 128, 128),
                        kb.bfv(O_XQ + k * S + t * TT, TT),
                        start=(k == 0), stop=(k == CD - 1),
                        skip_group_check=True),
                       "pe" if (k == CD - 1) else None)
            gmm = kb.cnt["pe"]
            W("scalar", "pe", gmm)
            if i > 0:
                W("scalar", "vec", h_marks[i - 1])
            for t in range(NT):
                gate_evac[t] = OP(
                    "scalar", lambda e, t=t, i=i: e.activation(
                        kb.bfv(O_SG + t * TT, TT),
                        ps_main[:, t * TT:(t + 1) * TT], AF.Silu,
                        scale=c["sxw_g"][:, i:i + 1]), "scl")
            if i > 0:
                for t in range(NT):
                    W("tensor", "scl", up_evac[t])
            for k in range(CD):
                for t in range(NT):
                    OP("tensor", lambda e, t=t, k=k, wu_s=wu_s: e.matmul(
                        ps_aux[:, t * TT:(t + 1) * TT],
                        kb.bfv(wu_s + k * 128, 128),
                        kb.bfv(O_XQ + k * S + t * TT, TT),
                        start=(k == 0), stop=(k == CD - 1),
                        skip_group_check=True),
                       "pe" if (k == CD - 1) else None)
            pe_w_marks.append(kb.cnt["pe"])
            W("scalar", "pe", kb.cnt["pe"])
            for t in range(NT):
                up_evac[t] = OP(
                    "scalar", lambda e, t=t, i=i: e.activation(
                        kb.bfv(O_SU + t * TT, TT),
                        ps_aux[:, t * TT:(t + 1) * TT], AF.Identity,
                        scale=c["sxw_u"][:, i:i + 1]), "scl")
            W("vector", "scl", kb.cnt["scl"])
            if i > 0:
                kb.swait("vector", "sth")
            OP("vector", lambda e: e.tensor_mul(
                kb.bfv(O_HFF, S), kb.bfv(O_SG, S), kb.bfv(O_SU, S)))
            DR("vector")
            h_marks[i] = OP("vector", lambda e, i=i: e.tensor_reduce(
                c["amax"][:, i:i + 1], kb.bfv(O_HFF, S), AX.X, ALU.max,
                apply_absolute_value=True), "vec")
            W("sync", "vec", h_marks[i])
            kb.dma("sync", lambda e, i=i: e.dma_start(
                out=kb.s_h[i, :].rearrange("(p s) -> p s", p=128),
                in_=kb.bfv(O_HFF, S)), "sth")

        exchange("clipd", CH)
        scalevec("sxw_d", "swd", CD, 0.5)
        kb.swait("sync", "sth")
        kb.swait("sync", "stw")

        dn_pe = {}
        hq_vec = {}
        x_evac = {}
        W("tensor", "scl", kb.cnt["scl"])
        W("tensor", "vec", kb.cnt["vec"])
        for t in range(NT):
            for k in range(CH):
                wslot = O_WS[k % 2]
                hslot = O_T0 if k % 2 == 0 else O_T1
                qslot = O_T2 if k % 2 == 0 else O_T3
                prev = (t, k - 2) if k >= 2 else (t - 1, CH - 2 + k)
                if prev in dn_pe:
                    W("sync", "pe", dn_pe[prev])
                if prev in hq_vec:
                    W("sync", "vec", hq_vec[prev])
                tokw = kb.dma("sync", lambda e, k=k, wslot=wslot:
                              e.dma_start(
                                  out=kb.bfv(wslot, 1024).rearrange(
                                      "p (d j) -> p d j", j=128),
                                  in_=kb.s_wdT[k, :].rearrange(
                                      "(d p j) -> p d j", p=128, j=128)),
                              "wg")
                tokh = kb.dma("sync", lambda e, k=k, t=t, hslot=hslot:
                              e.dma_start(out=kb.bfv(hslot, TT),
                                          in_=stile(kb.s_h, k, t)), "hl")
                kb.wtok("vector", tokh)
                OP("vector", lambda e, hslot=hslot: e.tensor_scalar(
                    int_t[:, 0:TT], kb.bfv(hslot, TT), c["pp"][:, 1:2],
                    None, ALU.mult))
                DR("vector")
                hq_vec[(t, k)] = OP(
                    "vector", lambda e, qslot=qslot: e.tensor_scalar(
                        kb.bfv(qslot, TT), int_t[:, 0:TT], -128.0, 127.0,
                        ALU.max, ALU.min), "vec")
                DR("vector")
                kb.wtok("tensor", tokw)
                W("tensor", "vec", hq_vec[(t, k)])
                if k == 0 and (t - 1) in x_evac:
                    W("tensor", "vec", x_evac[t - 1])
                for d in range(CD):
                    bank = ps_main if d < 4 else ps_aux
                    OP("tensor", lambda e, d=d, k=k, wslot=wslot,
                       qslot=qslot, bank=bank: e.matmul(
                           bank[:, (d % 4) * TT:(d % 4 + 1) * TT],
                           kb.bfv(wslot + d * 128, 128), kb.bfv(qslot, TT),
                           start=(k == 0), stop=(k == CH - 1),
                           skip_group_check=True),
                       "pe" if d == CD - 1 else None)
                dn_pe[(t, k)] = kb.cnt["pe"]
            W("vector", "pe", kb.cnt["pe"])
            for d in range(CD):
                bank = ps_main if d < 4 else ps_aux
                if not final:
                    OP("vector", lambda e, d=d, t=t, bank=bank:
                       e.scalar_tensor_tensor(
                           kb.bfv(O_X + d * S + t * TT, TT),
                           bank[:, (d % 4) * TT:(d % 4 + 1) * TT],
                           c["sxw_d"][:, d:d + 1],
                           kb.bfv(O_X + d * S + t * TT, TT),
                           ALU.mult, ALU.add), "vec")
                    DR("vector")
                else:
                    pk = (d - 2) % 8
                    if (t > 0 or d >= 2) and pk in out_st:
                        kb.wtok("vector", out_st[pk])
                    OP("vector", lambda e, d=d, t=t, bank=bank:
                       e.scalar_tensor_tensor(
                           kb.f32v(FL + (d % 2) * TT, TT),
                           bank[:, (d % 4) * TT:(d % 4 + 1) * TT],
                           c["sxw_d"][:, d:d + 1],
                           kb.bfv(O_X + d * S + t * TT, TT),
                           ALU.mult, ALU.add), "vec")
                    DR("vector")
                    W("sync", "vec", kb.cnt["vec"])
                    out_st[d] = kb.dma(
                        "sync", lambda e, d=d, t=t: e.dma_start(
                            out=kb.d_out[d * 128:(d + 1) * 128,
                                         t * TT:(t + 1) * TT],
                            in_=kb.f32v(FL + (d % 2) * TT, TT)), "out")
            x_evac[t] = kb.cnt["vec"]

    out_st = {}

    def mamba():
        layernorm(O_X, O_LN)
        W("sync", "vec", kb.cnt["vec"])
        for d in range(CD):
            kb.dma("sync", lambda e, d=d: e.dma_start(
                out=kb.s_x1[d, :].rearrange("(p s) -> p s", p=128),
                in_=kb.bfv(O_X + d * S, S)), "stx")

        win_cast = []
        win_pe = {}
        win_evac = {}
        last_sz_st = [0]
        kb.swait("scalar", "stx")
        for o in range(2 * CI):
            fslot = 0 if o % 2 == 0 else 2816
            wslot = O_WS[o % 2]
            if o >= 2:
                W("sync", "scl", win_cast[o - 2])
            tok = kb.dma("sync", lambda e, o=o, fslot=fslot: e.dma_start(
                out=kb.f32[:, fslot:fslot + 1024].rearrange(
                    "p (k j) -> p k j", j=128),
                in_=di["W_in"][:, o * 128:(o + 1) * 128].rearrange(
                    "(k p) j -> p k j", p=128)), "xw")
            kb.wtok("scalar", tok)
            if o >= 2:
                W("scalar", "pe", win_pe[o - 2])
            win_cast.append(OP(
                "scalar", lambda e, fslot=fslot, wslot=wslot: e.copy(
                    kb.bfv(wslot, 1024), kb.f32[:, fslot:fslot + 1024]),
                "scl"))
            W("tensor", "scl", kb.cnt["scl"])
            for t in range(NT):
                if (o - 2, t) in win_evac:
                    W("tensor", "scl", win_evac[(o - 2, t)])
            for k in range(CD):
                for t in range(NT):
                    OP("tensor", lambda e, t=t, k=k, wslot=wslot: e.matmul(
                        ps_main[:, t * TT:(t + 1) * TT],
                        kb.bfv(wslot + k * 128, 128),
                        kb.bfv(O_LN + k * S + t * TT, TT),
                        start=(k == 0), stop=(k == CD - 1),
                        skip_group_check=True),
                       "pe" if (k == CD - 1) else None)
            win_pe[o] = kb.cnt["pe"]
            for t in range(NT):
                W("scalar", "pe", kb.cnt["pe"])
                if o < CI:
                    win_evac[(o, t)] = OP(
                        "scalar", lambda e, o=o, t=t: e.activation(
                            kb.bfv(xhp_off(o) + t * TT, TT),
                            ps_main[:, t * TT:(t + 1) * TT], AF.Identity,
                            bias=c["bxh"][:, o:o + 1]), "scl")
                else:
                    if t == 0:
                        kb.swait("scalar", "stz")
                    win_evac[(o, t)] = OP(
                        "scalar", lambda e, o=o, t=t: e.activation(
                            kb.bfv(O_STG + t * TT, TT),
                            ps_main[:, t * TT:(t + 1) * TT], AF.Silu,
                            bias=c["bz"][:, o - CI:o - CI + 1]), "scl")
            if o >= CI:
                W("sync", "scl", kb.cnt["scl"])
                kb.dma("sync", lambda e, o=o: e.dma_start(
                    out=kb.s_sz[o - CI, :].rearrange("(p s) -> p s", p=128),
                    in_=kb.bfv(O_STG, S)), "stz")

        kb.barrier(["scalar", "sync"], ["vector"])
        conv_scl = 0
        for ci in range(CI):
            xp = xhp_off(ci)
            xh = xh_off(ci)
            if conv_scl:
                W("vector", "scl", conv_scl)
            OP("vector", lambda e, xp=xp: e.tensor_copy(
                c["pad8"][:, 4:8], kb.bfv(xp, 4)))
            DR("vector")
            OP("vector", lambda e, ci=ci: e.tensor_scalar(
                kb.f32v(FA, 4), c["pad8"][:, 4:8],
                c["convw"][:, ci * 4 + 3:ci * 4 + 4], None, ALU.mult))
            OP("vector", lambda e, xp=xp, ci=ci: e.tensor_scalar(
                kb.f32v(FA + 4, S - 4), kb.bfv(xp + 4, S - 4),
                c["convw"][:, ci * 4 + 3:ci * 4 + 4], None, ALU.mult))
            for j in range(3):
                DR("vector")
                OP("vector", lambda e, j=j, ci=ci: e.scalar_tensor_tensor(
                    kb.f32v(FA, 4), c["pad8"][:, 1 + j:5 + j],
                    c["convw"][:, ci * 4 + j:ci * 4 + j + 1],
                    kb.f32v(FA, 4), ALU.mult, ALU.add))
                isl = OP("vector", lambda e, j=j, ci=ci, xp=xp:
                         e.scalar_tensor_tensor(
                             kb.f32v(FA + 4, S - 4),
                             kb.bfv(xp + 1 + j, S - 4),
                             c["convw"][:, ci * 4 + j:ci * 4 + j + 1],
                             kb.f32v(FA + 4, S - 4), ALU.mult, ALU.add),
                         "vec" if j == 2 else None)
            W("scalar", "vec", kb.cnt["vec"])
            conv_scl = OP("scalar", lambda e, xh=xh, ci=ci: e.activation(
                kb.bfv(xh, S), kb.f32v(FA, S), AF.Silu,
                bias=c["convb"][:, ci:ci + 1]), "scl")

        kb.barrier(["scalar"], ["tensor", "vector"])
        bm_evac = {}
        for t in range(NT):
            if (t - 1) in bm_evac:
                W("tensor", "scl", bm_evac[t - 1])
            for k in range(CI):
                OP("tensor", lambda e, t=t, k=k: e.matmul(
                    ps_aux[0:DS, 0:TT], c["WxB"][:, k * DS:(k + 1) * DS],
                    kb.bfv(xh_off(k) + t * TT, TT),
                    start=(k == 0), stop=False, skip_group_check=True))
            OP("tensor", lambda e: e.matmul(
                ps_aux[0:DS, 0:TT], c["bxB"][0:1, :],
                c["onesrow_b"][0:1, :], start=False, stop=True,
                skip_group_check=True), "pe")
            W("scalar", "pe", kb.cnt["pe"])
            bm_evac[t] = OP("scalar", lambda e, t=t: e.copy(
                bmatv(t * TT, TT), ps_aux[0:DS, 0:TT]), "scl")
        br_evac = {}
        br_bank_user = {}
        for s in range(DS):
            for t in range(NT):
                bank = TT + (t % 2) * TT
                if bank in br_bank_user:
                    W("tensor", "scl", br_bank_user[bank])
                W("tensor", "scl", bm_evac[t])
                OP("tensor", lambda e, s=s, t=t, bank=bank: e.matmul(
                    ps_aux[:, bank:bank + TT],
                    c["oneh"][:, s * 128:(s + 1) * 128],
                    bmatv(t * TT, TT), start=True, stop=True,
                    skip_group_check=True), "pe")
                W("scalar", "pe", kb.cnt["pe"])
                if s < NBREP_RES:
                    br_evac[(s, t)] = OP(
                        "scalar", lambda e, s=s, t=t, bank=bank: e.copy(
                            kb.bfv(brep_off(s) + t * TT, TT),
                            ps_aux[:, bank:bank + TT]), "scl")
                else:
                    if s > NBREP_RES or t > 0:
                        kb.swait("scalar", "stx")
                    br_evac[(s, t)] = OP(
                        "scalar", lambda e, s=s, t=t, bank=bank: e.copy(
                            kb.bfv(O_STG + t * TT, TT),
                            ps_aux[:, bank:bank + TT]), "scl")
                br_bank_user[bank] = br_evac[(s, t)]
            if s >= NBREP_RES:
                W("sync", "scl", kb.cnt["scl"])
                kb.dma("sync", lambda e, s=s: e.dma_start(
                    out=kb.s_brep[s - NBREP_RES, :].rearrange(
                        "(p s) -> p s", p=128),
                    in_=kb.bfv(O_STG, S)), "stx")
        kb.barrier(["scalar", "sync", "vector"],
                   ["vector", "tensor", "sync"])

        wdt_pe = {}
        sp_evac = {}
        scan_marks = {}
        add_marks = {}
        for ci in range(CI):
            if (ci - 1) in wdt_pe:
                W("sync", "pe", wdt_pe[ci - 1])
            tok = kb.dma("sync", lambda e, ci=ci: e.dma_start(
                out=kb.bfv(O_WDT, 2048).rearrange("p (k j) -> p k j", j=128),
                in_=kb.s_wdtb[:, ci * 128:(ci + 1) * 128].rearrange(
                    "(k p) j -> p k j", p=128)), "xw")
            kb.wtok("tensor", tok)
            if (ci - 1) in sp_evac:
                W("tensor", "scl", sp_evac[ci - 1])
            for k in range(CI):
                for t in range(NT):
                    OP("tensor", lambda e, t=t, k=k: e.matmul(
                        ps_main[:, t * TT:(t + 1) * TT],
                        kb.bfv(O_WDT + k * 128, 128),
                        kb.bfv(xh_off(k) + t * TT, TT),
                        start=(k == 0), stop=(k == CI - 1),
                        skip_group_check=True),
                       "pe" if (k == CI - 1) else None)
            wdt_pe[ci] = kb.cnt["pe"]
            W("scalar", "pe", kb.cnt["pe"])
            if ci > 0:
                W("scalar", "vec", scan_marks[(ci - 1, DS - 1)])
                kb.wtok("scalar", add_marks[(ci - 1, DS - 1)])
            for t in range(NT):
                OP("scalar", lambda e, t=t, ci=ci: e.activation(
                    kb.f32v(FL + t * TT, TT),
                    ps_main[:, t * TT:(t + 1) * TT], AF.Abs,
                    bias=c["bdt"][:, ci:ci + 1]))
                DR("scalar")
                OP("scalar", lambda e, t=t: e.activation(
                    kb.f32v(FL + t * TT, TT), kb.f32v(FL + t * TT, TT),
                    AF.Exp, scale=-1.0))
                DR("scalar")
                OP("scalar", lambda e, t=t: e.activation(
                    kb.f32v(FL + t * TT, TT), kb.f32v(FL + t * TT, TT),
                    AF.Ln, bias=1.0))
                DR("scalar")
                OP("scalar", lambda e, t=t, ci=ci: e.activation(
                    kb.f32v(FB + t * TT, TT),
                    ps_main[:, t * TT:(t + 1) * TT], AF.Relu,
                    bias=c["bdt"][:, ci:ci + 1]),
                   "scl" if t == NT - 1 else None)
            sp_evac[ci] = kb.cnt["scl"]
            W("vector", "scl", kb.cnt["scl"])
            OP("vector", lambda e: e.tensor_add(
                kb.f32v(FL, S), kb.f32v(FL, S), kb.f32v(FB, S)))
            DR("vector")
            kb.swait("vector", "sty")
            OP("vector", lambda e, ci=ci: e.tensor_mul(
                kb.bfv(O_DU, S), kb.f32v(FL, S), kb.bfv(xh_off(ci), S)))
            OP("vector", lambda e, ci=ci: e.tensor_scalar(
                kb.f32v(FY, S), kb.bfv(xh_off(ci), S),
                c["D"][:, ci:ci + 1], None, ALU.mult), "vec")
            DR("vector")
            delta_mark = kb.cnt["vec"]
            bs_tok = {}
            for sp in (NBREP_RES, NBREP_RES + 1):
                W("sync", "vec", kb.cnt["vec"])
                bs_tok[sp] = kb.dma(
                    "sync", lambda e, sp=sp, bsrc=O_BS[sp % 2]: e.dma_start(
                        out=kb.bfv(bsrc, S),
                        in_=kb.s_brep[sp - NBREP_RES, :].rearrange(
                            "(p s) -> p s", p=128)), "bs")
            for s in range(DS):
                dslot = FA if s % 2 == 0 else FA2
                hslot = FH if s % 2 == 0 else FB
                W("scalar", "vec", delta_mark)
                if s >= 2:
                    W("scalar", "vec", scan_marks[(ci, s - 2)])
                elif ci > 0:
                    W("scalar", "vec", scan_marks[(ci - 1, DS - 2 + s)])
                da = OP("scalar", lambda e, ci=ci, s=s, dslot=dslot:
                        e.activation(
                            kb.f32v(dslot, S), kb.f32v(FL, S), AF.Exp,
                            scale=c["negA"][:, ci * DS + s:ci * DS + s + 1]),
                        "scl")
                if s < NBREP_RES:
                    bsrc = brep_off(s)
                else:
                    bsrc = O_BS[s % 2]
                    kb.wtok("vector", bs_tok[s])
                OP("vector", lambda e, bsrc=bsrc: e.tensor_mul(
                    kb.bfv(O_DBU, S), kb.bfv(O_DU, S), kb.bfv(bsrc, S)))
                DR("vector")
                W("vector", "scl", da)
                if s % 2 == 0:
                    if s >= 2:
                        kb.wtok("vector", add_marks[(ci, s - 1)])
                    elif ci > 0:
                        kb.wtok("vector", add_marks[(ci - 1, DS - 1)])
                scan_marks[(ci, s)] = OP(
                    "vector", lambda e, dslot=dslot, hslot=hslot:
                    e.tensor_tensor_scan(
                        kb.f32v(hslot, S), kb.f32v(dslot, S),
                        kb.bfv(O_DBU, S), 0.0, ALU.mult, ALU.add), "vec")
                DR("vector")
                if s % 2 == 1:
                    scan_marks[(ci, s)] = OP(
                        "vector", lambda e: e.tensor_add(
                            kb.f32v(FH, S), kb.f32v(FH, S),
                            kb.f32v(FB, S)), "vec")
                    DR("vector")
                    W("gpsimd", "vec", scan_marks[(ci, s)])
                    if s > 1:
                        kb.wtok("gpsimd", add_marks[(ci, s - 2)])
                    elif ci > 0:
                        kb.wtok("gpsimd", add_marks[(ci - 1, DS - 1)])
                    add_marks[(ci, s)] = kb.dma(
                        "gpsimd", lambda e: e.dma_start(
                            out=kb.f32v(FY, S), in_=kb.f32v(FH, S),
                            accum_op=ALU.add), "yacc")
                if NBREP_RES <= s <= DS - 3:
                    W("sync", "vec", scan_marks[(ci, s)])
                    bs_tok[s + 2] = kb.dma(
                        "sync", lambda e, s=s, bsrc=O_BS[s % 2]: e.dma_start(
                            out=kb.bfv(bsrc, S),
                            in_=kb.s_brep[s + 2 - NBREP_RES, :].rearrange(
                                "(p s) -> p s", p=128)), "bs")
            kb.wtok("vector", add_marks[(ci, DS - 1)])
            for t in range(NT):
                szslot = O_SZ[t % 2]
                W("sync", "vec", kb.cnt["vec"])
                tok = kb.dma("sync", lambda e, ci=ci, t=t, szslot=szslot:
                             e.dma_start(out=kb.bfv(szslot, TT),
                                         in_=stile(kb.s_sz, ci, t)), "szl")
                kb.wtok("vector", tok)
                OP("vector", lambda e, t=t, szslot=szslot: e.tensor_mul(
                    kb.bfv(O_DU + t * TT, TT), kb.f32v(FY + t * TT, TT),
                    kb.bfv(szslot, TT)), "vec")
                DR("vector")
            W("sync", "vec", kb.cnt["vec"])
            kb.dma("sync", lambda e, ci=ci: e.dma_start(
                out=kb.s_yg[ci, :].rearrange("(p s) -> p s", p=128),
                in_=kb.bfv(O_DU, S)), "sty")
        kb.barrier(["vector", "sync", "scalar"],
                   ["sync", "tensor", "vector"])

        wo_pe = {}
        wo_evac = {}
        for t in range(NT):
            for k in range(CI):
                wslot = O_WS[k % 2]
                ygslot = O_T0 if k % 2 == 0 else O_T1
                prev = (t, k - 2) if k >= 2 else (t - 1, CI - 2 + k)
                if prev in wo_pe:
                    W("sync", "pe", wo_pe[prev])
                tokw = kb.dma("sync", lambda e, k=k, wslot=wslot:
                              e.dma_start(
                                  out=kb.bfv(wslot, 1024),
                                  in_=kb.s_woutb[k * 128:(k + 1) * 128, :]),
                              "wg")
                toky = kb.dma("sync", lambda e, k=k, t=t, ygslot=ygslot:
                              e.dma_start(out=kb.bfv(ygslot, TT),
                                          in_=stile(kb.s_yg, k, t)), "ygl")
                kb.wtok("tensor", tokw)
                kb.wtok("tensor", toky)
                if k == 0 and (t - 1) in wo_evac:
                    W("tensor", "vec", wo_evac[t - 1])
                for d in range(CD):
                    bank = ps_main if d < 4 else ps_aux
                    OP("tensor", lambda e, d=d, k=k, wslot=wslot,
                       ygslot=ygslot, bank=bank: e.matmul(
                           bank[:, (d % 4) * TT:(d % 4 + 1) * TT],
                           kb.bfv(wslot + d * 128, 128),
                           kb.bfv(ygslot, TT),
                           start=(k == 0), stop=(k == CI - 1),
                           skip_group_check=True),
                       "pe" if d == CD - 1 else None)
                wo_pe[(t, k)] = kb.cnt["pe"]
            W("vector", "pe", kb.cnt["pe"])
            for d in range(CD):
                xslot = O_T2 if d % 2 == 0 else O_T3
                W("sync", "vec", kb.cnt["vec"])
                tok = kb.dma("sync", lambda e, d=d, t=t, xslot=xslot:
                             e.dma_start(out=kb.bfv(xslot, TT),
                                         in_=stile(kb.s_x1, d, t)), "x1l")
                kb.wtok("vector", tok)
                bank = ps_main if d < 4 else ps_aux
                OP("vector", lambda e, d=d, t=t, xslot=xslot, bank=bank:
                   e.scalar_tensor_tensor(
                       kb.bfv(O_X + d * S + t * TT, TT),
                       bank[:, (d % 4) * TT:(d % 4 + 1) * TT],
                       c["bout"][:, d:d + 1],
                       kb.bfv(xslot, TT), ALU.add, ALU.add), "vec")
                DR("vector")
            wo_evac[t] = kb.cnt["vec"]
        kb.barrier(["vector", "sync", "gpsimd"],
                   ["scalar", "tensor", "vector", "sync"])

    ffn(final=False)
    kb.barrier(["vector", "scalar", "sync", "tensor"],
               ["vector", "scalar", "sync", "tensor"])
    mamba()
    ffn(final=True)
    kb.swait_all_dma("sync")


_BUILD_CACHE = {}


def _get_build(S, n_cores):
    key = (S, n_cores)
    if key not in _BUILD_CACHE:
        _BUILD_CACHE[key] = build(S, n_cores)
    return _BUILD_CACHE[key]


def _pack_rows(v, nch):
    return np.ascontiguousarray(np.asarray(v, np.float32).reshape(nch, 128).T)


def make_inputs(xc, w, S):
    bf = ml_dtypes.bfloat16
    Alog = np.asarray(w["L0_attn_A_log"], np.float32)
    convw = np.asarray(w["L0_attn_conv_w"], np.float32)[:, 0, :]
    one = {}
    one["x"] = np.ascontiguousarray(np.asarray(xc, np.float32).T)
    one["W_in"] = np.ascontiguousarray(np.asarray(w["L0_attn_W_in"],
                                                  np.float32))
    one["W_dt"] = np.ascontiguousarray(np.asarray(w["L0_attn_W_dt"],
                                                   np.float32))
    one["W_out"] = np.ascontiguousarray(np.asarray(w["L0_attn_W_out"],
                                                    np.float32))
    Wx = np.asarray(w["L0_attn_W_x"], np.float32)
    one["WxB"] = np.ascontiguousarray(
        Wx[:, DS:].reshape(CI, 128, DS).transpose(1, 0, 2)
        .reshape(128, CI * DS))
    one["bxB"] = np.ascontiguousarray(
        np.asarray(w["L0_attn_b_x"], np.float32)[DS:].reshape(1, DS))
    one["w_gate"] = np.ascontiguousarray(np.asarray(w["L0_ffn_w_gate"],
                                                     np.float32))
    one["w_up"] = np.ascontiguousarray(np.asarray(w["L0_ffn_w_up"],
                                                   np.float32))
    one["w_down"] = np.ascontiguousarray(np.asarray(w["L0_ffn_w_down"],
                                                     np.float32))
    one["Alog"] = np.ascontiguousarray(
        Alog.reshape(CI, 128, DS).transpose(1, 0, 2).reshape(128, CI * DS))
    one["convw"] = np.ascontiguousarray(
        convw.reshape(CI, 128, DC).transpose(1, 0, 2).reshape(128, CI * DC))
    one["convb"] = _pack_rows(w["L0_attn_conv_b"], CI)
    one["Dvec"] = _pack_rows(w["L0_attn_D"], CI)
    bin_ = np.asarray(w["L0_attn_b_in"], np.float32)
    one["bxh"] = _pack_rows(bin_[:DI], CI)
    one["bz"] = _pack_rows(bin_[DI:], CI)
    one["bdt"] = _pack_rows(w["L0_attn_b_dt"], CI)
    one["bout"] = _pack_rows(w["L0_attn_b_out"], CD)
    one["clipx"] = np.asarray(
        [[float(np.asarray(w["L0_ffn_clip_x"]).reshape(-1)[0]), 0.0]],
        np.float32)
    one["clipd"] = np.asarray(
        [[float(np.asarray(w["L0_ffn_clip_down"]).reshape(-1)[0]), 0.0]],
        np.float32)
    one["eyeb"] = np.eye(128, dtype=bf)
    one["eyef"] = np.eye(128, dtype=np.float32)
    one["onesb"] = np.ones((128, 2), dtype=bf)
    one["ones1"] = np.ones((1, 128), np.float32)
    one["onesrow_b"] = np.ones((1, TT), dtype=bf)
    cv = np.zeros((128, 8), np.float32)
    cv[:, 0] = LN_EPS
    cv[:, 1] = 1.0
    cv[:, 2] = -1.0
    cv[:, 3] = -0.5
    one["cvals"] = cv
    oneh = np.zeros((DS, DS * 128), dtype=bf)
    for s in range(DS):
        oneh[s, s * 128:(s + 1) * 128] = 1
    one["oneh"] = oneh
    return one


def kernel(**inputs):
    x = np.asarray(inputs["L0_attn_x"], np.float32)
    S = x.shape[1]
    kb = _get_build(S, NCORE)
    in_maps = [make_inputs(x[cidx], inputs, S) for cidx in range(NCORE)]
    res = run_bass_kernel_spmd(kb.nc, in_maps, core_ids=list(range(NCORE)))
    outs = [np.asarray(r["out"]).T for r in res.results]
    return np.stack(outs, axis=0).astype(np.float32)


# version 07
# speedup vs baseline: 1.1139x; 1.0800x over previous
import numpy as np
import ml_dtypes
import concourse.bass as bass
import concourse.mybir as mybir
from concourse.bass_utils import run_bass_kernel_spmd
from contextlib import ExitStack

F32 = mybir.dt.float32
BF16 = mybir.dt.bfloat16
I32 = mybir.dt.int32
AF = mybir.ActivationFunctionType
ALU = mybir.AluOpType
AX = mybir.AxisListType

DM, DI, DS, DC, HF = 1024, 2048, 16, 4, 2816
CD, CI, CH = DM // 128, DI // 128, HF // 128
TT = 512
LN_EPS = 1e-5
QMAXF = 127.0
NCORE = 8
ENGS = ("sync", "vector", "scalar", "tensor", "gpsimd")


class KB:
    def __init__(self, S, n_cores):
        self.S = S
        self.NT = S // TT
        self.n_cores = n_cores
        self.nc = bass.Bass("TRN2", target_bir_lowering=False)
        self.es = ExitStack()
        self.q = {e: [] for e in ENGS}
        self._mk_dram()

    def _mk_dram(self):
        nc, S = self.nc, self.S
        self.di = {}

        def din(name, shape, dt=F32):
            self.di[name] = nc.dram_tensor(name, shape, dt,
                                           kind="ExternalInput").ap()

        din("x", [DM, S])
        din("W_in", [DM, 2 * DI])
        din("W_dt", [DI, DI])
        din("W_out", [DI, DM])
        din("WxB", [128, CI * DS])
        din("bxB", [1, DS])
        din("w_gate", [HF, DM])
        din("w_up", [HF, DM])
        din("w_down", [DM, HF])
        din("Alog", [128, CI * DS])
        din("convw", [128, CI * DC])
        din("convb", [128, CI])
        din("Dvec", [128, CI])
        din("bxh", [128, CI])
        din("bz", [128, CI])
        din("bdt", [128, CI])
        din("bout", [128, CD])
        din("clipx", [1, 2])
        din("clipd", [1, 2])
        din("cvals", [128, 8])
        din("eyeb", [128, 128], BF16)
        din("eyef", [128, 128], F32)
        din("onesb", [128, 2], BF16)
        din("ones1", [1, 128])
        din("onesrow_b", [1, TT], BF16)
        din("oneh", [DS, DS * 128], BF16)
        self.d_out = nc.dram_tensor("out", [DM, S], F32,
                                    kind="ExternalOutput").ap()

        def scr(name, shape, dt):
            return nc.dram_tensor(name, shape, dt, kind="Internal").ap()

        self.s_wdtb = scr("s_wdtb", [DI, DI], BF16)
        self.s_woutb = scr("s_woutb", [DI, DM], BF16)
        self.s_wgT = scr("s_wgT", [CH, CD * 128 * 128], BF16)
        self.s_wuT = scr("s_wuT", [CH, CD * 128 * 128], BF16)
        self.s_wdT = scr("s_wdT", [CH, CD * 128 * 128], BF16)
        self.s_h = scr("s_h", [CH, 128 * S], BF16)
        self.s_sz = scr("s_sz", [CI, 128 * S], BF16)
        self.s_yg = scr("s_yg", [CI, 128 * S], BF16)
        self.s_x1 = scr("s_x1", [CD, 128 * S], BF16)
        self.s_brep = scr("s_brep", [6, 128 * S], BF16)
        self.s_cci = scr("s_cci", [1, 2], F32)
        self.s_cco = scr("s_cco", [1, 2], F32)

    def mk_mem(self):
        nc, S, es = self.nc, self.S, self.es

        def sb(name, shape, dt=F32):
            return es.enter_context(nc.sbuf_tensor(name, shape, dt))

        self.bf = sb("bf_arena", [128, 32 * S], BF16)
        self.f32 = sb("f32_arena", [128, 7 * S], F32)
        self.int_t = sb("int_t", [128, S], I32)
        self.c = {}
        for name, shape, dt in [
            ("negA", [128, CI * DS], F32), ("convw", [128, CI * DC], F32),
            ("convb", [128, CI], F32), ("D", [128, CI], F32),
            ("bxh", [128, CI], F32), ("bz", [128, CI], F32),
            ("bdt", [128, CI], F32), ("bout", [128, CD], F32),
            ("WxB", [128, CI * DS], BF16), ("bxB", [1, DS], BF16),
            ("eyeb", [128, 128], BF16), ("eyef", [128, 128], F32),
            ("onesb", [128, 2], BF16),
            ("ones1", [1, 128], F32), ("onesrow_b", [1, TT], BF16),
            ("oneh", [DS, DS * 128], BF16),
            ("clipx", [1, 2], F32), ("clipd", [1, 2], F32),
            ("cvals", [128, 8], F32),
            ("swg", [128, CH], F32), ("swu", [128, CH], F32),
            ("swd", [128, CD], F32),
            ("sxw_g", [128, CH], F32), ("sxw_u", [128, CH], F32),
            ("sxw_d", [128, CD], F32),
            ("amax", [128, 32], F32), ("r1", [128, 2], F32),
            ("sc", [1, 8], F32), ("pp", [128, 4], F32),
            ("pad8", [128, 8], BF16),
        ]:
            self.c[name] = sb("c_" + name, shape, dt)
        self.sems = {}
        for n in ["pe", "vec", "scl", "gps", "cc"]:
            self.sems[n] = es.enter_context(nc.semaphore(n))
        self.cnt = {n: 0 for n in self.sems}
        self.sempool = [es.enter_context(nc.semaphore(f"dmas{i}"))
                        for i in range(44)]
        self.sempool_next = 0
        self.streams = {}
        self.ps_main = es.enter_context(
            nc.psum_tensor("ps_main", [128, 4 * TT], F32))
        self._ps_tb_scope = ExitStack()
        self.ps_tb = self._ps_tb_scope.enter_context(
            nc.psum_tensor("ps_tb", [128, 1024], BF16))
        self.ps_aux = None
        for col, val in enumerate([LN_EPS, 1.0, -1.0, -0.5]):
            self.nc.const_aps.register(F32, val, self.c["cvals"][:, col:col + 1])

    def close_p0_psum(self):
        self._ps_tb_scope.close()
        self.ps_tb = None
        self.ps_aux = self.es.enter_context(
            self.nc.psum_tensor("ps_aux", [128, 4 * TT], F32))

    def op(self, eng, fn, inc=None, n=1):
        if inc is None:
            self.q[eng].append(fn)
            return None
        self.cnt[inc] += n
        sem = self.sems[inc]
        self.q[eng].append(
            lambda e, fn=fn, sem=sem, n=n: fn(e).then_inc(sem, n))
        return self.cnt[inc]

    def wait(self, eng, semname, val):
        if val and val > 0:
            sem = self.sems[semname]
            self.q[eng].append(
                lambda e, sem=sem, val=val: e.wait_ge(sem, val))

    def _stream(self, name):
        st = self.streams.get(name)
        if st is None:
            a = self.sempool[self.sempool_next]
            b = self.sempool[self.sempool_next + 1]
            self.sempool_next += 2
            st = {"sems": [a, b], "cnt": [0, 0], "par": 0}
            self.streams[name] = st
        return st

    def dma(self, eng, fn, stream):
        st = self._stream(stream)
        p = st["par"]
        sem = st["sems"][p]
        if st["cnt"][p] > 0:
            v = st["cnt"][p]
            self.q[eng].append(
                lambda e, sem=sem, v=v: e.wait_ge(sem, v))
        st["cnt"][p] += 16
        v = st["cnt"][p]
        self.q[eng].append(
            lambda e, fn=fn, sem=sem: fn(e).then_inc(sem, 16))
        st["par"] = 1 - p
        return (sem, v)

    def wtok(self, eng, tok):
        if tok is None:
            return
        sem, v = tok
        self.q[eng].append(lambda e, sem=sem, v=v: e.wait_ge(sem, v))

    def swait(self, eng, stream):
        st = self.streams.get(stream)
        if not st:
            return
        for p in (0, 1):
            if st["cnt"][p] > 0:
                sem, v = st["sems"][p], st["cnt"][p]
                self.q[eng].append(
                    lambda e, sem=sem, v=v: e.wait_ge(sem, v))

    def swait_all_dma(self, eng):
        for name in list(self.streams):
            self.swait(eng, name)

    def drain(self, eng):
        self.q[eng].append(lambda e: e.drain())

    def barrier(self, writers, waiters):
        snap = []
        sync_snap = []
        for w in writers:
            if w == "sync":
                for name, st in self.streams.items():
                    for p in (0, 1):
                        if st["cnt"][p] > 0:
                            sync_snap.append((st["sems"][p], st["cnt"][p]))
            else:
                s = {"vector": "vec", "scalar": "scl", "tensor": "pe",
                     "gpsimd": "gps"}[w]
                snap.append((w, s, self.cnt[s]))
        for eng in waiters:
            for w, s, v in snap:
                if eng == w:
                    self.drain(eng)
                else:
                    self.wait(eng, s, v)
            for sem, v in sync_snap:
                self.q[eng].append(
                    lambda e, sem=sem, v=v: e.wait_ge(sem, v))

    def bfv(self, off, cols):
        return self.bf[:, off:off + cols]

    def f32v(self, off, cols):
        return self.f32[:, off:off + cols]


def build(S, n_cores):
    kb = KB(S, n_cores)
    with kb.es:
        kb.mk_mem()
        _emit(kb)
        blk = kb.es.enter_context(kb.nc.Block())
        for eng in ENGS:
            ops = kb.q[eng]
            if ops:
                getattr(blk, eng)(lambda e, ops=ops: [f(e) for f in ops])
    return kb


def _emit(kb: KB):
    S, NT = kb.S, kb.NT
    c, di = kb.c, kb.di
    int_t = kb.int_t
    ps_main = kb.ps_main

    def U(k):
        return k * S

    O_X, O_LN, O_XQ = U(0), U(8), U(16)
    O_SG, O_SU, O_HFF = U(24), U(25), U(26)
    O_WS = [U(27), U(27) + 1024, U(28), U(28) + 1024]
    O_T0, O_T1 = U(29), U(29) + TT
    O_T2, O_T3 = U(29) + 2 * TT, U(29) + 3 * TT
    O_STG = U(30)
    O_AB = U(26)
    O_BLK = U(24)

    def xh_off(ci):
        return U(8 + ci) if ci < 8 else U(ci - 8)

    def xhp_off(ci):
        return U(ci) if ci < 8 else U(16 + ci - 8)

    def brep_off(s):
        return U(16 + s)

    NBREP_RES = 10
    O_BS = [U(26), U(27)]
    O_WDT = U(28)
    O_DU = U(29)
    O_SZ = [U(30), U(30) + TT]
    O_DBU = U(31)
    O_BMAT = U(29)

    FA, FB, FH, FY, FL, FA2 = 0, S, 2 * S, 3 * S, 4 * S, 5 * S
    FH3 = 6 * S
    HS = [FH, FB, FH3]
    ROW_M, ROW_T, ROW_R = FA2, FA2 + TT, FA2 + 2 * TT

    W, OP, DR = kb.wait, kb.op, kb.drain

    def rowv(off):
        return kb.f32[0:1, off:off + TT]

    def bmatv(t0, cols):
        return kb.bf[0:DS, O_BMAT + t0:O_BMAT + t0 + cols]

    for name in ["cvals", "convw", "convb", "bxh", "bz", "bdt", "bout",
                 "eyeb", "eyef", "onesb", "ones1", "onesrow_b", "oneh",
                 "clipx", "clipd"]:
        kb.dma("sync", lambda e, n=name: e.dma_start(out=c[n][:], in_=di[n]),
               "cst")
    kb.dma("sync", lambda e: e.dma_start(out=c["D"][:], in_=di["Dvec"]),
           "cst")
    kb.dma("sync", lambda e: e.dma_start(out=c["negA"][:], in_=di["Alog"]),
           "cst")
    kb.dma("sync", lambda e: e.dma_start(out=kb.f32[:, 0:CI * DS],
                                         in_=di["WxB"]), "cst")
    kb.dma("sync", lambda e: e.dma_start(
        out=kb.f32[0:1, CI * DS:CI * DS + DS], in_=di["bxB"]), "cst")
    kb.swait("scalar", "cst")
    OP("scalar", lambda e: e.copy(c["WxB"][:], kb.f32[:, 0:CI * DS]))
    OP("scalar", lambda e: e.copy(c["bxB"][:],
                                  kb.f32[0:1, CI * DS:CI * DS + DS]), "scl")
    OP("scalar", lambda e: e.activation(c["negA"][:], c["negA"][:], AF.Exp))
    DR("scalar")
    OP("scalar", lambda e: e.mul(c["negA"][:], c["negA"][:], -1.0), "scl")
    kb.swait("vector", "cst")
    OP("vector", lambda e: e.memset(c["pad8"][:], 0.0), "vec")

    xcopy_marks = []
    W("sync", "scl", kb.cnt["scl"])
    for ch in range(CD):
        slot = FL if ch % 2 == 0 else FA
        if ch >= 2:
            W("sync", "scl", xcopy_marks[ch - 2])
        tok = kb.dma("sync", lambda e, ch=ch, slot=slot: e.dma_start(
            out=kb.f32v(slot, S), in_=di["x"][ch * 128:(ch + 1) * 128, :]),
            "xw")
        kb.wtok("scalar", tok)
        xcopy_marks.append(OP("scalar", lambda e, ch=ch, slot=slot: e.copy(
            kb.bfv(O_X + ch * S, S), kb.f32v(slot, S)), "scl"))

    qmarks = []

    def quant_transpose(wname, nch, kd, stile, dstT, toks,
                        scatter_rows=False):
        nkb = kd // 128
        for i in range(nch):
            slot = 0 if i % 2 == 0 else 2816
            if len(qmarks) >= 2:
                W("sync", "vec", qmarks[-2])
            tok = kb.dma("sync", lambda e, w=wname, i=i, slot=slot, kd=kd:
                         e.dma_start(out=kb.f32[:, slot:slot + kd],
                                     in_=di[w][i * 128:(i + 1) * 128, :]),
                         "xw")
            kb.wtok("vector", tok)
            OP("vector", lambda e, slot=slot, kd=kd: e.tensor_reduce(
                c["r1"][:, 0:1], kb.f32[:, slot:slot + kd], AX.X, ALU.max,
                apply_absolute_value=True))
            DR("vector")
            OP("vector", lambda e, st=stile, i=i: e.tensor_scalar(
                c[st][:, i:i + 1], c["r1"][:, 0:1], 1.0 / QMAXF, None,
                ALU.mult))
            OP("vector", lambda e: e.reciprocal(c["r1"][:, 1:2],
                                                c["r1"][:, 0:1]))
            DR("vector")
            W("vector", "pe", kb.cnt["pe"])
            mlast = None
            for p0 in range(0, kd, S):
                pn = min(S, kd - p0)
                OP("vector", lambda e, slot=slot, p0=p0, pn=pn:
                   e.tensor_scalar(int_t[:, 0:pn],
                                   kb.f32[:, slot + p0:slot + p0 + pn],
                                   c["r1"][:, 1:2], QMAXF,
                                   ALU.mult, ALU.mult))
                DR("vector")
                mlast = OP("vector", lambda e, p0=p0, pn=pn: e.tensor_copy(
                    kb.bf[:, O_BLK + p0:O_BLK + p0 + pn], int_t[:, 0:pn]), "vec")
                DR("vector")
            qmarks.append(mlast)
            kb.swait("scalar", "stw")
            for g0 in range(0, nkb, 8):
                gn = min(8, nkb - g0)
                W("tensor", "vec", kb.cnt["vec"])
                W("tensor", "scl", kb.cnt["scl"])
                for j in range(gn):
                    OP("tensor", lambda e, g0=g0, j=j, pstb=kb.ps_tb:
                       e.transpose(
                           pstb[:, j * 128:(j + 1) * 128],
                           kb.bf[:, O_BLK + (g0 + j) * 128:O_BLK + (g0 + j + 1) * 128],
                           c["eyeb"][:]), "pe" if j == gn - 1 else None)
                W("scalar", "pe", kb.cnt["pe"])
                for j in range(gn):
                    OP("scalar", lambda e, g0=g0, j=j, pstb=kb.ps_tb: e.copy(
                        kb.bfv(O_STG + (g0 + j) * 128, 128),
                        pstb[:, j * 128:(j + 1) * 128]),
                       "scl" if j == gn - 1 else None)
            W("sync", "scl", kb.cnt["scl"])
            if not scatter_rows:
                toks[i] = kb.dma(
                    "sync", lambda e, dstT=dstT, i=i, nkb=nkb: e.dma_start(
                        out=dstT[i, 0:nkb * 128 * 128].rearrange(
                            "(k p j) -> p k j", p=128, j=128),
                        in_=kb.bfv(O_STG, nkb * 128).rearrange(
                            "p (k j) -> p k j", j=128)), "stw")
            else:
                toks[i] = kb.dma(
                    "sync", lambda e, dstT=dstT, i=i, nkb=nkb: e.dma_start(
                        out=dstT[:, i * 128 * 128:(i + 1) * 128 * 128]
                        .rearrange("k (p j) -> p k j", j=128),
                        in_=kb.bfv(O_STG, nkb * 128).rearrange(
                            "p (k j) -> p k j", j=128)), "stw")

    kb.barrier(["scalar"], ["sync"])
    wgT_tok, wuT_tok, wdT_tok = {}, {}, {}
    quant_transpose("w_gate", CH, DM, "swg", kb.s_wgT, wgT_tok)
    quant_transpose("w_up", CH, DM, "swu", kb.s_wuT, wuT_tok)
    quant_transpose("w_down", CD, HF, "swd", kb.s_wdT, wdT_tok,
                    scatter_rows=True)

    cast_marks = []
    for wname, dst, nch, kd in [("W_dt", kb.s_wdtb, CI, DI),
                                ("W_out", kb.s_woutb, CI, DM)]:
        for i in range(nch):
            slot = 0 if i % 2 == 0 else 2816
            if len(cast_marks) >= 2:
                W("sync", "scl", cast_marks[-2])
            tok = kb.dma("sync", lambda e, w=wname, i=i, slot=slot, kd=kd:
                         e.dma_start(out=kb.f32[:, slot:slot + kd],
                                     in_=di[w][i * 128:(i + 1) * 128, :]),
                         "xw")
            kb.wtok("scalar", tok)
            kb.swait("scalar", "stw")
            cast_marks.append(OP("scalar", lambda e, slot=slot, kd=kd: e.copy(
                kb.bfv(O_STG, kd), kb.f32[:, slot:slot + kd]), "scl"))
            W("sync", "scl", kb.cnt["scl"])
            kb.dma("sync", lambda e, dst=dst, i=i, kd=kd: e.dma_start(
                out=dst[i * 128:(i + 1) * 128, :], in_=kb.bfv(O_STG, kd)),
                "stw")

    kb.close_p0_psum()
    ps_aux = kb.ps_aux

    def layernorm(src_o, dst_o):
        prev_stats_read = 0
        W("tensor", "scl", kb.cnt["scl"])
        for t in range(NT):
            tn = t * TT
            sq_marks = []
            pe_sq = []
            W("tensor", "vec", prev_stats_read)
            for ch in range(CD):
                sq = O_T0 if ch % 2 == 0 else O_T1
                if ch >= 2:
                    W("scalar", "pe", pe_sq[ch - 2])
                sq_marks.append(OP(
                    "scalar", lambda e, ch=ch, sq=sq, src_o=src_o, tn=tn:
                    e.activation(kb.bfv(sq, TT),
                                 kb.bfv(src_o + ch * S + tn, TT), AF.Square),
                    "scl"))
                OP("tensor", lambda e, ch=ch, src_o=src_o, tn=tn: e.matmul(
                    ps_main[0:1, 0:TT], c["onesb"][:, 0:1],
                    kb.bfv(src_o + ch * S + tn, TT),
                    start=(ch == 0), stop=(ch == CD - 1),
                    skip_group_check=True))
                W("tensor", "scl", sq_marks[ch])
                pe_sq.append(OP("tensor", lambda e, ch=ch, sq=sq: e.matmul(
                    ps_main[0:1, TT:2 * TT], c["onesb"][:, 0:1],
                    kb.bfv(sq, TT),
                    start=(ch == 0), stop=(ch == CD - 1),
                    skip_group_check=True), "pe"))
            W("vector", "pe", kb.cnt["pe"])
            OP("vector", lambda e: e.tensor_scalar(
                rowv(ROW_M), ps_main[0:1, 0:TT], 1.0 / DM, None, ALU.mult))
            OP("vector", lambda e: e.tensor_scalar(
                rowv(ROW_T), ps_main[0:1, TT:2 * TT], 1.0 / DM, None,
                ALU.mult))
            DR("vector")
            OP("vector", lambda e: e.tensor_mul(rowv(ROW_R), rowv(ROW_M),
                                                rowv(ROW_M)))
            DR("vector")
            OP("vector", lambda e: e.tensor_sub(rowv(ROW_T), rowv(ROW_T),
                                                rowv(ROW_R)), "vec")
            stats_read = kb.cnt["vec"]
            W("scalar", "vec", stats_read)
            OP("scalar", lambda e: e.activation(rowv(ROW_R), rowv(ROW_T),
                                                AF.Ln, bias=LN_EPS))
            DR("scalar")
            OP("scalar", lambda e: e.activation(rowv(ROW_R), rowv(ROW_R),
                                                AF.Exp, scale=-0.5), "scl")
            W("vector", "scl", kb.cnt["scl"])
            OP("vector", lambda e: e.tensor_mul(rowv(ROW_M), rowv(ROW_M),
                                                rowv(ROW_R)), "vec")
            W("tensor", "vec", kb.cnt["vec"])
            OP("tensor", lambda e: e.matmul(
                ps_main[:, 2 * TT:3 * TT], c["ones1"][:],
                kb.f32[0:1, ROW_R:ROW_R + TT],
                start=True, stop=True))
            OP("tensor", lambda e: e.matmul(
                ps_main[:, 3 * TT:4 * TT], c["ones1"][:],
                kb.f32[0:1, ROW_M:ROW_M + TT],
                start=True, stop=True), "pe")
            W("scalar", "pe", kb.cnt["pe"])
            OP("scalar", lambda e: e.copy(kb.bfv(O_AB, TT),
                                          ps_main[:, 2 * TT:3 * TT]))
            OP("scalar", lambda e: e.copy(kb.bfv(O_AB + TT, TT),
                                          ps_main[:, 3 * TT:4 * TT]), "scl")
            W("vector", "scl", kb.cnt["scl"])
            for ch in range(CD):
                d = dst_o + ch * S + tn
                OP("vector", lambda e, d=d, ch=ch, src_o=src_o, tn=tn:
                   e.tensor_mul(kb.bfv(d, TT),
                                kb.bfv(src_o + ch * S + tn, TT),
                                kb.bfv(O_AB, TT)))
                DR("vector")
                OP("vector", lambda e, d=d: e.tensor_sub(
                    kb.bfv(d, TT), kb.bfv(d, TT), kb.bfv(O_AB + TT, TT)))
                DR("vector")
                OP("vector", lambda e, d=d, ch=ch, t=t:
                   e.tensor_reduce(c["amax"][:, t * CD + ch:t * CD + ch + 1],
                                   kb.bfv(d, TT), AX.X, ALU.max,
                                   apply_absolute_value=True),
                   "vec" if ch == CD - 1 else None)
            prev_stats_read = stats_read

    def exchange(clip_name, amax_cols):
        DR("vector")
        OP("vector", lambda e, n=amax_cols: e.tensor_reduce(
            c["r1"][:, 0:1], c["amax"][:, 0:n], AX.X, ALU.max), "vec")
        W("tensor", "vec", kb.cnt["vec"])
        OP("tensor", lambda e: e.transpose(
            ps_main[0:1, 0:128], c["r1"][:, 0:1], c["eyef"][:]), "pe")
        W("vector", "pe", kb.cnt["pe"])
        OP("vector", lambda e: e.reduce_max(
            c["sc"][0:1, 0:1], ps_main[0:1, 0:128], AX.X), "vec")
        W("sync", "vec", kb.cnt["vec"])
        tok = kb.dma("sync", lambda e: e.dma_start(
            out=kb.s_cci[0:1, 0:1], in_=c["sc"][0:1, 0:1]), "stc")
        kb.wtok("gpsimd", tok)
        OP("gpsimd", lambda e: e.collective_compute(
            "AllReduce", ALU.max,
            replica_groups=[list(range(kb.n_cores))],
            ins=[kb.s_cci[0:1, 0:1]], outs=[kb.s_cco[0:1, 0:1]]), "cc")
        W("sync", "cc", kb.cnt["cc"])
        tok = kb.dma("sync", lambda e: e.dma_start(
            out=c["sc"][0:1, 1:2], in_=kb.s_cco[0:1, 0:1]), "ccl")
        kb.wtok("scalar", tok)
        OP("scalar", lambda e, cn=clip_name: e.activation(
            c["sc"][0:1, 2:3], c[cn][0:1, 0:1], AF.Exp, scale=-1.0), "scl")
        W("vector", "scl", kb.cnt["scl"])
        OP("vector", lambda e: e.tensor_scalar(
            c["sc"][0:1, 3:4], c["sc"][0:1, 2:3], 1.0, None, ALU.add))
        DR("vector")
        OP("vector", lambda e: e.reciprocal(c["sc"][0:1, 2:3],
                                            c["sc"][0:1, 3:4]))
        DR("vector")
        OP("vector", lambda e: e.tensor_mul(c["sc"][0:1, 3:4],
                                            c["sc"][0:1, 2:3],
                                            c["sc"][0:1, 1:2]))
        DR("vector")
        OP("vector", lambda e: e.tensor_scalar(
            c["sc"][0:1, 4:5], c["sc"][0:1, 3:4], 1.0 / QMAXF, None,
            ALU.mult))
        DR("vector")
        OP("vector", lambda e: e.reciprocal(c["sc"][0:1, 5:6],
                                            c["sc"][0:1, 4:5]), "vec")
        W("tensor", "vec", kb.cnt["vec"])
        OP("tensor", lambda e: e.matmul(
            ps_main[:, 0:2], c["ones1"][:], c["sc"][0:1, 4:6],
            start=True, stop=True), "pe")
        W("vector", "pe", kb.cnt["pe"])
        OP("vector", lambda e: e.tensor_copy(c["pp"][:, 0:2],
                                             ps_main[:, 0:2]), "vec")
        DR("vector")

    def quantize(src_o, dst_o, nch):
        for ch in range(nch):
            OP("vector", lambda e, ch=ch, src_o=src_o: e.tensor_scalar(
                int_t[:, 0:S], kb.bfv(src_o + ch * S, S), c["pp"][:, 1:2],
                None, ALU.mult))
            DR("vector")
            OP("vector", lambda e, ch=ch, dst_o=dst_o: e.tensor_scalar(
                kb.bfv(dst_o + ch * S, S), int_t[:, 0:S], -128.0, 127.0,
                ALU.max, ALU.min), "vec")
            DR("vector")

    def scalevec(dst, srctile, ncols, extra):
        OP("vector", lambda e, dst=dst, s=srctile, n=ncols, x=extra:
           e.tensor_scalar(c[dst][:, 0:n], c[s][:, 0:n], c["pp"][:, 0:1],
                           x, ALU.mult, ALU.mult), "vec")
        DR("vector")

    def stile(scratch, k, t):
        return scratch[k, :].rearrange("(p s) -> p s", p=128)[
            :, t * TT:(t + 1) * TT]

    def ffn(final):
        layernorm(O_X, O_LN)
        exchange("clipx", NT * CD)
        scalevec("sxw_g", "swg", CH, 1.0)
        scalevec("sxw_u", "swu", CH, 1.0)
        quantize(O_LN, O_XQ, CD)
        xq_ready = kb.cnt["vec"]

        pe_w_marks = []
        gate_evac = {}
        up_evac = {t: 0 for t in range(NT)}
        h_marks = {}
        for i in range(CH):
            wg_s, wu_s = O_WS[i % 2], O_WS[2 + i % 2]
            if i >= 2:
                W("sync", "pe", pe_w_marks[i - 2])
            kb.wtok("sync", wgT_tok.get(i))
            kb.wtok("sync", wuT_tok.get(i))
            tokg = kb.dma("sync", lambda e, i=i, wg_s=wg_s: e.dma_start(
                out=kb.bfv(wg_s, 1024).rearrange("p (k j) -> p k j", j=128),
                in_=kb.s_wgT[i, :].rearrange("(k p j) -> p k j",
                                             p=128, j=128)), "wg")
            toku = kb.dma("sync", lambda e, i=i, wu_s=wu_s: e.dma_start(
                out=kb.bfv(wu_s, 1024).rearrange("p (k j) -> p k j", j=128),
                in_=kb.s_wuT[i, :].rearrange("(k p j) -> p k j",
                                             p=128, j=128)), "wu")
            kb.wtok("tensor", tokg)
            kb.wtok("tensor", toku)
            if i == 0:
                W("tensor", "vec", xq_ready)
            if i > 0:
                for t in range(NT):
                    W("tensor", "scl", gate_evac[t])
            for k in range(CD):
                for t in range(NT):
                    OP("tensor", lambda e, t=t, k=k, wg_s=wg_s: e.matmul(
                        ps_main[:, t * TT:(t + 1) * TT],
                        kb.bfv(wg_s + k * 128, 128),
                        kb.bfv(O_XQ + k * S + t * TT, TT),
                        start=(k == 0), stop=(k == CD - 1),
                        skip_group_check=True),
                       "pe" if (k == CD - 1) else None)
            gmm = kb.cnt["pe"]
            W("scalar", "pe", gmm)
            if i > 0:
                W("scalar", "vec", h_marks[i - 1])
            for t in range(NT):
                gate_evac[t] = OP(
                    "scalar", lambda e, t=t, i=i: e.activation(
                        kb.bfv(O_SG + t * TT, TT),
                        ps_main[:, t * TT:(t + 1) * TT], AF.Silu,
                        scale=c["sxw_g"][:, i:i + 1]), "scl")
            if i > 0:
                for t in range(NT):
                    W("tensor", "scl", up_evac[t])
            for k in range(CD):
                for t in range(NT):
                    OP("tensor", lambda e, t=t, k=k, wu_s=wu_s: e.matmul(
                        ps_aux[:, t * TT:(t + 1) * TT],
                        kb.bfv(wu_s + k * 128, 128),
                        kb.bfv(O_XQ + k * S + t * TT, TT),
                        start=(k == 0), stop=(k == CD - 1),
                        skip_group_check=True),
                       "pe" if (k == CD - 1) else None)
            pe_w_marks.append(kb.cnt["pe"])
            W("scalar", "pe", kb.cnt["pe"])
            for t in range(NT):
                up_evac[t] = OP(
                    "scalar", lambda e, t=t, i=i: e.activation(
                        kb.bfv(O_SU + t * TT, TT),
                        ps_aux[:, t * TT:(t + 1) * TT], AF.Identity,
                        scale=c["sxw_u"][:, i:i + 1]), "scl")
            W("vector", "scl", kb.cnt["scl"])
            if i > 0:
                kb.swait("vector", "sth")
            OP("vector", lambda e: e.tensor_mul(
                kb.bfv(O_HFF, S), kb.bfv(O_SG, S), kb.bfv(O_SU, S)))
            DR("vector")
            h_marks[i] = OP("vector", lambda e, i=i: e.tensor_reduce(
                c["amax"][:, i:i + 1], kb.bfv(O_HFF, S), AX.X, ALU.max,
                apply_absolute_value=True), "vec")
            W("sync", "vec", h_marks[i])
            kb.dma("sync", lambda e, i=i: e.dma_start(
                out=kb.s_h[i, :].rearrange("(p s) -> p s", p=128),
                in_=kb.bfv(O_HFF, S)), "sth")

        exchange("clipd", CH)
        scalevec("sxw_d", "swd", CD, 0.5)
        kb.swait("sync", "sth")
        kb.swait("sync", "stw")

        dn_pe = {}
        hq_vec = {}
        x_evac = {}
        W("tensor", "scl", kb.cnt["scl"])
        W("tensor", "vec", kb.cnt["vec"])
        for t in range(NT):
            for k in range(CH):
                wslot = O_WS[k % 2]
                hslot = O_T0 if k % 2 == 0 else O_T1
                qslot = O_T2 if k % 2 == 0 else O_T3
                prev = (t, k - 2) if k >= 2 else (t - 1, CH - 2 + k)
                if prev in dn_pe:
                    W("sync", "pe", dn_pe[prev])
                if prev in hq_vec:
                    W("sync", "vec", hq_vec[prev])
                tokw = kb.dma("sync", lambda e, k=k, wslot=wslot:
                              e.dma_start(
                                  out=kb.bfv(wslot, 1024).rearrange(
                                      "p (d j) -> p d j", j=128),
                                  in_=kb.s_wdT[k, :].rearrange(
                                      "(d p j) -> p d j", p=128, j=128)),
                              "wg")
                tokh = kb.dma("sync", lambda e, k=k, t=t, hslot=hslot:
                              e.dma_start(out=kb.bfv(hslot, TT),
                                          in_=stile(kb.s_h, k, t)), "hl")
                kb.wtok("vector", tokh)
                OP("vector", lambda e, hslot=hslot: e.tensor_scalar(
                    int_t[:, 0:TT], kb.bfv(hslot, TT), c["pp"][:, 1:2],
                    None, ALU.mult))
                DR("vector")
                hq_vec[(t, k)] = OP(
                    "vector", lambda e, qslot=qslot: e.tensor_scalar(
                        kb.bfv(qslot, TT), int_t[:, 0:TT], -128.0, 127.0,
                        ALU.max, ALU.min), "vec")
                DR("vector")
                kb.wtok("tensor", tokw)
                W("tensor", "vec", hq_vec[(t, k)])
                if k == 0 and (t - 1) in x_evac:
                    W("tensor", "vec", x_evac[t - 1])
                for d in range(CD):
                    bank = ps_main if d < 4 else ps_aux
                    OP("tensor", lambda e, d=d, k=k, wslot=wslot,
                       qslot=qslot, bank=bank: e.matmul(
                           bank[:, (d % 4) * TT:(d % 4 + 1) * TT],
                           kb.bfv(wslot + d * 128, 128), kb.bfv(qslot, TT),
                           start=(k == 0), stop=(k == CH - 1),
                           skip_group_check=True),
                       "pe" if d == CD - 1 else None)
                dn_pe[(t, k)] = kb.cnt["pe"]
            W("vector", "pe", kb.cnt["pe"])
            for d in range(CD):
                bank = ps_main if d < 4 else ps_aux
                if not final:
                    OP("vector", lambda e, d=d, t=t, bank=bank:
                       e.scalar_tensor_tensor(
                           kb.bfv(O_X + d * S + t * TT, TT),
                           bank[:, (d % 4) * TT:(d % 4 + 1) * TT],
                           c["sxw_d"][:, d:d + 1],
                           kb.bfv(O_X + d * S + t * TT, TT),
                           ALU.mult, ALU.add), "vec")
                    DR("vector")
                else:
                    pk = (d - 2) % 8
                    if (t > 0 or d >= 2) and pk in out_st:
                        kb.wtok("vector", out_st[pk])
                    OP("vector", lambda e, d=d, t=t, bank=bank:
                       e.scalar_tensor_tensor(
                           kb.f32v(FL + (d % 2) * TT, TT),
                           bank[:, (d % 4) * TT:(d % 4 + 1) * TT],
                           c["sxw_d"][:, d:d + 1],
                           kb.bfv(O_X + d * S + t * TT, TT),
                           ALU.mult, ALU.add), "vec")
                    DR("vector")
                    W("sync", "vec", kb.cnt["vec"])
                    out_st[d] = kb.dma(
                        "sync", lambda e, d=d, t=t: e.dma_start(
                            out=kb.d_out[d * 128:(d + 1) * 128,
                                         t * TT:(t + 1) * TT],
                            in_=kb.f32v(FL + (d % 2) * TT, TT)), "out")
            x_evac[t] = kb.cnt["vec"]

    out_st = {}

    def mamba():
        layernorm(O_X, O_LN)
        W("sync", "vec", kb.cnt["vec"])
        for d in range(CD):
            kb.dma("sync", lambda e, d=d: e.dma_start(
                out=kb.s_x1[d, :].rearrange("(p s) -> p s", p=128),
                in_=kb.bfv(O_X + d * S, S)), "stx")

        win_cast = []
        win_pe = {}
        win_evac = {}
        last_sz_st = [0]
        kb.swait("scalar", "stx")
        for o in range(2 * CI):
            fslot = 0 if o % 2 == 0 else 2816
            wslot = O_WS[o % 2]
            if o >= 2:
                W("sync", "scl", win_cast[o - 2])
            tok = kb.dma("sync", lambda e, o=o, fslot=fslot: e.dma_start(
                out=kb.f32[:, fslot:fslot + 1024].rearrange(
                    "p (k j) -> p k j", j=128),
                in_=di["W_in"][:, o * 128:(o + 1) * 128].rearrange(
                    "(k p) j -> p k j", p=128)), "xw")
            kb.wtok("scalar", tok)
            if o >= 2:
                W("scalar", "pe", win_pe[o - 2])
            win_cast.append(OP(
                "scalar", lambda e, fslot=fslot, wslot=wslot: e.copy(
                    kb.bfv(wslot, 1024), kb.f32[:, fslot:fslot + 1024]),
                "scl"))
            W("tensor", "scl", kb.cnt["scl"])
            for t in range(NT):
                if (o - 2, t) in win_evac:
                    W("tensor", "scl", win_evac[(o - 2, t)])
            for k in range(CD):
                for t in range(NT):
                    OP("tensor", lambda e, t=t, k=k, wslot=wslot: e.matmul(
                        ps_main[:, t * TT:(t + 1) * TT],
                        kb.bfv(wslot + k * 128, 128),
                        kb.bfv(O_LN + k * S + t * TT, TT),
                        start=(k == 0), stop=(k == CD - 1),
                        skip_group_check=True),
                       "pe" if (k == CD - 1) else None)
            win_pe[o] = kb.cnt["pe"]
            for t in range(NT):
                W("scalar", "pe", kb.cnt["pe"])
                if o < CI:
                    win_evac[(o, t)] = OP(
                        "scalar", lambda e, o=o, t=t: e.activation(
                            kb.bfv(xhp_off(o) + t * TT, TT),
                            ps_main[:, t * TT:(t + 1) * TT], AF.Identity,
                            bias=c["bxh"][:, o:o + 1]), "scl")
                else:
                    if t == 0:
                        kb.swait("scalar", "stz")
                    win_evac[(o, t)] = OP(
                        "scalar", lambda e, o=o, t=t: e.activation(
                            kb.bfv(O_STG + t * TT, TT),
                            ps_main[:, t * TT:(t + 1) * TT], AF.Silu,
                            bias=c["bz"][:, o - CI:o - CI + 1]), "scl")
            if o >= CI:
                W("sync", "scl", kb.cnt["scl"])
                kb.dma("sync", lambda e, o=o: e.dma_start(
                    out=kb.s_sz[o - CI, :].rearrange("(p s) -> p s", p=128),
                    in_=kb.bfv(O_STG, S)), "stz")

        kb.barrier(["scalar", "sync"], ["vector"])
        conv_scl = 0
        for ci in range(CI):
            xp = xhp_off(ci)
            xh = xh_off(ci)
            if conv_scl:
                W("vector", "scl", conv_scl)
            OP("vector", lambda e, xp=xp: e.tensor_copy(
                c["pad8"][:, 4:8], kb.bfv(xp, 4)))
            DR("vector")
            OP("vector", lambda e, ci=ci: e.tensor_scalar(
                kb.f32v(FA, 4), c["pad8"][:, 4:8],
                c["convw"][:, ci * 4 + 3:ci * 4 + 4], None, ALU.mult))
            OP("vector", lambda e, xp=xp, ci=ci: e.tensor_scalar(
                kb.f32v(FA + 4, S - 4), kb.bfv(xp + 4, S - 4),
                c["convw"][:, ci * 4 + 3:ci * 4 + 4], None, ALU.mult))
            for j in range(3):
                DR("vector")
                OP("vector", lambda e, j=j, ci=ci: e.scalar_tensor_tensor(
                    kb.f32v(FA, 4), c["pad8"][:, 1 + j:5 + j],
                    c["convw"][:, ci * 4 + j:ci * 4 + j + 1],
                    kb.f32v(FA, 4), ALU.mult, ALU.add))
                isl = OP("vector", lambda e, j=j, ci=ci, xp=xp:
                         e.scalar_tensor_tensor(
                             kb.f32v(FA + 4, S - 4),
                             kb.bfv(xp + 1 + j, S - 4),
                             c["convw"][:, ci * 4 + j:ci * 4 + j + 1],
                             kb.f32v(FA + 4, S - 4), ALU.mult, ALU.add),
                         "vec" if j == 2 else None)
            W("scalar", "vec", kb.cnt["vec"])
            conv_scl = OP("scalar", lambda e, xh=xh, ci=ci: e.activation(
                kb.bfv(xh, S), kb.f32v(FA, S), AF.Silu,
                bias=c["convb"][:, ci:ci + 1]), "scl")

        kb.barrier(["scalar"], ["tensor", "vector"])
        bm_evac = {}
        for t in range(NT):
            if (t - 1) in bm_evac:
                W("tensor", "scl", bm_evac[t - 1])
            for k in range(CI):
                OP("tensor", lambda e, t=t, k=k: e.matmul(
                    ps_aux[0:DS, 0:TT], c["WxB"][:, k * DS:(k + 1) * DS],
                    kb.bfv(xh_off(k) + t * TT, TT),
                    start=(k == 0), stop=False, skip_group_check=True))
            OP("tensor", lambda e: e.matmul(
                ps_aux[0:DS, 0:TT], c["bxB"][0:1, :],
                c["onesrow_b"][0:1, :], start=False, stop=True,
                skip_group_check=True), "pe")
            W("scalar", "pe", kb.cnt["pe"])
            bm_evac[t] = OP("scalar", lambda e, t=t: e.copy(
                bmatv(t * TT, TT), ps_aux[0:DS, 0:TT]), "scl")
        br_evac = {}
        br_bank_user = {}
        for s in range(DS):
            for t in range(NT):
                bank = TT + (t % 2) * TT
                if bank in br_bank_user:
                    W("tensor", "scl", br_bank_user[bank])
                W("tensor", "scl", bm_evac[t])
                OP("tensor", lambda e, s=s, t=t, bank=bank: e.matmul(
                    ps_aux[:, bank:bank + TT],
                    c["oneh"][:, s * 128:(s + 1) * 128],
                    bmatv(t * TT, TT), start=True, stop=True,
                    skip_group_check=True), "pe")
                W("scalar", "pe", kb.cnt["pe"])
                if s < NBREP_RES:
                    br_evac[(s, t)] = OP(
                        "scalar", lambda e, s=s, t=t, bank=bank: e.copy(
                            kb.bfv(brep_off(s) + t * TT, TT),
                            ps_aux[:, bank:bank + TT]), "scl")
                else:
                    if s > NBREP_RES or t > 0:
                        kb.swait("scalar", "stx")
                    br_evac[(s, t)] = OP(
                        "scalar", lambda e, s=s, t=t, bank=bank: e.copy(
                            kb.bfv(O_STG + t * TT, TT),
                            ps_aux[:, bank:bank + TT]), "scl")
                br_bank_user[bank] = br_evac[(s, t)]
            if s >= NBREP_RES:
                W("sync", "scl", kb.cnt["scl"])
                kb.dma("sync", lambda e, s=s: e.dma_start(
                    out=kb.s_brep[s - NBREP_RES, :].rearrange(
                        "(p s) -> p s", p=128),
                    in_=kb.bfv(O_STG, S)), "stx")
        kb.barrier(["scalar", "sync", "vector"],
                   ["vector", "tensor", "sync"])

        wdt_pe = {}
        sp_evac = {}
        scan_marks = {}
        add_marks = {}
        for ci in range(CI):
            if (ci - 1) in wdt_pe:
                W("sync", "pe", wdt_pe[ci - 1])
            tok = kb.dma("sync", lambda e, ci=ci: e.dma_start(
                out=kb.bfv(O_WDT, 2048).rearrange("p (k j) -> p k j", j=128),
                in_=kb.s_wdtb[:, ci * 128:(ci + 1) * 128].rearrange(
                    "(k p) j -> p k j", p=128)), "xw")
            kb.wtok("tensor", tok)
            if (ci - 1) in sp_evac:
                W("tensor", "scl", sp_evac[ci - 1])
            for k in range(CI):
                for t in range(NT):
                    OP("tensor", lambda e, t=t, k=k: e.matmul(
                        ps_main[:, t * TT:(t + 1) * TT],
                        kb.bfv(O_WDT + k * 128, 128),
                        kb.bfv(xh_off(k) + t * TT, TT),
                        start=(k == 0), stop=(k == CI - 1),
                        skip_group_check=True),
                       "pe" if (k == CI - 1) else None)
            wdt_pe[ci] = kb.cnt["pe"]
            W("scalar", "pe", kb.cnt["pe"])
            if ci > 0:
                g_last = ci * DS - 1
                W("scalar", "vec", scan_marks[g_last])
                kb.wtok("scalar", add_marks[g_last // 2])
            for t in range(NT):
                OP("scalar", lambda e, t=t, ci=ci: e.activation(
                    kb.f32v(FL + t * TT, TT),
                    ps_main[:, t * TT:(t + 1) * TT], AF.Abs,
                    bias=c["bdt"][:, ci:ci + 1]))
                DR("scalar")
                OP("scalar", lambda e, t=t: e.activation(
                    kb.f32v(FL + t * TT, TT), kb.f32v(FL + t * TT, TT),
                    AF.Exp, scale=-1.0))
                DR("scalar")
                OP("scalar", lambda e, t=t: e.activation(
                    kb.f32v(FL + t * TT, TT), kb.f32v(FL + t * TT, TT),
                    AF.Ln, bias=1.0))
                DR("scalar")
                OP("scalar", lambda e, t=t, ci=ci: e.activation(
                    kb.f32v(FB + t * TT, TT),
                    ps_main[:, t * TT:(t + 1) * TT], AF.Relu,
                    bias=c["bdt"][:, ci:ci + 1]),
                   "scl" if t == NT - 1 else None)
            sp_evac[ci] = kb.cnt["scl"]
            W("vector", "scl", kb.cnt["scl"])
            OP("vector", lambda e: e.tensor_add(
                kb.f32v(FL, S), kb.f32v(FL, S), kb.f32v(FB, S)))
            DR("vector")
            kb.swait("vector", "sty")
            OP("vector", lambda e, ci=ci: e.tensor_mul(
                kb.bfv(O_DU, S), kb.f32v(FL, S), kb.bfv(xh_off(ci), S)))
            OP("vector", lambda e, ci=ci: e.tensor_scalar(
                kb.f32v(FY, S), kb.bfv(xh_off(ci), S),
                c["D"][:, ci:ci + 1], None, ALU.mult), "vec")
            DR("vector")
            delta_mark = kb.cnt["vec"]
            bs_tok = {}
            for sp in (NBREP_RES, NBREP_RES + 1):
                W("sync", "vec", kb.cnt["vec"])
                bs_tok[sp] = kb.dma(
                    "sync", lambda e, sp=sp, bsrc=O_BS[sp % 2]: e.dma_start(
                        out=kb.bfv(bsrc, S),
                        in_=kb.s_brep[sp - NBREP_RES, :].rearrange(
                            "(p s) -> p s", p=128)), "bs")
            for s in range(DS):
                g = ci * DS + s
                p = g // 2
                dslot = FA if s % 2 == 0 else FA2
                hslot = HS[g % 3]
                W("scalar", "vec", delta_mark)
                if g >= 2:
                    W("scalar", "vec", scan_marks[g - 2])
                da = OP("scalar", lambda e, ci=ci, s=s, dslot=dslot:
                        e.activation(
                            kb.f32v(dslot, S), kb.f32v(FL, S), AF.Exp,
                            scale=c["negA"][:, ci * DS + s:ci * DS + s + 1]),
                        "scl")
                if s < NBREP_RES:
                    bsrc = brep_off(s)
                else:
                    bsrc = O_BS[s % 2]
                    kb.wtok("vector", bs_tok[s])
                OP("vector", lambda e, bsrc=bsrc: e.tensor_mul(
                    kb.bfv(O_DBU, S), kb.bfv(O_DU, S), kb.bfv(bsrc, S)))
                DR("vector")
                W("vector", "scl", da)
                if g % 2 == 1 and p >= 1:
                    kb.wtok("vector", add_marks[p - 1])
                scan_marks[g] = OP(
                    "vector", lambda e, dslot=dslot, hslot=hslot:
                    e.tensor_tensor_scan(
                        kb.f32v(hslot, S), kb.f32v(dslot, S),
                        kb.bfv(O_DBU, S), 0.0, ALU.mult, ALU.add), "vec")
                DR("vector")
                if g % 2 == 1:
                    fslot = HS[(g - 1) % 3]
                    scan_marks[g] = OP(
                        "vector", lambda e, fslot=fslot, hslot=hslot:
                        e.tensor_add(
                            kb.f32v(fslot, S), kb.f32v(fslot, S),
                            kb.f32v(hslot, S)), "vec")
                    DR("vector")
                    W("gpsimd", "vec", scan_marks[g])
                    if p >= 1:
                        kb.wtok("gpsimd", add_marks[p - 1])
                    add_marks[p] = kb.dma(
                        "gpsimd", lambda e, fslot=fslot: e.dma_start(
                            out=kb.f32v(FY, S), in_=kb.f32v(fslot, S),
                            accum_op=ALU.add), "yacc")
                if NBREP_RES <= s <= DS - 3:
                    W("sync", "vec", scan_marks[g])
                    bs_tok[s + 2] = kb.dma(
                        "sync", lambda e, s=s, bsrc=O_BS[s % 2]: e.dma_start(
                            out=kb.bfv(bsrc, S),
                            in_=kb.s_brep[s + 2 - NBREP_RES, :].rearrange(
                                "(p s) -> p s", p=128)), "bs")
            kb.wtok("vector", add_marks[(ci * DS + DS - 1) // 2])
            for t in range(NT):
                szslot = O_SZ[t % 2]
                W("sync", "vec", kb.cnt["vec"])
                tok = kb.dma("sync", lambda e, ci=ci, t=t, szslot=szslot:
                             e.dma_start(out=kb.bfv(szslot, TT),
                                         in_=stile(kb.s_sz, ci, t)), "szl")
                kb.wtok("vector", tok)
                OP("vector", lambda e, t=t, szslot=szslot: e.tensor_mul(
                    kb.bfv(O_DU + t * TT, TT), kb.f32v(FY + t * TT, TT),
                    kb.bfv(szslot, TT)), "vec")
                DR("vector")
            W("sync", "vec", kb.cnt["vec"])
            kb.dma("sync", lambda e, ci=ci: e.dma_start(
                out=kb.s_yg[ci, :].rearrange("(p s) -> p s", p=128),
                in_=kb.bfv(O_DU, S)), "sty")
        kb.barrier(["vector", "sync", "scalar"],
                   ["sync", "tensor", "vector"])

        wo_pe = {}
        wo_evac = {}
        for t in range(NT):
            for k in range(CI):
                wslot = O_WS[k % 2]
                ygslot = O_T0 if k % 2 == 0 else O_T1
                prev = (t, k - 2) if k >= 2 else (t - 1, CI - 2 + k)
                if prev in wo_pe:
                    W("sync", "pe", wo_pe[prev])
                tokw = kb.dma("sync", lambda e, k=k, wslot=wslot:
                              e.dma_start(
                                  out=kb.bfv(wslot, 1024),
                                  in_=kb.s_woutb[k * 128:(k + 1) * 128, :]),
                              "wg")
                toky = kb.dma("sync", lambda e, k=k, t=t, ygslot=ygslot:
                              e.dma_start(out=kb.bfv(ygslot, TT),
                                          in_=stile(kb.s_yg, k, t)), "ygl")
                kb.wtok("tensor", tokw)
                kb.wtok("tensor", toky)
                if k == 0 and (t - 1) in wo_evac:
                    W("tensor", "vec", wo_evac[t - 1])
                for d in range(CD):
                    bank = ps_main if d < 4 else ps_aux
                    OP("tensor", lambda e, d=d, k=k, wslot=wslot,
                       ygslot=ygslot, bank=bank: e.matmul(
                           bank[:, (d % 4) * TT:(d % 4 + 1) * TT],
                           kb.bfv(wslot + d * 128, 128),
                           kb.bfv(ygslot, TT),
                           start=(k == 0), stop=(k == CI - 1),
                           skip_group_check=True),
                       "pe" if d == CD - 1 else None)
                wo_pe[(t, k)] = kb.cnt["pe"]
            W("vector", "pe", kb.cnt["pe"])
            for d in range(CD):
                xslot = O_T2 if d % 2 == 0 else O_T3
                W("sync", "vec", kb.cnt["vec"])
                tok = kb.dma("sync", lambda e, d=d, t=t, xslot=xslot:
                             e.dma_start(out=kb.bfv(xslot, TT),
                                         in_=stile(kb.s_x1, d, t)), "x1l")
                kb.wtok("vector", tok)
                bank = ps_main if d < 4 else ps_aux
                OP("vector", lambda e, d=d, t=t, xslot=xslot, bank=bank:
                   e.scalar_tensor_tensor(
                       kb.bfv(O_X + d * S + t * TT, TT),
                       bank[:, (d % 4) * TT:(d % 4 + 1) * TT],
                       c["bout"][:, d:d + 1],
                       kb.bfv(xslot, TT), ALU.add, ALU.add), "vec")
                DR("vector")
            wo_evac[t] = kb.cnt["vec"]
        kb.barrier(["vector", "sync", "gpsimd"],
                   ["scalar", "tensor", "vector", "sync"])

    ffn(final=False)
    kb.barrier(["vector", "scalar", "sync", "tensor"],
               ["vector", "scalar", "sync", "tensor"])
    mamba()
    ffn(final=True)
    kb.swait_all_dma("sync")


_BUILD_CACHE = {}


def _get_build(S, n_cores):
    key = (S, n_cores)
    if key not in _BUILD_CACHE:
        _BUILD_CACHE[key] = build(S, n_cores)
    return _BUILD_CACHE[key]


def _pack_rows(v, nch):
    return np.ascontiguousarray(np.asarray(v, np.float32).reshape(nch, 128).T)


def make_inputs(xc, w, S):
    bf = ml_dtypes.bfloat16
    Alog = np.asarray(w["L0_attn_A_log"], np.float32)
    convw = np.asarray(w["L0_attn_conv_w"], np.float32)[:, 0, :]
    one = {}
    one["x"] = np.ascontiguousarray(np.asarray(xc, np.float32).T)
    one["W_in"] = np.ascontiguousarray(np.asarray(w["L0_attn_W_in"],
                                                  np.float32))
    one["W_dt"] = np.ascontiguousarray(np.asarray(w["L0_attn_W_dt"],
                                                   np.float32))
    one["W_out"] = np.ascontiguousarray(np.asarray(w["L0_attn_W_out"],
                                                    np.float32))
    Wx = np.asarray(w["L0_attn_W_x"], np.float32)
    one["WxB"] = np.ascontiguousarray(
        Wx[:, DS:].reshape(CI, 128, DS).transpose(1, 0, 2)
        .reshape(128, CI * DS))
    one["bxB"] = np.ascontiguousarray(
        np.asarray(w["L0_attn_b_x"], np.float32)[DS:].reshape(1, DS))
    one["w_gate"] = np.ascontiguousarray(np.asarray(w["L0_ffn_w_gate"],
                                                     np.float32))
    one["w_up"] = np.ascontiguousarray(np.asarray(w["L0_ffn_w_up"],
                                                   np.float32))
    one["w_down"] = np.ascontiguousarray(np.asarray(w["L0_ffn_w_down"],
                                                     np.float32))
    one["Alog"] = np.ascontiguousarray(
        Alog.reshape(CI, 128, DS).transpose(1, 0, 2).reshape(128, CI * DS))
    one["convw"] = np.ascontiguousarray(
        convw.reshape(CI, 128, DC).transpose(1, 0, 2).reshape(128, CI * DC))
    one["convb"] = _pack_rows(w["L0_attn_conv_b"], CI)
    one["Dvec"] = _pack_rows(w["L0_attn_D"], CI)
    bin_ = np.asarray(w["L0_attn_b_in"], np.float32)
    one["bxh"] = _pack_rows(bin_[:DI], CI)
    one["bz"] = _pack_rows(bin_[DI:], CI)
    one["bdt"] = _pack_rows(w["L0_attn_b_dt"], CI)
    one["bout"] = _pack_rows(w["L0_attn_b_out"], CD)
    one["clipx"] = np.asarray(
        [[float(np.asarray(w["L0_ffn_clip_x"]).reshape(-1)[0]), 0.0]],
        np.float32)
    one["clipd"] = np.asarray(
        [[float(np.asarray(w["L0_ffn_clip_down"]).reshape(-1)[0]), 0.0]],
        np.float32)
    one["eyeb"] = np.eye(128, dtype=bf)
    one["eyef"] = np.eye(128, dtype=np.float32)
    one["onesb"] = np.ones((128, 2), dtype=bf)
    one["ones1"] = np.ones((1, 128), np.float32)
    one["onesrow_b"] = np.ones((1, TT), dtype=bf)
    cv = np.zeros((128, 8), np.float32)
    cv[:, 0] = LN_EPS
    cv[:, 1] = 1.0
    cv[:, 2] = -1.0
    cv[:, 3] = -0.5
    one["cvals"] = cv
    oneh = np.zeros((DS, DS * 128), dtype=bf)
    for s in range(DS):
        oneh[s, s * 128:(s + 1) * 128] = 1
    one["oneh"] = oneh
    return one


def kernel(**inputs):
    x = np.asarray(inputs["L0_attn_x"], np.float32)
    S = x.shape[1]
    kb = _get_build(S, NCORE)
    in_maps = [make_inputs(x[cidx], inputs, S) for cidx in range(NCORE)]
    res = run_bass_kernel_spmd(kb.nc, in_maps, core_ids=list(range(NCORE)))
    outs = [np.asarray(r["out"]).T for r in res.results]
    return np.stack(outs, axis=0).astype(np.float32)


# version 09
# speedup vs baseline: 1.1408x; 1.0242x over previous
import numpy as np
import ml_dtypes
import concourse.bass as bass
import concourse.mybir as mybir
from concourse.bass_utils import run_bass_kernel_spmd
from contextlib import ExitStack

F32 = mybir.dt.float32
BF16 = mybir.dt.bfloat16
I32 = mybir.dt.int32
AF = mybir.ActivationFunctionType
ALU = mybir.AluOpType
AX = mybir.AxisListType

DM, DI, DS, DC, HF = 1024, 2048, 16, 4, 2816
CD, CI, CH = DM // 128, DI // 128, HF // 128
TT = 512
LN_EPS = 1e-5
QMAXF = 127.0
NCORE = 8
ENGS = ("sync", "vector", "scalar", "tensor", "gpsimd")


class KB:
    def __init__(self, S, n_cores):
        self.S = S
        self.NT = S // TT
        self.n_cores = n_cores
        self.nc = bass.Bass("TRN2", target_bir_lowering=False)
        self.es = ExitStack()
        self.q = {e: [] for e in ENGS}
        self._mk_dram()

    def _mk_dram(self):
        nc, S = self.nc, self.S
        self.di = {}

        def din(name, shape, dt=F32):
            self.di[name] = nc.dram_tensor(name, shape, dt,
                                           kind="ExternalInput").ap()

        din("x", [DM, S])
        din("W_in", [DM, 2 * DI])
        din("W_dt", [DI, DI])
        din("W_out", [DI, DM])
        din("WxB", [128, CI * DS])
        din("bxB", [1, DS])
        din("w_gate", [HF, DM])
        din("w_up", [HF, DM])
        din("w_down", [DM, HF])
        din("Alog", [128, CI * DS])
        din("convw", [128, CI * DC])
        din("convb", [128, CI])
        din("Dvec", [128, CI])
        din("bxh", [128, CI])
        din("bz", [128, CI])
        din("bdt", [128, CI])
        din("bout", [128, CD])
        din("clipx", [1, 2])
        din("clipd", [1, 2])
        din("cvals", [128, 8])
        din("eyeb", [128, 128], BF16)
        din("eyef", [128, 128], F32)
        din("onesb", [128, 2], BF16)
        din("ones1", [1, 128])
        din("onesrow_b", [1, TT], BF16)
        din("oneh", [DS, DS * 128], BF16)
        self.d_out = nc.dram_tensor("out", [DM, S], F32,
                                    kind="ExternalOutput").ap()

        def scr(name, shape, dt):
            return nc.dram_tensor(name, shape, dt, kind="Internal").ap()

        self.s_wdtb = scr("s_wdtb", [DI, DI], BF16)
        self.s_woutb = scr("s_woutb", [DI, DM], BF16)
        self.s_wgT = scr("s_wgT", [CH, CD * 128 * 128], BF16)
        self.s_wuT = scr("s_wuT", [CH, CD * 128 * 128], BF16)
        self.s_wdT = scr("s_wdT", [CH, CD * 128 * 128], BF16)
        self.s_h = scr("s_h", [CH, 128 * S], BF16)
        self.s_sz = scr("s_sz", [CI, 128 * S], BF16)
        self.s_yg = scr("s_yg", [CI, 128 * S], BF16)
        self.s_x1 = scr("s_x1", [CD, 128 * S], BF16)
        self.s_brep = scr("s_brep", [6, 128 * S], BF16)
        self.s_cci = scr("s_cci", [1, 2], F32)
        self.s_cco = scr("s_cco", [1, 2], F32)

    def mk_mem(self):
        nc, S, es = self.nc, self.S, self.es

        def sb(name, shape, dt=F32):
            return es.enter_context(nc.sbuf_tensor(name, shape, dt))

        self.bf = sb("bf_arena", [128, 32 * S], BF16)
        self.f32 = sb("f32_arena", [128, 8 * S], F32)
        self.int_t = sb("int_t", [128, S // 2], I32)
        self.c = {}
        for name, shape, dt in [
            ("negA", [128, CI * DS], F32), ("convw", [128, CI * DC], F32),
            ("convb", [128, CI], F32), ("D", [128, CI], F32),
            ("bxh", [128, CI], F32), ("bz", [128, CI], F32),
            ("bdt", [128, CI], F32), ("bout", [128, CD], F32),
            ("WxB", [128, CI * DS], BF16), ("bxB", [1, DS], BF16),
            ("eyeb", [128, 128], BF16), ("eyef", [128, 128], F32),
            ("onesb", [128, 2], BF16),
            ("ones1", [1, 128], F32), ("onesrow_b", [1, TT], BF16),
            ("oneh", [DS, DS * 128], BF16),
            ("clipx", [1, 2], F32), ("clipd", [1, 2], F32),
            ("cvals", [128, 8], F32),
            ("swg", [128, CH], F32), ("swu", [128, CH], F32),
            ("swd", [128, CD], F32),
            ("sxw_g", [128, CH], F32), ("sxw_u", [128, CH], F32),
            ("sxw_d", [128, CD], F32),
            ("amax", [128, 32], F32), ("r1", [128, 2], F32),
            ("sc", [1, 8], F32), ("pp", [128, 4], F32),
            ("pad8", [128, 8], BF16),
        ]:
            self.c[name] = sb("c_" + name, shape, dt)
        self.sems = {}
        for n in ["pe", "vec", "scl", "gps", "cc"]:
            self.sems[n] = es.enter_context(nc.semaphore(n))
        self.cnt = {n: 0 for n in self.sems}
        self.sempool = [es.enter_context(nc.semaphore(f"dmas{i}"))
                        for i in range(44)]
        self.sempool_next = 0
        self.streams = {}
        self.ps_main = es.enter_context(
            nc.psum_tensor("ps_main", [128, 4 * TT], F32))
        self._ps_tb_scope = ExitStack()
        self.ps_tb = self._ps_tb_scope.enter_context(
            nc.psum_tensor("ps_tb", [128, 1024], BF16))
        self.ps_aux = None
        for col, val in enumerate([LN_EPS, 1.0, -1.0, -0.5]):
            self.nc.const_aps.register(F32, val, self.c["cvals"][:, col:col + 1])

    def close_p0_psum(self):
        self._ps_tb_scope.close()
        self.ps_tb = None
        self.ps_aux = self.es.enter_context(
            self.nc.psum_tensor("ps_aux", [128, 4 * TT], F32))

    def op(self, eng, fn, inc=None, n=1):
        if inc is None:
            self.q[eng].append(fn)
            return None
        self.cnt[inc] += n
        sem = self.sems[inc]
        self.q[eng].append(
            lambda e, fn=fn, sem=sem, n=n: fn(e).then_inc(sem, n))
        return self.cnt[inc]

    def wait(self, eng, semname, val):
        if val and val > 0:
            sem = self.sems[semname]
            self.q[eng].append(
                lambda e, sem=sem, val=val: e.wait_ge(sem, val))

    def _stream(self, name):
        st = self.streams.get(name)
        if st is None:
            a = self.sempool[self.sempool_next]
            b = self.sempool[self.sempool_next + 1]
            self.sempool_next += 2
            st = {"sems": [a, b], "cnt": [0, 0], "par": 0}
            self.streams[name] = st
        return st

    def dma(self, eng, fn, stream):
        st = self._stream(stream)
        p = st["par"]
        sem = st["sems"][p]
        if st["cnt"][p] > 0:
            v = st["cnt"][p]
            self.q[eng].append(
                lambda e, sem=sem, v=v: e.wait_ge(sem, v))
        st["cnt"][p] += 16
        v = st["cnt"][p]
        self.q[eng].append(
            lambda e, fn=fn, sem=sem: fn(e).then_inc(sem, 16))
        st["par"] = 1 - p
        return (sem, v)

    def wtok(self, eng, tok):
        if tok is None:
            return
        sem, v = tok
        self.q[eng].append(lambda e, sem=sem, v=v: e.wait_ge(sem, v))

    def swait(self, eng, stream):
        st = self.streams.get(stream)
        if not st:
            return
        for p in (0, 1):
            if st["cnt"][p] > 0:
                sem, v = st["sems"][p], st["cnt"][p]
                self.q[eng].append(
                    lambda e, sem=sem, v=v: e.wait_ge(sem, v))

    def swait_all_dma(self, eng):
        for name in list(self.streams):
            self.swait(eng, name)

    def drain(self, eng):
        self.q[eng].append(lambda e: e.drain())

    def barrier(self, writers, waiters):
        snap = []
        sync_snap = []
        for w in writers:
            if w == "sync":
                for name, st in self.streams.items():
                    for p in (0, 1):
                        if st["cnt"][p] > 0:
                            sync_snap.append((st["sems"][p], st["cnt"][p]))
            else:
                s = {"vector": "vec", "scalar": "scl", "tensor": "pe",
                     "gpsimd": "gps"}[w]
                snap.append((w, s, self.cnt[s]))
        for eng in waiters:
            for w, s, v in snap:
                if eng == w:
                    self.drain(eng)
                else:
                    self.wait(eng, s, v)
            for sem, v in sync_snap:
                self.q[eng].append(
                    lambda e, sem=sem, v=v: e.wait_ge(sem, v))

    def bfv(self, off, cols):
        return self.bf[:, off:off + cols]

    def f32v(self, off, cols):
        return self.f32[:, off:off + cols]


def build(S, n_cores):
    kb = KB(S, n_cores)
    with kb.es:
        kb.mk_mem()
        _emit(kb)
        blk = kb.es.enter_context(kb.nc.Block())
        for eng in ENGS:
            ops = kb.q[eng]
            if ops:
                getattr(blk, eng)(lambda e, ops=ops: [f(e) for f in ops])
    return kb


def _emit(kb: KB):
    S, NT = kb.S, kb.NT
    c, di = kb.c, kb.di
    int_t = kb.int_t
    ps_main = kb.ps_main

    def U(k):
        return k * S

    O_X, O_LN, O_XQ = U(0), U(8), U(16)
    O_SG, O_SU, O_HFF = U(24), U(25), U(26)
    O_WS = [U(27), U(27) + 1024, U(28), U(28) + 1024]
    O_T0, O_T1 = U(29), U(29) + TT
    O_T2, O_T3 = U(29) + 2 * TT, U(29) + 3 * TT
    O_STG = U(30)
    O_AB = U(26)
    O_BLK = U(24)

    def xh_off(ci):
        return U(8 + ci) if ci < 8 else U(ci - 8)

    def xhp_off(ci):
        return U(ci) if ci < 8 else U(16 + ci - 8)

    def brep_off(s):
        return U(16 + s)

    NBREP_RES = 10
    O_BS = [U(26), U(27)]
    O_WDT = U(28)
    O_DU = U(29)
    O_SZ = [U(30) + t * TT for t in range(min(NT, 4))]
    O_DBU = U(31)
    O_BMAT = U(29)

    FA, FB, FH, FY, FL, FA2 = 0, S, 2 * S, 3 * S, 4 * S, 5 * S
    FH3, FH4 = 6 * S, 7 * S
    HS = [FH, FB, FH3, FH4]
    ROW_M, ROW_T, ROW_R = FA2, FA2 + TT, FA2 + 2 * TT

    W, OP, DR = kb.wait, kb.op, kb.drain

    def rowv(off):
        return kb.f32[0:1, off:off + TT]

    def bmatv(t0, cols):
        return kb.bf[0:DS, O_BMAT + t0:O_BMAT + t0 + cols]

    for name in ["cvals", "convw", "convb", "bxh", "bz", "bdt", "bout",
                 "eyeb", "eyef", "onesb", "ones1", "onesrow_b", "oneh",
                 "clipx", "clipd"]:
        kb.dma("sync", lambda e, n=name: e.dma_start(out=c[n][:], in_=di[n]),
               "cst")
    kb.dma("sync", lambda e: e.dma_start(out=c["D"][:], in_=di["Dvec"]),
           "cst")
    kb.dma("sync", lambda e: e.dma_start(out=c["negA"][:], in_=di["Alog"]),
           "cst")
    kb.dma("sync", lambda e: e.dma_start(out=kb.f32[:, 0:CI * DS],
                                         in_=di["WxB"]), "cst")
    kb.dma("sync", lambda e: e.dma_start(
        out=kb.f32[0:1, CI * DS:CI * DS + DS], in_=di["bxB"]), "cst")
    kb.swait("scalar", "cst")
    OP("scalar", lambda e: e.copy(c["WxB"][:], kb.f32[:, 0:CI * DS]))
    OP("scalar", lambda e: e.copy(c["bxB"][:],
                                  kb.f32[0:1, CI * DS:CI * DS + DS]), "scl")
    OP("scalar", lambda e: e.activation(c["negA"][:], c["negA"][:], AF.Exp))
    DR("scalar")
    OP("scalar", lambda e: e.mul(c["negA"][:], c["negA"][:], -1.0), "scl")
    kb.swait("vector", "cst")
    OP("vector", lambda e: e.memset(c["pad8"][:], 0.0), "vec")

    xcopy_marks = []
    W("sync", "scl", kb.cnt["scl"])
    for ch in range(CD):
        slot = FL if ch % 2 == 0 else FA
        if ch >= 2:
            W("sync", "scl", xcopy_marks[ch - 2])
        tok = kb.dma("sync", lambda e, ch=ch, slot=slot: e.dma_start(
            out=kb.f32v(slot, S), in_=di["x"][ch * 128:(ch + 1) * 128, :]),
            "xw")
        kb.wtok("scalar", tok)
        xcopy_marks.append(OP("scalar", lambda e, ch=ch, slot=slot: e.copy(
            kb.bfv(O_X + ch * S, S), kb.f32v(slot, S)), "scl"))

    qmarks = []

    def quant_transpose(wname, nch, kd, stile, dstT, toks,
                        scatter_rows=False):
        nkb = kd // 128
        for i in range(nch):
            slot = 0 if i % 2 == 0 else 2816
            if len(qmarks) >= 2:
                W("sync", "vec", qmarks[-2])
            tok = kb.dma("sync", lambda e, w=wname, i=i, slot=slot, kd=kd:
                         e.dma_start(out=kb.f32[:, slot:slot + kd],
                                     in_=di[w][i * 128:(i + 1) * 128, :]),
                         "xw")
            kb.wtok("vector", tok)
            OP("vector", lambda e, slot=slot, kd=kd: e.tensor_reduce(
                c["r1"][:, 0:1], kb.f32[:, slot:slot + kd], AX.X, ALU.max,
                apply_absolute_value=True))
            DR("vector")
            OP("vector", lambda e, st=stile, i=i: e.tensor_scalar(
                c[st][:, i:i + 1], c["r1"][:, 0:1], 1.0 / QMAXF, None,
                ALU.mult))
            OP("vector", lambda e: e.reciprocal(c["r1"][:, 1:2],
                                                c["r1"][:, 0:1]))
            DR("vector")
            W("vector", "pe", kb.cnt["pe"])
            mlast = None
            for p0 in range(0, kd, S // 2):
                pn = min(S // 2, kd - p0)
                OP("vector", lambda e, slot=slot, p0=p0, pn=pn:
                   e.tensor_scalar(int_t[:, 0:pn],
                                   kb.f32[:, slot + p0:slot + p0 + pn],
                                   c["r1"][:, 1:2], QMAXF,
                                   ALU.mult, ALU.mult))
                DR("vector")
                mlast = OP("vector", lambda e, p0=p0, pn=pn: e.tensor_copy(
                    kb.bf[:, O_BLK + p0:O_BLK + p0 + pn], int_t[:, 0:pn]), "vec")
                DR("vector")
            qmarks.append(mlast)
            kb.swait("scalar", "stw")
            for g0 in range(0, nkb, 8):
                gn = min(8, nkb - g0)
                W("tensor", "vec", kb.cnt["vec"])
                W("tensor", "scl", kb.cnt["scl"])
                for j in range(gn):
                    OP("tensor", lambda e, g0=g0, j=j, pstb=kb.ps_tb:
                       e.transpose(
                           pstb[:, j * 128:(j + 1) * 128],
                           kb.bf[:, O_BLK + (g0 + j) * 128:O_BLK + (g0 + j + 1) * 128],
                           c["eyeb"][:]), "pe" if j == gn - 1 else None)
                W("scalar", "pe", kb.cnt["pe"])
                for j in range(gn):
                    OP("scalar", lambda e, g0=g0, j=j, pstb=kb.ps_tb: e.copy(
                        kb.bfv(O_STG + (g0 + j) * 128, 128),
                        pstb[:, j * 128:(j + 1) * 128]),
                       "scl" if j == gn - 1 else None)
            W("sync", "scl", kb.cnt["scl"])
            if not scatter_rows:
                toks[i] = kb.dma(
                    "sync", lambda e, dstT=dstT, i=i, nkb=nkb: e.dma_start(
                        out=dstT[i, 0:nkb * 128 * 128].rearrange(
                            "(k p j) -> p k j", p=128, j=128),
                        in_=kb.bfv(O_STG, nkb * 128).rearrange(
                            "p (k j) -> p k j", j=128)), "stw")
            else:
                toks[i] = kb.dma(
                    "sync", lambda e, dstT=dstT, i=i, nkb=nkb: e.dma_start(
                        out=dstT[:, i * 128 * 128:(i + 1) * 128 * 128]
                        .rearrange("k (p j) -> p k j", j=128),
                        in_=kb.bfv(O_STG, nkb * 128).rearrange(
                            "p (k j) -> p k j", j=128)), "stw")

    kb.barrier(["scalar"], ["sync"])
    wgT_tok, wuT_tok, wdT_tok = {}, {}, {}
    quant_transpose("w_gate", CH, DM, "swg", kb.s_wgT, wgT_tok)
    quant_transpose("w_up", CH, DM, "swu", kb.s_wuT, wuT_tok)
    quant_transpose("w_down", CD, HF, "swd", kb.s_wdT, wdT_tok,
                    scatter_rows=True)

    cast_marks = []
    for wname, dst, nch, kd in [("W_dt", kb.s_wdtb, CI, DI),
                                ("W_out", kb.s_woutb, CI, DM)]:
        for i in range(nch):
            slot = 0 if i % 2 == 0 else 2816
            if len(cast_marks) >= 2:
                W("sync", "scl", cast_marks[-2])
            tok = kb.dma("sync", lambda e, w=wname, i=i, slot=slot, kd=kd:
                         e.dma_start(out=kb.f32[:, slot:slot + kd],
                                     in_=di[w][i * 128:(i + 1) * 128, :]),
                         "xw")
            kb.wtok("scalar", tok)
            kb.swait("scalar", "stw")
            cast_marks.append(OP("scalar", lambda e, slot=slot, kd=kd: e.copy(
                kb.bfv(O_STG, kd), kb.f32[:, slot:slot + kd]), "scl"))
            W("sync", "scl", kb.cnt["scl"])
            kb.dma("sync", lambda e, dst=dst, i=i, kd=kd: e.dma_start(
                out=dst[i * 128:(i + 1) * 128, :], in_=kb.bfv(O_STG, kd)),
                "stw")

    kb.close_p0_psum()
    ps_aux = kb.ps_aux

    def layernorm(src_o, dst_o):
        prev_stats_read = 0
        W("tensor", "scl", kb.cnt["scl"])
        for t in range(NT):
            tn = t * TT
            sq_marks = []
            pe_sq = []
            W("tensor", "vec", prev_stats_read)
            for ch in range(CD):
                sq = O_T0 if ch % 2 == 0 else O_T1
                if ch >= 2:
                    W("scalar", "pe", pe_sq[ch - 2])
                sq_marks.append(OP(
                    "scalar", lambda e, ch=ch, sq=sq, src_o=src_o, tn=tn:
                    e.activation(kb.bfv(sq, TT),
                                 kb.bfv(src_o + ch * S + tn, TT), AF.Square),
                    "scl"))
                OP("tensor", lambda e, ch=ch, src_o=src_o, tn=tn: e.matmul(
                    ps_main[0:1, 0:TT], c["onesb"][:, 0:1],
                    kb.bfv(src_o + ch * S + tn, TT),
                    start=(ch == 0), stop=(ch == CD - 1),
                    skip_group_check=True))
                W("tensor", "scl", sq_marks[ch])
                pe_sq.append(OP("tensor", lambda e, ch=ch, sq=sq: e.matmul(
                    ps_main[0:1, TT:2 * TT], c["onesb"][:, 0:1],
                    kb.bfv(sq, TT),
                    start=(ch == 0), stop=(ch == CD - 1),
                    skip_group_check=True), "pe"))
            W("vector", "pe", kb.cnt["pe"])
            OP("vector", lambda e: e.tensor_scalar(
                rowv(ROW_M), ps_main[0:1, 0:TT], 1.0 / DM, None, ALU.mult))
            OP("vector", lambda e: e.tensor_scalar(
                rowv(ROW_T), ps_main[0:1, TT:2 * TT], 1.0 / DM, None,
                ALU.mult))
            DR("vector")
            OP("vector", lambda e: e.tensor_mul(rowv(ROW_R), rowv(ROW_M),
                                                rowv(ROW_M)))
            DR("vector")
            OP("vector", lambda e: e.tensor_sub(rowv(ROW_T), rowv(ROW_T),
                                                rowv(ROW_R)), "vec")
            stats_read = kb.cnt["vec"]
            W("scalar", "vec", stats_read)
            OP("scalar", lambda e: e.activation(rowv(ROW_R), rowv(ROW_T),
                                                AF.Ln, bias=LN_EPS))
            DR("scalar")
            OP("scalar", lambda e: e.activation(rowv(ROW_R), rowv(ROW_R),
                                                AF.Exp, scale=-0.5), "scl")
            W("vector", "scl", kb.cnt["scl"])
            OP("vector", lambda e: e.tensor_mul(rowv(ROW_M), rowv(ROW_M),
                                                rowv(ROW_R)), "vec")
            W("tensor", "vec", kb.cnt["vec"])
            OP("tensor", lambda e: e.matmul(
                ps_main[:, 2 * TT:3 * TT], c["ones1"][:],
                kb.f32[0:1, ROW_R:ROW_R + TT],
                start=True, stop=True))
            OP("tensor", lambda e: e.matmul(
                ps_main[:, 3 * TT:4 * TT], c["ones1"][:],
                kb.f32[0:1, ROW_M:ROW_M + TT],
                start=True, stop=True), "pe")
            W("scalar", "pe", kb.cnt["pe"])
            OP("scalar", lambda e: e.copy(kb.bfv(O_AB, TT),
                                          ps_main[:, 2 * TT:3 * TT]))
            OP("scalar", lambda e: e.copy(kb.bfv(O_AB + TT, TT),
                                          ps_main[:, 3 * TT:4 * TT]), "scl")
            W("vector", "scl", kb.cnt["scl"])
            for ch in range(CD):
                d = dst_o + ch * S + tn
                OP("vector", lambda e, d=d, ch=ch, src_o=src_o, tn=tn:
                   e.tensor_mul(kb.bfv(d, TT),
                                kb.bfv(src_o + ch * S + tn, TT),
                                kb.bfv(O_AB, TT)))
                DR("vector")
                OP("vector", lambda e, d=d: e.tensor_sub(
                    kb.bfv(d, TT), kb.bfv(d, TT), kb.bfv(O_AB + TT, TT)))
                DR("vector")
                OP("vector", lambda e, d=d, ch=ch, t=t:
                   e.tensor_reduce(c["amax"][:, t * CD + ch:t * CD + ch + 1],
                                   kb.bfv(d, TT), AX.X, ALU.max,
                                   apply_absolute_value=True),
                   "vec" if ch == CD - 1 else None)
            prev_stats_read = stats_read

    def exchange(clip_name, amax_cols):
        DR("vector")
        OP("vector", lambda e, n=amax_cols: e.tensor_reduce(
            c["r1"][:, 0:1], c["amax"][:, 0:n], AX.X, ALU.max), "vec")
        W("tensor", "vec", kb.cnt["vec"])
        OP("tensor", lambda e: e.transpose(
            ps_main[0:1, 0:128], c["r1"][:, 0:1], c["eyef"][:]), "pe")
        W("vector", "pe", kb.cnt["pe"])
        OP("vector", lambda e: e.reduce_max(
            c["sc"][0:1, 0:1], ps_main[0:1, 0:128], AX.X), "vec")
        W("sync", "vec", kb.cnt["vec"])
        tok = kb.dma("sync", lambda e: e.dma_start(
            out=kb.s_cci[0:1, 0:1], in_=c["sc"][0:1, 0:1]), "stc")
        kb.wtok("gpsimd", tok)
        OP("gpsimd", lambda e: e.collective_compute(
            "AllReduce", ALU.max,
            replica_groups=[list(range(kb.n_cores))],
            ins=[kb.s_cci[0:1, 0:1]], outs=[kb.s_cco[0:1, 0:1]]), "cc")
        W("sync", "cc", kb.cnt["cc"])
        tok = kb.dma("sync", lambda e: e.dma_start(
            out=c["sc"][0:1, 1:2], in_=kb.s_cco[0:1, 0:1]), "ccl")
        kb.wtok("scalar", tok)
        OP("scalar", lambda e, cn=clip_name: e.activation(
            c["sc"][0:1, 2:3], c[cn][0:1, 0:1], AF.Exp, scale=-1.0), "scl")
        W("vector", "scl", kb.cnt["scl"])
        OP("vector", lambda e: e.tensor_scalar(
            c["sc"][0:1, 3:4], c["sc"][0:1, 2:3], 1.0, None, ALU.add))
        DR("vector")
        OP("vector", lambda e: e.reciprocal(c["sc"][0:1, 2:3],
                                            c["sc"][0:1, 3:4]))
        DR("vector")
        OP("vector", lambda e: e.tensor_mul(c["sc"][0:1, 3:4],
                                            c["sc"][0:1, 2:3],
                                            c["sc"][0:1, 1:2]))
        DR("vector")
        OP("vector", lambda e: e.tensor_scalar(
            c["sc"][0:1, 4:5], c["sc"][0:1, 3:4], 1.0 / QMAXF, None,
            ALU.mult))
        DR("vector")
        OP("vector", lambda e: e.reciprocal(c["sc"][0:1, 5:6],
                                            c["sc"][0:1, 4:5]), "vec")
        W("tensor", "vec", kb.cnt["vec"])
        OP("tensor", lambda e: e.matmul(
            ps_main[:, 0:2], c["ones1"][:], c["sc"][0:1, 4:6],
            start=True, stop=True), "pe")
        W("vector", "pe", kb.cnt["pe"])
        OP("vector", lambda e: e.tensor_copy(c["pp"][:, 0:2],
                                             ps_main[:, 0:2]), "vec")
        DR("vector")

    def quantize(src_o, dst_o, nch):
        H = S // 2
        for ch in range(nch):
            for h0 in (0, H):
                OP("vector", lambda e, ch=ch, src_o=src_o, h0=h0:
                   e.tensor_scalar(
                       int_t[:, 0:H], kb.bfv(src_o + ch * S + h0, H),
                       c["pp"][:, 1:2], None, ALU.mult))
                DR("vector")
                OP("vector", lambda e, ch=ch, dst_o=dst_o, h0=h0:
                   e.tensor_scalar(
                       kb.bfv(dst_o + ch * S + h0, H), int_t[:, 0:H],
                       -128.0, 127.0, ALU.max, ALU.min), "vec")
                DR("vector")

    def scalevec(dst, srctile, ncols, extra):
        OP("vector", lambda e, dst=dst, s=srctile, n=ncols, x=extra:
           e.tensor_scalar(c[dst][:, 0:n], c[s][:, 0:n], c["pp"][:, 0:1],
                           x, ALU.mult, ALU.mult), "vec")
        DR("vector")

    def stile(scratch, k, t):
        return scratch[k, :].rearrange("(p s) -> p s", p=128)[
            :, t * TT:(t + 1) * TT]

    def ffn(final):
        layernorm(O_X, O_LN)
        exchange("clipx", NT * CD)
        scalevec("sxw_g", "swg", CH, 1.0)
        scalevec("sxw_u", "swu", CH, 1.0)
        quantize(O_LN, O_XQ, CD)
        xq_ready = kb.cnt["vec"]

        pe_w_marks = []
        gate_evac = {}
        up_evac = {t: 0 for t in range(NT)}
        h_marks = {}
        for i in range(CH):
            wg_s, wu_s = O_WS[i % 2], O_WS[2 + i % 2]
            if i >= 2:
                W("sync", "pe", pe_w_marks[i - 2])
            kb.wtok("sync", wgT_tok.get(i))
            kb.wtok("sync", wuT_tok.get(i))
            tokg = kb.dma("sync", lambda e, i=i, wg_s=wg_s: e.dma_start(
                out=kb.bfv(wg_s, 1024).rearrange("p (k j) -> p k j", j=128),
                in_=kb.s_wgT[i, :].rearrange("(k p j) -> p k j",
                                             p=128, j=128)), "wg")
            toku = kb.dma("sync", lambda e, i=i, wu_s=wu_s: e.dma_start(
                out=kb.bfv(wu_s, 1024).rearrange("p (k j) -> p k j", j=128),
                in_=kb.s_wuT[i, :].rearrange("(k p j) -> p k j",
                                             p=128, j=128)), "wu")
            kb.wtok("tensor", tokg)
            kb.wtok("tensor", toku)
            if i == 0:
                W("tensor", "vec", xq_ready)
            if i > 0:
                for t in range(NT):
                    W("tensor", "scl", gate_evac[t])
            for k in range(CD):
                for t in range(NT):
                    OP("tensor", lambda e, t=t, k=k, wg_s=wg_s: e.matmul(
                        ps_main[:, t * TT:(t + 1) * TT],
                        kb.bfv(wg_s + k * 128, 128),
                        kb.bfv(O_XQ + k * S + t * TT, TT),
                        start=(k == 0), stop=(k == CD - 1),
                        skip_group_check=True),
                       "pe" if (k == CD - 1) else None)
            gmm = kb.cnt["pe"]
            W("scalar", "pe", gmm)
            if i > 0:
                W("scalar", "vec", h_marks[i - 1])
            for t in range(NT):
                gate_evac[t] = OP(
                    "scalar", lambda e, t=t, i=i: e.activation(
                        kb.bfv(O_SG + t * TT, TT),
                        ps_main[:, t * TT:(t + 1) * TT], AF.Silu,
                        scale=c["sxw_g"][:, i:i + 1]), "scl")
            if i > 0:
                for t in range(NT):
                    W("tensor", "scl", up_evac[t])
            for k in range(CD):
                for t in range(NT):
                    OP("tensor", lambda e, t=t, k=k, wu_s=wu_s: e.matmul(
                        ps_aux[:, t * TT:(t + 1) * TT],
                        kb.bfv(wu_s + k * 128, 128),
                        kb.bfv(O_XQ + k * S + t * TT, TT),
                        start=(k == 0), stop=(k == CD - 1),
                        skip_group_check=True),
                       "pe" if (k == CD - 1) else None)
            pe_w_marks.append(kb.cnt["pe"])
            W("scalar", "pe", kb.cnt["pe"])
            for t in range(NT):
                up_evac[t] = OP(
                    "scalar", lambda e, t=t, i=i: e.activation(
                        kb.bfv(O_SU + t * TT, TT),
                        ps_aux[:, t * TT:(t + 1) * TT], AF.Identity,
                        scale=c["sxw_u"][:, i:i + 1]), "scl")
            W("vector", "scl", kb.cnt["scl"])
            if i > 0:
                kb.swait("vector", "sth")
            OP("vector", lambda e: e.tensor_mul(
                kb.bfv(O_HFF, S), kb.bfv(O_SG, S), kb.bfv(O_SU, S)))
            DR("vector")
            h_marks[i] = OP("vector", lambda e, i=i: e.tensor_reduce(
                c["amax"][:, i:i + 1], kb.bfv(O_HFF, S), AX.X, ALU.max,
                apply_absolute_value=True), "vec")
            W("sync", "vec", h_marks[i])
            kb.dma("sync", lambda e, i=i: e.dma_start(
                out=kb.s_h[i, :].rearrange("(p s) -> p s", p=128),
                in_=kb.bfv(O_HFF, S)), "sth")

        exchange("clipd", CH)
        scalevec("sxw_d", "swd", CD, 0.5)
        kb.swait("sync", "sth")
        kb.swait("sync", "stw")

        dn_pe = {}
        hq_vec = {}
        x_evac = {}
        W("tensor", "scl", kb.cnt["scl"])
        W("tensor", "vec", kb.cnt["vec"])
        for t in range(NT):
            for k in range(CH):
                wslot = O_WS[k % 2]
                hslot = O_T0 if k % 2 == 0 else O_T1
                qslot = O_T2 if k % 2 == 0 else O_T3
                prev = (t, k - 2) if k >= 2 else (t - 1, CH - 2 + k)
                if prev in dn_pe:
                    W("sync", "pe", dn_pe[prev])
                if prev in hq_vec:
                    W("sync", "vec", hq_vec[prev])
                tokw = kb.dma("sync", lambda e, k=k, wslot=wslot:
                              e.dma_start(
                                  out=kb.bfv(wslot, 1024).rearrange(
                                      "p (d j) -> p d j", j=128),
                                  in_=kb.s_wdT[k, :].rearrange(
                                      "(d p j) -> p d j", p=128, j=128)),
                              "wg")
                tokh = kb.dma("sync", lambda e, k=k, t=t, hslot=hslot:
                              e.dma_start(out=kb.bfv(hslot, TT),
                                          in_=stile(kb.s_h, k, t)), "hl")
                kb.wtok("vector", tokh)
                OP("vector", lambda e, hslot=hslot: e.tensor_scalar(
                    int_t[:, 0:TT], kb.bfv(hslot, TT), c["pp"][:, 1:2],
                    None, ALU.mult))
                DR("vector")
                hq_vec[(t, k)] = OP(
                    "vector", lambda e, qslot=qslot: e.tensor_scalar(
                        kb.bfv(qslot, TT), int_t[:, 0:TT], -128.0, 127.0,
                        ALU.max, ALU.min), "vec")
                DR("vector")
                kb.wtok("tensor", tokw)
                W("tensor", "vec", hq_vec[(t, k)])
                if k == 0 and (t - 1) in x_evac:
                    W("tensor", "vec", x_evac[t - 1])
                for d in range(CD):
                    bank = ps_main if d < 4 else ps_aux
                    OP("tensor", lambda e, d=d, k=k, wslot=wslot,
                       qslot=qslot, bank=bank: e.matmul(
                           bank[:, (d % 4) * TT:(d % 4 + 1) * TT],
                           kb.bfv(wslot + d * 128, 128), kb.bfv(qslot, TT),
                           start=(k == 0), stop=(k == CH - 1),
                           skip_group_check=True),
                       "pe" if d == CD - 1 else None)
                dn_pe[(t, k)] = kb.cnt["pe"]
            W("vector", "pe", kb.cnt["pe"])
            for d in range(CD):
                bank = ps_main if d < 4 else ps_aux
                if not final:
                    OP("vector", lambda e, d=d, t=t, bank=bank:
                       e.scalar_tensor_tensor(
                           kb.bfv(O_X + d * S + t * TT, TT),
                           bank[:, (d % 4) * TT:(d % 4 + 1) * TT],
                           c["sxw_d"][:, d:d + 1],
                           kb.bfv(O_X + d * S + t * TT, TT),
                           ALU.mult, ALU.add), "vec")
                    DR("vector")
                else:
                    pk = (d - 2) % 8
                    if (t > 0 or d >= 2) and pk in out_st:
                        kb.wtok("vector", out_st[pk])
                    OP("vector", lambda e, d=d, t=t, bank=bank:
                       e.scalar_tensor_tensor(
                           kb.f32v(FL + (d % 2) * TT, TT),
                           bank[:, (d % 4) * TT:(d % 4 + 1) * TT],
                           c["sxw_d"][:, d:d + 1],
                           kb.bfv(O_X + d * S + t * TT, TT),
                           ALU.mult, ALU.add), "vec")
                    DR("vector")
                    W("sync", "vec", kb.cnt["vec"])
                    out_st[d] = kb.dma(
                        "sync", lambda e, d=d, t=t: e.dma_start(
                            out=kb.d_out[d * 128:(d + 1) * 128,
                                         t * TT:(t + 1) * TT],
                            in_=kb.f32v(FL + (d % 2) * TT, TT)), "out")
            x_evac[t] = kb.cnt["vec"]

    out_st = {}

    def mamba():
        layernorm(O_X, O_LN)
        W("sync", "vec", kb.cnt["vec"])
        for d in range(CD):
            kb.dma("sync", lambda e, d=d: e.dma_start(
                out=kb.s_x1[d, :].rearrange("(p s) -> p s", p=128),
                in_=kb.bfv(O_X + d * S, S)), "stx")

        win_cast = []
        win_pe = {}
        win_evac = {}
        last_sz_st = [0]
        kb.swait("scalar", "stx")
        for o in range(2 * CI):
            fslot = 0 if o % 2 == 0 else 2816
            wslot = O_WS[o % 2]
            if o >= 2:
                W("sync", "scl", win_cast[o - 2])
            tok = kb.dma("sync", lambda e, o=o, fslot=fslot: e.dma_start(
                out=kb.f32[:, fslot:fslot + 1024].rearrange(
                    "p (k j) -> p k j", j=128),
                in_=di["W_in"][:, o * 128:(o + 1) * 128].rearrange(
                    "(k p) j -> p k j", p=128)), "xw")
            kb.wtok("scalar", tok)
            if o >= 2:
                W("scalar", "pe", win_pe[o - 2])
            win_cast.append(OP(
                "scalar", lambda e, fslot=fslot, wslot=wslot: e.copy(
                    kb.bfv(wslot, 1024), kb.f32[:, fslot:fslot + 1024]),
                "scl"))
            W("tensor", "scl", kb.cnt["scl"])
            for t in range(NT):
                if (o - 2, t) in win_evac:
                    W("tensor", "scl", win_evac[(o - 2, t)])
            for k in range(CD):
                for t in range(NT):
                    OP("tensor", lambda e, t=t, k=k, wslot=wslot: e.matmul(
                        ps_main[:, t * TT:(t + 1) * TT],
                        kb.bfv(wslot + k * 128, 128),
                        kb.bfv(O_LN + k * S + t * TT, TT),
                        start=(k == 0), stop=(k == CD - 1),
                        skip_group_check=True),
                       "pe" if (k == CD - 1) else None)
            win_pe[o] = kb.cnt["pe"]
            for t in range(NT):
                W("scalar", "pe", kb.cnt["pe"])
                if o < CI:
                    win_evac[(o, t)] = OP(
                        "scalar", lambda e, o=o, t=t: e.activation(
                            kb.bfv(xhp_off(o) + t * TT, TT),
                            ps_main[:, t * TT:(t + 1) * TT], AF.Identity,
                            bias=c["bxh"][:, o:o + 1]), "scl")
                else:
                    if t == 0:
                        kb.swait("scalar", "stz")
                    win_evac[(o, t)] = OP(
                        "scalar", lambda e, o=o, t=t: e.activation(
                            kb.bfv(O_STG + t * TT, TT),
                            ps_main[:, t * TT:(t + 1) * TT], AF.Silu,
                            bias=c["bz"][:, o - CI:o - CI + 1]), "scl")
            if o >= CI:
                W("sync", "scl", kb.cnt["scl"])
                kb.dma("sync", lambda e, o=o: e.dma_start(
                    out=kb.s_sz[o - CI, :].rearrange("(p s) -> p s", p=128),
                    in_=kb.bfv(O_STG, S)), "stz")

        kb.barrier(["scalar", "sync"], ["vector"])
        conv_scl = 0
        for ci in range(CI):
            xp = xhp_off(ci)
            xh = xh_off(ci)
            if conv_scl:
                W("vector", "scl", conv_scl)
            OP("vector", lambda e, xp=xp: e.tensor_copy(
                c["pad8"][:, 4:8], kb.bfv(xp, 4)))
            DR("vector")
            OP("vector", lambda e, ci=ci: e.tensor_scalar(
                kb.f32v(FA, 4), c["pad8"][:, 4:8],
                c["convw"][:, ci * 4 + 3:ci * 4 + 4], None, ALU.mult))
            OP("vector", lambda e, xp=xp, ci=ci: e.tensor_scalar(
                kb.f32v(FA + 4, S - 4), kb.bfv(xp + 4, S - 4),
                c["convw"][:, ci * 4 + 3:ci * 4 + 4], None, ALU.mult))
            for j in range(3):
                DR("vector")
                OP("vector", lambda e, j=j, ci=ci: e.scalar_tensor_tensor(
                    kb.f32v(FA, 4), c["pad8"][:, 1 + j:5 + j],
                    c["convw"][:, ci * 4 + j:ci * 4 + j + 1],
                    kb.f32v(FA, 4), ALU.mult, ALU.add))
                isl = OP("vector", lambda e, j=j, ci=ci, xp=xp:
                         e.scalar_tensor_tensor(
                             kb.f32v(FA + 4, S - 4),
                             kb.bfv(xp + 1 + j, S - 4),
                             c["convw"][:, ci * 4 + j:ci * 4 + j + 1],
                             kb.f32v(FA + 4, S - 4), ALU.mult, ALU.add),
                         "vec" if j == 2 else None)
            W("scalar", "vec", kb.cnt["vec"])
            conv_scl = OP("scalar", lambda e, xh=xh, ci=ci: e.activation(
                kb.bfv(xh, S), kb.f32v(FA, S), AF.Silu,
                bias=c["convb"][:, ci:ci + 1]), "scl")

        kb.barrier(["scalar"], ["tensor", "vector"])
        bm_evac = {}
        for t in range(NT):
            if (t - 1) in bm_evac:
                W("tensor", "scl", bm_evac[t - 1])
            for k in range(CI):
                OP("tensor", lambda e, t=t, k=k: e.matmul(
                    ps_aux[0:DS, 0:TT], c["WxB"][:, k * DS:(k + 1) * DS],
                    kb.bfv(xh_off(k) + t * TT, TT),
                    start=(k == 0), stop=False, skip_group_check=True))
            OP("tensor", lambda e: e.matmul(
                ps_aux[0:DS, 0:TT], c["bxB"][0:1, :],
                c["onesrow_b"][0:1, :], start=False, stop=True,
                skip_group_check=True), "pe")
            W("scalar", "pe", kb.cnt["pe"])
            bm_evac[t] = OP("scalar", lambda e, t=t: e.copy(
                bmatv(t * TT, TT), ps_aux[0:DS, 0:TT]), "scl")
        br_evac = {}
        br_bank_user = {}
        for s in range(DS):
            for t in range(NT):
                bank = TT + (t % 2) * TT
                if bank in br_bank_user:
                    W("tensor", "scl", br_bank_user[bank])
                W("tensor", "scl", bm_evac[t])
                OP("tensor", lambda e, s=s, t=t, bank=bank: e.matmul(
                    ps_aux[:, bank:bank + TT],
                    c["oneh"][:, s * 128:(s + 1) * 128],
                    bmatv(t * TT, TT), start=True, stop=True,
                    skip_group_check=True), "pe")
                W("scalar", "pe", kb.cnt["pe"])
                if s < NBREP_RES:
                    br_evac[(s, t)] = OP(
                        "scalar", lambda e, s=s, t=t, bank=bank: e.copy(
                            kb.bfv(brep_off(s) + t * TT, TT),
                            ps_aux[:, bank:bank + TT]), "scl")
                else:
                    if s > NBREP_RES or t > 0:
                        kb.swait("scalar", "stx")
                    br_evac[(s, t)] = OP(
                        "scalar", lambda e, s=s, t=t, bank=bank: e.copy(
                            kb.bfv(O_STG + t * TT, TT),
                            ps_aux[:, bank:bank + TT]), "scl")
                br_bank_user[bank] = br_evac[(s, t)]
            if s >= NBREP_RES:
                W("sync", "scl", kb.cnt["scl"])
                kb.dma("sync", lambda e, s=s: e.dma_start(
                    out=kb.s_brep[s - NBREP_RES, :].rearrange(
                        "(p s) -> p s", p=128),
                    in_=kb.bfv(O_STG, S)), "stx")
        kb.barrier(["scalar", "sync", "vector"],
                   ["vector", "tensor", "sync"])

        wdt_pe = {}
        sp_evac = {}
        scan_marks = {}
        add_marks = {}
        gat_marks = {}
        for ci in range(CI):
            if (ci - 1) in wdt_pe:
                W("sync", "pe", wdt_pe[ci - 1])
            tok = kb.dma("sync", lambda e, ci=ci: e.dma_start(
                out=kb.bfv(O_WDT, 2048).rearrange("p (k j) -> p k j", j=128),
                in_=kb.s_wdtb[:, ci * 128:(ci + 1) * 128].rearrange(
                    "(k p) j -> p k j", p=128)), "xw")
            kb.wtok("tensor", tok)
            if (ci - 1) in sp_evac:
                W("tensor", "scl", sp_evac[ci - 1])
            for k in range(CI):
                for t in range(NT):
                    OP("tensor", lambda e, t=t, k=k: e.matmul(
                        ps_main[:, t * TT:(t + 1) * TT],
                        kb.bfv(O_WDT + k * 128, 128),
                        kb.bfv(xh_off(k) + t * TT, TT),
                        start=(k == 0), stop=(k == CI - 1),
                        skip_group_check=True),
                       "pe" if (k == CI - 1) else None)
            wdt_pe[ci] = kb.cnt["pe"]
            W("scalar", "pe", kb.cnt["pe"])
            if ci > 0:
                g_last = ci * DS - 1
                W("scalar", "vec", scan_marks[g_last])
            for t in range(NT):
                OP("scalar", lambda e, t=t, ci=ci: e.activation(
                    kb.f32v(FL + t * TT, TT),
                    ps_main[:, t * TT:(t + 1) * TT], AF.Abs,
                    bias=c["bdt"][:, ci:ci + 1]))
                DR("scalar")
                OP("scalar", lambda e, t=t: e.activation(
                    kb.f32v(FL + t * TT, TT), kb.f32v(FL + t * TT, TT),
                    AF.Exp, scale=-1.0))
                DR("scalar")
                OP("scalar", lambda e, t=t: e.activation(
                    kb.f32v(FL + t * TT, TT), kb.f32v(FL + t * TT, TT),
                    AF.Ln, bias=1.0))
                DR("scalar")
                OP("scalar", lambda e, t=t, ci=ci: e.activation(
                    kb.f32v(FB + t * TT, TT),
                    ps_main[:, t * TT:(t + 1) * TT], AF.Relu,
                    bias=c["bdt"][:, ci:ci + 1]),
                   "scl" if t == NT - 1 else None)
            sp_evac[ci] = kb.cnt["scl"]
            W("vector", "scl", kb.cnt["scl"])
            OP("vector", lambda e: e.tensor_add(
                kb.f32v(FL, S), kb.f32v(FL, S), kb.f32v(FB, S)))
            DR("vector")
            kb.swait("vector", "sty")
            OP("vector", lambda e, ci=ci: e.tensor_mul(
                kb.bfv(O_DU, S), kb.f32v(FL, S), kb.bfv(xh_off(ci), S)))
            OP("vector", lambda e, ci=ci: e.tensor_scalar(
                kb.f32v(FY, S), kb.bfv(xh_off(ci), S),
                c["D"][:, ci:ci + 1], None, ALU.mult), "vec")
            DR("vector")
            delta_mark = kb.cnt["vec"]
            sz_tok = {}
            for t in range(NT):
                szslot = O_SZ[t % len(O_SZ)]
                if (ci - 1, t) in gat_marks:
                    W("sync", "vec", gat_marks[(ci - 1, t)])
                sz_tok[t] = kb.dma(
                    "sync", lambda e, ci=ci, t=t, szslot=szslot:
                    e.dma_start(out=kb.bfv(szslot, TT),
                                in_=stile(kb.s_sz, ci, t)), "szl")
            bs_tok = {}
            for sp in (NBREP_RES, NBREP_RES + 1):
                W("sync", "vec", kb.cnt["vec"])
                bs_tok[sp] = kb.dma(
                    "sync", lambda e, sp=sp, bsrc=O_BS[sp % 2]: e.dma_start(
                        out=kb.bfv(bsrc, S),
                        in_=kb.s_brep[sp - NBREP_RES, :].rearrange(
                            "(p s) -> p s", p=128)), "bs")
            for s in range(DS):
                g = ci * DS + s
                p = g // 2
                dslot = FA if s % 2 == 0 else FA2
                hslot = HS[g % 4]
                W("scalar", "vec", delta_mark)
                if g >= 2:
                    W("scalar", "vec", scan_marks[g - 2])
                da = OP("scalar", lambda e, ci=ci, s=s, dslot=dslot:
                        e.activation(
                            kb.f32v(dslot, S), kb.f32v(FL, S), AF.Exp,
                            scale=c["negA"][:, ci * DS + s:ci * DS + s + 1]),
                        "scl")
                if s < NBREP_RES:
                    bsrc = brep_off(s)
                else:
                    bsrc = O_BS[s % 2]
                    kb.wtok("vector", bs_tok[s])
                OP("vector", lambda e, bsrc=bsrc: e.tensor_mul(
                    kb.bfv(O_DBU, S), kb.bfv(O_DU, S), kb.bfv(bsrc, S)))
                DR("vector")
                W("vector", "scl", da)
                if g % 2 == 0 and p >= 2:
                    kb.wtok("vector", add_marks[p - 2])
                scan_marks[g] = OP(
                    "vector", lambda e, dslot=dslot, hslot=hslot:
                    e.tensor_tensor_scan(
                        kb.f32v(hslot, S), kb.f32v(dslot, S),
                        kb.bfv(O_DBU, S), 0.0, ALU.mult, ALU.add), "vec")
                DR("vector")
                if g % 2 == 1:
                    fslot = HS[(g - 1) % 4]
                    scan_marks[g] = OP(
                        "vector", lambda e, fslot=fslot, hslot=hslot:
                        e.tensor_add(
                            kb.f32v(fslot, S), kb.f32v(fslot, S),
                            kb.f32v(hslot, S)), "vec")
                    DR("vector")
                    W("gpsimd", "vec", scan_marks[g])
                    if p >= 1:
                        kb.wtok("gpsimd", add_marks[p - 1])
                    add_marks[p] = kb.dma(
                        "gpsimd", lambda e, fslot=fslot: e.dma_start(
                            out=kb.f32v(FY, S), in_=kb.f32v(fslot, S),
                            accum_op=ALU.add), "yacc")
                if NBREP_RES <= s <= DS - 3:
                    W("sync", "vec", scan_marks[g])
                    bs_tok[s + 2] = kb.dma(
                        "sync", lambda e, s=s, bsrc=O_BS[s % 2]: e.dma_start(
                            out=kb.bfv(bsrc, S),
                            in_=kb.s_brep[s + 2 - NBREP_RES, :].rearrange(
                                "(p s) -> p s", p=128)), "bs")
            kb.wtok("vector", add_marks[(ci * DS + DS - 1) // 2])
            for t in range(NT):
                szslot = O_SZ[t % len(O_SZ)]
                kb.wtok("vector", sz_tok[t])
                gat_marks[(ci, t)] = OP(
                    "vector", lambda e, t=t, szslot=szslot: e.tensor_mul(
                        kb.bfv(O_DU + t * TT, TT), kb.f32v(FY + t * TT, TT),
                        kb.bfv(szslot, TT)), "vec")
                DR("vector")
            W("sync", "vec", kb.cnt["vec"])
            kb.dma("sync", lambda e, ci=ci: e.dma_start(
                out=kb.s_yg[ci, :].rearrange("(p s) -> p s", p=128),
                in_=kb.bfv(O_DU, S)), "sty")
        kb.barrier(["vector", "sync", "scalar"],
                   ["sync", "tensor", "vector"])

        wo_pe = {}
        wo_evac = {}
        for t in range(NT):
            for k in range(CI):
                wslot = O_WS[k % 2]
                ygslot = O_T0 if k % 2 == 0 else O_T1
                prev = (t, k - 2) if k >= 2 else (t - 1, CI - 2 + k)
                if prev in wo_pe:
                    W("sync", "pe", wo_pe[prev])
                tokw = kb.dma("sync", lambda e, k=k, wslot=wslot:
                              e.dma_start(
                                  out=kb.bfv(wslot, 1024),
                                  in_=kb.s_woutb[k * 128:(k + 1) * 128, :]),
                              "wg")
                toky = kb.dma("sync", lambda e, k=k, t=t, ygslot=ygslot:
                              e.dma_start(out=kb.bfv(ygslot, TT),
                                          in_=stile(kb.s_yg, k, t)), "ygl")
                kb.wtok("tensor", tokw)
                kb.wtok("tensor", toky)
                if k == 0 and (t - 1) in wo_evac:
                    W("tensor", "vec", wo_evac[t - 1])
                for d in range(CD):
                    bank = ps_main if d < 4 else ps_aux
                    OP("tensor", lambda e, d=d, k=k, wslot=wslot,
                       ygslot=ygslot, bank=bank: e.matmul(
                           bank[:, (d % 4) * TT:(d % 4 + 1) * TT],
                           kb.bfv(wslot + d * 128, 128),
                           kb.bfv(ygslot, TT),
                           start=(k == 0), stop=(k == CI - 1),
                           skip_group_check=True),
                       "pe" if d == CD - 1 else None)
                wo_pe[(t, k)] = kb.cnt["pe"]
            W("vector", "pe", kb.cnt["pe"])
            for d in range(CD):
                xslot = O_T2 if d % 2 == 0 else O_T3
                W("sync", "vec", kb.cnt["vec"])
                tok = kb.dma("sync", lambda e, d=d, t=t, xslot=xslot:
                             e.dma_start(out=kb.bfv(xslot, TT),
                                         in_=stile(kb.s_x1, d, t)), "x1l")
                kb.wtok("vector", tok)
                bank = ps_main if d < 4 else ps_aux
                OP("vector", lambda e, d=d, t=t, xslot=xslot, bank=bank:
                   e.scalar_tensor_tensor(
                       kb.bfv(O_X + d * S + t * TT, TT),
                       bank[:, (d % 4) * TT:(d % 4 + 1) * TT],
                       c["bout"][:, d:d + 1],
                       kb.bfv(xslot, TT), ALU.add, ALU.add), "vec")
                DR("vector")
            wo_evac[t] = kb.cnt["vec"]
        kb.barrier(["vector", "sync", "gpsimd"],
                   ["scalar", "tensor", "vector", "sync"])

    ffn(final=False)
    kb.barrier(["vector", "scalar", "sync", "tensor"],
               ["vector", "scalar", "sync", "tensor"])
    mamba()
    ffn(final=True)
    kb.swait_all_dma("sync")


_BUILD_CACHE = {}


def _get_build(S, n_cores):
    key = (S, n_cores)
    if key not in _BUILD_CACHE:
        _BUILD_CACHE[key] = build(S, n_cores)
    return _BUILD_CACHE[key]


def _pack_rows(v, nch):
    return np.ascontiguousarray(np.asarray(v, np.float32).reshape(nch, 128).T)


def make_inputs(xc, w, S):
    bf = ml_dtypes.bfloat16
    Alog = np.asarray(w["L0_attn_A_log"], np.float32)
    convw = np.asarray(w["L0_attn_conv_w"], np.float32)[:, 0, :]
    one = {}
    one["x"] = np.ascontiguousarray(np.asarray(xc, np.float32).T)
    one["W_in"] = np.ascontiguousarray(np.asarray(w["L0_attn_W_in"],
                                                  np.float32))
    one["W_dt"] = np.ascontiguousarray(np.asarray(w["L0_attn_W_dt"],
                                                   np.float32))
    one["W_out"] = np.ascontiguousarray(np.asarray(w["L0_attn_W_out"],
                                                    np.float32))
    Wx = np.asarray(w["L0_attn_W_x"], np.float32)
    one["WxB"] = np.ascontiguousarray(
        Wx[:, DS:].reshape(CI, 128, DS).transpose(1, 0, 2)
        .reshape(128, CI * DS))
    one["bxB"] = np.ascontiguousarray(
        np.asarray(w["L0_attn_b_x"], np.float32)[DS:].reshape(1, DS))
    one["w_gate"] = np.ascontiguousarray(np.asarray(w["L0_ffn_w_gate"],
                                                     np.float32))
    one["w_up"] = np.ascontiguousarray(np.asarray(w["L0_ffn_w_up"],
                                                   np.float32))
    one["w_down"] = np.ascontiguousarray(np.asarray(w["L0_ffn_w_down"],
                                                     np.float32))
    one["Alog"] = np.ascontiguousarray(
        Alog.reshape(CI, 128, DS).transpose(1, 0, 2).reshape(128, CI * DS))
    one["convw"] = np.ascontiguousarray(
        convw.reshape(CI, 128, DC).transpose(1, 0, 2).reshape(128, CI * DC))
    one["convb"] = _pack_rows(w["L0_attn_conv_b"], CI)
    one["Dvec"] = _pack_rows(w["L0_attn_D"], CI)
    bin_ = np.asarray(w["L0_attn_b_in"], np.float32)
    one["bxh"] = _pack_rows(bin_[:DI], CI)
    one["bz"] = _pack_rows(bin_[DI:], CI)
    one["bdt"] = _pack_rows(w["L0_attn_b_dt"], CI)
    one["bout"] = _pack_rows(w["L0_attn_b_out"], CD)
    one["clipx"] = np.asarray(
        [[float(np.asarray(w["L0_ffn_clip_x"]).reshape(-1)[0]), 0.0]],
        np.float32)
    one["clipd"] = np.asarray(
        [[float(np.asarray(w["L0_ffn_clip_down"]).reshape(-1)[0]), 0.0]],
        np.float32)
    one["eyeb"] = np.eye(128, dtype=bf)
    one["eyef"] = np.eye(128, dtype=np.float32)
    one["onesb"] = np.ones((128, 2), dtype=bf)
    one["ones1"] = np.ones((1, 128), np.float32)
    one["onesrow_b"] = np.ones((1, TT), dtype=bf)
    cv = np.zeros((128, 8), np.float32)
    cv[:, 0] = LN_EPS
    cv[:, 1] = 1.0
    cv[:, 2] = -1.0
    cv[:, 3] = -0.5
    one["cvals"] = cv
    oneh = np.zeros((DS, DS * 128), dtype=bf)
    for s in range(DS):
        oneh[s, s * 128:(s + 1) * 128] = 1
    one["oneh"] = oneh
    return one


def kernel(**inputs):
    x = np.asarray(inputs["L0_attn_x"], np.float32)
    S = x.shape[1]
    kb = _get_build(S, NCORE)
    in_maps = [make_inputs(x[cidx], inputs, S) for cidx in range(NCORE)]
    res = run_bass_kernel_spmd(kb.nc, in_maps, core_ids=list(range(NCORE)))
    outs = [np.asarray(r["out"]).T for r in res.results]
    return np.stack(outs, axis=0).astype(np.float32)


# version 10
# speedup vs baseline: 1.1473x; 1.0057x over previous
import numpy as np
import ml_dtypes
import concourse.bass as bass
import concourse.mybir as mybir
from concourse.bass_utils import run_bass_kernel_spmd
from contextlib import ExitStack

F32 = mybir.dt.float32
BF16 = mybir.dt.bfloat16
I32 = mybir.dt.int32
AF = mybir.ActivationFunctionType
ALU = mybir.AluOpType
AX = mybir.AxisListType

DM, DI, DS, DC, HF = 1024, 2048, 16, 4, 2816
CD, CI, CH = DM // 128, DI // 128, HF // 128
TT = 512
LN_EPS = 1e-5
QMAXF = 127.0
NCORE = 8
ENGS = ("sync", "vector", "scalar", "tensor", "gpsimd")


class KB:
    def __init__(self, S, n_cores):
        self.S = S
        self.NT = S // TT
        self.n_cores = n_cores
        self.nc = bass.Bass("TRN2", target_bir_lowering=False)
        self.es = ExitStack()
        self.q = {e: [] for e in ENGS}
        self._mk_dram()

    def _mk_dram(self):
        nc, S = self.nc, self.S
        self.di = {}

        def din(name, shape, dt=F32):
            self.di[name] = nc.dram_tensor(name, shape, dt,
                                           kind="ExternalInput").ap()

        din("x", [DM, S])
        din("W_in", [DM, 2 * DI])
        din("W_dt", [DI, DI])
        din("W_out", [DI, DM])
        din("WxB", [128, CI * DS])
        din("bxB", [1, DS])
        din("w_gate", [HF, DM])
        din("w_up", [HF, DM])
        din("w_down", [DM, HF])
        din("Alog", [128, CI * DS])
        din("convw", [128, CI * DC])
        din("convb", [128, CI])
        din("Dvec", [128, CI])
        din("bxh", [128, CI])
        din("bz", [128, CI])
        din("bdt", [128, CI])
        din("bout", [128, CD])
        din("clipx", [1, 2])
        din("clipd", [1, 2])
        din("cvals", [128, 8])
        din("eyeb", [128, 128], BF16)
        din("eyef", [128, 128], F32)
        din("onesb", [128, 2], BF16)
        din("ones1", [1, 128])
        din("onesrow_b", [1, TT], BF16)
        din("oneh", [DS, DS * 128], BF16)
        self.d_out = nc.dram_tensor("out", [DM, S], F32,
                                    kind="ExternalOutput").ap()

        def scr(name, shape, dt):
            return nc.dram_tensor(name, shape, dt, kind="Internal").ap()

        self.s_wdtb = scr("s_wdtb", [DI, DI], BF16)
        self.s_woutb = scr("s_woutb", [DI, DM], BF16)
        self.s_wgT = scr("s_wgT", [CH, CD * 128 * 128], BF16)
        self.s_wuT = scr("s_wuT", [CH, CD * 128 * 128], BF16)
        self.s_wdT = scr("s_wdT", [CH, CD * 128 * 128], BF16)
        self.s_h = scr("s_h", [CH, 128 * S], BF16)
        self.s_sz = scr("s_sz", [CI, 128 * S], BF16)
        self.s_yg = scr("s_yg", [CI, 128 * S], BF16)
        self.s_x1 = scr("s_x1", [CD, 128 * S], BF16)
        self.s_brep = scr("s_brep", [6, 128 * S], BF16)
        self.s_cci = scr("s_cci", [1, 2], F32)
        self.s_cco = scr("s_cco", [1, 2], F32)

    def mk_mem(self):
        nc, S, es = self.nc, self.S, self.es

        def sb(name, shape, dt=F32):
            return es.enter_context(nc.sbuf_tensor(name, shape, dt))

        self.bf = sb("bf_arena", [128, 32 * S], BF16)
        self.f32 = sb("f32_arena", [128, 8 * S], F32)
        self.int_t = sb("int_t", [128, S // 2], I32)
        self.c = {}
        for name, shape, dt in [
            ("negA", [128, CI * DS], F32), ("convw", [128, CI * DC], F32),
            ("convb", [128, CI], F32), ("D", [128, CI], F32),
            ("bxh", [128, CI], F32), ("bz", [128, CI], F32),
            ("bdt", [128, CI], F32), ("bout", [128, CD], F32),
            ("WxB", [128, CI * DS], BF16), ("bxB", [1, DS], BF16),
            ("eyeb", [128, 128], BF16), ("eyef", [128, 128], F32),
            ("onesb", [128, 2], BF16),
            ("ones1", [1, 128], F32), ("onesrow_b", [1, TT], BF16),
            ("oneh", [DS, DS * 128], BF16),
            ("clipx", [1, 2], F32), ("clipd", [1, 2], F32),
            ("cvals", [128, 8], F32),
            ("swg", [128, CH], F32), ("swu", [128, CH], F32),
            ("swd", [128, CD], F32),
            ("sxw_g", [128, CH], F32), ("sxw_u", [128, CH], F32),
            ("sxw_d", [128, CD], F32),
            ("amax", [128, 32], F32), ("r1", [128, 2], F32),
            ("sc", [1, 8], F32), ("pp", [128, 4], F32),
            ("pad8", [128, 8], BF16),
        ]:
            self.c[name] = sb("c_" + name, shape, dt)
        self.sems = {}
        for n in ["pe", "vec", "scl", "gps", "cc"]:
            self.sems[n] = es.enter_context(nc.semaphore(n))
        self.cnt = {n: 0 for n in self.sems}
        self.sempool = [es.enter_context(nc.semaphore(f"dmas{i}"))
                        for i in range(44)]
        self.sempool_next = 0
        self.streams = {}
        self.ps_main = es.enter_context(
            nc.psum_tensor("ps_main", [128, 4 * TT], F32))
        self._ps_tb_scope = ExitStack()
        self.ps_tb = self._ps_tb_scope.enter_context(
            nc.psum_tensor("ps_tb", [128, 1024], BF16))
        self.ps_aux = None
        for col, val in enumerate([LN_EPS, 1.0, -1.0, -0.5]):
            self.nc.const_aps.register(F32, val, self.c["cvals"][:, col:col + 1])

    def close_p0_psum(self):
        self._ps_tb_scope.close()
        self.ps_tb = None
        self.ps_aux = self.es.enter_context(
            self.nc.psum_tensor("ps_aux", [128, 4 * TT], F32))

    def op(self, eng, fn, inc=None, n=1):
        if inc is None:
            self.q[eng].append(fn)
            return None
        self.cnt[inc] += n
        sem = self.sems[inc]
        self.q[eng].append(
            lambda e, fn=fn, sem=sem, n=n: fn(e).then_inc(sem, n))
        return self.cnt[inc]

    def wait(self, eng, semname, val):
        if val and val > 0:
            sem = self.sems[semname]
            self.q[eng].append(
                lambda e, sem=sem, val=val: e.wait_ge(sem, val))

    def _stream(self, name):
        st = self.streams.get(name)
        if st is None:
            a = self.sempool[self.sempool_next]
            b = self.sempool[self.sempool_next + 1]
            self.sempool_next += 2
            st = {"sems": [a, b], "cnt": [0, 0], "par": 0}
            self.streams[name] = st
        return st

    def dma(self, eng, fn, stream):
        st = self._stream(stream)
        p = st["par"]
        sem = st["sems"][p]
        if st["cnt"][p] > 0:
            v = st["cnt"][p]
            self.q[eng].append(
                lambda e, sem=sem, v=v: e.wait_ge(sem, v))
        st["cnt"][p] += 16
        v = st["cnt"][p]
        self.q[eng].append(
            lambda e, fn=fn, sem=sem: fn(e).then_inc(sem, 16))
        st["par"] = 1 - p
        return (sem, v)

    def wtok(self, eng, tok):
        if tok is None:
            return
        sem, v = tok
        self.q[eng].append(lambda e, sem=sem, v=v: e.wait_ge(sem, v))

    def swait(self, eng, stream):
        st = self.streams.get(stream)
        if not st:
            return
        for p in (0, 1):
            if st["cnt"][p] > 0:
                sem, v = st["sems"][p], st["cnt"][p]
                self.q[eng].append(
                    lambda e, sem=sem, v=v: e.wait_ge(sem, v))

    def swait_all_dma(self, eng):
        for name in list(self.streams):
            self.swait(eng, name)

    def drain(self, eng):
        self.q[eng].append(lambda e: e.drain())

    def barrier(self, writers, waiters):
        snap = []
        sync_snap = []
        for w in writers:
            if w == "sync":
                for name, st in self.streams.items():
                    for p in (0, 1):
                        if st["cnt"][p] > 0:
                            sync_snap.append((st["sems"][p], st["cnt"][p]))
            else:
                s = {"vector": "vec", "scalar": "scl", "tensor": "pe",
                     "gpsimd": "gps"}[w]
                snap.append((w, s, self.cnt[s]))
        for eng in waiters:
            for w, s, v in snap:
                if eng == w:
                    self.drain(eng)
                else:
                    self.wait(eng, s, v)
            for sem, v in sync_snap:
                self.q[eng].append(
                    lambda e, sem=sem, v=v: e.wait_ge(sem, v))

    def bfv(self, off, cols):
        return self.bf[:, off:off + cols]

    def f32v(self, off, cols):
        return self.f32[:, off:off + cols]


def build(S, n_cores):
    kb = KB(S, n_cores)
    with kb.es:
        kb.mk_mem()
        _emit(kb)
        blk = kb.es.enter_context(kb.nc.Block())
        for eng in ENGS:
            ops = kb.q[eng]
            if ops:
                getattr(blk, eng)(lambda e, ops=ops: [f(e) for f in ops])
    return kb


def _emit(kb: KB):
    S, NT = kb.S, kb.NT
    c, di = kb.c, kb.di
    int_t = kb.int_t
    ps_main = kb.ps_main

    def U(k):
        return k * S

    O_X, O_LN, O_XQ = U(0), U(8), U(16)
    O_SG, O_SU, O_HFF = U(24), U(25), U(26)
    O_WS = [U(27), U(27) + 1024, U(28), U(28) + 1024]
    O_T0, O_T1 = U(29), U(29) + TT
    O_T2, O_T3 = U(29) + 2 * TT, U(29) + 3 * TT
    O_STG = U(30)
    O_AB = U(26)
    O_BLK = U(24)

    def xh_off(ci):
        return U(8 + ci) if ci < 8 else U(ci - 8)

    def xhp_off(ci):
        return U(ci) if ci < 8 else U(16 + ci - 8)

    def brep_off(s):
        return U(16 + s)

    NBREP_RES = 10
    O_BS = [U(26), U(27)]
    O_WDT = U(28)
    O_DU = U(29)
    O_SZ = [U(30) + t * TT for t in range(min(NT, 4))]
    O_DBU = U(31)
    O_BMAT = U(29)

    FA, FB, FH, FY, FL, FA2 = 0, S, 2 * S, 3 * S, 4 * S, 5 * S
    FH3, FH4 = 6 * S, 7 * S
    HS = [FH, FB, FH3, FH4]
    ROW_M, ROW_T, ROW_R = FA2, FA2 + TT, FA2 + 2 * TT

    W, OP, DR = kb.wait, kb.op, kb.drain

    def rowv(off):
        return kb.f32[0:1, off:off + TT]

    def bmatv(t0, cols):
        return kb.bf[0:DS, O_BMAT + t0:O_BMAT + t0 + cols]

    for name in ["cvals", "convw", "convb", "bxh", "bz", "bdt", "bout",
                 "eyeb", "eyef", "onesb", "ones1", "onesrow_b", "oneh",
                 "clipx", "clipd"]:
        kb.dma("sync", lambda e, n=name: e.dma_start(out=c[n][:], in_=di[n]),
               "cst")
    kb.dma("sync", lambda e: e.dma_start(out=c["D"][:], in_=di["Dvec"]),
           "cst")
    kb.dma("sync", lambda e: e.dma_start(out=c["negA"][:], in_=di["Alog"]),
           "cst")
    kb.dma("sync", lambda e: e.dma_start(out=kb.f32[:, 0:CI * DS],
                                         in_=di["WxB"]), "cst")
    kb.dma("sync", lambda e: e.dma_start(
        out=kb.f32[0:1, CI * DS:CI * DS + DS], in_=di["bxB"]), "cst")
    kb.swait("scalar", "cst")
    OP("scalar", lambda e: e.copy(c["WxB"][:], kb.f32[:, 0:CI * DS]))
    OP("scalar", lambda e: e.copy(c["bxB"][:],
                                  kb.f32[0:1, CI * DS:CI * DS + DS]), "scl")
    OP("scalar", lambda e: e.activation(c["negA"][:], c["negA"][:], AF.Exp))
    DR("scalar")
    OP("scalar", lambda e: e.mul(c["negA"][:], c["negA"][:], -1.0), "scl")
    kb.swait("vector", "cst")
    OP("vector", lambda e: e.memset(c["pad8"][:], 0.0), "vec")

    xcopy_marks = []
    W("sync", "scl", kb.cnt["scl"])
    for ch in range(CD):
        slot = FL if ch % 2 == 0 else FA
        if ch >= 2:
            W("sync", "scl", xcopy_marks[ch - 2])
        tok = kb.dma("sync", lambda e, ch=ch, slot=slot: e.dma_start(
            out=kb.f32v(slot, S), in_=di["x"][ch * 128:(ch + 1) * 128, :]),
            "xw")
        kb.wtok("scalar", tok)
        xcopy_marks.append(OP("scalar", lambda e, ch=ch, slot=slot: e.copy(
            kb.bfv(O_X + ch * S, S), kb.f32v(slot, S)), "scl"))

    qmarks = []

    def quant_transpose(wname, nch, kd, stile, dstT, toks,
                        scatter_rows=False):
        nkb = kd // 128
        for i in range(nch):
            slot = 0 if i % 2 == 0 else 2816
            if len(qmarks) >= 2:
                W("sync", "vec", qmarks[-2])
            tok = kb.dma("sync", lambda e, w=wname, i=i, slot=slot, kd=kd:
                         e.dma_start(out=kb.f32[:, slot:slot + kd],
                                     in_=di[w][i * 128:(i + 1) * 128, :]),
                         "xw")
            kb.wtok("vector", tok)
            OP("vector", lambda e, slot=slot, kd=kd: e.tensor_reduce(
                c["r1"][:, 0:1], kb.f32[:, slot:slot + kd], AX.X, ALU.max,
                apply_absolute_value=True))
            DR("vector")
            OP("vector", lambda e, st=stile, i=i: e.tensor_scalar(
                c[st][:, i:i + 1], c["r1"][:, 0:1], 1.0 / QMAXF, None,
                ALU.mult))
            OP("vector", lambda e: e.reciprocal(c["r1"][:, 1:2],
                                                c["r1"][:, 0:1]))
            DR("vector")
            W("vector", "pe", kb.cnt["pe"])
            mlast = None
            for p0 in range(0, kd, S // 2):
                pn = min(S // 2, kd - p0)
                OP("vector", lambda e, slot=slot, p0=p0, pn=pn:
                   e.tensor_scalar(int_t[:, 0:pn],
                                   kb.f32[:, slot + p0:slot + p0 + pn],
                                   c["r1"][:, 1:2], QMAXF,
                                   ALU.mult, ALU.mult))
                DR("vector")
                mlast = OP("vector", lambda e, p0=p0, pn=pn: e.tensor_copy(
                    kb.bf[:, O_BLK + p0:O_BLK + p0 + pn], int_t[:, 0:pn]), "vec")
                DR("vector")
            qmarks.append(mlast)
            kb.swait("scalar", "stw")
            for g0 in range(0, nkb, 8):
                gn = min(8, nkb - g0)
                W("tensor", "vec", kb.cnt["vec"])
                W("tensor", "scl", kb.cnt["scl"])
                for j in range(gn):
                    OP("tensor", lambda e, g0=g0, j=j, pstb=kb.ps_tb:
                       e.transpose(
                           pstb[:, j * 128:(j + 1) * 128],
                           kb.bf[:, O_BLK + (g0 + j) * 128:O_BLK + (g0 + j + 1) * 128],
                           c["eyeb"][:]), "pe" if j == gn - 1 else None)
                W("scalar", "pe", kb.cnt["pe"])
                for j in range(gn):
                    OP("scalar", lambda e, g0=g0, j=j, pstb=kb.ps_tb: e.copy(
                        kb.bfv(O_STG + (g0 + j) * 128, 128),
                        pstb[:, j * 128:(j + 1) * 128]),
                       "scl" if j == gn - 1 else None)
            W("sync", "scl", kb.cnt["scl"])
            if not scatter_rows:
                toks[i] = kb.dma(
                    "sync", lambda e, dstT=dstT, i=i, nkb=nkb: e.dma_start(
                        out=dstT[i, 0:nkb * 128 * 128].rearrange(
                            "(k p j) -> p k j", p=128, j=128),
                        in_=kb.bfv(O_STG, nkb * 128).rearrange(
                            "p (k j) -> p k j", j=128)), "stw")
            else:
                toks[i] = kb.dma(
                    "sync", lambda e, dstT=dstT, i=i, nkb=nkb: e.dma_start(
                        out=dstT[:, i * 128 * 128:(i + 1) * 128 * 128]
                        .rearrange("k (p j) -> p k j", j=128),
                        in_=kb.bfv(O_STG, nkb * 128).rearrange(
                            "p (k j) -> p k j", j=128)), "stw")

    pre_scl_mark = kb.cnt["scl"]
    wgT_tok, wuT_tok, wdT_tok = {}, {}, {}

    def p0c_weights():
        W("sync", "scl", pre_scl_mark)
        quant_transpose("w_gate", CH, DM, "swg", kb.s_wgT, wgT_tok)
        quant_transpose("w_up", CH, DM, "swu", kb.s_wuT, wuT_tok)
        quant_transpose("w_down", CD, HF, "swd", kb.s_wdT, wdT_tok,
                        scatter_rows=True)

        cast_marks = []
        for wname, dst, nch, kd in [("W_dt", kb.s_wdtb, CI, DI),
                                    ("W_out", kb.s_woutb, CI, DM)]:
            for i in range(nch):
                slot = 0 if i % 2 == 0 else 2816
                if len(cast_marks) >= 2:
                    W("sync", "scl", cast_marks[-2])
                tok = kb.dma("sync",
                             lambda e, w=wname, i=i, slot=slot, kd=kd:
                             e.dma_start(out=kb.f32[:, slot:slot + kd],
                                         in_=di[w][i * 128:(i + 1) * 128, :]),
                             "xw")
                kb.wtok("scalar", tok)
                kb.swait("scalar", "stw")
                cast_marks.append(OP(
                    "scalar", lambda e, slot=slot, kd=kd: e.copy(
                        kb.bfv(O_STG, kd), kb.f32[:, slot:slot + kd]), "scl"))
                W("sync", "scl", kb.cnt["scl"])
                kb.dma("sync", lambda e, dst=dst, i=i, kd=kd: e.dma_start(
                    out=dst[i * 128:(i + 1) * 128, :], in_=kb.bfv(O_STG, kd)),
                    "stw")


    def layernorm(src_o, dst_o):
        prev_stats_read = 0
        W("tensor", "scl", kb.cnt["scl"])
        for t in range(NT):
            tn = t * TT
            sq_marks = []
            pe_sq = []
            W("tensor", "vec", prev_stats_read)
            for ch in range(CD):
                sq = O_T0 if ch % 2 == 0 else O_T1
                if ch >= 2:
                    W("scalar", "pe", pe_sq[ch - 2])
                sq_marks.append(OP(
                    "scalar", lambda e, ch=ch, sq=sq, src_o=src_o, tn=tn:
                    e.activation(kb.bfv(sq, TT),
                                 kb.bfv(src_o + ch * S + tn, TT), AF.Square),
                    "scl"))
                OP("tensor", lambda e, ch=ch, src_o=src_o, tn=tn: e.matmul(
                    ps_main[0:1, 0:TT], c["onesb"][:, 0:1],
                    kb.bfv(src_o + ch * S + tn, TT),
                    start=(ch == 0), stop=(ch == CD - 1),
                    skip_group_check=True))
                W("tensor", "scl", sq_marks[ch])
                pe_sq.append(OP("tensor", lambda e, ch=ch, sq=sq: e.matmul(
                    ps_main[0:1, TT:2 * TT], c["onesb"][:, 0:1],
                    kb.bfv(sq, TT),
                    start=(ch == 0), stop=(ch == CD - 1),
                    skip_group_check=True), "pe"))
            W("vector", "pe", kb.cnt["pe"])
            OP("vector", lambda e: e.tensor_scalar(
                rowv(ROW_M), ps_main[0:1, 0:TT], 1.0 / DM, None, ALU.mult))
            OP("vector", lambda e: e.tensor_scalar(
                rowv(ROW_T), ps_main[0:1, TT:2 * TT], 1.0 / DM, None,
                ALU.mult))
            DR("vector")
            OP("vector", lambda e: e.tensor_mul(rowv(ROW_R), rowv(ROW_M),
                                                rowv(ROW_M)))
            DR("vector")
            OP("vector", lambda e: e.tensor_sub(rowv(ROW_T), rowv(ROW_T),
                                                rowv(ROW_R)), "vec")
            stats_read = kb.cnt["vec"]
            W("scalar", "vec", stats_read)
            OP("scalar", lambda e: e.activation(rowv(ROW_R), rowv(ROW_T),
                                                AF.Ln, bias=LN_EPS))
            DR("scalar")
            OP("scalar", lambda e: e.activation(rowv(ROW_R), rowv(ROW_R),
                                                AF.Exp, scale=-0.5), "scl")
            W("vector", "scl", kb.cnt["scl"])
            OP("vector", lambda e: e.tensor_mul(rowv(ROW_M), rowv(ROW_M),
                                                rowv(ROW_R)), "vec")
            W("tensor", "vec", kb.cnt["vec"])
            OP("tensor", lambda e: e.matmul(
                ps_main[:, 2 * TT:3 * TT], c["ones1"][:],
                kb.f32[0:1, ROW_R:ROW_R + TT],
                start=True, stop=True))
            OP("tensor", lambda e: e.matmul(
                ps_main[:, 3 * TT:4 * TT], c["ones1"][:],
                kb.f32[0:1, ROW_M:ROW_M + TT],
                start=True, stop=True), "pe")
            W("scalar", "pe", kb.cnt["pe"])
            OP("scalar", lambda e: e.copy(kb.bfv(O_AB, TT),
                                          ps_main[:, 2 * TT:3 * TT]))
            OP("scalar", lambda e: e.copy(kb.bfv(O_AB + TT, TT),
                                          ps_main[:, 3 * TT:4 * TT]), "scl")
            W("vector", "scl", kb.cnt["scl"])
            for ch in range(CD):
                d = dst_o + ch * S + tn
                OP("vector", lambda e, d=d, ch=ch, src_o=src_o, tn=tn:
                   e.tensor_mul(kb.bfv(d, TT),
                                kb.bfv(src_o + ch * S + tn, TT),
                                kb.bfv(O_AB, TT)))
                DR("vector")
                OP("vector", lambda e, d=d: e.tensor_sub(
                    kb.bfv(d, TT), kb.bfv(d, TT), kb.bfv(O_AB + TT, TT)))
                DR("vector")
                OP("vector", lambda e, d=d, ch=ch, t=t:
                   e.tensor_reduce(c["amax"][:, t * CD + ch:t * CD + ch + 1],
                                   kb.bfv(d, TT), AX.X, ALU.max,
                                   apply_absolute_value=True),
                   "vec" if ch == CD - 1 else None)
            prev_stats_read = stats_read

    def exchange(clip_name, amax_cols):
        DR("vector")
        OP("vector", lambda e, n=amax_cols: e.tensor_reduce(
            c["r1"][:, 0:1], c["amax"][:, 0:n], AX.X, ALU.max), "vec")
        W("tensor", "vec", kb.cnt["vec"])
        OP("tensor", lambda e: e.transpose(
            ps_main[0:1, 0:128], c["r1"][:, 0:1], c["eyef"][:]), "pe")
        W("vector", "pe", kb.cnt["pe"])
        OP("vector", lambda e: e.reduce_max(
            c["sc"][0:1, 0:1], ps_main[0:1, 0:128], AX.X), "vec")
        W("gpsimd", "vec", kb.cnt["vec"])
        tok = kb.dma("gpsimd", lambda e: e.dma_start(
            out=kb.s_cci[0:1, 0:1], in_=c["sc"][0:1, 0:1]), "stc")
        kb.wtok("gpsimd", tok)
        OP("gpsimd", lambda e: e.collective_compute(
            "AllReduce", ALU.max,
            replica_groups=[list(range(kb.n_cores))],
            ins=[kb.s_cci[0:1, 0:1]], outs=[kb.s_cco[0:1, 0:1]]), "cc")
        W("gpsimd", "cc", kb.cnt["cc"])
        tok = kb.dma("gpsimd", lambda e: e.dma_start(
            out=c["sc"][0:1, 1:2], in_=kb.s_cco[0:1, 0:1]), "ccl")
        kb.wtok("scalar", tok)
        OP("scalar", lambda e, cn=clip_name: e.activation(
            c["sc"][0:1, 2:3], c[cn][0:1, 0:1], AF.Exp, scale=-1.0), "scl")
        W("vector", "scl", kb.cnt["scl"])
        OP("vector", lambda e: e.tensor_scalar(
            c["sc"][0:1, 3:4], c["sc"][0:1, 2:3], 1.0, None, ALU.add))
        DR("vector")
        OP("vector", lambda e: e.reciprocal(c["sc"][0:1, 2:3],
                                            c["sc"][0:1, 3:4]))
        DR("vector")
        OP("vector", lambda e: e.tensor_mul(c["sc"][0:1, 3:4],
                                            c["sc"][0:1, 2:3],
                                            c["sc"][0:1, 1:2]))
        DR("vector")
        OP("vector", lambda e: e.tensor_scalar(
            c["sc"][0:1, 4:5], c["sc"][0:1, 3:4], 1.0 / QMAXF, None,
            ALU.mult))
        DR("vector")
        OP("vector", lambda e: e.reciprocal(c["sc"][0:1, 5:6],
                                            c["sc"][0:1, 4:5]), "vec")
        W("tensor", "vec", kb.cnt["vec"])
        OP("tensor", lambda e: e.matmul(
            ps_main[:, 0:2], c["ones1"][:], c["sc"][0:1, 4:6],
            start=True, stop=True), "pe")
        W("vector", "pe", kb.cnt["pe"])
        OP("vector", lambda e: e.tensor_copy(c["pp"][:, 0:2],
                                             ps_main[:, 0:2]), "vec")
        DR("vector")

    def quantize(src_o, dst_o, nch):
        H = S // 2
        for ch in range(nch):
            for h0 in (0, H):
                OP("vector", lambda e, ch=ch, src_o=src_o, h0=h0:
                   e.tensor_scalar(
                       int_t[:, 0:H], kb.bfv(src_o + ch * S + h0, H),
                       c["pp"][:, 1:2], None, ALU.mult))
                DR("vector")
                OP("vector", lambda e, ch=ch, dst_o=dst_o, h0=h0:
                   e.tensor_scalar(
                       kb.bfv(dst_o + ch * S + h0, H), int_t[:, 0:H],
                       -128.0, 127.0, ALU.max, ALU.min), "vec")
                DR("vector")

    def scalevec(dst, srctile, ncols, extra):
        OP("vector", lambda e, dst=dst, s=srctile, n=ncols, x=extra:
           e.tensor_scalar(c[dst][:, 0:n], c[s][:, 0:n], c["pp"][:, 0:1],
                           x, ALU.mult, ALU.mult), "vec")
        DR("vector")

    def stile(scratch, k, t):
        return scratch[k, :].rearrange("(p s) -> p s", p=128)[
            :, t * TT:(t + 1) * TT]

    def ffn(final, xq_ready=None):
        if xq_ready is None:
            layernorm(O_X, O_LN)
            exchange("clipx", NT * CD)
        scalevec("sxw_g", "swg", CH, 1.0)
        scalevec("sxw_u", "swu", CH, 1.0)
        sv_ready = kb.cnt["vec"]
        if xq_ready is None:
            quantize(O_LN, O_XQ, CD)
            xq_ready = kb.cnt["vec"]

        pe_w_marks = []
        gate_evac = {}
        up_evac = {t: 0 for t in range(NT)}
        h_marks = {}
        for i in range(CH):
            wg_s, wu_s = O_WS[i % 2], O_WS[2 + i % 2]
            if i >= 2:
                W("sync", "pe", pe_w_marks[i - 2])
            kb.wtok("sync", wgT_tok.get(i))
            kb.wtok("sync", wuT_tok.get(i))
            tokg = kb.dma("sync", lambda e, i=i, wg_s=wg_s: e.dma_start(
                out=kb.bfv(wg_s, 1024).rearrange("p (k j) -> p k j", j=128),
                in_=kb.s_wgT[i, :].rearrange("(k p j) -> p k j",
                                             p=128, j=128)), "wg")
            toku = kb.dma("sync", lambda e, i=i, wu_s=wu_s: e.dma_start(
                out=kb.bfv(wu_s, 1024).rearrange("p (k j) -> p k j", j=128),
                in_=kb.s_wuT[i, :].rearrange("(k p j) -> p k j",
                                             p=128, j=128)), "wu")
            kb.wtok("tensor", tokg)
            kb.wtok("tensor", toku)
            if i == 0:
                W("tensor", "vec", xq_ready)
            if i > 0:
                for t in range(NT):
                    W("tensor", "scl", gate_evac[t])
            for k in range(CD):
                for t in range(NT):
                    OP("tensor", lambda e, t=t, k=k, wg_s=wg_s: e.matmul(
                        ps_main[:, t * TT:(t + 1) * TT],
                        kb.bfv(wg_s + k * 128, 128),
                        kb.bfv(O_XQ + k * S + t * TT, TT),
                        start=(k == 0), stop=(k == CD - 1),
                        skip_group_check=True),
                       "pe" if (k == CD - 1) else None)
            gmm = kb.cnt["pe"]
            W("scalar", "pe", gmm)
            if i == 0:
                W("scalar", "vec", sv_ready)
            if i > 0:
                W("scalar", "vec", h_marks[i - 1])
            for t in range(NT):
                gate_evac[t] = OP(
                    "scalar", lambda e, t=t, i=i: e.activation(
                        kb.bfv(O_SG + t * TT, TT),
                        ps_main[:, t * TT:(t + 1) * TT], AF.Silu,
                        scale=c["sxw_g"][:, i:i + 1]), "scl")
            if i > 0:
                for t in range(NT):
                    W("tensor", "scl", up_evac[t])
            for k in range(CD):
                for t in range(NT):
                    OP("tensor", lambda e, t=t, k=k, wu_s=wu_s: e.matmul(
                        ps_aux[:, t * TT:(t + 1) * TT],
                        kb.bfv(wu_s + k * 128, 128),
                        kb.bfv(O_XQ + k * S + t * TT, TT),
                        start=(k == 0), stop=(k == CD - 1),
                        skip_group_check=True),
                       "pe" if (k == CD - 1) else None)
            pe_w_marks.append(kb.cnt["pe"])
            W("scalar", "pe", kb.cnt["pe"])
            for t in range(NT):
                up_evac[t] = OP(
                    "scalar", lambda e, t=t, i=i: e.activation(
                        kb.bfv(O_SU + t * TT, TT),
                        ps_aux[:, t * TT:(t + 1) * TT], AF.Identity,
                        scale=c["sxw_u"][:, i:i + 1]), "scl")
            W("vector", "scl", kb.cnt["scl"])
            if i > 0:
                kb.swait("vector", "sth")
            OP("vector", lambda e: e.tensor_mul(
                kb.bfv(O_HFF, S), kb.bfv(O_SG, S), kb.bfv(O_SU, S)))
            DR("vector")
            h_marks[i] = OP("vector", lambda e, i=i: e.tensor_reduce(
                c["amax"][:, i:i + 1], kb.bfv(O_HFF, S), AX.X, ALU.max,
                apply_absolute_value=True), "vec")
            W("sync", "vec", h_marks[i])
            kb.dma("sync", lambda e, i=i: e.dma_start(
                out=kb.s_h[i, :].rearrange("(p s) -> p s", p=128),
                in_=kb.bfv(O_HFF, S)), "sth")

        exchange("clipd", CH)
        scalevec("sxw_d", "swd", CD, 0.5)
        kb.swait("sync", "sth")
        kb.swait("sync", "stw")

        dn_pe = {}
        hq_vec = {}
        x_evac = {}
        W("tensor", "scl", kb.cnt["scl"])
        W("tensor", "vec", kb.cnt["vec"])
        for t in range(NT):
            for k in range(CH):
                wslot = O_WS[k % 2]
                hslot = O_T0 if k % 2 == 0 else O_T1
                qslot = O_T2 if k % 2 == 0 else O_T3
                prev = (t, k - 2) if k >= 2 else (t - 1, CH - 2 + k)
                if prev in dn_pe:
                    W("sync", "pe", dn_pe[prev])
                if prev in hq_vec:
                    W("sync", "vec", hq_vec[prev])
                tokw = kb.dma("sync", lambda e, k=k, wslot=wslot:
                              e.dma_start(
                                  out=kb.bfv(wslot, 1024).rearrange(
                                      "p (d j) -> p d j", j=128),
                                  in_=kb.s_wdT[k, :].rearrange(
                                      "(d p j) -> p d j", p=128, j=128)),
                              "wg")
                tokh = kb.dma("sync", lambda e, k=k, t=t, hslot=hslot:
                              e.dma_start(out=kb.bfv(hslot, TT),
                                          in_=stile(kb.s_h, k, t)), "hl")
                kb.wtok("vector", tokh)
                OP("vector", lambda e, hslot=hslot: e.tensor_scalar(
                    int_t[:, 0:TT], kb.bfv(hslot, TT), c["pp"][:, 1:2],
                    None, ALU.mult))
                DR("vector")
                hq_vec[(t, k)] = OP(
                    "vector", lambda e, qslot=qslot: e.tensor_scalar(
                        kb.bfv(qslot, TT), int_t[:, 0:TT], -128.0, 127.0,
                        ALU.max, ALU.min), "vec")
                DR("vector")
                kb.wtok("tensor", tokw)
                W("tensor", "vec", hq_vec[(t, k)])
                if k == 0 and (t - 1) in x_evac:
                    W("tensor", "vec", x_evac[t - 1])
                for d in range(CD):
                    bank = ps_main if d < 4 else ps_aux
                    OP("tensor", lambda e, d=d, k=k, wslot=wslot,
                       qslot=qslot, bank=bank: e.matmul(
                           bank[:, (d % 4) * TT:(d % 4 + 1) * TT],
                           kb.bfv(wslot + d * 128, 128), kb.bfv(qslot, TT),
                           start=(k == 0), stop=(k == CH - 1),
                           skip_group_check=True),
                       "pe" if d == CD - 1 else None)
                dn_pe[(t, k)] = kb.cnt["pe"]
            W("vector", "pe", kb.cnt["pe"])
            for d in range(CD):
                bank = ps_main if d < 4 else ps_aux
                if not final:
                    OP("vector", lambda e, d=d, t=t, bank=bank:
                       e.scalar_tensor_tensor(
                           kb.bfv(O_X + d * S + t * TT, TT),
                           bank[:, (d % 4) * TT:(d % 4 + 1) * TT],
                           c["sxw_d"][:, d:d + 1],
                           kb.bfv(O_X + d * S + t * TT, TT),
                           ALU.mult, ALU.add), "vec")
                    DR("vector")
                else:
                    pk = (d - 2) % 8
                    if (t > 0 or d >= 2) and pk in out_st:
                        kb.wtok("vector", out_st[pk])
                    OP("vector", lambda e, d=d, t=t, bank=bank:
                       e.scalar_tensor_tensor(
                           kb.f32v(FL + (d % 2) * TT, TT),
                           bank[:, (d % 4) * TT:(d % 4 + 1) * TT],
                           c["sxw_d"][:, d:d + 1],
                           kb.bfv(O_X + d * S + t * TT, TT),
                           ALU.mult, ALU.add), "vec")
                    DR("vector")
                    W("sync", "vec", kb.cnt["vec"])
                    out_st[d] = kb.dma(
                        "sync", lambda e, d=d, t=t: e.dma_start(
                            out=kb.d_out[d * 128:(d + 1) * 128,
                                         t * TT:(t + 1) * TT],
                            in_=kb.f32v(FL + (d % 2) * TT, TT)), "out")
            x_evac[t] = kb.cnt["vec"]

    out_st = {}

    def mamba():
        layernorm(O_X, O_LN)
        W("sync", "vec", kb.cnt["vec"])
        for d in range(CD):
            kb.dma("sync", lambda e, d=d: e.dma_start(
                out=kb.s_x1[d, :].rearrange("(p s) -> p s", p=128),
                in_=kb.bfv(O_X + d * S, S)), "stx")

        win_cast = []
        win_pe = {}
        win_evac = {}
        last_sz_st = [0]
        kb.swait("scalar", "stx")
        for o in range(2 * CI):
            fslot = 0 if o % 2 == 0 else 2816
            wslot = O_WS[o % 2]
            if o >= 2:
                W("sync", "scl", win_cast[o - 2])
            tok = kb.dma("sync", lambda e, o=o, fslot=fslot: e.dma_start(
                out=kb.f32[:, fslot:fslot + 1024].rearrange(
                    "p (k j) -> p k j", j=128),
                in_=di["W_in"][:, o * 128:(o + 1) * 128].rearrange(
                    "(k p) j -> p k j", p=128)), "xw")
            kb.wtok("scalar", tok)
            if o >= 2:
                W("scalar", "pe", win_pe[o - 2])
            win_cast.append(OP(
                "scalar", lambda e, fslot=fslot, wslot=wslot: e.copy(
                    kb.bfv(wslot, 1024), kb.f32[:, fslot:fslot + 1024]),
                "scl"))
            W("tensor", "scl", kb.cnt["scl"])
            for t in range(NT):
                if (o - 2, t) in win_evac:
                    W("tensor", "scl", win_evac[(o - 2, t)])
            for k in range(CD):
                for t in range(NT):
                    OP("tensor", lambda e, t=t, k=k, wslot=wslot: e.matmul(
                        ps_main[:, t * TT:(t + 1) * TT],
                        kb.bfv(wslot + k * 128, 128),
                        kb.bfv(O_LN + k * S + t * TT, TT),
                        start=(k == 0), stop=(k == CD - 1),
                        skip_group_check=True),
                       "pe" if (k == CD - 1) else None)
            win_pe[o] = kb.cnt["pe"]
            for t in range(NT):
                W("scalar", "pe", kb.cnt["pe"])
                if o < CI:
                    win_evac[(o, t)] = OP(
                        "scalar", lambda e, o=o, t=t: e.activation(
                            kb.bfv(xhp_off(o) + t * TT, TT),
                            ps_main[:, t * TT:(t + 1) * TT], AF.Identity,
                            bias=c["bxh"][:, o:o + 1]), "scl")
                else:
                    if t == 0:
                        kb.swait("scalar", "stz")
                    win_evac[(o, t)] = OP(
                        "scalar", lambda e, o=o, t=t: e.activation(
                            kb.bfv(O_STG + t * TT, TT),
                            ps_main[:, t * TT:(t + 1) * TT], AF.Silu,
                            bias=c["bz"][:, o - CI:o - CI + 1]), "scl")
            if o >= CI:
                W("sync", "scl", kb.cnt["scl"])
                kb.dma("sync", lambda e, o=o: e.dma_start(
                    out=kb.s_sz[o - CI, :].rearrange("(p s) -> p s", p=128),
                    in_=kb.bfv(O_STG, S)), "stz")

        kb.barrier(["scalar", "sync"], ["vector"])
        conv_scl = 0
        for ci in range(CI):
            xp = xhp_off(ci)
            xh = xh_off(ci)
            if conv_scl:
                W("vector", "scl", conv_scl)
            OP("vector", lambda e, xp=xp: e.tensor_copy(
                c["pad8"][:, 4:8], kb.bfv(xp, 4)))
            DR("vector")
            OP("vector", lambda e, ci=ci: e.tensor_scalar(
                kb.f32v(FA, 4), c["pad8"][:, 4:8],
                c["convw"][:, ci * 4 + 3:ci * 4 + 4], None, ALU.mult))
            OP("vector", lambda e, xp=xp, ci=ci: e.tensor_scalar(
                kb.f32v(FA + 4, S - 4), kb.bfv(xp + 4, S - 4),
                c["convw"][:, ci * 4 + 3:ci * 4 + 4], None, ALU.mult))
            for j in range(3):
                DR("vector")
                OP("vector", lambda e, j=j, ci=ci: e.scalar_tensor_tensor(
                    kb.f32v(FA, 4), c["pad8"][:, 1 + j:5 + j],
                    c["convw"][:, ci * 4 + j:ci * 4 + j + 1],
                    kb.f32v(FA, 4), ALU.mult, ALU.add))
                isl = OP("vector", lambda e, j=j, ci=ci, xp=xp:
                         e.scalar_tensor_tensor(
                             kb.f32v(FA + 4, S - 4),
                             kb.bfv(xp + 1 + j, S - 4),
                             c["convw"][:, ci * 4 + j:ci * 4 + j + 1],
                             kb.f32v(FA + 4, S - 4), ALU.mult, ALU.add),
                         "vec" if j == 2 else None)
            W("scalar", "vec", kb.cnt["vec"])
            conv_scl = OP("scalar", lambda e, xh=xh, ci=ci: e.activation(
                kb.bfv(xh, S), kb.f32v(FA, S), AF.Silu,
                bias=c["convb"][:, ci:ci + 1]), "scl")

        kb.barrier(["scalar"], ["tensor", "vector"])
        bm_evac = {}
        for t in range(NT):
            if (t - 1) in bm_evac:
                W("tensor", "scl", bm_evac[t - 1])
            for k in range(CI):
                OP("tensor", lambda e, t=t, k=k: e.matmul(
                    ps_aux[0:DS, 0:TT], c["WxB"][:, k * DS:(k + 1) * DS],
                    kb.bfv(xh_off(k) + t * TT, TT),
                    start=(k == 0), stop=False, skip_group_check=True))
            OP("tensor", lambda e: e.matmul(
                ps_aux[0:DS, 0:TT], c["bxB"][0:1, :],
                c["onesrow_b"][0:1, :], start=False, stop=True,
                skip_group_check=True), "pe")
            W("scalar", "pe", kb.cnt["pe"])
            bm_evac[t] = OP("scalar", lambda e, t=t: e.copy(
                bmatv(t * TT, TT), ps_aux[0:DS, 0:TT]), "scl")
        br_evac = {}
        br_bank_user = {}
        for s in range(DS):
            for t in range(NT):
                bank = TT + (t % 2) * TT
                if bank in br_bank_user:
                    W("tensor", "scl", br_bank_user[bank])
                W("tensor", "scl", bm_evac[t])
                OP("tensor", lambda e, s=s, t=t, bank=bank: e.matmul(
                    ps_aux[:, bank:bank + TT],
                    c["oneh"][:, s * 128:(s + 1) * 128],
                    bmatv(t * TT, TT), start=True, stop=True,
                    skip_group_check=True), "pe")
                W("scalar", "pe", kb.cnt["pe"])
                if s < NBREP_RES:
                    br_evac[(s, t)] = OP(
                        "scalar", lambda e, s=s, t=t, bank=bank: e.copy(
                            kb.bfv(brep_off(s) + t * TT, TT),
                            ps_aux[:, bank:bank + TT]), "scl")
                else:
                    if s > NBREP_RES or t > 0:
                        kb.swait("scalar", "stx")
                    br_evac[(s, t)] = OP(
                        "scalar", lambda e, s=s, t=t, bank=bank: e.copy(
                            kb.bfv(O_STG + t * TT, TT),
                            ps_aux[:, bank:bank + TT]), "scl")
                br_bank_user[bank] = br_evac[(s, t)]
            if s >= NBREP_RES:
                W("sync", "scl", kb.cnt["scl"])
                kb.dma("sync", lambda e, s=s: e.dma_start(
                    out=kb.s_brep[s - NBREP_RES, :].rearrange(
                        "(p s) -> p s", p=128),
                    in_=kb.bfv(O_STG, S)), "stx")
        kb.barrier(["scalar", "sync", "vector"],
                   ["vector", "tensor", "sync"])

        wdt_pe = {}
        sp_evac = {}
        scan_marks = {}
        add_marks = {}
        gat_marks = {}
        for ci in range(CI):
            if (ci - 1) in wdt_pe:
                W("sync", "pe", wdt_pe[ci - 1])
            tok = kb.dma("sync", lambda e, ci=ci: e.dma_start(
                out=kb.bfv(O_WDT, 2048).rearrange("p (k j) -> p k j", j=128),
                in_=kb.s_wdtb[:, ci * 128:(ci + 1) * 128].rearrange(
                    "(k p) j -> p k j", p=128)), "xw")
            kb.wtok("tensor", tok)
            if (ci - 1) in sp_evac:
                W("tensor", "scl", sp_evac[ci - 1])
            for k in range(CI):
                for t in range(NT):
                    OP("tensor", lambda e, t=t, k=k: e.matmul(
                        ps_main[:, t * TT:(t + 1) * TT],
                        kb.bfv(O_WDT + k * 128, 128),
                        kb.bfv(xh_off(k) + t * TT, TT),
                        start=(k == 0), stop=(k == CI - 1),
                        skip_group_check=True),
                       "pe" if (k == CI - 1) else None)
            wdt_pe[ci] = kb.cnt["pe"]
            W("scalar", "pe", kb.cnt["pe"])
            if ci > 0:
                g_last = ci * DS - 1
                W("scalar", "vec", scan_marks[g_last])
            for t in range(NT):
                OP("scalar", lambda e, t=t, ci=ci: e.activation(
                    kb.f32v(FL + t * TT, TT),
                    ps_main[:, t * TT:(t + 1) * TT], AF.Abs,
                    bias=c["bdt"][:, ci:ci + 1]))
                DR("scalar")
                OP("scalar", lambda e, t=t: e.activation(
                    kb.f32v(FL + t * TT, TT), kb.f32v(FL + t * TT, TT),
                    AF.Exp, scale=-1.0))
                DR("scalar")
                OP("scalar", lambda e, t=t: e.activation(
                    kb.f32v(FL + t * TT, TT), kb.f32v(FL + t * TT, TT),
                    AF.Ln, bias=1.0))
                DR("scalar")
                OP("scalar", lambda e, t=t, ci=ci: e.activation(
                    kb.f32v(FB + t * TT, TT),
                    ps_main[:, t * TT:(t + 1) * TT], AF.Relu,
                    bias=c["bdt"][:, ci:ci + 1]),
                   "scl" if t == NT - 1 else None)
            sp_evac[ci] = kb.cnt["scl"]
            W("vector", "scl", kb.cnt["scl"])
            OP("vector", lambda e: e.tensor_add(
                kb.f32v(FL, S), kb.f32v(FL, S), kb.f32v(FB, S)))
            DR("vector")
            kb.swait("vector", "sty")
            OP("vector", lambda e, ci=ci: e.tensor_mul(
                kb.bfv(O_DU, S), kb.f32v(FL, S), kb.bfv(xh_off(ci), S)))
            OP("vector", lambda e, ci=ci: e.tensor_scalar(
                kb.f32v(FY, S), kb.bfv(xh_off(ci), S),
                c["D"][:, ci:ci + 1], None, ALU.mult), "vec")
            DR("vector")
            delta_mark = kb.cnt["vec"]
            sz_tok = {}
            for t in range(NT):
                szslot = O_SZ[t % len(O_SZ)]
                if (ci - 1, t) in gat_marks:
                    W("sync", "vec", gat_marks[(ci - 1, t)])
                sz_tok[t] = kb.dma(
                    "sync", lambda e, ci=ci, t=t, szslot=szslot:
                    e.dma_start(out=kb.bfv(szslot, TT),
                                in_=stile(kb.s_sz, ci, t)), "szl")
            bs_tok = {}
            for sp in (NBREP_RES, NBREP_RES + 1):
                W("sync", "vec", kb.cnt["vec"])
                bs_tok[sp] = kb.dma(
                    "sync", lambda e, sp=sp, bsrc=O_BS[sp % 2]: e.dma_start(
                        out=kb.bfv(bsrc, S),
                        in_=kb.s_brep[sp - NBREP_RES, :].rearrange(
                            "(p s) -> p s", p=128)), "bs")
            for s in range(DS):
                g = ci * DS + s
                p = g // 2
                dslot = FA if s % 2 == 0 else FA2
                hslot = HS[g % 4]
                W("scalar", "vec", delta_mark)
                if g >= 2:
                    W("scalar", "vec", scan_marks[g - 2])
                da = OP("scalar", lambda e, ci=ci, s=s, dslot=dslot:
                        e.activation(
                            kb.f32v(dslot, S), kb.f32v(FL, S), AF.Exp,
                            scale=c["negA"][:, ci * DS + s:ci * DS + s + 1]),
                        "scl")
                if s < NBREP_RES:
                    bsrc = brep_off(s)
                else:
                    bsrc = O_BS[s % 2]
                    kb.wtok("vector", bs_tok[s])
                OP("vector", lambda e, bsrc=bsrc: e.tensor_mul(
                    kb.bfv(O_DBU, S), kb.bfv(O_DU, S), kb.bfv(bsrc, S)))
                DR("vector")
                W("vector", "scl", da)
                if g % 2 == 0 and p >= 2:
                    kb.wtok("vector", add_marks[p - 2])
                scan_marks[g] = OP(
                    "vector", lambda e, dslot=dslot, hslot=hslot:
                    e.tensor_tensor_scan(
                        kb.f32v(hslot, S), kb.f32v(dslot, S),
                        kb.bfv(O_DBU, S), 0.0, ALU.mult, ALU.add), "vec")
                DR("vector")
                if g % 2 == 1:
                    fslot = HS[(g - 1) % 4]
                    scan_marks[g] = OP(
                        "vector", lambda e, fslot=fslot, hslot=hslot:
                        e.tensor_add(
                            kb.f32v(fslot, S), kb.f32v(fslot, S),
                            kb.f32v(hslot, S)), "vec")
                    DR("vector")
                    W("gpsimd", "vec", scan_marks[g])
                    if p >= 1:
                        kb.wtok("gpsimd", add_marks[p - 1])
                    add_marks[p] = kb.dma(
                        "gpsimd", lambda e, fslot=fslot: e.dma_start(
                            out=kb.f32v(FY, S), in_=kb.f32v(fslot, S),
                            accum_op=ALU.add), "yacc")
                if NBREP_RES <= s <= DS - 3:
                    W("sync", "vec", scan_marks[g])
                    bs_tok[s + 2] = kb.dma(
                        "sync", lambda e, s=s, bsrc=O_BS[s % 2]: e.dma_start(
                            out=kb.bfv(bsrc, S),
                            in_=kb.s_brep[s + 2 - NBREP_RES, :].rearrange(
                                "(p s) -> p s", p=128)), "bs")
            kb.wtok("vector", add_marks[(ci * DS + DS - 1) // 2])
            for t in range(NT):
                szslot = O_SZ[t % len(O_SZ)]
                kb.wtok("vector", sz_tok[t])
                gat_marks[(ci, t)] = OP(
                    "vector", lambda e, t=t, szslot=szslot: e.tensor_mul(
                        kb.bfv(O_DU + t * TT, TT), kb.f32v(FY + t * TT, TT),
                        kb.bfv(szslot, TT)), "vec")
                DR("vector")
            W("sync", "vec", kb.cnt["vec"])
            kb.dma("sync", lambda e, ci=ci: e.dma_start(
                out=kb.s_yg[ci, :].rearrange("(p s) -> p s", p=128),
                in_=kb.bfv(O_DU, S)), "sty")
        kb.barrier(["vector", "sync", "scalar"],
                   ["sync", "tensor", "vector"])

        wo_pe = {}
        wo_evac = {}
        for t in range(NT):
            for k in range(CI):
                wslot = O_WS[k % 2]
                ygslot = O_T0 if k % 2 == 0 else O_T1
                prev = (t, k - 2) if k >= 2 else (t - 1, CI - 2 + k)
                if prev in wo_pe:
                    W("sync", "pe", wo_pe[prev])
                tokw = kb.dma("sync", lambda e, k=k, wslot=wslot:
                              e.dma_start(
                                  out=kb.bfv(wslot, 1024),
                                  in_=kb.s_woutb[k * 128:(k + 1) * 128, :]),
                              "wg")
                toky = kb.dma("sync", lambda e, k=k, t=t, ygslot=ygslot:
                              e.dma_start(out=kb.bfv(ygslot, TT),
                                          in_=stile(kb.s_yg, k, t)), "ygl")
                kb.wtok("tensor", tokw)
                kb.wtok("tensor", toky)
                if k == 0 and (t - 1) in wo_evac:
                    W("tensor", "vec", wo_evac[t - 1])
                for d in range(CD):
                    bank = ps_main if d < 4 else ps_aux
                    OP("tensor", lambda e, d=d, k=k, wslot=wslot,
                       ygslot=ygslot, bank=bank: e.matmul(
                           bank[:, (d % 4) * TT:(d % 4 + 1) * TT],
                           kb.bfv(wslot + d * 128, 128),
                           kb.bfv(ygslot, TT),
                           start=(k == 0), stop=(k == CI - 1),
                           skip_group_check=True),
                       "pe" if d == CD - 1 else None)
                wo_pe[(t, k)] = kb.cnt["pe"]
            W("vector", "pe", kb.cnt["pe"])
            for d in range(CD):
                xslot = O_T2 if d % 2 == 0 else O_T3
                W("sync", "vec", kb.cnt["vec"])
                tok = kb.dma("sync", lambda e, d=d, t=t, xslot=xslot:
                             e.dma_start(out=kb.bfv(xslot, TT),
                                         in_=stile(kb.s_x1, d, t)), "x1l")
                kb.wtok("vector", tok)
                bank = ps_main if d < 4 else ps_aux
                OP("vector", lambda e, d=d, t=t, xslot=xslot, bank=bank:
                   e.scalar_tensor_tensor(
                       kb.bfv(O_X + d * S + t * TT, TT),
                       bank[:, (d % 4) * TT:(d % 4 + 1) * TT],
                       c["bout"][:, d:d + 1],
                       kb.bfv(xslot, TT), ALU.add, ALU.add), "vec")
                DR("vector")
            wo_evac[t] = kb.cnt["vec"]
        kb.barrier(["vector", "sync", "gpsimd"],
                   ["scalar", "tensor", "vector", "sync"])

    layernorm(O_X, O_LN)
    exchange("clipx", NT * CD)
    quantize(O_LN, O_XQ, CD)
    _xq1 = kb.cnt["vec"]
    p0c_weights()
    kb.close_p0_psum()
    ps_aux = kb.ps_aux
    ffn(final=False, xq_ready=_xq1)
    kb.barrier(["vector", "scalar", "sync", "tensor"],
               ["vector", "scalar", "sync", "tensor"])
    mamba()
    ffn(final=True)
    kb.swait_all_dma("sync")


_BUILD_CACHE = {}


def _get_build(S, n_cores):
    key = (S, n_cores)
    if key not in _BUILD_CACHE:
        _BUILD_CACHE[key] = build(S, n_cores)
    return _BUILD_CACHE[key]


def _pack_rows(v, nch):
    return np.ascontiguousarray(np.asarray(v, np.float32).reshape(nch, 128).T)


def make_inputs(xc, w, S):
    bf = ml_dtypes.bfloat16
    Alog = np.asarray(w["L0_attn_A_log"], np.float32)
    convw = np.asarray(w["L0_attn_conv_w"], np.float32)[:, 0, :]
    one = {}
    one["x"] = np.ascontiguousarray(np.asarray(xc, np.float32).T)
    one["W_in"] = np.ascontiguousarray(np.asarray(w["L0_attn_W_in"],
                                                  np.float32))
    one["W_dt"] = np.ascontiguousarray(np.asarray(w["L0_attn_W_dt"],
                                                   np.float32))
    one["W_out"] = np.ascontiguousarray(np.asarray(w["L0_attn_W_out"],
                                                    np.float32))
    Wx = np.asarray(w["L0_attn_W_x"], np.float32)
    one["WxB"] = np.ascontiguousarray(
        Wx[:, DS:].reshape(CI, 128, DS).transpose(1, 0, 2)
        .reshape(128, CI * DS))
    one["bxB"] = np.ascontiguousarray(
        np.asarray(w["L0_attn_b_x"], np.float32)[DS:].reshape(1, DS))
    one["w_gate"] = np.ascontiguousarray(np.asarray(w["L0_ffn_w_gate"],
                                                     np.float32))
    one["w_up"] = np.ascontiguousarray(np.asarray(w["L0_ffn_w_up"],
                                                   np.float32))
    one["w_down"] = np.ascontiguousarray(np.asarray(w["L0_ffn_w_down"],
                                                     np.float32))
    one["Alog"] = np.ascontiguousarray(
        Alog.reshape(CI, 128, DS).transpose(1, 0, 2).reshape(128, CI * DS))
    one["convw"] = np.ascontiguousarray(
        convw.reshape(CI, 128, DC).transpose(1, 0, 2).reshape(128, CI * DC))
    one["convb"] = _pack_rows(w["L0_attn_conv_b"], CI)
    one["Dvec"] = _pack_rows(w["L0_attn_D"], CI)
    bin_ = np.asarray(w["L0_attn_b_in"], np.float32)
    one["bxh"] = _pack_rows(bin_[:DI], CI)
    one["bz"] = _pack_rows(bin_[DI:], CI)
    one["bdt"] = _pack_rows(w["L0_attn_b_dt"], CI)
    one["bout"] = _pack_rows(w["L0_attn_b_out"], CD)
    one["clipx"] = np.asarray(
        [[float(np.asarray(w["L0_ffn_clip_x"]).reshape(-1)[0]), 0.0]],
        np.float32)
    one["clipd"] = np.asarray(
        [[float(np.asarray(w["L0_ffn_clip_down"]).reshape(-1)[0]), 0.0]],
        np.float32)
    one["eyeb"] = np.eye(128, dtype=bf)
    one["eyef"] = np.eye(128, dtype=np.float32)
    one["onesb"] = np.ones((128, 2), dtype=bf)
    one["ones1"] = np.ones((1, 128), np.float32)
    one["onesrow_b"] = np.ones((1, TT), dtype=bf)
    cv = np.zeros((128, 8), np.float32)
    cv[:, 0] = LN_EPS
    cv[:, 1] = 1.0
    cv[:, 2] = -1.0
    cv[:, 3] = -0.5
    one["cvals"] = cv
    oneh = np.zeros((DS, DS * 128), dtype=bf)
    for s in range(DS):
        oneh[s, s * 128:(s + 1) * 128] = 1
    one["oneh"] = oneh
    return one


def kernel(**inputs):
    x = np.asarray(inputs["L0_attn_x"], np.float32)
    S = x.shape[1]
    kb = _get_build(S, NCORE)
    in_maps = [make_inputs(x[cidx], inputs, S) for cidx in range(NCORE)]
    res = run_bass_kernel_spmd(kb.nc, in_maps, core_ids=list(range(NCORE)))
    outs = [np.asarray(r["out"]).T for r in res.results]
    return np.stack(outs, axis=0).astype(np.float32)


# version 11
# speedup vs baseline: 1.1496x; 1.0020x over previous
import numpy as np
import ml_dtypes
import concourse.bass as bass
import concourse.mybir as mybir
from concourse.bass_utils import run_bass_kernel_spmd
from contextlib import ExitStack

F32 = mybir.dt.float32
BF16 = mybir.dt.bfloat16
I32 = mybir.dt.int32
AF = mybir.ActivationFunctionType
ALU = mybir.AluOpType
AX = mybir.AxisListType

DM, DI, DS, DC, HF = 1024, 2048, 16, 4, 2816
CD, CI, CH = DM // 128, DI // 128, HF // 128
TT = 512
LN_EPS = 1e-5
QMAXF = 127.0
NCORE = 8
ENGS = ("sync", "vector", "scalar", "tensor", "gpsimd")


class KB:
    def __init__(self, S, n_cores):
        self.S = S
        self.NT = S // TT
        self.n_cores = n_cores
        self.nc = bass.Bass("TRN2", target_bir_lowering=False)
        self.es = ExitStack()
        self.q = {e: [] for e in ENGS}
        self._mk_dram()

    def _mk_dram(self):
        nc, S = self.nc, self.S
        self.di = {}

        def din(name, shape, dt=F32):
            self.di[name] = nc.dram_tensor(name, shape, dt,
                                           kind="ExternalInput").ap()

        din("x", [DM, S])
        din("W_in", [DM, 2 * DI])
        din("W_dt", [DI, DI])
        din("W_out", [DI, DM])
        din("WxB", [128, CI * DS])
        din("bxB", [1, DS])
        din("w_gate", [HF, DM])
        din("w_up", [HF, DM])
        din("w_down", [DM, HF])
        din("Alog", [128, CI * DS])
        din("convw", [128, CI * DC])
        din("convb", [128, CI])
        din("Dvec", [128, CI])
        din("bxh", [128, CI])
        din("bz", [128, CI])
        din("bdt", [128, CI])
        din("bout", [128, CD])
        din("clipx", [1, 2])
        din("clipd", [1, 2])
        din("cvals", [128, 8])
        din("eyeb", [128, 128], BF16)
        din("eyef", [128, 128], F32)
        din("onesb", [128, 2], BF16)
        din("ones1", [1, 128])
        din("onesrow_b", [1, TT], BF16)
        din("oneh", [DS, DS * 128], BF16)
        self.d_out = nc.dram_tensor("out", [DM, S], F32,
                                    kind="ExternalOutput").ap()

        def scr(name, shape, dt):
            return nc.dram_tensor(name, shape, dt, kind="Internal").ap()

        self.s_wdtb = scr("s_wdtb", [DI, DI], BF16)
        self.s_woutb = scr("s_woutb", [DI, DM], BF16)
        self.s_wgT = scr("s_wgT", [CH, CD * 128 * 128], BF16)
        self.s_wuT = scr("s_wuT", [CH, CD * 128 * 128], BF16)
        self.s_wdT = scr("s_wdT", [CH, CD * 128 * 128], BF16)
        self.s_h = scr("s_h", [CH, 128 * S], BF16)
        self.s_sz = scr("s_sz", [CI, 128 * S], BF16)
        self.s_yg = scr("s_yg", [CI, 128 * S], BF16)
        self.s_x1 = scr("s_x1", [CD, 128 * S], BF16)
        self.s_brep = scr("s_brep", [6, 128 * S], BF16)
        self.s_cci = scr("s_cci", [1, 2], F32)
        self.s_cco = scr("s_cco", [1, 2], F32)

    def mk_mem(self):
        nc, S, es = self.nc, self.S, self.es

        def sb(name, shape, dt=F32):
            return es.enter_context(nc.sbuf_tensor(name, shape, dt))

        self.bf = sb("bf_arena", [128, 32 * S], BF16)
        self.f32 = sb("f32_arena", [128, 8 * S], F32)
        self.int_t = sb("int_t", [128, S // 2], I32)
        self.c = {}
        for name, shape, dt in [
            ("negA", [128, CI * DS], F32), ("convw", [128, CI * DC], F32),
            ("convb", [128, CI], F32), ("D", [128, CI], F32),
            ("bxh", [128, CI], F32), ("bz", [128, CI], F32),
            ("bdt", [128, CI], F32), ("bout", [128, CD], F32),
            ("WxB", [128, CI * DS], BF16), ("bxB", [1, DS], BF16),
            ("eyeb", [128, 128], BF16), ("eyef", [128, 128], F32),
            ("onesb", [128, 2], BF16),
            ("ones1", [1, 128], F32), ("onesrow_b", [1, TT], BF16),
            ("oneh", [DS, DS * 128], BF16),
            ("clipx", [1, 2], F32), ("clipd", [1, 2], F32),
            ("cvals", [128, 8], F32),
            ("swg", [128, CH], F32), ("swu", [128, CH], F32),
            ("swd", [128, CD], F32),
            ("sxw_g", [128, CH], F32), ("sxw_u", [128, CH], F32),
            ("sxw_d", [128, CD], F32),
            ("amax", [128, 32], F32), ("r1", [128, 2], F32),
            ("sc", [1, 8], F32), ("pp", [128, 4], F32),
            ("pad8", [128, 8], BF16),
        ]:
            self.c[name] = sb("c_" + name, shape, dt)
        self.sems = {}
        for n in ["pe", "vec", "scl", "gps", "cc"]:
            self.sems[n] = es.enter_context(nc.semaphore(n))
        self.cnt = {n: 0 for n in self.sems}
        self.sempool = [es.enter_context(nc.semaphore(f"dmas{i}"))
                        for i in range(44)]
        self.sempool_next = 0
        self.streams = {}
        self.ps_main = es.enter_context(
            nc.psum_tensor("ps_main", [128, 4 * TT], F32))
        self._ps_tb_scope = ExitStack()
        self.ps_tb = self._ps_tb_scope.enter_context(
            nc.psum_tensor("ps_tb", [128, 1024], BF16))
        self.ps_aux = None
        for col, val in enumerate([LN_EPS, 1.0, -1.0, -0.5]):
            self.nc.const_aps.register(F32, val, self.c["cvals"][:, col:col + 1])

    def close_p0_psum(self):
        self._ps_tb_scope.close()
        self.ps_tb = None
        self.ps_aux = self.es.enter_context(
            self.nc.psum_tensor("ps_aux", [128, 4 * TT], F32))

    def op(self, eng, fn, inc=None, n=1):
        if inc is None:
            self.q[eng].append(fn)
            return None
        self.cnt[inc] += n
        sem = self.sems[inc]
        self.q[eng].append(
            lambda e, fn=fn, sem=sem, n=n: fn(e).then_inc(sem, n))
        return self.cnt[inc]

    def wait(self, eng, semname, val):
        if val and val > 0:
            sem = self.sems[semname]
            self.q[eng].append(
                lambda e, sem=sem, val=val: e.wait_ge(sem, val))

    def _stream(self, name):
        st = self.streams.get(name)
        if st is None:
            a = self.sempool[self.sempool_next]
            b = self.sempool[self.sempool_next + 1]
            self.sempool_next += 2
            st = {"sems": [a, b], "cnt": [0, 0], "par": 0}
            self.streams[name] = st
        return st

    def dma(self, eng, fn, stream):
        st = self._stream(stream)
        p = st["par"]
        sem = st["sems"][p]
        if st["cnt"][p] > 0:
            v = st["cnt"][p]
            self.q[eng].append(
                lambda e, sem=sem, v=v: e.wait_ge(sem, v))
        st["cnt"][p] += 16
        v = st["cnt"][p]
        self.q[eng].append(
            lambda e, fn=fn, sem=sem: fn(e).then_inc(sem, 16))
        st["par"] = 1 - p
        return (sem, v)

    def wtok(self, eng, tok):
        if tok is None:
            return
        sem, v = tok
        self.q[eng].append(lambda e, sem=sem, v=v: e.wait_ge(sem, v))

    def swait(self, eng, stream):
        st = self.streams.get(stream)
        if not st:
            return
        for p in (0, 1):
            if st["cnt"][p] > 0:
                sem, v = st["sems"][p], st["cnt"][p]
                self.q[eng].append(
                    lambda e, sem=sem, v=v: e.wait_ge(sem, v))

    def swait_all_dma(self, eng):
        for name in list(self.streams):
            self.swait(eng, name)

    def drain(self, eng):
        self.q[eng].append(lambda e: e.drain())

    def barrier(self, writers, waiters):
        snap = []
        sync_snap = []
        for w in writers:
            if w == "sync":
                for name, st in self.streams.items():
                    for p in (0, 1):
                        if st["cnt"][p] > 0:
                            sync_snap.append((st["sems"][p], st["cnt"][p]))
            else:
                s = {"vector": "vec", "scalar": "scl", "tensor": "pe",
                     "gpsimd": "gps"}[w]
                snap.append((w, s, self.cnt[s]))
        for eng in waiters:
            for w, s, v in snap:
                if eng == w:
                    self.drain(eng)
                else:
                    self.wait(eng, s, v)
            for sem, v in sync_snap:
                self.q[eng].append(
                    lambda e, sem=sem, v=v: e.wait_ge(sem, v))

    def bfv(self, off, cols):
        return self.bf[:, off:off + cols]

    def f32v(self, off, cols):
        return self.f32[:, off:off + cols]


def build(S, n_cores):
    kb = KB(S, n_cores)
    with kb.es:
        kb.mk_mem()
        _emit(kb)
        blk = kb.es.enter_context(kb.nc.Block())
        for eng in ENGS:
            ops = kb.q[eng]
            if ops:
                getattr(blk, eng)(lambda e, ops=ops: [f(e) for f in ops])
    return kb


def _emit(kb: KB):
    S, NT = kb.S, kb.NT
    c, di = kb.c, kb.di
    int_t = kb.int_t
    ps_main = kb.ps_main

    def U(k):
        return k * S

    O_X, O_LN, O_XQ = U(0), U(8), U(16)
    O_SG, O_SU, O_HFF = U(24), U(25), U(26)
    O_WS = [U(27), U(27) + 1024, U(28), U(28) + 1024]
    O_T0, O_T1 = U(29), U(29) + TT
    O_T2, O_T3 = U(29) + 2 * TT, U(29) + 3 * TT
    O_STG = U(30)
    O_AB = U(26)
    O_BLK = U(24)

    def xh_off(ci):
        return U(8 + ci) if ci < 8 else U(ci - 8)

    def xhp_off(ci):
        return U(ci) if ci < 8 else U(16 + ci - 8)

    def brep_off(s):
        return U(16 + s)

    NBREP_RES = 10
    O_BS = [U(26), U(27)]
    O_WDT = U(28)
    O_DU = U(29)
    O_SZ = [U(30) + t * TT for t in range(min(NT, 4))]
    O_DBU = U(31)
    O_BMAT = U(29)

    FA, FB, FH, FY, FL, FA2 = 0, S, 2 * S, 3 * S, 4 * S, 5 * S
    FH3, FH4 = 6 * S, 7 * S
    HS = [FH, FB, FH3, FH4]
    ROW_M, ROW_T, ROW_R = FA2, FA2 + TT, FA2 + 2 * TT

    W, OP, DR = kb.wait, kb.op, kb.drain

    def rowv(off):
        return kb.f32[0:1, off:off + TT]

    def bmatv(t0, cols):
        return kb.bf[0:DS, O_BMAT + t0:O_BMAT + t0 + cols]

    for name in ["cvals", "convw", "convb", "bxh", "bz", "bdt", "bout",
                 "eyeb", "eyef", "onesb", "ones1", "onesrow_b", "oneh",
                 "clipx", "clipd"]:
        kb.dma("sync", lambda e, n=name: e.dma_start(out=c[n][:], in_=di[n]),
               "cst")
    kb.dma("sync", lambda e: e.dma_start(out=c["D"][:], in_=di["Dvec"]),
           "cst")
    kb.dma("sync", lambda e: e.dma_start(out=c["negA"][:], in_=di["Alog"]),
           "cst")
    kb.dma("sync", lambda e: e.dma_start(out=kb.f32[:, 0:CI * DS],
                                         in_=di["WxB"]), "cst")
    kb.dma("sync", lambda e: e.dma_start(
        out=kb.f32[0:1, CI * DS:CI * DS + DS], in_=di["bxB"]), "cst")
    kb.swait("scalar", "cst")
    OP("scalar", lambda e: e.copy(c["WxB"][:], kb.f32[:, 0:CI * DS]))
    OP("scalar", lambda e: e.copy(c["bxB"][:],
                                  kb.f32[0:1, CI * DS:CI * DS + DS]), "scl")
    OP("scalar", lambda e: e.activation(c["negA"][:], c["negA"][:], AF.Exp))
    DR("scalar")
    OP("scalar", lambda e: e.mul(c["negA"][:], c["negA"][:], -1.0), "scl")
    kb.swait("vector", "cst")
    OP("vector", lambda e: e.memset(c["pad8"][:], 0.0), "vec")

    xcopy_marks = []
    W("sync", "scl", kb.cnt["scl"])
    for ch in range(CD):
        slot = FL if ch % 2 == 0 else FA
        if ch >= 2:
            W("sync", "scl", xcopy_marks[ch - 2])
        tok = kb.dma("sync", lambda e, ch=ch, slot=slot: e.dma_start(
            out=kb.f32v(slot, S), in_=di["x"][ch * 128:(ch + 1) * 128, :]),
            "xw")
        kb.wtok("scalar", tok)
        xcopy_marks.append(OP("scalar", lambda e, ch=ch, slot=slot: e.copy(
            kb.bfv(O_X + ch * S, S), kb.f32v(slot, S)), "scl"))

    qmarks = []
    wq_vmarks = {}

    def quant_transpose(wname, nch, kd, stile, dstT, toks,
                        scatter_rows=False, fold_dst=None):
        nkb = kd // 128
        for i in range(nch):
            slot = 0 if i % 2 == 0 else 2816
            if len(qmarks) >= 2:
                W("sync", "vec", qmarks[-2])
            tok = kb.dma("sync", lambda e, w=wname, i=i, slot=slot, kd=kd:
                         e.dma_start(out=kb.f32[:, slot:slot + kd],
                                     in_=di[w][i * 128:(i + 1) * 128, :]),
                         "xw")
            kb.wtok("vector", tok)
            OP("vector", lambda e, slot=slot, kd=kd: e.tensor_reduce(
                c["r1"][:, 0:1], kb.f32[:, slot:slot + kd], AX.X, ALU.max,
                apply_absolute_value=True))
            DR("vector")
            OP("vector", lambda e, st=stile, i=i: e.tensor_scalar(
                c[st][:, i:i + 1], c["r1"][:, 0:1], 1.0 / QMAXF, None,
                ALU.mult))
            OP("vector", lambda e: e.reciprocal(c["r1"][:, 1:2],
                                                c["r1"][:, 0:1]))
            DR("vector")
            if fold_dst is not None:
                OP("vector", lambda e, st=stile, i=i, fd=fold_dst:
                   e.tensor_scalar(c[fd][:, i:i + 1], c[st][:, i:i + 1],
                                   c["pp"][:, 0:1], 1.0,
                                   ALU.mult, ALU.mult))
                DR("vector")
            W("vector", "pe", kb.cnt["pe"])
            mlast = None
            for p0 in range(0, kd, S // 2):
                pn = min(S // 2, kd - p0)
                OP("vector", lambda e, slot=slot, p0=p0, pn=pn:
                   e.tensor_scalar(int_t[:, 0:pn],
                                   kb.f32[:, slot + p0:slot + p0 + pn],
                                   c["r1"][:, 1:2], QMAXF,
                                   ALU.mult, ALU.mult))
                DR("vector")
                mlast = OP("vector", lambda e, p0=p0, pn=pn: e.tensor_copy(
                    kb.bf[:, O_BLK + p0:O_BLK + p0 + pn], int_t[:, 0:pn]), "vec")
                DR("vector")
            qmarks.append(mlast)
            wq_vmarks[(wname, i)] = kb.cnt["vec"]
            kb.swait("scalar", "stw")
            for g0 in range(0, nkb, 8):
                gn = min(8, nkb - g0)
                W("tensor", "vec", kb.cnt["vec"])
                W("tensor", "scl", kb.cnt["scl"])
                for j in range(gn):
                    OP("tensor", lambda e, g0=g0, j=j, pstb=kb.ps_tb:
                       e.transpose(
                           pstb[:, j * 128:(j + 1) * 128],
                           kb.bf[:, O_BLK + (g0 + j) * 128:O_BLK + (g0 + j + 1) * 128],
                           c["eyeb"][:]), "pe" if j == gn - 1 else None)
                W("scalar", "pe", kb.cnt["pe"])
                for j in range(gn):
                    OP("scalar", lambda e, g0=g0, j=j, pstb=kb.ps_tb: e.copy(
                        kb.bfv(O_STG + (g0 + j) * 128, 128),
                        pstb[:, j * 128:(j + 1) * 128]),
                       "scl" if j == gn - 1 else None)
            W("sync", "scl", kb.cnt["scl"])
            if not scatter_rows:
                toks[i] = kb.dma(
                    "sync", lambda e, dstT=dstT, i=i, nkb=nkb: e.dma_start(
                        out=dstT[i, 0:nkb * 128 * 128].rearrange(
                            "(k p j) -> p k j", p=128, j=128),
                        in_=kb.bfv(O_STG, nkb * 128).rearrange(
                            "p (k j) -> p k j", j=128)), "stw")
            else:
                toks[i] = kb.dma(
                    "sync", lambda e, dstT=dstT, i=i, nkb=nkb: e.dma_start(
                        out=dstT[:, i * 128 * 128:(i + 1) * 128 * 128]
                        .rearrange("k (p j) -> p k j", j=128),
                        in_=kb.bfv(O_STG, nkb * 128).rearrange(
                            "p (k j) -> p k j", j=128)), "stw")

    pre_scl_mark = kb.cnt["scl"]
    wgT_tok, wuT_tok, wdT_tok = {}, {}, {}

    def p0c_weights():
        W("sync", "scl", pre_scl_mark)
        quant_transpose("w_gate", CH, DM, "swg", kb.s_wgT, wgT_tok,
                        fold_dst="sxw_g")
        quant_transpose("w_up", CH, DM, "swu", kb.s_wuT, wuT_tok,
                        fold_dst="sxw_u")
        quant_transpose("w_down", CD, HF, "swd", kb.s_wdT, wdT_tok,
                        scatter_rows=True)

        cast_marks = []
        for wname, dst, nch, kd in [("W_dt", kb.s_wdtb, CI, DI),
                                    ("W_out", kb.s_woutb, CI, DM)]:
            for i in range(nch):
                slot = 0 if i % 2 == 0 else 2816
                if len(cast_marks) >= 2:
                    W("sync", "scl", cast_marks[-2])
                tok = kb.dma("sync",
                             lambda e, w=wname, i=i, slot=slot, kd=kd:
                             e.dma_start(out=kb.f32[:, slot:slot + kd],
                                         in_=di[w][i * 128:(i + 1) * 128, :]),
                             "xw")
                kb.wtok("scalar", tok)
                kb.swait("scalar", "stw")
                cast_marks.append(OP(
                    "scalar", lambda e, slot=slot, kd=kd: e.copy(
                        kb.bfv(O_STG, kd), kb.f32[:, slot:slot + kd]), "scl"))
                W("sync", "scl", kb.cnt["scl"])
                kb.dma("sync", lambda e, dst=dst, i=i, kd=kd: e.dma_start(
                    out=dst[i * 128:(i + 1) * 128, :], in_=kb.bfv(O_STG, kd)),
                    "stw")


    def layernorm(src_o, dst_o):
        prev_stats_read = 0
        W("tensor", "scl", kb.cnt["scl"])
        for t in range(NT):
            tn = t * TT
            sq_marks = []
            pe_sq = []
            W("tensor", "vec", prev_stats_read)
            for ch in range(CD):
                sq = O_T0 if ch % 2 == 0 else O_T1
                if ch >= 2:
                    W("scalar", "pe", pe_sq[ch - 2])
                sq_marks.append(OP(
                    "scalar", lambda e, ch=ch, sq=sq, src_o=src_o, tn=tn:
                    e.activation(kb.bfv(sq, TT),
                                 kb.bfv(src_o + ch * S + tn, TT), AF.Square),
                    "scl"))
                OP("tensor", lambda e, ch=ch, src_o=src_o, tn=tn: e.matmul(
                    ps_main[0:1, 0:TT], c["onesb"][:, 0:1],
                    kb.bfv(src_o + ch * S + tn, TT),
                    start=(ch == 0), stop=(ch == CD - 1),
                    skip_group_check=True))
                W("tensor", "scl", sq_marks[ch])
                pe_sq.append(OP("tensor", lambda e, ch=ch, sq=sq: e.matmul(
                    ps_main[0:1, TT:2 * TT], c["onesb"][:, 0:1],
                    kb.bfv(sq, TT),
                    start=(ch == 0), stop=(ch == CD - 1),
                    skip_group_check=True), "pe"))
            W("vector", "pe", kb.cnt["pe"])
            OP("vector", lambda e: e.tensor_scalar(
                rowv(ROW_M), ps_main[0:1, 0:TT], 1.0 / DM, None, ALU.mult))
            OP("vector", lambda e: e.tensor_scalar(
                rowv(ROW_T), ps_main[0:1, TT:2 * TT], 1.0 / DM, None,
                ALU.mult))
            DR("vector")
            OP("vector", lambda e: e.tensor_mul(rowv(ROW_R), rowv(ROW_M),
                                                rowv(ROW_M)))
            DR("vector")
            OP("vector", lambda e: e.tensor_sub(rowv(ROW_T), rowv(ROW_T),
                                                rowv(ROW_R)), "vec")
            stats_read = kb.cnt["vec"]
            W("scalar", "vec", stats_read)
            OP("scalar", lambda e: e.activation(rowv(ROW_R), rowv(ROW_T),
                                                AF.Ln, bias=LN_EPS))
            DR("scalar")
            OP("scalar", lambda e: e.activation(rowv(ROW_R), rowv(ROW_R),
                                                AF.Exp, scale=-0.5), "scl")
            W("vector", "scl", kb.cnt["scl"])
            OP("vector", lambda e: e.tensor_mul(rowv(ROW_M), rowv(ROW_M),
                                                rowv(ROW_R)), "vec")
            W("tensor", "vec", kb.cnt["vec"])
            OP("tensor", lambda e: e.matmul(
                ps_main[:, 2 * TT:3 * TT], c["ones1"][:],
                kb.f32[0:1, ROW_R:ROW_R + TT],
                start=True, stop=True))
            OP("tensor", lambda e: e.matmul(
                ps_main[:, 3 * TT:4 * TT], c["ones1"][:],
                kb.f32[0:1, ROW_M:ROW_M + TT],
                start=True, stop=True), "pe")
            W("scalar", "pe", kb.cnt["pe"])
            OP("scalar", lambda e: e.copy(kb.bfv(O_AB, TT),
                                          ps_main[:, 2 * TT:3 * TT]))
            OP("scalar", lambda e: e.copy(kb.bfv(O_AB + TT, TT),
                                          ps_main[:, 3 * TT:4 * TT]), "scl")
            W("vector", "scl", kb.cnt["scl"])
            for ch in range(CD):
                d = dst_o + ch * S + tn
                OP("vector", lambda e, d=d, ch=ch, src_o=src_o, tn=tn:
                   e.tensor_mul(kb.bfv(d, TT),
                                kb.bfv(src_o + ch * S + tn, TT),
                                kb.bfv(O_AB, TT)))
                DR("vector")
                OP("vector", lambda e, d=d: e.tensor_sub(
                    kb.bfv(d, TT), kb.bfv(d, TT), kb.bfv(O_AB + TT, TT)))
                DR("vector")
                OP("vector", lambda e, d=d, ch=ch, t=t:
                   e.tensor_reduce(c["amax"][:, t * CD + ch:t * CD + ch + 1],
                                   kb.bfv(d, TT), AX.X, ALU.max,
                                   apply_absolute_value=True),
                   "vec" if ch == CD - 1 else None)
            prev_stats_read = stats_read

    def exchange(clip_name, amax_cols):
        DR("vector")
        OP("vector", lambda e, n=amax_cols: e.tensor_reduce(
            c["r1"][:, 0:1], c["amax"][:, 0:n], AX.X, ALU.max), "vec")
        W("tensor", "vec", kb.cnt["vec"])
        OP("tensor", lambda e: e.transpose(
            ps_main[0:1, 0:128], c["r1"][:, 0:1], c["eyef"][:]), "pe")
        W("vector", "pe", kb.cnt["pe"])
        OP("vector", lambda e: e.reduce_max(
            c["sc"][0:1, 0:1], ps_main[0:1, 0:128], AX.X), "vec")
        W("gpsimd", "vec", kb.cnt["vec"])
        tok = kb.dma("gpsimd", lambda e: e.dma_start(
            out=kb.s_cci[0:1, 0:1], in_=c["sc"][0:1, 0:1]), "stc")
        kb.wtok("gpsimd", tok)
        OP("gpsimd", lambda e: e.collective_compute(
            "AllReduce", ALU.max,
            replica_groups=[list(range(kb.n_cores))],
            ins=[kb.s_cci[0:1, 0:1]], outs=[kb.s_cco[0:1, 0:1]]), "cc")
        W("gpsimd", "cc", kb.cnt["cc"])
        tok = kb.dma("gpsimd", lambda e: e.dma_start(
            out=c["sc"][0:1, 1:2], in_=kb.s_cco[0:1, 0:1]), "ccl")
        kb.wtok("scalar", tok)
        OP("scalar", lambda e, cn=clip_name: e.activation(
            c["sc"][0:1, 2:3], c[cn][0:1, 0:1], AF.Exp, scale=-1.0), "scl")
        W("vector", "scl", kb.cnt["scl"])
        OP("vector", lambda e: e.tensor_scalar(
            c["sc"][0:1, 3:4], c["sc"][0:1, 2:3], 1.0, None, ALU.add))
        DR("vector")
        OP("vector", lambda e: e.reciprocal(c["sc"][0:1, 2:3],
                                            c["sc"][0:1, 3:4]))
        DR("vector")
        OP("vector", lambda e: e.tensor_mul(c["sc"][0:1, 3:4],
                                            c["sc"][0:1, 2:3],
                                            c["sc"][0:1, 1:2]))
        DR("vector")
        OP("vector", lambda e: e.tensor_scalar(
            c["sc"][0:1, 4:5], c["sc"][0:1, 3:4], 1.0 / QMAXF, None,
            ALU.mult))
        DR("vector")
        OP("vector", lambda e: e.reciprocal(c["sc"][0:1, 5:6],
                                            c["sc"][0:1, 4:5]), "vec")
        W("tensor", "vec", kb.cnt["vec"])
        OP("tensor", lambda e: e.matmul(
            ps_main[:, 0:2], c["ones1"][:], c["sc"][0:1, 4:6],
            start=True, stop=True), "pe")
        W("vector", "pe", kb.cnt["pe"])
        OP("vector", lambda e: e.tensor_copy(c["pp"][:, 0:2],
                                             ps_main[:, 0:2]), "vec")
        DR("vector")

    def quantize(src_o, dst_o, nch):
        H = S // 2
        for ch in range(nch):
            for h0 in (0, H):
                OP("vector", lambda e, ch=ch, src_o=src_o, h0=h0:
                   e.tensor_scalar(
                       int_t[:, 0:H], kb.bfv(src_o + ch * S + h0, H),
                       c["pp"][:, 1:2], None, ALU.mult))
                DR("vector")
                OP("vector", lambda e, ch=ch, dst_o=dst_o, h0=h0:
                   e.tensor_scalar(
                       kb.bfv(dst_o + ch * S + h0, H), int_t[:, 0:H],
                       -128.0, 127.0, ALU.max, ALU.min), "vec")
                DR("vector")

    def scalevec(dst, srctile, ncols, extra):
        OP("vector", lambda e, dst=dst, s=srctile, n=ncols, x=extra:
           e.tensor_scalar(c[dst][:, 0:n], c[s][:, 0:n], c["pp"][:, 0:1],
                           x, ALU.mult, ALU.mult), "vec")
        DR("vector")

    def stile(scratch, k, t):
        return scratch[k, :].rearrange("(p s) -> p s", p=128)[
            :, t * TT:(t + 1) * TT]

    def ffn(final, xq_ready=None):
        folded = xq_ready is not None
        if not folded:
            layernorm(O_X, O_LN)
            exchange("clipx", NT * CD)
            scalevec("sxw_g", "swg", CH, 1.0)
            scalevec("sxw_u", "swu", CH, 1.0)
            sv_ready = kb.cnt["vec"]
            quantize(O_LN, O_XQ, CD)
            xq_ready = kb.cnt["vec"]

        pe_w_marks = []
        gate_evac = {}
        up_evac = {t: 0 for t in range(NT)}
        h_marks = {}
        for i in range(CH):
            wg_s, wu_s = O_WS[i % 2], O_WS[2 + i % 2]
            if i >= 2:
                W("sync", "pe", pe_w_marks[i - 2])
            kb.wtok("sync", wgT_tok.get(i))
            kb.wtok("sync", wuT_tok.get(i))
            tokg = kb.dma("sync", lambda e, i=i, wg_s=wg_s: e.dma_start(
                out=kb.bfv(wg_s, 1024).rearrange("p (k j) -> p k j", j=128),
                in_=kb.s_wgT[i, :].rearrange("(k p j) -> p k j",
                                             p=128, j=128)), "wg")
            toku = kb.dma("sync", lambda e, i=i, wu_s=wu_s: e.dma_start(
                out=kb.bfv(wu_s, 1024).rearrange("p (k j) -> p k j", j=128),
                in_=kb.s_wuT[i, :].rearrange("(k p j) -> p k j",
                                             p=128, j=128)), "wu")
            kb.wtok("tensor", tokg)
            kb.wtok("tensor", toku)
            if i == 0:
                W("tensor", "vec", xq_ready)
            if i > 0:
                for t in range(NT):
                    W("tensor", "scl", gate_evac[t])
            for k in range(CD):
                for t in range(NT):
                    OP("tensor", lambda e, t=t, k=k, wg_s=wg_s: e.matmul(
                        ps_main[:, t * TT:(t + 1) * TT],
                        kb.bfv(wg_s + k * 128, 128),
                        kb.bfv(O_XQ + k * S + t * TT, TT),
                        start=(k == 0), stop=(k == CD - 1),
                        skip_group_check=True),
                       "pe" if (k == CD - 1) else None)
            gmm = kb.cnt["pe"]
            W("scalar", "pe", gmm)
            if i == 0:
                W("scalar", "vec",
                  wq_vmarks[("w_gate", i)] if folded else sv_ready)
            elif folded:
                W("scalar", "vec", wq_vmarks[("w_gate", i)])
            if i > 0:
                W("scalar", "vec", h_marks[i - 1])
            for t in range(NT):
                gate_evac[t] = OP(
                    "scalar", lambda e, t=t, i=i: e.activation(
                        kb.bfv(O_SG + t * TT, TT),
                        ps_main[:, t * TT:(t + 1) * TT], AF.Silu,
                        scale=c["sxw_g"][:, i:i + 1]), "scl")
            if i > 0:
                for t in range(NT):
                    W("tensor", "scl", up_evac[t])
            for k in range(CD):
                for t in range(NT):
                    OP("tensor", lambda e, t=t, k=k, wu_s=wu_s: e.matmul(
                        ps_aux[:, t * TT:(t + 1) * TT],
                        kb.bfv(wu_s + k * 128, 128),
                        kb.bfv(O_XQ + k * S + t * TT, TT),
                        start=(k == 0), stop=(k == CD - 1),
                        skip_group_check=True),
                       "pe" if (k == CD - 1) else None)
            pe_w_marks.append(kb.cnt["pe"])
            W("scalar", "pe", kb.cnt["pe"])
            if folded:
                W("scalar", "vec", wq_vmarks[("w_up", i)])
            for t in range(NT):
                up_evac[t] = OP(
                    "scalar", lambda e, t=t, i=i: e.activation(
                        kb.bfv(O_SU + t * TT, TT),
                        ps_aux[:, t * TT:(t + 1) * TT], AF.Identity,
                        scale=c["sxw_u"][:, i:i + 1]), "scl")
            W("vector", "scl", kb.cnt["scl"])
            if i > 0:
                kb.swait("vector", "sth")
            OP("vector", lambda e: e.tensor_mul(
                kb.bfv(O_HFF, S), kb.bfv(O_SG, S), kb.bfv(O_SU, S)))
            DR("vector")
            h_marks[i] = OP("vector", lambda e, i=i: e.tensor_reduce(
                c["amax"][:, i:i + 1], kb.bfv(O_HFF, S), AX.X, ALU.max,
                apply_absolute_value=True), "vec")
            W("sync", "vec", h_marks[i])
            kb.dma("sync", lambda e, i=i: e.dma_start(
                out=kb.s_h[i, :].rearrange("(p s) -> p s", p=128),
                in_=kb.bfv(O_HFF, S)), "sth")

        exchange("clipd", CH)
        scalevec("sxw_d", "swd", CD, 0.5)
        kb.swait("sync", "sth")
        kb.swait("sync", "stw")

        dn_pe = {}
        hq_vec = {}
        x_evac = {}
        W("tensor", "scl", kb.cnt["scl"])
        W("tensor", "vec", kb.cnt["vec"])
        for t in range(NT):
            for k in range(CH):
                wslot = O_WS[k % 2]
                hslot = O_T0 if k % 2 == 0 else O_T1
                qslot = O_T2 if k % 2 == 0 else O_T3
                prev = (t, k - 2) if k >= 2 else (t - 1, CH - 2 + k)
                if prev in dn_pe:
                    W("sync", "pe", dn_pe[prev])
                if prev in hq_vec:
                    W("sync", "vec", hq_vec[prev])
                tokw = kb.dma("sync", lambda e, k=k, wslot=wslot:
                              e.dma_start(
                                  out=kb.bfv(wslot, 1024).rearrange(
                                      "p (d j) -> p d j", j=128),
                                  in_=kb.s_wdT[k, :].rearrange(
                                      "(d p j) -> p d j", p=128, j=128)),
                              "wg")
                tokh = kb.dma("sync", lambda e, k=k, t=t, hslot=hslot:
                              e.dma_start(out=kb.bfv(hslot, TT),
                                          in_=stile(kb.s_h, k, t)), "hl")
                kb.wtok("vector", tokh)
                OP("vector", lambda e, hslot=hslot: e.tensor_scalar(
                    int_t[:, 0:TT], kb.bfv(hslot, TT), c["pp"][:, 1:2],
                    None, ALU.mult))
                DR("vector")
                hq_vec[(t, k)] = OP(
                    "vector", lambda e, qslot=qslot: e.tensor_scalar(
                        kb.bfv(qslot, TT), int_t[:, 0:TT], -128.0, 127.0,
                        ALU.max, ALU.min), "vec")
                DR("vector")
                kb.wtok("tensor", tokw)
                W("tensor", "vec", hq_vec[(t, k)])
                if k == 0 and (t - 1) in x_evac:
                    W("tensor", "vec", x_evac[t - 1])
                for d in range(CD):
                    bank = ps_main if d < 4 else ps_aux
                    OP("tensor", lambda e, d=d, k=k, wslot=wslot,
                       qslot=qslot, bank=bank: e.matmul(
                           bank[:, (d % 4) * TT:(d % 4 + 1) * TT],
                           kb.bfv(wslot + d * 128, 128), kb.bfv(qslot, TT),
                           start=(k == 0), stop=(k == CH - 1),
                           skip_group_check=True),
                       "pe" if d == CD - 1 else None)
                dn_pe[(t, k)] = kb.cnt["pe"]
            W("vector", "pe", kb.cnt["pe"])
            for d in range(CD):
                bank = ps_main if d < 4 else ps_aux
                if not final:
                    OP("vector", lambda e, d=d, t=t, bank=bank:
                       e.scalar_tensor_tensor(
                           kb.bfv(O_X + d * S + t * TT, TT),
                           bank[:, (d % 4) * TT:(d % 4 + 1) * TT],
                           c["sxw_d"][:, d:d + 1],
                           kb.bfv(O_X + d * S + t * TT, TT),
                           ALU.mult, ALU.add), "vec")
                    DR("vector")
                else:
                    pk = (d - 2) % 8
                    if (t > 0 or d >= 2) and pk in out_st:
                        kb.wtok("vector", out_st[pk])
                    OP("vector", lambda e, d=d, t=t, bank=bank:
                       e.scalar_tensor_tensor(
                           kb.f32v(FL + (d % 2) * TT, TT),
                           bank[:, (d % 4) * TT:(d % 4 + 1) * TT],
                           c["sxw_d"][:, d:d + 1],
                           kb.bfv(O_X + d * S + t * TT, TT),
                           ALU.mult, ALU.add), "vec")
                    DR("vector")
                    W("sync", "vec", kb.cnt["vec"])
                    out_st[d] = kb.dma(
                        "sync", lambda e, d=d, t=t: e.dma_start(
                            out=kb.d_out[d * 128:(d + 1) * 128,
                                         t * TT:(t + 1) * TT],
                            in_=kb.f32v(FL + (d % 2) * TT, TT)), "out")
            x_evac[t] = kb.cnt["vec"]

    out_st = {}

    def mamba():
        layernorm(O_X, O_LN)
        W("sync", "vec", kb.cnt["vec"])
        for d in range(CD):
            kb.dma("sync", lambda e, d=d: e.dma_start(
                out=kb.s_x1[d, :].rearrange("(p s) -> p s", p=128),
                in_=kb.bfv(O_X + d * S, S)), "stx")

        win_cast = []
        win_pe = {}
        win_evac = {}
        last_sz_st = [0]
        kb.swait("scalar", "stx")
        for o in range(2 * CI):
            fslot = 0 if o % 2 == 0 else 2816
            wslot = O_WS[o % 2]
            if o >= 2:
                W("sync", "scl", win_cast[o - 2])
            tok = kb.dma("sync", lambda e, o=o, fslot=fslot: e.dma_start(
                out=kb.f32[:, fslot:fslot + 1024].rearrange(
                    "p (k j) -> p k j", j=128),
                in_=di["W_in"][:, o * 128:(o + 1) * 128].rearrange(
                    "(k p) j -> p k j", p=128)), "xw")
            kb.wtok("scalar", tok)
            if o >= 2:
                W("scalar", "pe", win_pe[o - 2])
            win_cast.append(OP(
                "scalar", lambda e, fslot=fslot, wslot=wslot: e.copy(
                    kb.bfv(wslot, 1024), kb.f32[:, fslot:fslot + 1024]),
                "scl"))
            W("tensor", "scl", kb.cnt["scl"])
            for t in range(NT):
                if (o - 2, t) in win_evac:
                    W("tensor", "scl", win_evac[(o - 2, t)])
            for k in range(CD):
                for t in range(NT):
                    OP("tensor", lambda e, t=t, k=k, wslot=wslot: e.matmul(
                        ps_main[:, t * TT:(t + 1) * TT],
                        kb.bfv(wslot + k * 128, 128),
                        kb.bfv(O_LN + k * S + t * TT, TT),
                        start=(k == 0), stop=(k == CD - 1),
                        skip_group_check=True),
                       "pe" if (k == CD - 1) else None)
            win_pe[o] = kb.cnt["pe"]
            for t in range(NT):
                W("scalar", "pe", kb.cnt["pe"])
                if o < CI:
                    win_evac[(o, t)] = OP(
                        "scalar", lambda e, o=o, t=t: e.activation(
                            kb.bfv(xhp_off(o) + t * TT, TT),
                            ps_main[:, t * TT:(t + 1) * TT], AF.Identity,
                            bias=c["bxh"][:, o:o + 1]), "scl")
                else:
                    if t == 0:
                        kb.swait("scalar", "stz")
                    win_evac[(o, t)] = OP(
                        "scalar", lambda e, o=o, t=t: e.activation(
                            kb.bfv(O_STG + t * TT, TT),
                            ps_main[:, t * TT:(t + 1) * TT], AF.Silu,
                            bias=c["bz"][:, o - CI:o - CI + 1]), "scl")
            if o >= CI:
                W("sync", "scl", kb.cnt["scl"])
                kb.dma("sync", lambda e, o=o: e.dma_start(
                    out=kb.s_sz[o - CI, :].rearrange("(p s) -> p s", p=128),
                    in_=kb.bfv(O_STG, S)), "stz")

        kb.barrier(["scalar", "sync"], ["vector"])
        conv_scl = 0
        for ci in range(CI):
            xp = xhp_off(ci)
            xh = xh_off(ci)
            if conv_scl:
                W("vector", "scl", conv_scl)
            OP("vector", lambda e, xp=xp: e.tensor_copy(
                c["pad8"][:, 4:8], kb.bfv(xp, 4)))
            DR("vector")
            OP("vector", lambda e, ci=ci: e.tensor_scalar(
                kb.f32v(FA, 4), c["pad8"][:, 4:8],
                c["convw"][:, ci * 4 + 3:ci * 4 + 4], None, ALU.mult))
            OP("vector", lambda e, xp=xp, ci=ci: e.tensor_scalar(
                kb.f32v(FA + 4, S - 4), kb.bfv(xp + 4, S - 4),
                c["convw"][:, ci * 4 + 3:ci * 4 + 4], None, ALU.mult))
            for j in range(3):
                DR("vector")
                OP("vector", lambda e, j=j, ci=ci: e.scalar_tensor_tensor(
                    kb.f32v(FA, 4), c["pad8"][:, 1 + j:5 + j],
                    c["convw"][:, ci * 4 + j:ci * 4 + j + 1],
                    kb.f32v(FA, 4), ALU.mult, ALU.add))
                isl = OP("vector", lambda e, j=j, ci=ci, xp=xp:
                         e.scalar_tensor_tensor(
                             kb.f32v(FA + 4, S - 4),
                             kb.bfv(xp + 1 + j, S - 4),
                             c["convw"][:, ci * 4 + j:ci * 4 + j + 1],
                             kb.f32v(FA + 4, S - 4), ALU.mult, ALU.add),
                         "vec" if j == 2 else None)
            W("scalar", "vec", kb.cnt["vec"])
            conv_scl = OP("scalar", lambda e, xh=xh, ci=ci: e.activation(
                kb.bfv(xh, S), kb.f32v(FA, S), AF.Silu,
                bias=c["convb"][:, ci:ci + 1]), "scl")

        kb.barrier(["scalar"], ["tensor", "vector"])
        bm_evac = {}
        for t in range(NT):
            if (t - 1) in bm_evac:
                W("tensor", "scl", bm_evac[t - 1])
            for k in range(CI):
                OP("tensor", lambda e, t=t, k=k: e.matmul(
                    ps_aux[0:DS, 0:TT], c["WxB"][:, k * DS:(k + 1) * DS],
                    kb.bfv(xh_off(k) + t * TT, TT),
                    start=(k == 0), stop=False, skip_group_check=True))
            OP("tensor", lambda e: e.matmul(
                ps_aux[0:DS, 0:TT], c["bxB"][0:1, :],
                c["onesrow_b"][0:1, :], start=False, stop=True,
                skip_group_check=True), "pe")
            W("scalar", "pe", kb.cnt["pe"])
            bm_evac[t] = OP("scalar", lambda e, t=t: e.copy(
                bmatv(t * TT, TT), ps_aux[0:DS, 0:TT]), "scl")
        br_evac = {}
        br_bank_user = {}
        for s in range(DS):
            for t in range(NT):
                bank = TT + (t % 2) * TT
                if bank in br_bank_user:
                    W("tensor", "scl", br_bank_user[bank])
                W("tensor", "scl", bm_evac[t])
                OP("tensor", lambda e, s=s, t=t, bank=bank: e.matmul(
                    ps_aux[:, bank:bank + TT],
                    c["oneh"][:, s * 128:(s + 1) * 128],
                    bmatv(t * TT, TT), start=True, stop=True,
                    skip_group_check=True), "pe")
                W("scalar", "pe", kb.cnt["pe"])
                if s < NBREP_RES:
                    br_evac[(s, t)] = OP(
                        "scalar", lambda e, s=s, t=t, bank=bank: e.copy(
                            kb.bfv(brep_off(s) + t * TT, TT),
                            ps_aux[:, bank:bank + TT]), "scl")
                else:
                    if s > NBREP_RES or t > 0:
                        kb.swait("scalar", "stx")
                    br_evac[(s, t)] = OP(
                        "scalar", lambda e, s=s, t=t, bank=bank: e.copy(
                            kb.bfv(O_STG + t * TT, TT),
                            ps_aux[:, bank:bank + TT]), "scl")
                br_bank_user[bank] = br_evac[(s, t)]
            if s >= NBREP_RES:
                W("sync", "scl", kb.cnt["scl"])
                kb.dma("sync", lambda e, s=s: e.dma_start(
                    out=kb.s_brep[s - NBREP_RES, :].rearrange(
                        "(p s) -> p s", p=128),
                    in_=kb.bfv(O_STG, S)), "stx")
        kb.barrier(["scalar", "sync", "vector"],
                   ["vector", "tensor", "sync"])

        wdt_pe = {}
        sp_evac = {}
        scan_marks = {}
        add_marks = {}
        gat_marks = {}
        for ci in range(CI):
            if (ci - 1) in wdt_pe:
                W("sync", "pe", wdt_pe[ci - 1])
            tok = kb.dma("sync", lambda e, ci=ci: e.dma_start(
                out=kb.bfv(O_WDT, 2048).rearrange("p (k j) -> p k j", j=128),
                in_=kb.s_wdtb[:, ci * 128:(ci + 1) * 128].rearrange(
                    "(k p) j -> p k j", p=128)), "xw")
            kb.wtok("tensor", tok)
            if (ci - 1) in sp_evac:
                W("tensor", "scl", sp_evac[ci - 1])
            for k in range(CI):
                for t in range(NT):
                    OP("tensor", lambda e, t=t, k=k: e.matmul(
                        ps_main[:, t * TT:(t + 1) * TT],
                        kb.bfv(O_WDT + k * 128, 128),
                        kb.bfv(xh_off(k) + t * TT, TT),
                        start=(k == 0), stop=(k == CI - 1),
                        skip_group_check=True),
                       "pe" if (k == CI - 1) else None)
            wdt_pe[ci] = kb.cnt["pe"]
            W("scalar", "pe", kb.cnt["pe"])
            if ci > 0:
                g_last = ci * DS - 1
                W("scalar", "vec", scan_marks[g_last])
            for t in range(NT):
                OP("scalar", lambda e, t=t, ci=ci: e.activation(
                    kb.f32v(FL + t * TT, TT),
                    ps_main[:, t * TT:(t + 1) * TT], AF.Abs,
                    bias=c["bdt"][:, ci:ci + 1]))
                DR("scalar")
                OP("scalar", lambda e, t=t: e.activation(
                    kb.f32v(FL + t * TT, TT), kb.f32v(FL + t * TT, TT),
                    AF.Exp, scale=-1.0))
                DR("scalar")
                OP("scalar", lambda e, t=t: e.activation(
                    kb.f32v(FL + t * TT, TT), kb.f32v(FL + t * TT, TT),
                    AF.Ln, bias=1.0))
                DR("scalar")
                OP("scalar", lambda e, t=t, ci=ci: e.activation(
                    kb.f32v(FB + t * TT, TT),
                    ps_main[:, t * TT:(t + 1) * TT], AF.Relu,
                    bias=c["bdt"][:, ci:ci + 1]),
                   "scl" if t == NT - 1 else None)
            sp_evac[ci] = kb.cnt["scl"]
            W("vector", "scl", kb.cnt["scl"])
            OP("vector", lambda e: e.tensor_add(
                kb.f32v(FL, S), kb.f32v(FL, S), kb.f32v(FB, S)))
            DR("vector")
            kb.swait("vector", "sty")
            OP("vector", lambda e, ci=ci: e.tensor_mul(
                kb.bfv(O_DU, S), kb.f32v(FL, S), kb.bfv(xh_off(ci), S)))
            OP("vector", lambda e, ci=ci: e.tensor_scalar(
                kb.f32v(FY, S), kb.bfv(xh_off(ci), S),
                c["D"][:, ci:ci + 1], None, ALU.mult), "vec")
            DR("vector")
            delta_mark = kb.cnt["vec"]
            sz_tok = {}
            for t in range(NT):
                szslot = O_SZ[t % len(O_SZ)]
                if (ci - 1, t) in gat_marks:
                    W("sync", "vec", gat_marks[(ci - 1, t)])
                sz_tok[t] = kb.dma(
                    "sync", lambda e, ci=ci, t=t, szslot=szslot:
                    e.dma_start(out=kb.bfv(szslot, TT),
                                in_=stile(kb.s_sz, ci, t)), "szl")
            bs_tok = {}
            for sp in (NBREP_RES, NBREP_RES + 1):
                W("sync", "vec", kb.cnt["vec"])
                bs_tok[sp] = kb.dma(
                    "sync", lambda e, sp=sp, bsrc=O_BS[sp % 2]: e.dma_start(
                        out=kb.bfv(bsrc, S),
                        in_=kb.s_brep[sp - NBREP_RES, :].rearrange(
                            "(p s) -> p s", p=128)), "bs")
            for s in range(DS):
                g = ci * DS + s
                p = g // 2
                dslot = FA if s % 2 == 0 else FA2
                hslot = HS[g % 4]
                W("scalar", "vec", delta_mark)
                if g >= 2:
                    W("scalar", "vec", scan_marks[g - 2])
                da = OP("scalar", lambda e, ci=ci, s=s, dslot=dslot:
                        e.activation(
                            kb.f32v(dslot, S), kb.f32v(FL, S), AF.Exp,
                            scale=c["negA"][:, ci * DS + s:ci * DS + s + 1]),
                        "scl")
                if s < NBREP_RES:
                    bsrc = brep_off(s)
                else:
                    bsrc = O_BS[s % 2]
                    kb.wtok("vector", bs_tok[s])
                OP("vector", lambda e, bsrc=bsrc: e.tensor_mul(
                    kb.bfv(O_DBU, S), kb.bfv(O_DU, S), kb.bfv(bsrc, S)))
                DR("vector")
                W("vector", "scl", da)
                if g % 2 == 0 and p >= 2:
                    kb.wtok("vector", add_marks[p - 2])
                scan_marks[g] = OP(
                    "vector", lambda e, dslot=dslot, hslot=hslot:
                    e.tensor_tensor_scan(
                        kb.f32v(hslot, S), kb.f32v(dslot, S),
                        kb.bfv(O_DBU, S), 0.0, ALU.mult, ALU.add), "vec")
                DR("vector")
                if g % 2 == 1:
                    fslot = HS[(g - 1) % 4]
                    scan_marks[g] = OP(
                        "vector", lambda e, fslot=fslot, hslot=hslot:
                        e.tensor_add(
                            kb.f32v(fslot, S), kb.f32v(fslot, S),
                            kb.f32v(hslot, S)), "vec")
                    DR("vector")
                    W("gpsimd", "vec", scan_marks[g])
                    if p >= 1:
                        kb.wtok("gpsimd", add_marks[p - 1])
                    add_marks[p] = kb.dma(
                        "gpsimd", lambda e, fslot=fslot: e.dma_start(
                            out=kb.f32v(FY, S), in_=kb.f32v(fslot, S),
                            accum_op=ALU.add), "yacc")
                if NBREP_RES <= s <= DS - 3:
                    W("sync", "vec", scan_marks[g])
                    bs_tok[s + 2] = kb.dma(
                        "sync", lambda e, s=s, bsrc=O_BS[s % 2]: e.dma_start(
                            out=kb.bfv(bsrc, S),
                            in_=kb.s_brep[s + 2 - NBREP_RES, :].rearrange(
                                "(p s) -> p s", p=128)), "bs")
            kb.wtok("vector", add_marks[(ci * DS + DS - 1) // 2])
            for t in range(NT):
                szslot = O_SZ[t % len(O_SZ)]
                kb.wtok("vector", sz_tok[t])
                gat_marks[(ci, t)] = OP(
                    "vector", lambda e, t=t, szslot=szslot: e.tensor_mul(
                        kb.bfv(O_DU + t * TT, TT), kb.f32v(FY + t * TT, TT),
                        kb.bfv(szslot, TT)), "vec")
                DR("vector")
            W("sync", "vec", kb.cnt["vec"])
            kb.dma("sync", lambda e, ci=ci: e.dma_start(
                out=kb.s_yg[ci, :].rearrange("(p s) -> p s", p=128),
                in_=kb.bfv(O_DU, S)), "sty")
        kb.barrier(["vector", "sync", "scalar"],
                   ["sync", "tensor", "vector"])

        wo_pe = {}
        wo_evac = {}
        for t in range(NT):
            for k in range(CI):
                wslot = O_WS[k % 2]
                ygslot = O_T0 if k % 2 == 0 else O_T1
                prev = (t, k - 2) if k >= 2 else (t - 1, CI - 2 + k)
                if prev in wo_pe:
                    W("sync", "pe", wo_pe[prev])
                tokw = kb.dma("sync", lambda e, k=k, wslot=wslot:
                              e.dma_start(
                                  out=kb.bfv(wslot, 1024),
                                  in_=kb.s_woutb[k * 128:(k + 1) * 128, :]),
                              "wg")
                toky = kb.dma("sync", lambda e, k=k, t=t, ygslot=ygslot:
                              e.dma_start(out=kb.bfv(ygslot, TT),
                                          in_=stile(kb.s_yg, k, t)), "ygl")
                kb.wtok("tensor", tokw)
                kb.wtok("tensor", toky)
                if k == 0 and (t - 1) in wo_evac:
                    W("tensor", "vec", wo_evac[t - 1])
                for d in range(CD):
                    bank = ps_main if d < 4 else ps_aux
                    OP("tensor", lambda e, d=d, k=k, wslot=wslot,
                       ygslot=ygslot, bank=bank: e.matmul(
                           bank[:, (d % 4) * TT:(d % 4 + 1) * TT],
                           kb.bfv(wslot + d * 128, 128),
                           kb.bfv(ygslot, TT),
                           start=(k == 0), stop=(k == CI - 1),
                           skip_group_check=True),
                       "pe" if d == CD - 1 else None)
                wo_pe[(t, k)] = kb.cnt["pe"]
            W("vector", "pe", kb.cnt["pe"])
            for d in range(CD):
                xslot = O_T2 if d % 2 == 0 else O_T3
                W("sync", "vec", kb.cnt["vec"])
                tok = kb.dma("sync", lambda e, d=d, t=t, xslot=xslot:
                             e.dma_start(out=kb.bfv(xslot, TT),
                                         in_=stile(kb.s_x1, d, t)), "x1l")
                kb.wtok("vector", tok)
                bank = ps_main if d < 4 else ps_aux
                OP("vector", lambda e, d=d, t=t, xslot=xslot, bank=bank:
                   e.scalar_tensor_tensor(
                       kb.bfv(O_X + d * S + t * TT, TT),
                       bank[:, (d % 4) * TT:(d % 4 + 1) * TT],
                       c["bout"][:, d:d + 1],
                       kb.bfv(xslot, TT), ALU.add, ALU.add), "vec")
                DR("vector")
            wo_evac[t] = kb.cnt["vec"]
        kb.barrier(["vector", "sync", "gpsimd"],
                   ["scalar", "tensor", "vector", "sync"])

    layernorm(O_X, O_LN)
    exchange("clipx", NT * CD)
    quantize(O_LN, O_XQ, CD)
    _xq1 = kb.cnt["vec"]
    p0c_weights()
    kb.close_p0_psum()
    ps_aux = kb.ps_aux
    ffn(final=False, xq_ready=_xq1)
    kb.barrier(["vector", "scalar", "sync", "tensor"],
               ["vector", "scalar", "sync", "tensor"])
    mamba()
    ffn(final=True)
    kb.swait_all_dma("sync")


_BUILD_CACHE = {}


def _get_build(S, n_cores):
    key = (S, n_cores)
    if key not in _BUILD_CACHE:
        _BUILD_CACHE[key] = build(S, n_cores)
    return _BUILD_CACHE[key]


def _pack_rows(v, nch):
    return np.ascontiguousarray(np.asarray(v, np.float32).reshape(nch, 128).T)


def make_inputs(xc, w, S):
    bf = ml_dtypes.bfloat16
    Alog = np.asarray(w["L0_attn_A_log"], np.float32)
    convw = np.asarray(w["L0_attn_conv_w"], np.float32)[:, 0, :]
    one = {}
    one["x"] = np.ascontiguousarray(np.asarray(xc, np.float32).T)
    one["W_in"] = np.ascontiguousarray(np.asarray(w["L0_attn_W_in"],
                                                  np.float32))
    one["W_dt"] = np.ascontiguousarray(np.asarray(w["L0_attn_W_dt"],
                                                   np.float32))
    one["W_out"] = np.ascontiguousarray(np.asarray(w["L0_attn_W_out"],
                                                    np.float32))
    Wx = np.asarray(w["L0_attn_W_x"], np.float32)
    one["WxB"] = np.ascontiguousarray(
        Wx[:, DS:].reshape(CI, 128, DS).transpose(1, 0, 2)
        .reshape(128, CI * DS))
    one["bxB"] = np.ascontiguousarray(
        np.asarray(w["L0_attn_b_x"], np.float32)[DS:].reshape(1, DS))
    one["w_gate"] = np.ascontiguousarray(np.asarray(w["L0_ffn_w_gate"],
                                                     np.float32))
    one["w_up"] = np.ascontiguousarray(np.asarray(w["L0_ffn_w_up"],
                                                   np.float32))
    one["w_down"] = np.ascontiguousarray(np.asarray(w["L0_ffn_w_down"],
                                                     np.float32))
    one["Alog"] = np.ascontiguousarray(
        Alog.reshape(CI, 128, DS).transpose(1, 0, 2).reshape(128, CI * DS))
    one["convw"] = np.ascontiguousarray(
        convw.reshape(CI, 128, DC).transpose(1, 0, 2).reshape(128, CI * DC))
    one["convb"] = _pack_rows(w["L0_attn_conv_b"], CI)
    one["Dvec"] = _pack_rows(w["L0_attn_D"], CI)
    bin_ = np.asarray(w["L0_attn_b_in"], np.float32)
    one["bxh"] = _pack_rows(bin_[:DI], CI)
    one["bz"] = _pack_rows(bin_[DI:], CI)
    one["bdt"] = _pack_rows(w["L0_attn_b_dt"], CI)
    one["bout"] = _pack_rows(w["L0_attn_b_out"], CD)
    one["clipx"] = np.asarray(
        [[float(np.asarray(w["L0_ffn_clip_x"]).reshape(-1)[0]), 0.0]],
        np.float32)
    one["clipd"] = np.asarray(
        [[float(np.asarray(w["L0_ffn_clip_down"]).reshape(-1)[0]), 0.0]],
        np.float32)
    one["eyeb"] = np.eye(128, dtype=bf)
    one["eyef"] = np.eye(128, dtype=np.float32)
    one["onesb"] = np.ones((128, 2), dtype=bf)
    one["ones1"] = np.ones((1, 128), np.float32)
    one["onesrow_b"] = np.ones((1, TT), dtype=bf)
    cv = np.zeros((128, 8), np.float32)
    cv[:, 0] = LN_EPS
    cv[:, 1] = 1.0
    cv[:, 2] = -1.0
    cv[:, 3] = -0.5
    one["cvals"] = cv
    oneh = np.zeros((DS, DS * 128), dtype=bf)
    for s in range(DS):
        oneh[s, s * 128:(s + 1) * 128] = 1
    one["oneh"] = oneh
    return one


def kernel(**inputs):
    x = np.asarray(inputs["L0_attn_x"], np.float32)
    S = x.shape[1]
    kb = _get_build(S, NCORE)
    in_maps = [make_inputs(x[cidx], inputs, S) for cidx in range(NCORE)]
    res = run_bass_kernel_spmd(kb.nc, in_maps, core_ids=list(range(NCORE)))
    outs = [np.asarray(r["out"]).T for r in res.results]
    return np.stack(outs, axis=0).astype(np.float32)


# version 12 (final)
# speedup vs baseline: 1.1594x; 1.0085x over previous
import numpy as np
import ml_dtypes
import concourse.bass as bass
import concourse.mybir as mybir
from concourse.bass_utils import run_bass_kernel_spmd
from contextlib import ExitStack

F32 = mybir.dt.float32
BF16 = mybir.dt.bfloat16
I32 = mybir.dt.int32
AF = mybir.ActivationFunctionType
ALU = mybir.AluOpType
AX = mybir.AxisListType

DM, DI, DS, DC, HF = 1024, 2048, 16, 4, 2816
CD, CI, CH = DM // 128, DI // 128, HF // 128
TT = 512
LN_EPS = 1e-5
QMAXF = 127.0
NCORE = 8
ENGS = ("sync", "vector", "scalar", "tensor", "gpsimd")


class KB:
    def __init__(self, S, n_cores):
        self.S = S
        self.NT = S // TT
        self.n_cores = n_cores
        self.nc = bass.Bass("TRN2", target_bir_lowering=False)
        self.es = ExitStack()
        self.q = {e: [] for e in ENGS}
        self._mk_dram()

    def _mk_dram(self):
        nc, S = self.nc, self.S
        self.di = {}

        def din(name, shape, dt=F32):
            self.di[name] = nc.dram_tensor(name, shape, dt,
                                           kind="ExternalInput").ap()

        din("x", [DM, S])
        din("W_in", [DM, 2 * DI])
        din("W_dt", [DI, DI])
        din("W_out", [DI, DM])
        din("WxB", [128, CI * DS])
        din("bxB", [1, DS])
        din("w_gate", [HF, DM])
        din("w_up", [HF, DM])
        din("w_down", [DM, HF])
        din("Alog", [128, CI * DS])
        din("convw", [128, CI * DC])
        din("convb", [128, CI])
        din("Dvec", [128, CI])
        din("bxh", [128, CI])
        din("bz", [128, CI])
        din("bdt", [128, CI])
        din("bout", [128, CD])
        din("clipx", [1, 2])
        din("clipd", [1, 2])
        din("cvals", [128, 8])
        din("eyeb", [128, 128], BF16)
        din("eyef", [128, 128], F32)
        din("onesb", [128, 2], BF16)
        din("ones1", [1, 128])
        din("onesrow_b", [1, TT], BF16)
        din("oneh", [DS, DS * 128], BF16)
        self.d_out = nc.dram_tensor("out", [DM, S], F32,
                                    kind="ExternalOutput").ap()

        def scr(name, shape, dt):
            return nc.dram_tensor(name, shape, dt, kind="Internal").ap()

        self.s_wdtb = scr("s_wdtb", [DI, DI], BF16)
        self.s_woutb = scr("s_woutb", [DI, DM], BF16)
        self.s_wgT = scr("s_wgT", [CH, CD * 128 * 128], BF16)
        self.s_wuT = scr("s_wuT", [CH, CD * 128 * 128], BF16)
        self.s_wdT = scr("s_wdT", [CH, CD * 128 * 128], BF16)
        self.s_h = scr("s_h", [CH, 128 * S], BF16)
        self.s_sz = scr("s_sz", [CI, 128 * S], BF16)
        self.s_yg = scr("s_yg", [CI, 128 * S], BF16)
        self.s_x1 = scr("s_x1", [CD, 128 * S], BF16)
        self.s_brep = scr("s_brep", [6, 128 * S], BF16)
        self.s_cci = scr("s_cci", [1, 2], F32)
        self.s_cco = scr("s_cco", [1, 2], F32)

    def mk_mem(self):
        nc, S, es = self.nc, self.S, self.es

        def sb(name, shape, dt=F32):
            return es.enter_context(nc.sbuf_tensor(name, shape, dt))

        self.bf = sb("bf_arena", [128, 32 * S], BF16)
        self.f32 = sb("f32_arena", [128, 8 * S], F32)
        self.int_t = sb("int_t", [128, S // 2], I32)
        self.c = {}
        for name, shape, dt in [
            ("negA", [128, CI * DS], F32), ("convw", [128, CI * DC], F32),
            ("convb", [128, CI], F32), ("D", [128, CI], F32),
            ("bxh", [128, CI], F32), ("bz", [128, CI], F32),
            ("bdt", [128, CI], F32), ("bout", [128, CD], F32),
            ("WxB", [128, CI * DS], BF16), ("bxB", [1, DS], BF16),
            ("eyeb", [128, 128], BF16), ("eyef", [128, 128], F32),
            ("onesb", [128, 2], BF16),
            ("ones1", [1, 128], F32), ("onesrow_b", [1, TT], BF16),
            ("oneh", [DS, DS * 128], BF16),
            ("clipx", [1, 2], F32), ("clipd", [1, 2], F32),
            ("cvals", [128, 8], F32),
            ("swg", [128, CH], F32), ("swu", [128, CH], F32),
            ("swd", [128, CD], F32),
            ("sxw_g", [128, CH], F32), ("sxw_u", [128, CH], F32),
            ("sxw_d", [128, CD], F32),
            ("amax", [128, 32], F32), ("r1", [128, 2], F32),
            ("sc", [1, 8], F32), ("pp", [128, 4], F32),
            ("pad8", [128, 8], BF16),
        ]:
            self.c[name] = sb("c_" + name, shape, dt)
        self.sems = {}
        for n in ["pe", "vec", "scl", "gps", "cc"]:
            self.sems[n] = es.enter_context(nc.semaphore(n))
        self.cnt = {n: 0 for n in self.sems}
        self.sempool = [es.enter_context(nc.semaphore(f"dmas{i}"))
                        for i in range(44)]
        self.sempool_next = 0
        self.streams = {}
        self.ps_main = es.enter_context(
            nc.psum_tensor("ps_main", [128, 4 * TT], F32))
        self._ps_tb_scope = ExitStack()
        self.ps_tb = self._ps_tb_scope.enter_context(
            nc.psum_tensor("ps_tb", [128, 1024], BF16))
        self.ps_aux = None
        for col, val in enumerate([LN_EPS, 1.0, -1.0, -0.5]):
            self.nc.const_aps.register(F32, val, self.c["cvals"][:, col:col + 1])

    def close_p0_psum(self):
        self._ps_tb_scope.close()
        self.ps_tb = None
        self.ps_aux = self.es.enter_context(
            self.nc.psum_tensor("ps_aux", [128, 4 * TT], F32))

    def op(self, eng, fn, inc=None, n=1):
        if inc is None:
            self.q[eng].append(fn)
            return None
        self.cnt[inc] += n
        sem = self.sems[inc]
        self.q[eng].append(
            lambda e, fn=fn, sem=sem, n=n: fn(e).then_inc(sem, n))
        return self.cnt[inc]

    def wait(self, eng, semname, val):
        if val and val > 0:
            sem = self.sems[semname]
            self.q[eng].append(
                lambda e, sem=sem, val=val: e.wait_ge(sem, val))

    def _stream(self, name):
        st = self.streams.get(name)
        if st is None:
            a = self.sempool[self.sempool_next]
            b = self.sempool[self.sempool_next + 1]
            self.sempool_next += 2
            st = {"sems": [a, b], "cnt": [0, 0], "par": 0}
            self.streams[name] = st
        return st

    def dma(self, eng, fn, stream):
        st = self._stream(stream)
        p = st["par"]
        sem = st["sems"][p]
        if st["cnt"][p] > 0:
            v = st["cnt"][p]
            self.q[eng].append(
                lambda e, sem=sem, v=v: e.wait_ge(sem, v))
        st["cnt"][p] += 16
        v = st["cnt"][p]
        self.q[eng].append(
            lambda e, fn=fn, sem=sem: fn(e).then_inc(sem, 16))
        st["par"] = 1 - p
        return (sem, v)

    def wtok(self, eng, tok):
        if tok is None:
            return
        sem, v = tok
        self.q[eng].append(lambda e, sem=sem, v=v: e.wait_ge(sem, v))

    def swait(self, eng, stream):
        st = self.streams.get(stream)
        if not st:
            return
        for p in (0, 1):
            if st["cnt"][p] > 0:
                sem, v = st["sems"][p], st["cnt"][p]
                self.q[eng].append(
                    lambda e, sem=sem, v=v: e.wait_ge(sem, v))

    def swait_all_dma(self, eng):
        for name in list(self.streams):
            self.swait(eng, name)

    def drain(self, eng):
        self.q[eng].append(lambda e: e.drain())

    def barrier(self, writers, waiters):
        snap = []
        sync_snap = []
        for w in writers:
            if w == "sync":
                for name, st in self.streams.items():
                    for p in (0, 1):
                        if st["cnt"][p] > 0:
                            sync_snap.append((st["sems"][p], st["cnt"][p]))
            else:
                s = {"vector": "vec", "scalar": "scl", "tensor": "pe",
                     "gpsimd": "gps"}[w]
                snap.append((w, s, self.cnt[s]))
        for eng in waiters:
            for w, s, v in snap:
                if eng == w:
                    self.drain(eng)
                else:
                    self.wait(eng, s, v)
            for sem, v in sync_snap:
                self.q[eng].append(
                    lambda e, sem=sem, v=v: e.wait_ge(sem, v))

    def bfv(self, off, cols):
        return self.bf[:, off:off + cols]

    def f32v(self, off, cols):
        return self.f32[:, off:off + cols]


def build(S, n_cores):
    kb = KB(S, n_cores)
    with kb.es:
        kb.mk_mem()
        _emit(kb)
        blk = kb.es.enter_context(kb.nc.Block())
        for eng in ENGS:
            ops = kb.q[eng]
            if ops:
                getattr(blk, eng)(lambda e, ops=ops: [f(e) for f in ops])
    return kb


def _emit(kb: KB):
    S, NT = kb.S, kb.NT
    c, di = kb.c, kb.di
    int_t = kb.int_t
    ps_main = kb.ps_main

    def U(k):
        return k * S

    O_X, O_LN, O_XQ = U(0), U(8), U(16)
    O_SG, O_SU, O_HFF = U(24), U(25), U(26)
    O_WS = [U(27), U(27) + 1024, U(28), U(28) + 1024]
    O_T0, O_T1 = U(29), U(29) + TT
    O_T2, O_T3 = U(29) + 2 * TT, U(29) + 3 * TT
    O_STG = U(30)
    O_AB = U(26)
    O_BLK = U(24)

    def xh_off(ci):
        return U(8 + ci) if ci < 8 else U(ci - 8)

    def xhp_off(ci):
        return U(ci) if ci < 8 else U(16 + ci - 8)

    def brep_off(s):
        return U(16 + s)

    NBREP_RES = 10
    O_BS = [U(26), U(27)]
    O_WDT = U(28)
    O_DU = U(29)
    O_SZ = [U(30) + t * TT for t in range(min(NT, 4))]
    O_DBU = U(31)
    O_BMAT = U(29)

    FA, FB, FH, FY, FL, FA2 = 0, S, 2 * S, 3 * S, 4 * S, 5 * S
    FH3, FH4 = 6 * S, 7 * S
    HS = [FH, FB, FH3, FH4]
    ROW_M, ROW_T, ROW_R = FA2, FA2 + TT, FA2 + 2 * TT

    W, OP, DR = kb.wait, kb.op, kb.drain

    def rowv(off):
        return kb.f32[0:1, off:off + TT]

    def bmatv(t0, cols):
        return kb.bf[0:DS, O_BMAT + t0:O_BMAT + t0 + cols]

    for name in ["cvals", "convw", "convb", "bxh", "bz", "bdt", "bout",
                 "eyeb", "eyef", "onesb", "ones1", "onesrow_b", "oneh",
                 "clipx", "clipd"]:
        kb.dma("sync", lambda e, n=name: e.dma_start(out=c[n][:], in_=di[n]),
               "cst")
    kb.dma("sync", lambda e: e.dma_start(out=c["D"][:], in_=di["Dvec"]),
           "cst")
    kb.dma("sync", lambda e: e.dma_start(out=c["negA"][:], in_=di["Alog"]),
           "cst")
    kb.dma("sync", lambda e: e.dma_start(out=kb.f32[:, 0:CI * DS],
                                         in_=di["WxB"]), "cst")
    kb.dma("sync", lambda e: e.dma_start(
        out=kb.f32[0:1, CI * DS:CI * DS + DS], in_=di["bxB"]), "cst")
    kb.swait("scalar", "cst")
    OP("scalar", lambda e: e.copy(c["WxB"][:], kb.f32[:, 0:CI * DS]))
    OP("scalar", lambda e: e.copy(c["bxB"][:],
                                  kb.f32[0:1, CI * DS:CI * DS + DS]), "scl")
    OP("scalar", lambda e: e.activation(c["negA"][:], c["negA"][:], AF.Exp))
    DR("scalar")
    OP("scalar", lambda e: e.mul(c["negA"][:], c["negA"][:], -1.0), "scl")
    kb.swait("vector", "cst")
    OP("vector", lambda e: e.memset(c["pad8"][:], 0.0), "vec")

    xcopy_marks = []
    W("sync", "scl", kb.cnt["scl"])
    for ch in range(CD):
        slot = FL if ch % 2 == 0 else FA
        if ch >= 2:
            W("sync", "scl", xcopy_marks[ch - 2])
        tok = kb.dma("sync", lambda e, ch=ch, slot=slot: e.dma_start(
            out=kb.f32v(slot, S), in_=di["x"][ch * 128:(ch + 1) * 128, :]),
            "xw")
        kb.wtok("scalar", tok)
        xcopy_marks.append(OP("scalar", lambda e, ch=ch, slot=slot: e.copy(
            kb.bfv(O_X + ch * S, S), kb.f32v(slot, S)), "scl"))

    qmarks = []

    def quant_transpose(wname, nch, kd, stile, dstT, toks,
                        scatter_rows=False):
        nkb = kd // 128
        for i in range(nch):
            slot = 0 if i % 2 == 0 else 2816
            if len(qmarks) >= 2:
                W("sync", "vec", qmarks[-2])
            tok = kb.dma("sync", lambda e, w=wname, i=i, slot=slot, kd=kd:
                         e.dma_start(out=kb.f32[:, slot:slot + kd],
                                     in_=di[w][i * 128:(i + 1) * 128, :]),
                         "xw")
            kb.wtok("vector", tok)
            OP("vector", lambda e, slot=slot, kd=kd: e.tensor_reduce(
                c["r1"][:, 0:1], kb.f32[:, slot:slot + kd], AX.X, ALU.max,
                apply_absolute_value=True))
            DR("vector")
            OP("vector", lambda e, st=stile, i=i: e.tensor_scalar(
                c[st][:, i:i + 1], c["r1"][:, 0:1], 1.0 / QMAXF, None,
                ALU.mult))
            OP("vector", lambda e: e.reciprocal(c["r1"][:, 1:2],
                                                c["r1"][:, 0:1]))
            DR("vector")
            W("vector", "pe", kb.cnt["pe"])
            mlast = None
            for p0 in range(0, kd, S // 2):
                pn = min(S // 2, kd - p0)
                OP("vector", lambda e, slot=slot, p0=p0, pn=pn:
                   e.tensor_scalar(int_t[:, 0:pn],
                                   kb.f32[:, slot + p0:slot + p0 + pn],
                                   c["r1"][:, 1:2], QMAXF,
                                   ALU.mult, ALU.mult))
                DR("vector")
                mlast = OP("vector", lambda e, p0=p0, pn=pn: e.tensor_copy(
                    kb.bf[:, O_BLK + p0:O_BLK + p0 + pn], int_t[:, 0:pn]), "vec")
                DR("vector")
            qmarks.append(mlast)
            kb.swait("scalar", "stw")
            for g0 in range(0, nkb, 8):
                gn = min(8, nkb - g0)
                W("tensor", "vec", kb.cnt["vec"])
                W("tensor", "scl", kb.cnt["scl"])
                for j in range(gn):
                    OP("tensor", lambda e, g0=g0, j=j, pstb=kb.ps_tb:
                       e.transpose(
                           pstb[:, j * 128:(j + 1) * 128],
                           kb.bf[:, O_BLK + (g0 + j) * 128:O_BLK + (g0 + j + 1) * 128],
                           c["eyeb"][:]), "pe" if j == gn - 1 else None)
                W("scalar", "pe", kb.cnt["pe"])
                for j in range(gn):
                    OP("scalar", lambda e, g0=g0, j=j, pstb=kb.ps_tb: e.copy(
                        kb.bfv(O_STG + (g0 + j) * 128, 128),
                        pstb[:, j * 128:(j + 1) * 128]),
                       "scl" if j == gn - 1 else None)
            W("sync", "scl", kb.cnt["scl"])
            if not scatter_rows:
                toks[i] = kb.dma(
                    "sync", lambda e, dstT=dstT, i=i, nkb=nkb: e.dma_start(
                        out=dstT[i, 0:nkb * 128 * 128].rearrange(
                            "(k p j) -> p k j", p=128, j=128),
                        in_=kb.bfv(O_STG, nkb * 128).rearrange(
                            "p (k j) -> p k j", j=128)), "stw")
            else:
                toks[i] = kb.dma(
                    "sync", lambda e, dstT=dstT, i=i, nkb=nkb: e.dma_start(
                        out=dstT[:, i * 128 * 128:(i + 1) * 128 * 128]
                        .rearrange("k (p j) -> p k j", j=128),
                        in_=kb.bfv(O_STG, nkb * 128).rearrange(
                            "p (k j) -> p k j", j=128)), "stw")

    pre_scl_mark = kb.cnt["scl"]
    wgT_tok, wuT_tok, wdT_tok = {}, {}, {}

    def p0c_weights():
        W("sync", "scl", pre_scl_mark)
        quant_transpose("w_gate", CH, DM, "swg", kb.s_wgT, wgT_tok)
        quant_transpose("w_up", CH, DM, "swu", kb.s_wuT, wuT_tok)
        quant_transpose("w_down", CD, HF, "swd", kb.s_wdT, wdT_tok,
                        scatter_rows=True)

    def p0c_casts():
        cast_marks = []
        for wname, dst, nch, kd in [("W_dt", kb.s_wdtb, CI, DI),
                                    ("W_out", kb.s_woutb, CI, DM)]:
            for i in range(nch):
                slot = 0 if i % 2 == 0 else 2816
                if len(cast_marks) >= 2:
                    W("sync", "scl", cast_marks[-2])
                tok = kb.dma("sync",
                             lambda e, w=wname, i=i, slot=slot, kd=kd:
                             e.dma_start(out=kb.f32[:, slot:slot + kd],
                                         in_=di[w][i * 128:(i + 1) * 128, :]),
                             "xw")
                kb.wtok("scalar", tok)
                kb.swait("scalar", "stw")
                cast_marks.append(OP(
                    "scalar", lambda e, slot=slot, kd=kd: e.copy(
                        kb.bfv(O_STG, kd), kb.f32[:, slot:slot + kd]), "scl"))
                W("sync", "scl", kb.cnt["scl"])
                kb.dma("sync", lambda e, dst=dst, i=i, kd=kd: e.dma_start(
                    out=dst[i * 128:(i + 1) * 128, :], in_=kb.bfv(O_STG, kd)),
                    "stw")


    def layernorm(src_o, dst_o):
        prev_stats_read = 0
        W("tensor", "scl", kb.cnt["scl"])
        for t in range(NT):
            tn = t * TT
            sq_marks = []
            pe_sq = []
            W("tensor", "vec", prev_stats_read)
            for ch in range(CD):
                sq = O_T0 if ch % 2 == 0 else O_T1
                if ch >= 2:
                    W("scalar", "pe", pe_sq[ch - 2])
                sq_marks.append(OP(
                    "scalar", lambda e, ch=ch, sq=sq, src_o=src_o, tn=tn:
                    e.activation(kb.bfv(sq, TT),
                                 kb.bfv(src_o + ch * S + tn, TT), AF.Square),
                    "scl"))
                OP("tensor", lambda e, ch=ch, src_o=src_o, tn=tn: e.matmul(
                    ps_main[0:1, 0:TT], c["onesb"][:, 0:1],
                    kb.bfv(src_o + ch * S + tn, TT),
                    start=(ch == 0), stop=(ch == CD - 1),
                    skip_group_check=True))
                W("tensor", "scl", sq_marks[ch])
                pe_sq.append(OP("tensor", lambda e, ch=ch, sq=sq: e.matmul(
                    ps_main[0:1, TT:2 * TT], c["onesb"][:, 0:1],
                    kb.bfv(sq, TT),
                    start=(ch == 0), stop=(ch == CD - 1),
                    skip_group_check=True), "pe"))
            W("vector", "pe", kb.cnt["pe"])
            OP("vector", lambda e: e.tensor_scalar(
                rowv(ROW_M), ps_main[0:1, 0:TT], 1.0 / DM, None, ALU.mult))
            OP("vector", lambda e: e.tensor_scalar(
                rowv(ROW_T), ps_main[0:1, TT:2 * TT], 1.0 / DM, None,
                ALU.mult))
            DR("vector")
            OP("vector", lambda e: e.tensor_mul(rowv(ROW_R), rowv(ROW_M),
                                                rowv(ROW_M)))
            DR("vector")
            OP("vector", lambda e: e.tensor_sub(rowv(ROW_T), rowv(ROW_T),
                                                rowv(ROW_R)), "vec")
            stats_read = kb.cnt["vec"]
            W("scalar", "vec", stats_read)
            OP("scalar", lambda e: e.activation(rowv(ROW_R), rowv(ROW_T),
                                                AF.Ln, bias=LN_EPS))
            DR("scalar")
            OP("scalar", lambda e: e.activation(rowv(ROW_R), rowv(ROW_R),
                                                AF.Exp, scale=-0.5), "scl")
            W("vector", "scl", kb.cnt["scl"])
            OP("vector", lambda e: e.tensor_mul(rowv(ROW_M), rowv(ROW_M),
                                                rowv(ROW_R)), "vec")
            W("tensor", "vec", kb.cnt["vec"])
            OP("tensor", lambda e: e.matmul(
                ps_main[:, 2 * TT:3 * TT], c["ones1"][:],
                kb.f32[0:1, ROW_R:ROW_R + TT],
                start=True, stop=True))
            OP("tensor", lambda e: e.matmul(
                ps_main[:, 3 * TT:4 * TT], c["ones1"][:],
                kb.f32[0:1, ROW_M:ROW_M + TT],
                start=True, stop=True), "pe")
            W("scalar", "pe", kb.cnt["pe"])
            OP("scalar", lambda e: e.copy(kb.bfv(O_AB, TT),
                                          ps_main[:, 2 * TT:3 * TT]))
            OP("scalar", lambda e: e.copy(kb.bfv(O_AB + TT, TT),
                                          ps_main[:, 3 * TT:4 * TT]), "scl")
            W("vector", "scl", kb.cnt["scl"])
            for ch in range(CD):
                d = dst_o + ch * S + tn
                OP("vector", lambda e, d=d, ch=ch, src_o=src_o, tn=tn:
                   e.tensor_mul(kb.bfv(d, TT),
                                kb.bfv(src_o + ch * S + tn, TT),
                                kb.bfv(O_AB, TT)))
                DR("vector")
                OP("vector", lambda e, d=d: e.tensor_sub(
                    kb.bfv(d, TT), kb.bfv(d, TT), kb.bfv(O_AB + TT, TT)))
                DR("vector")
                OP("vector", lambda e, d=d, ch=ch, t=t:
                   e.tensor_reduce(c["amax"][:, t * CD + ch:t * CD + ch + 1],
                                   kb.bfv(d, TT), AX.X, ALU.max,
                                   apply_absolute_value=True),
                   "vec" if ch == CD - 1 else None)
            prev_stats_read = stats_read

    def exchange(clip_name, amax_cols):
        DR("vector")
        OP("vector", lambda e, n=amax_cols: e.tensor_reduce(
            c["r1"][:, 0:1], c["amax"][:, 0:n], AX.X, ALU.max), "vec")
        W("tensor", "vec", kb.cnt["vec"])
        OP("tensor", lambda e: e.transpose(
            ps_main[0:1, 0:128], c["r1"][:, 0:1], c["eyef"][:]), "pe")
        W("vector", "pe", kb.cnt["pe"])
        OP("vector", lambda e: e.reduce_max(
            c["sc"][0:1, 0:1], ps_main[0:1, 0:128], AX.X), "vec")
        W("gpsimd", "vec", kb.cnt["vec"])
        tok = kb.dma("gpsimd", lambda e: e.dma_start(
            out=kb.s_cci[0:1, 0:1], in_=c["sc"][0:1, 0:1]), "stc")
        kb.wtok("gpsimd", tok)
        OP("gpsimd", lambda e: e.collective_compute(
            "AllReduce", ALU.max,
            replica_groups=[list(range(kb.n_cores))],
            ins=[kb.s_cci[0:1, 0:1]], outs=[kb.s_cco[0:1, 0:1]]), "cc")
        W("gpsimd", "cc", kb.cnt["cc"])
        tok = kb.dma("gpsimd", lambda e: e.dma_start(
            out=c["sc"][0:1, 1:2], in_=kb.s_cco[0:1, 0:1]), "ccl")
        kb.wtok("scalar", tok)
        OP("scalar", lambda e, cn=clip_name: e.activation(
            c["sc"][0:1, 2:3], c[cn][0:1, 0:1], AF.Exp, scale=-1.0), "scl")
        W("vector", "scl", kb.cnt["scl"])
        OP("vector", lambda e: e.tensor_scalar(
            c["sc"][0:1, 3:4], c["sc"][0:1, 2:3], 1.0, None, ALU.add))
        DR("vector")
        OP("vector", lambda e: e.reciprocal(c["sc"][0:1, 2:3],
                                            c["sc"][0:1, 3:4]))
        DR("vector")
        OP("vector", lambda e: e.tensor_mul(c["sc"][0:1, 3:4],
                                            c["sc"][0:1, 2:3],
                                            c["sc"][0:1, 1:2]))
        DR("vector")
        OP("vector", lambda e: e.tensor_scalar(
            c["sc"][0:1, 4:5], c["sc"][0:1, 3:4], 1.0 / QMAXF, None,
            ALU.mult))
        DR("vector")
        OP("vector", lambda e: e.reciprocal(c["sc"][0:1, 5:6],
                                            c["sc"][0:1, 4:5]), "vec")
        W("tensor", "vec", kb.cnt["vec"])
        OP("tensor", lambda e: e.matmul(
            ps_main[:, 0:2], c["ones1"][:], c["sc"][0:1, 4:6],
            start=True, stop=True), "pe")
        W("vector", "pe", kb.cnt["pe"])
        OP("vector", lambda e: e.tensor_copy(c["pp"][:, 0:2],
                                             ps_main[:, 0:2]), "vec")
        DR("vector")

    def quantize(src_o, dst_o, nch):
        H = S // 2
        for ch in range(nch):
            for h0 in (0, H):
                OP("vector", lambda e, ch=ch, src_o=src_o, h0=h0:
                   e.tensor_scalar(
                       int_t[:, 0:H], kb.bfv(src_o + ch * S + h0, H),
                       c["pp"][:, 1:2], None, ALU.mult))
                DR("vector")
                OP("vector", lambda e, ch=ch, dst_o=dst_o, h0=h0:
                   e.tensor_scalar(
                       kb.bfv(dst_o + ch * S + h0, H), int_t[:, 0:H],
                       -128.0, 127.0, ALU.max, ALU.min), "vec")
                DR("vector")

    def scalevec(dst, srctile, ncols, extra):
        OP("vector", lambda e, dst=dst, s=srctile, n=ncols, x=extra:
           e.tensor_scalar(c[dst][:, 0:n], c[s][:, 0:n], c["pp"][:, 0:1],
                           x, ALU.mult, ALU.mult), "vec")
        DR("vector")

    def stile(scratch, k, t):
        return scratch[k, :].rearrange("(p s) -> p s", p=128)[
            :, t * TT:(t + 1) * TT]

    def ffn(final, xq_ready=None):
        if xq_ready is None:
            layernorm(O_X, O_LN)
            exchange("clipx", NT * CD)
        scalevec("sxw_g", "swg", CH, 1.0)
        scalevec("sxw_u", "swu", CH, 1.0)
        sv_ready = kb.cnt["vec"]
        if xq_ready is None:
            quantize(O_LN, O_XQ, CD)
            xq_ready = kb.cnt["vec"]

        pe_w_marks = []
        gate_evac = {}
        up_evac = {t: 0 for t in range(NT)}
        h_marks = {}
        for i in range(CH):
            wg_s, wu_s = O_WS[i % 2], O_WS[2 + i % 2]
            if i >= 2:
                W("sync", "pe", pe_w_marks[i - 2])
            kb.wtok("sync", wgT_tok.get(i))
            kb.wtok("sync", wuT_tok.get(i))
            tokg = kb.dma("sync", lambda e, i=i, wg_s=wg_s: e.dma_start(
                out=kb.bfv(wg_s, 1024).rearrange("p (k j) -> p k j", j=128),
                in_=kb.s_wgT[i, :].rearrange("(k p j) -> p k j",
                                             p=128, j=128)), "wg")
            toku = kb.dma("sync", lambda e, i=i, wu_s=wu_s: e.dma_start(
                out=kb.bfv(wu_s, 1024).rearrange("p (k j) -> p k j", j=128),
                in_=kb.s_wuT[i, :].rearrange("(k p j) -> p k j",
                                             p=128, j=128)), "wu")
            kb.wtok("tensor", tokg)
            kb.wtok("tensor", toku)
            if i == 0:
                W("tensor", "vec", xq_ready)
            if i > 0:
                for t in range(NT):
                    W("tensor", "scl", gate_evac[t])
            for k in range(CD):
                for t in range(NT):
                    OP("tensor", lambda e, t=t, k=k, wg_s=wg_s: e.matmul(
                        ps_main[:, t * TT:(t + 1) * TT],
                        kb.bfv(wg_s + k * 128, 128),
                        kb.bfv(O_XQ + k * S + t * TT, TT),
                        start=(k == 0), stop=(k == CD - 1),
                        skip_group_check=True),
                       "pe" if (k == CD - 1) else None)
            gmm = kb.cnt["pe"]
            W("scalar", "pe", gmm)
            if i == 0:
                W("scalar", "vec", sv_ready)
            if i > 0:
                W("scalar", "vec", h_marks[i - 1])
            for t in range(NT):
                gate_evac[t] = OP(
                    "scalar", lambda e, t=t, i=i: e.activation(
                        kb.bfv(O_SG + t * TT, TT),
                        ps_main[:, t * TT:(t + 1) * TT], AF.Silu,
                        scale=c["sxw_g"][:, i:i + 1]), "scl")
            if i > 0:
                for t in range(NT):
                    W("tensor", "scl", up_evac[t])
            for k in range(CD):
                for t in range(NT):
                    OP("tensor", lambda e, t=t, k=k, wu_s=wu_s: e.matmul(
                        ps_aux[:, t * TT:(t + 1) * TT],
                        kb.bfv(wu_s + k * 128, 128),
                        kb.bfv(O_XQ + k * S + t * TT, TT),
                        start=(k == 0), stop=(k == CD - 1),
                        skip_group_check=True),
                       "pe" if (k == CD - 1) else None)
            pe_w_marks.append(kb.cnt["pe"])
            W("scalar", "pe", kb.cnt["pe"])
            for t in range(NT):
                up_evac[t] = OP(
                    "scalar", lambda e, t=t, i=i: e.activation(
                        kb.bfv(O_SU + t * TT, TT),
                        ps_aux[:, t * TT:(t + 1) * TT], AF.Identity,
                        scale=c["sxw_u"][:, i:i + 1]), "scl")
            W("vector", "scl", kb.cnt["scl"])
            if i > 0:
                kb.swait("vector", "sth")
            OP("vector", lambda e: e.tensor_mul(
                kb.bfv(O_HFF, S), kb.bfv(O_SG, S), kb.bfv(O_SU, S)))
            DR("vector")
            h_marks[i] = OP("vector", lambda e, i=i: e.tensor_reduce(
                c["amax"][:, i:i + 1], kb.bfv(O_HFF, S), AX.X, ALU.max,
                apply_absolute_value=True), "vec")
            W("sync", "vec", h_marks[i])
            kb.dma("sync", lambda e, i=i: e.dma_start(
                out=kb.s_h[i, :].rearrange("(p s) -> p s", p=128),
                in_=kb.bfv(O_HFF, S)), "sth")

        exchange("clipd", CH)
        scalevec("sxw_d", "swd", CD, 0.5)
        kb.swait("sync", "sth")
        kb.swait("sync", "stw")

        dn_pe = {}
        hq_vec = {}
        x_evac = {}
        W("tensor", "scl", kb.cnt["scl"])
        W("tensor", "vec", kb.cnt["vec"])
        for t in range(NT):
            for k in range(CH):
                wslot = O_WS[k % 2]
                hslot = O_T0 if k % 2 == 0 else O_T1
                qslot = O_T2 if k % 2 == 0 else O_T3
                prev = (t, k - 2) if k >= 2 else (t - 1, CH - 2 + k)
                if prev in dn_pe:
                    W("sync", "pe", dn_pe[prev])
                if prev in hq_vec:
                    W("sync", "vec", hq_vec[prev])
                tokw = kb.dma("sync", lambda e, k=k, wslot=wslot:
                              e.dma_start(
                                  out=kb.bfv(wslot, 1024).rearrange(
                                      "p (d j) -> p d j", j=128),
                                  in_=kb.s_wdT[k, :].rearrange(
                                      "(d p j) -> p d j", p=128, j=128)),
                              "wg")
                tokh = kb.dma("sync", lambda e, k=k, t=t, hslot=hslot:
                              e.dma_start(out=kb.bfv(hslot, TT),
                                          in_=stile(kb.s_h, k, t)), "hl")
                kb.wtok("vector", tokh)
                OP("vector", lambda e, hslot=hslot: e.tensor_scalar(
                    int_t[:, 0:TT], kb.bfv(hslot, TT), c["pp"][:, 1:2],
                    None, ALU.mult))
                DR("vector")
                hq_vec[(t, k)] = OP(
                    "vector", lambda e, qslot=qslot: e.tensor_scalar(
                        kb.bfv(qslot, TT), int_t[:, 0:TT], -128.0, 127.0,
                        ALU.max, ALU.min), "vec")
                DR("vector")
                kb.wtok("tensor", tokw)
                W("tensor", "vec", hq_vec[(t, k)])
                if k == 0 and (t - 1) in x_evac:
                    W("tensor", "vec", x_evac[t - 1])
                for d in range(CD):
                    bank = ps_main if d < 4 else ps_aux
                    OP("tensor", lambda e, d=d, k=k, wslot=wslot,
                       qslot=qslot, bank=bank: e.matmul(
                           bank[:, (d % 4) * TT:(d % 4 + 1) * TT],
                           kb.bfv(wslot + d * 128, 128), kb.bfv(qslot, TT),
                           start=(k == 0), stop=(k == CH - 1),
                           skip_group_check=True),
                       "pe" if d == CD - 1 else None)
                dn_pe[(t, k)] = kb.cnt["pe"]
            W("vector", "pe", kb.cnt["pe"])
            for d in range(CD):
                bank = ps_main if d < 4 else ps_aux
                if not final:
                    OP("vector", lambda e, d=d, t=t, bank=bank:
                       e.scalar_tensor_tensor(
                           kb.bfv(O_X + d * S + t * TT, TT),
                           bank[:, (d % 4) * TT:(d % 4 + 1) * TT],
                           c["sxw_d"][:, d:d + 1],
                           kb.bfv(O_X + d * S + t * TT, TT),
                           ALU.mult, ALU.add), "vec")
                    DR("vector")
                else:
                    pk = (d - 2) % 8
                    if (t > 0 or d >= 2) and pk in out_st:
                        kb.wtok("vector", out_st[pk])
                    OP("vector", lambda e, d=d, t=t, bank=bank:
                       e.scalar_tensor_tensor(
                           kb.f32v(FL + (d % 2) * TT, TT),
                           bank[:, (d % 4) * TT:(d % 4 + 1) * TT],
                           c["sxw_d"][:, d:d + 1],
                           kb.bfv(O_X + d * S + t * TT, TT),
                           ALU.mult, ALU.add), "vec")
                    DR("vector")
                    W("sync", "vec", kb.cnt["vec"])
                    out_st[d] = kb.dma(
                        "sync", lambda e, d=d, t=t: e.dma_start(
                            out=kb.d_out[d * 128:(d + 1) * 128,
                                         t * TT:(t + 1) * TT],
                            in_=kb.f32v(FL + (d % 2) * TT, TT)), "out")
            x_evac[t] = kb.cnt["vec"]

    out_st = {}

    def mamba():
        layernorm(O_X, O_LN)
        W("sync", "vec", kb.cnt["vec"])
        for d in range(CD):
            kb.dma("sync", lambda e, d=d: e.dma_start(
                out=kb.s_x1[d, :].rearrange("(p s) -> p s", p=128),
                in_=kb.bfv(O_X + d * S, S)), "stx")

        win_cast = []
        win_pe = {}
        win_evac = {}
        last_sz_st = [0]
        kb.swait("scalar", "stx")
        for o in range(2 * CI):
            fslot = 0 if o % 2 == 0 else 2816
            wslot = O_WS[o % 2]
            if o >= 2:
                W("sync", "scl", win_cast[o - 2])
            tok = kb.dma("sync", lambda e, o=o, fslot=fslot: e.dma_start(
                out=kb.f32[:, fslot:fslot + 1024].rearrange(
                    "p (k j) -> p k j", j=128),
                in_=di["W_in"][:, o * 128:(o + 1) * 128].rearrange(
                    "(k p) j -> p k j", p=128)), "xw")
            kb.wtok("scalar", tok)
            if o >= 2:
                W("scalar", "pe", win_pe[o - 2])
            win_cast.append(OP(
                "scalar", lambda e, fslot=fslot, wslot=wslot: e.copy(
                    kb.bfv(wslot, 1024), kb.f32[:, fslot:fslot + 1024]),
                "scl"))
            W("tensor", "scl", kb.cnt["scl"])
            for t in range(NT):
                if (o - 2, t) in win_evac:
                    W("tensor", "scl", win_evac[(o - 2, t)])
            for k in range(CD):
                for t in range(NT):
                    OP("tensor", lambda e, t=t, k=k, wslot=wslot: e.matmul(
                        ps_main[:, t * TT:(t + 1) * TT],
                        kb.bfv(wslot + k * 128, 128),
                        kb.bfv(O_LN + k * S + t * TT, TT),
                        start=(k == 0), stop=(k == CD - 1),
                        skip_group_check=True),
                       "pe" if (k == CD - 1) else None)
            win_pe[o] = kb.cnt["pe"]
            for t in range(NT):
                W("scalar", "pe", kb.cnt["pe"])
                if o < CI:
                    win_evac[(o, t)] = OP(
                        "scalar", lambda e, o=o, t=t: e.activation(
                            kb.bfv(xhp_off(o) + t * TT, TT),
                            ps_main[:, t * TT:(t + 1) * TT], AF.Identity,
                            bias=c["bxh"][:, o:o + 1]), "scl")
                else:
                    if t == 0:
                        kb.swait("scalar", "stz")
                    win_evac[(o, t)] = OP(
                        "scalar", lambda e, o=o, t=t: e.activation(
                            kb.bfv(O_STG + t * TT, TT),
                            ps_main[:, t * TT:(t + 1) * TT], AF.Silu,
                            bias=c["bz"][:, o - CI:o - CI + 1]), "scl")
            if o >= CI:
                W("sync", "scl", kb.cnt["scl"])
                kb.dma("sync", lambda e, o=o: e.dma_start(
                    out=kb.s_sz[o - CI, :].rearrange("(p s) -> p s", p=128),
                    in_=kb.bfv(O_STG, S)), "stz")

        kb.barrier(["scalar", "sync"], ["vector"])
        conv_scl = 0
        for ci in range(CI):
            xp = xhp_off(ci)
            xh = xh_off(ci)
            if conv_scl:
                W("vector", "scl", conv_scl)
            OP("vector", lambda e, xp=xp: e.tensor_copy(
                c["pad8"][:, 4:8], kb.bfv(xp, 4)))
            DR("vector")
            OP("vector", lambda e, ci=ci: e.tensor_scalar(
                kb.f32v(FA, 4), c["pad8"][:, 4:8],
                c["convw"][:, ci * 4 + 3:ci * 4 + 4], None, ALU.mult))
            OP("vector", lambda e, xp=xp, ci=ci: e.tensor_scalar(
                kb.f32v(FA + 4, S - 4), kb.bfv(xp + 4, S - 4),
                c["convw"][:, ci * 4 + 3:ci * 4 + 4], None, ALU.mult))
            for j in range(3):
                DR("vector")
                OP("vector", lambda e, j=j, ci=ci: e.scalar_tensor_tensor(
                    kb.f32v(FA, 4), c["pad8"][:, 1 + j:5 + j],
                    c["convw"][:, ci * 4 + j:ci * 4 + j + 1],
                    kb.f32v(FA, 4), ALU.mult, ALU.add))
                isl = OP("vector", lambda e, j=j, ci=ci, xp=xp:
                         e.scalar_tensor_tensor(
                             kb.f32v(FA + 4, S - 4),
                             kb.bfv(xp + 1 + j, S - 4),
                             c["convw"][:, ci * 4 + j:ci * 4 + j + 1],
                             kb.f32v(FA + 4, S - 4), ALU.mult, ALU.add),
                         "vec" if j == 2 else None)
            W("scalar", "vec", kb.cnt["vec"])
            conv_scl = OP("scalar", lambda e, xh=xh, ci=ci: e.activation(
                kb.bfv(xh, S), kb.f32v(FA, S), AF.Silu,
                bias=c["convb"][:, ci:ci + 1]), "scl")

        kb.barrier(["scalar"], ["tensor", "vector"])
        bm_evac = {}
        for t in range(NT):
            if (t - 1) in bm_evac:
                W("tensor", "scl", bm_evac[t - 1])
            for k in range(CI):
                OP("tensor", lambda e, t=t, k=k: e.matmul(
                    ps_aux[0:DS, 0:TT], c["WxB"][:, k * DS:(k + 1) * DS],
                    kb.bfv(xh_off(k) + t * TT, TT),
                    start=(k == 0), stop=False, skip_group_check=True))
            OP("tensor", lambda e: e.matmul(
                ps_aux[0:DS, 0:TT], c["bxB"][0:1, :],
                c["onesrow_b"][0:1, :], start=False, stop=True,
                skip_group_check=True), "pe")
            W("scalar", "pe", kb.cnt["pe"])
            bm_evac[t] = OP("scalar", lambda e, t=t: e.copy(
                bmatv(t * TT, TT), ps_aux[0:DS, 0:TT]), "scl")
        br_evac = {}
        br_bank_user = {}
        for s in range(DS):
            for t in range(NT):
                bank = TT + (t % 2) * TT
                if bank in br_bank_user:
                    W("tensor", "scl", br_bank_user[bank])
                W("tensor", "scl", bm_evac[t])
                OP("tensor", lambda e, s=s, t=t, bank=bank: e.matmul(
                    ps_aux[:, bank:bank + TT],
                    c["oneh"][:, s * 128:(s + 1) * 128],
                    bmatv(t * TT, TT), start=True, stop=True,
                    skip_group_check=True), "pe")
                W("scalar", "pe", kb.cnt["pe"])
                if s < NBREP_RES:
                    br_evac[(s, t)] = OP(
                        "scalar", lambda e, s=s, t=t, bank=bank: e.copy(
                            kb.bfv(brep_off(s) + t * TT, TT),
                            ps_aux[:, bank:bank + TT]), "scl")
                else:
                    if s > NBREP_RES or t > 0:
                        kb.swait("scalar", "stx")
                    br_evac[(s, t)] = OP(
                        "scalar", lambda e, s=s, t=t, bank=bank: e.copy(
                            kb.bfv(O_STG + t * TT, TT),
                            ps_aux[:, bank:bank + TT]), "scl")
                br_bank_user[bank] = br_evac[(s, t)]
            if s >= NBREP_RES:
                W("sync", "scl", kb.cnt["scl"])
                kb.dma("sync", lambda e, s=s: e.dma_start(
                    out=kb.s_brep[s - NBREP_RES, :].rearrange(
                        "(p s) -> p s", p=128),
                    in_=kb.bfv(O_STG, S)), "stx")
        kb.barrier(["scalar", "sync", "vector"],
                   ["vector", "tensor", "sync"])

        kb.swait("sync", "stw")
        wdt_pe = {}
        sp_evac = {}
        scan_marks = {}
        add_marks = {}
        gat_marks = {}
        for ci in range(CI):
            if (ci - 1) in wdt_pe:
                W("sync", "pe", wdt_pe[ci - 1])
            tok = kb.dma("sync", lambda e, ci=ci: e.dma_start(
                out=kb.bfv(O_WDT, 2048).rearrange("p (k j) -> p k j", j=128),
                in_=kb.s_wdtb[:, ci * 128:(ci + 1) * 128].rearrange(
                    "(k p) j -> p k j", p=128)), "xw")
            kb.wtok("tensor", tok)
            if (ci - 1) in sp_evac:
                W("tensor", "scl", sp_evac[ci - 1])
            for k in range(CI):
                for t in range(NT):
                    OP("tensor", lambda e, t=t, k=k: e.matmul(
                        ps_main[:, t * TT:(t + 1) * TT],
                        kb.bfv(O_WDT + k * 128, 128),
                        kb.bfv(xh_off(k) + t * TT, TT),
                        start=(k == 0), stop=(k == CI - 1),
                        skip_group_check=True),
                       "pe" if (k == CI - 1) else None)
            wdt_pe[ci] = kb.cnt["pe"]
            W("scalar", "pe", kb.cnt["pe"])
            if ci > 0:
                g_last = ci * DS - 1
                W("scalar", "vec", scan_marks[g_last])
            for t in range(NT):
                OP("scalar", lambda e, t=t, ci=ci: e.activation(
                    kb.f32v(FL + t * TT, TT),
                    ps_main[:, t * TT:(t + 1) * TT], AF.Abs,
                    bias=c["bdt"][:, ci:ci + 1]))
                DR("scalar")
                OP("scalar", lambda e, t=t: e.activation(
                    kb.f32v(FL + t * TT, TT), kb.f32v(FL + t * TT, TT),
                    AF.Exp, scale=-1.0))
                DR("scalar")
                OP("scalar", lambda e, t=t: e.activation(
                    kb.f32v(FL + t * TT, TT), kb.f32v(FL + t * TT, TT),
                    AF.Ln, bias=1.0))
                DR("scalar")
                OP("scalar", lambda e, t=t, ci=ci: e.activation(
                    kb.f32v(FB + t * TT, TT),
                    ps_main[:, t * TT:(t + 1) * TT], AF.Relu,
                    bias=c["bdt"][:, ci:ci + 1]),
                   "scl" if t == NT - 1 else None)
            sp_evac[ci] = kb.cnt["scl"]
            W("vector", "scl", kb.cnt["scl"])
            OP("vector", lambda e: e.tensor_add(
                kb.f32v(FL, S), kb.f32v(FL, S), kb.f32v(FB, S)))
            DR("vector")
            kb.swait("vector", "sty")
            OP("vector", lambda e, ci=ci: e.tensor_mul(
                kb.bfv(O_DU, S), kb.f32v(FL, S), kb.bfv(xh_off(ci), S)))
            OP("vector", lambda e, ci=ci: e.tensor_scalar(
                kb.f32v(FY, S), kb.bfv(xh_off(ci), S),
                c["D"][:, ci:ci + 1], None, ALU.mult), "vec")
            DR("vector")
            delta_mark = kb.cnt["vec"]
            sz_tok = {}
            for t in range(NT):
                szslot = O_SZ[t % len(O_SZ)]
                if (ci - 1, t) in gat_marks:
                    W("sync", "vec", gat_marks[(ci - 1, t)])
                sz_tok[t] = kb.dma(
                    "sync", lambda e, ci=ci, t=t, szslot=szslot:
                    e.dma_start(out=kb.bfv(szslot, TT),
                                in_=stile(kb.s_sz, ci, t)), "szl")
            bs_tok = {}
            for sp in (NBREP_RES, NBREP_RES + 1):
                W("sync", "vec", kb.cnt["vec"])
                bs_tok[sp] = kb.dma(
                    "sync", lambda e, sp=sp, bsrc=O_BS[sp % 2]: e.dma_start(
                        out=kb.bfv(bsrc, S),
                        in_=kb.s_brep[sp - NBREP_RES, :].rearrange(
                            "(p s) -> p s", p=128)), "bs")
            for s in range(DS):
                g = ci * DS + s
                p = g // 2
                dslot = FA if s % 2 == 0 else FA2
                hslot = HS[g % 4]
                W("scalar", "vec", delta_mark)
                if g >= 2:
                    W("scalar", "vec", scan_marks[g - 2])
                da = OP("scalar", lambda e, ci=ci, s=s, dslot=dslot:
                        e.activation(
                            kb.f32v(dslot, S), kb.f32v(FL, S), AF.Exp,
                            scale=c["negA"][:, ci * DS + s:ci * DS + s + 1]),
                        "scl")
                if s < NBREP_RES:
                    bsrc = brep_off(s)
                else:
                    bsrc = O_BS[s % 2]
                    kb.wtok("vector", bs_tok[s])
                OP("vector", lambda e, bsrc=bsrc: e.tensor_mul(
                    kb.bfv(O_DBU, S), kb.bfv(O_DU, S), kb.bfv(bsrc, S)))
                DR("vector")
                W("vector", "scl", da)
                if g % 2 == 0 and p >= 2:
                    kb.wtok("vector", add_marks[p - 2])
                scan_marks[g] = OP(
                    "vector", lambda e, dslot=dslot, hslot=hslot:
                    e.tensor_tensor_scan(
                        kb.f32v(hslot, S), kb.f32v(dslot, S),
                        kb.bfv(O_DBU, S), 0.0, ALU.mult, ALU.add), "vec")
                DR("vector")
                if g % 2 == 1:
                    fslot = HS[(g - 1) % 4]
                    scan_marks[g] = OP(
                        "vector", lambda e, fslot=fslot, hslot=hslot:
                        e.tensor_add(
                            kb.f32v(fslot, S), kb.f32v(fslot, S),
                            kb.f32v(hslot, S)), "vec")
                    DR("vector")
                    W("gpsimd", "vec", scan_marks[g])
                    if p >= 1:
                        kb.wtok("gpsimd", add_marks[p - 1])
                    add_marks[p] = kb.dma(
                        "gpsimd", lambda e, fslot=fslot: e.dma_start(
                            out=kb.f32v(FY, S), in_=kb.f32v(fslot, S),
                            accum_op=ALU.add), "yacc")
                if NBREP_RES <= s <= DS - 3:
                    W("sync", "vec", scan_marks[g])
                    bs_tok[s + 2] = kb.dma(
                        "sync", lambda e, s=s, bsrc=O_BS[s % 2]: e.dma_start(
                            out=kb.bfv(bsrc, S),
                            in_=kb.s_brep[s + 2 - NBREP_RES, :].rearrange(
                                "(p s) -> p s", p=128)), "bs")
            kb.wtok("vector", add_marks[(ci * DS + DS - 1) // 2])
            for t in range(NT):
                szslot = O_SZ[t % len(O_SZ)]
                kb.wtok("vector", sz_tok[t])
                gat_marks[(ci, t)] = OP(
                    "vector", lambda e, t=t, szslot=szslot: e.tensor_mul(
                        kb.bfv(O_DU + t * TT, TT), kb.f32v(FY + t * TT, TT),
                        kb.bfv(szslot, TT)), "vec")
                DR("vector")
            W("sync", "vec", kb.cnt["vec"])
            kb.dma("sync", lambda e, ci=ci: e.dma_start(
                out=kb.s_yg[ci, :].rearrange("(p s) -> p s", p=128),
                in_=kb.bfv(O_DU, S)), "sty")
        kb.barrier(["vector", "sync", "scalar"],
                   ["sync", "tensor", "vector"])

        wo_pe = {}
        wo_evac = {}
        for t in range(NT):
            for k in range(CI):
                wslot = O_WS[k % 2]
                ygslot = O_T0 if k % 2 == 0 else O_T1
                prev = (t, k - 2) if k >= 2 else (t - 1, CI - 2 + k)
                if prev in wo_pe:
                    W("sync", "pe", wo_pe[prev])
                tokw = kb.dma("sync", lambda e, k=k, wslot=wslot:
                              e.dma_start(
                                  out=kb.bfv(wslot, 1024),
                                  in_=kb.s_woutb[k * 128:(k + 1) * 128, :]),
                              "wg")
                toky = kb.dma("sync", lambda e, k=k, t=t, ygslot=ygslot:
                              e.dma_start(out=kb.bfv(ygslot, TT),
                                          in_=stile(kb.s_yg, k, t)), "ygl")
                kb.wtok("tensor", tokw)
                kb.wtok("tensor", toky)
                if k == 0 and (t - 1) in wo_evac:
                    W("tensor", "vec", wo_evac[t - 1])
                for d in range(CD):
                    bank = ps_main if d < 4 else ps_aux
                    OP("tensor", lambda e, d=d, k=k, wslot=wslot,
                       ygslot=ygslot, bank=bank: e.matmul(
                           bank[:, (d % 4) * TT:(d % 4 + 1) * TT],
                           kb.bfv(wslot + d * 128, 128),
                           kb.bfv(ygslot, TT),
                           start=(k == 0), stop=(k == CI - 1),
                           skip_group_check=True),
                       "pe" if d == CD - 1 else None)
                wo_pe[(t, k)] = kb.cnt["pe"]
            W("vector", "pe", kb.cnt["pe"])
            for d in range(CD):
                xslot = O_T2 if d % 2 == 0 else O_T3
                W("sync", "vec", kb.cnt["vec"])
                tok = kb.dma("sync", lambda e, d=d, t=t, xslot=xslot:
                             e.dma_start(out=kb.bfv(xslot, TT),
                                         in_=stile(kb.s_x1, d, t)), "x1l")
                kb.wtok("vector", tok)
                bank = ps_main if d < 4 else ps_aux
                OP("vector", lambda e, d=d, t=t, xslot=xslot, bank=bank:
                   e.scalar_tensor_tensor(
                       kb.bfv(O_X + d * S + t * TT, TT),
                       bank[:, (d % 4) * TT:(d % 4 + 1) * TT],
                       c["bout"][:, d:d + 1],
                       kb.bfv(xslot, TT), ALU.add, ALU.add), "vec")
                DR("vector")
            wo_evac[t] = kb.cnt["vec"]
        kb.barrier(["vector", "sync", "gpsimd"],
                   ["scalar", "tensor", "vector", "sync"])

    layernorm(O_X, O_LN)
    exchange("clipx", NT * CD)
    quantize(O_LN, O_XQ, CD)
    _xq1 = kb.cnt["vec"]
    p0c_weights()
    kb.close_p0_psum()
    ps_aux = kb.ps_aux
    ffn(final=False, xq_ready=_xq1)
    p0c_casts()
    kb.barrier(["vector", "scalar", "sync", "tensor"],
               ["vector", "scalar", "sync", "tensor"])
    mamba()
    ffn(final=True)
    kb.swait_all_dma("sync")


_BUILD_CACHE = {}


def _get_build(S, n_cores):
    key = (S, n_cores)
    if key not in _BUILD_CACHE:
        _BUILD_CACHE[key] = build(S, n_cores)
    return _BUILD_CACHE[key]


def _pack_rows(v, nch):
    return np.ascontiguousarray(np.asarray(v, np.float32).reshape(nch, 128).T)


def make_inputs(xc, w, S):
    bf = ml_dtypes.bfloat16
    Alog = np.asarray(w["L0_attn_A_log"], np.float32)
    convw = np.asarray(w["L0_attn_conv_w"], np.float32)[:, 0, :]
    one = {}
    one["x"] = np.ascontiguousarray(np.asarray(xc, np.float32).T)
    one["W_in"] = np.ascontiguousarray(np.asarray(w["L0_attn_W_in"],
                                                  np.float32))
    one["W_dt"] = np.ascontiguousarray(np.asarray(w["L0_attn_W_dt"],
                                                   np.float32))
    one["W_out"] = np.ascontiguousarray(np.asarray(w["L0_attn_W_out"],
                                                    np.float32))
    Wx = np.asarray(w["L0_attn_W_x"], np.float32)
    one["WxB"] = np.ascontiguousarray(
        Wx[:, DS:].reshape(CI, 128, DS).transpose(1, 0, 2)
        .reshape(128, CI * DS))
    one["bxB"] = np.ascontiguousarray(
        np.asarray(w["L0_attn_b_x"], np.float32)[DS:].reshape(1, DS))
    one["w_gate"] = np.ascontiguousarray(np.asarray(w["L0_ffn_w_gate"],
                                                     np.float32))
    one["w_up"] = np.ascontiguousarray(np.asarray(w["L0_ffn_w_up"],
                                                   np.float32))
    one["w_down"] = np.ascontiguousarray(np.asarray(w["L0_ffn_w_down"],
                                                     np.float32))
    one["Alog"] = np.ascontiguousarray(
        Alog.reshape(CI, 128, DS).transpose(1, 0, 2).reshape(128, CI * DS))
    one["convw"] = np.ascontiguousarray(
        convw.reshape(CI, 128, DC).transpose(1, 0, 2).reshape(128, CI * DC))
    one["convb"] = _pack_rows(w["L0_attn_conv_b"], CI)
    one["Dvec"] = _pack_rows(w["L0_attn_D"], CI)
    bin_ = np.asarray(w["L0_attn_b_in"], np.float32)
    one["bxh"] = _pack_rows(bin_[:DI], CI)
    one["bz"] = _pack_rows(bin_[DI:], CI)
    one["bdt"] = _pack_rows(w["L0_attn_b_dt"], CI)
    one["bout"] = _pack_rows(w["L0_attn_b_out"], CD)
    one["clipx"] = np.asarray(
        [[float(np.asarray(w["L0_ffn_clip_x"]).reshape(-1)[0]), 0.0]],
        np.float32)
    one["clipd"] = np.asarray(
        [[float(np.asarray(w["L0_ffn_clip_down"]).reshape(-1)[0]), 0.0]],
        np.float32)
    one["eyeb"] = np.eye(128, dtype=bf)
    one["eyef"] = np.eye(128, dtype=np.float32)
    one["onesb"] = np.ones((128, 2), dtype=bf)
    one["ones1"] = np.ones((1, 128), np.float32)
    one["onesrow_b"] = np.ones((1, TT), dtype=bf)
    cv = np.zeros((128, 8), np.float32)
    cv[:, 0] = LN_EPS
    cv[:, 1] = 1.0
    cv[:, 2] = -1.0
    cv[:, 3] = -0.5
    one["cvals"] = cv
    oneh = np.zeros((DS, DS * 128), dtype=bf)
    for s in range(DS):
        oneh[s, s * 128:(s + 1) * 128] = 1
    one["oneh"] = oneh
    return one


def kernel(**inputs):
    x = np.asarray(inputs["L0_attn_x"], np.float32)
    S = x.shape[1]
    kb = _get_build(S, NCORE)
    in_maps = [make_inputs(x[cidx], inputs, S) for cidx in range(NCORE)]
    res = run_bass_kernel_spmd(kb.nc, in_maps, core_ids=list(range(NCORE)))
    outs = [np.asarray(r["out"]).T for r in res.results]
    return np.stack(outs, axis=0).astype(np.float32)
